# version 06
# baseline (speedup 1.0000x reference)
import numpy as np
import concourse.bass as bass
import concourse.mybir as mybir

F32 = mybir.dt.float32
F32R = mybir.dt.float32r
BF16 = mybir.dt.bfloat16
I32 = mybir.dt.int32
AF = mybir.ActivationFunctionType
ALU = mybir.AluOpType
AX = mybir.AxisListType

D = 1024
S = 2048
H = 16
DK = 64
DE = 128
NDC = D // 128
NSC = S // 512
NST = S // 128
FFI = 2816
SCALE = 0.125
EPS = 1e-5


class KB:

    def __init__(self, nc):
        self.nc = nc
        self.items = []
        self.sems = {}
        self.cnt = {}
        self.guard = {}
        self.ready = {}
        self._waited = {}
        self._es = None

    def sem(self, name):
        if name not in self.sems:
            h = self.nc.semaphore(name)
            self.sems[name] = h.__enter__()
            self.cnt[name] = 0
        return self.sems[name]

    def inc(self, name, amount=1):
        self.sem(name)
        self.cnt[name] += amount
        return self.cnt[name]

    def now(self, name):
        self.sem(name)
        return self.cnt[name]

    def add(self, engine, fn):
        self.items.append((engine, fn))

    def set_guard(self, res, semname, val):
        d = self.guard.setdefault(res, {})
        d[semname] = max(d.get(semname, 0), val)

    def clear_guard(self, res):
        self.guard.pop(res, None)

    def get_guard(self, res):
        return list(self.guard.get(res, {}).items())

    def mark(self, res, semname, val):
        self.ready[res] = (semname, val)

    def need(self, *res):
        out = []
        for r in res:
            g = self.ready.get(r)
            if g:
                out.append(g)
        return out

    def vchain(self, steps, waits0=()):
        prev = None
        for i, emit in enumerate(steps):
            w = list(waits0) if i == 0 else [("vC", prev)]
            prev = self.op("vector", w, emit, ("vC", 1))
        return prev

    def op(self, engine, waits, emit, inc=None):
        newval = None
        incsem = None
        incamt = 0
        if inc is not None:
            name, incamt = (inc[0], inc[1]) if len(inc) > 1 else (inc[0], 1)
            newval = self.inc(name, incamt)
            incsem = self.sems[name]
        wl = []
        wmax = self._waited.setdefault(engine, {})
        for n, v in waits:
            if v and v > wmax.get(n, 0):
                wmax[n] = v
                wl.append((self.sem(n), v))

        def fn(eng, wl=wl, emit=emit, incsem=incsem, incamt=incamt):
            for s, v in wl:
                eng.wait_ge(s, v)
            ins = emit(eng)
            if incsem is not None and ins is not None:
                ins.then_inc(incsem, incamt)
        self.add(engine, fn)
        return newval

    def finalize(self):
        self._lp = self.nc.allow_low_precision("f32r intermediates are intentional")
        self._lp.__enter__()
        block = self.nc.Block()
        block.__enter__()
        by_eng = {}
        for e, fn in self.items:
            by_eng.setdefault(e, []).append(fn)
        for e, fns in by_eng.items():
            def body(engine, fns=fns):
                for f in fns:
                    f(engine)
            getattr(block, e)(body)
        block.__exit__(None, None, None)
        self._lp.__exit__(None, None, None)


def fn_engine(e):
    return e


def ceil_div(a, b):
    return (a + b - 1) // b



def chunk_T(w):
    w = np.asarray(w)
    d, f = w.shape
    t = w.reshape(d // 128, 128, f // 128, 128)
    t = t.transpose(2, 0, 1, 3)
    return np.ascontiguousarray(t)


"""Static SBUF / PSUM allocation shared by all sublayer emitters."""


def alloc_bufs(kb, es):
    nc = kb.nc
    B = {}

    def sb(name, shape, dtype):
        h = es.enter_context(nc.sbuf_tensor(name, shape, dtype))
        B[name] = h
        return h

    sb("sx", [128, NDC * S], F32R)
    sb("kt", [128, NDC * S], BF16)
    sb("qt", [128, NDC * S], BF16)
    sb("vv", [128, NST * (H * 65 + 0)], BF16)
    sb("cx", [128, S], BF16)
    sb("ao", [128, NDC * 512], BF16)
    sb("at", [128, 4 * 512], BF16)
    sb("wa", [128, 2 * 1024], F32R)
    sb("wc", [128, 2 * 1024], BF16)
    sb("sq", [128, 2 * 512], F32R)
    sb("rr", [1, 512], F32R)
    sb("br", [64, 512], F32R)
    sb("rows", [1, 3 * 512], F32R)
    sb("onesc", [128, 1], F32R)
    sb("ones1", [1, 128], F32R)
    sb("ident_b", [128, 128], BF16)
    sb("ident_r", [128, 128], F32R)
    sb("ident_f", [128, 128], F32)
    sb("am8", [128, 8], F32)
    sb("onesb", [128, 16], BF16)
    sb("scl", [128, 4], F32)
    sb("biasb", [128, 8], F32)
    sb("am2", [128, 2], F32)
    sb("srow", [1, 160], F32)

    banks = {}
    for n in ("ps0a", "ps1a", "ps0b", "ps1b", "po0", "po1", "pu"):
        banks[n] = es.enter_context(nc.psum_tensor(n, [128, 512], F32))
    banks["pvB"] = es.enter_context(nc.psum_tensor("pvB", [128, 1024], BF16))
    B["banks"] = banks
    return B


def emit_vv_ones(kb, B):
    v1 = None
    for st in range(NST):
        dst = B["vv"][:, st * 1040: (st + 1) * 1040]             .rearrange("p (h c) -> p h c", c=65)[:, :, 64:65]
        src = B["onesb"][:, :].rearrange("p (h o) -> p h o", o=1)
        v1 = kb.op("vector", kb.get_guard("vv") if st == 0 else [],
                   lambda e, dst=dst, src=src: e.tensor_copy(dst, src),
                   ("vC", 1) if st == NST - 1 else None)
    return v1


def emit_setup(kb, B):
    gv = kb.op("gpsimd", [], lambda e: e.memset(B["ident_f"][:, :], 1.0), ("gP", 1))
    v1 = kb.op("vector", [("gP", gv)],
               lambda e: e.tensor_copy(B["onesb"][:, :], B["ident_f"][:, 0:16]), None)
    v1 = kb.op("vector", [],
               lambda e: e.tensor_copy(B["onesc"][:, :], B["ident_f"][:, 0:1]), None)
    v1 = kb.op("vector", [],
               lambda e: e.tensor_copy(B["ones1"][:, :], B["ident_f"][0:1, :]), ("vC", 1))
    v1 = emit_vv_ones(kb, B)
    gv = kb.op("gpsimd", [("vC", v1)], lambda e: e.memset(B["ident_f"][:, :], 0.0), None)
    gv = kb.op("gpsimd", [],
               lambda e: e.affine_select(
                   out=B["ident_f"][:, :], in_=B["ident_f"][:, :],
                   compare_op=ALU.not_equal, fill=1.0, base=0,
                   pattern=[[-1, 128]], channel_multiplier=1),
               ("gP", 1))
    v1 = kb.op("vector", [("gP", gv)],
               lambda e: e.tensor_copy(B["ident_b"][:, :], B["ident_f"][:, :]), None)
    v1 = kb.op("vector", [],
               lambda e: e.tensor_copy(B["ident_r"][:, :], B["ident_f"][:, :]), ("vC", 1))
    kb.mark("consts", "vC", v1)
    kb.mark("consts_g", "gP", gv)


"""LayerNorm over the feature dim, in place on bufs['sx'] ([d,seq] layout).

Stats via ones-matmuls over partitions (accumulated across the 8 d-chunks),
row math on [1,512] vectors, rstd = Exp(-0.5*Ln(var+eps)) (one act table),
broadcast back via rank-1 matmuls, normalize with two TT passes per chunk.
Marks ("sx", dc) and ("sx", dc, "ln") when chunk dc is fully normalized.
"""


def emit_ln(kb, B, rdeps, dbg=None):
    sx, sq = B["sx"], B["sq"]
    rows = B["rows"]
    banks = B["banks"]
    st_name, st = "po1", banks["po1"]
    s2_name, st2 = "pu", banks["pu"]
    b1n, b1 = "ps0a", banks["ps0a"]
    b2n, b2 = "ps1a", banks["ps1a"]
    onesc, ones1 = B["onesc"], B["ones1"]
    lastv = {}

    def sxsl(dc, c0, n):
        return sx[:, dc * S + c0: dc * S + c0 + n]

    for sc in range(NSC):
        tv = None
        for dc in range(NDC):
            waits = [w for w in [kb.ready.get(("sx", dc, "r", sc))] if w]
            waits = [(n, v) for (n, v) in waits]
            if dc == 0:
                waits = waits + kb.get_guard(st_name) + kb.need("consts")
            tv = kb.op("tensor", waits,
                       lambda e, dc=dc, sc=sc: e.matmul(
                           st[0:1, :], onesc[:, 0:1], sxsl(dc, sc * 512, 512),
                           start=(dc == 0), stop=(dc == NDC - 1)),
                       ("tP", 1) if dc == NDC - 1 else None)
        for dc in range(NDC):
            slot = dc % 2
            waits = kb.get_guard(("sq", slot))
            w2 = kb.ready.get(("sx", dc, "r", sc))
            if w2:
                waits = waits + [w2]
            sv = kb.op("scalar", waits,
                       lambda e, dc=dc, sc=sc, slot=slot: e.activation(
                           sq[:, slot * 512:(slot + 1) * 512],
                           sxsl(dc, sc * 512, 512).bitcast(F32), AF.Square),
                       ("sC", 1))
            w3 = [("sC", sv)] + (kb.get_guard(s2_name) if dc == 0 else [])
            tv = kb.op("tensor", w3,
                       lambda e, slot=slot, dc=dc: e.matmul(
                           st2[0:1, :], onesc[:, 0:1], sq[:, slot * 512:(slot + 1) * 512],
                           start=(dc == 0), stop=(dc == NDC - 1)),
                       ("tP", 1) if dc == NDC - 1 else None)
            kb.set_guard(("sq", slot), "tP", tv if tv is not None else kb.now("tP"))
        r_m = rows[:, 0:512]
        r_v = rows[:, 512:1024]
        r_t = rows[:, 1024:1536]
        rr_s = rows[:, 512:1024]
        rr_mr = rows[:, 1024:1536]
        kb.op("vector", [("tP", tv)] + kb.get_guard("rows"),
              lambda e: e.tensor_scalar(r_m, st[0:1, :], 1.0 / D, None, ALU.mult), None)
        kb.op("vector", [],
              lambda e: e.tensor_scalar(r_v, st2[0:1, :], 1.0 / D, None, ALU.mult), None)
        kb.op("vector", [], lambda e: e.tensor_tensor(r_t, r_m, r_m, ALU.mult), None)
        kb.op("vector", [], lambda e: e.tensor_tensor(r_v, r_v, r_t, ALU.subtract), None)
        vv_ = kb.op("vector", [],
                    lambda e: e.tensor_scalar(r_v, r_v, EPS, None, ALU.add), ("vC", 1))
        kb.set_guard(st_name, "vC", vv_)
        kb.set_guard(s2_name, "vC", vv_)
        sv = kb.op("scalar", [("vC", vv_)],
                   lambda e: e.activation(r_t, r_v.bitcast(F32), AF.Ln), None)
        sv = kb.op("scalar", [],
                   lambda e: e.activation(rr_s, r_t.bitcast(F32), AF.Exp, scale=-0.5),
                   ("sC", 1))
        mv = kb.op("vector", [("sC", sv)],
                   lambda e: e.tensor_tensor(rr_mr, r_m, rr_s, ALU.mult), ("vC", 1))
        if dbg is not None and sc == 0:
            dbg(kb, mv)
        bv1 = kb.op("tensor", kb.get_guard(b1n) + [("sC", sv)],
                    lambda e: e.matmul(b1[:, :], ones1[0:1, :], rr_s,
                                       start=True, stop=True), ("tP", 1))
        bv2 = kb.op("tensor", kb.get_guard(b2n) + [("vC", mv)],
                    lambda e: e.matmul(b2[:, :], ones1[0:1, :], rr_mr,
                                       start=True, stop=True), ("tP", 1))
        kb.set_guard("rows", "tP", bv2)
        nv = None
        for dc in range(NDC):
            slot = dc % 2
            waits = [("tP", bv2)] + kb.get_guard(("sq", slot))
            kb.op("vector", waits,
                  lambda e, dc=dc, sc=sc, slot=slot: e.tensor_tensor(
                      sq[:, slot * 512:(slot + 1) * 512],
                      sxsl(dc, sc * 512, 512).bitcast(F32), b1[:, :], ALU.mult),
                  None)
            nv = kb.op("vector", [],
                       lambda e, dc=dc, sc=sc, slot=slot: e.tensor_tensor(
                           sxsl(dc, sc * 512, 512),
                           sq[:, slot * 512:(slot + 1) * 512].bitcast(F32),
                           b2[:, :], ALU.subtract),
                       ("vC", 1))
            kb.set_guard(("sq", slot), "vC", nv)
            lastv[dc] = nv
        kb.set_guard(b1n, "vC", nv)
        kb.set_guard(b2n, "vC", nv)
    for dc in range(NDC):
        kb.mark(("sx", dc), "vC", lastv[dc])
        kb.mark(("sx", dc, "ln"), "vC", lastv[dc])


"""L0 attention sublayer emitter (softmax attn + ctx-influence bias).

x_out = LN(x + Attn(x)) written in place into bufs['sx'];
per-batch |x_out| partition-max column into bufs['am2'][:, b].

See common.py for layouts. psum banks (f32 [128,512] unless noted):
ps0a ps1a ps0b ps1b po0 po1 pu, pvB ([128,1024] bf16).
"""


def wcol_ap(wdram, fc):
    return wdram[fc].transpose([1, 0, 2])


def wdst_ap(buf, off, n=1024):
    return buf[:, off: off + n].rearrange("p (c k) -> p c k", k=128)


def emit_l0attn(kb, B, xt_dram, w, b, dbg=None, nc_=None, skip_ln=False, layer=0):
    sx, kt, qt, vv, cx, ao, at = B["sx"], B["kt"], B["qt"], B["vv"], B["cx"], B["ao"], B["at"]
    wa, wc, rr, br = B["wa"], B["wc"], B["rr"], B["br"]
    banks = B["banks"]
    ps = [banks[n] for n in ("ps0a", "ps1a", "ps0b", "ps1b")]
    psn = ["ps0a", "ps1a", "ps0b", "ps1b"]
    po = [banks["po0"], banks["po1"]]
    pon = ["po0", "po1"]
    pu, po1b, pvB = banks["pu"], banks["po1"], banks["pvB"]

    def sxsl(dc, c0, n):
        return sx[:, dc * S + c0: dc * S + c0 + n]

    emit_vv_ones(kb, B)

    if xt_dram is not None:
        for dc in range(NDC):
            v = kb.op("sync", kb.get_guard(("sx", dc)),
                      lambda e, dc=dc: e.dma_start(
                          out=sx[:, dc * S:(dc + 1) * S], in_=xt_dram[dc]),
                      ("dX", 16))
        for dc in range(NDC):
            kb.mark(("sx", dc), "dX", v)

    def proj_phase(wdram, evac, nfc=NDC):
        util = [("pu", pu), ("po1", po1b)]
        for fc in range(nfc):
            slot = fc % 2
            v = kb.op("sync", kb.get_guard(("wa", slot)),
                      lambda e, wdram=wdram, fc=fc, slot=slot: e.dma_start(
                          out=wdst_ap(wa, slot * 1024), in_=wcol_ap(wdram, fc)),
                      ("dA%d" % slot, 16))
            kb.mark(("wa", slot), "dA%d" % slot, v)
            for sc in range(NSC):
                j = fc * NSC + sc
                bname, bank = util[j % 2]
                tv = None
                for dc in range(NDC):
                    waits = []
                    if dc == 0:
                        waits = kb.get_guard(bname) + kb.need(("wa", slot))
                    waits = waits + kb.need(("sx", dc))
                    lhs = wa[:, slot * 1024 + dc * 128: slot * 1024 + (dc + 1) * 128]
                    rhs = sxsl(dc, sc * 512, 512)
                    last = dc == NDC - 1
                    tv = kb.op("tensor", waits,
                               lambda e, bank=bank, lhs=lhs, rhs=rhs, dc=dc, last=last: e.matmul(
                                   bank[:, :], lhs, rhs, start=(dc == 0), stop=last),
                               ("tP", 1) if last else None)
                if sc == NSC - 1:
                    kb.set_guard(("wa", slot), "tP", tv)
                evac(bname, bank, fc, sc, tv)

    def scalar_evac(dst_ap, bank, bname, tv, scale):
        v = kb.op("scalar", [("tP", tv)],
                  lambda e, dst=dst_ap, bank=bank, scale=scale: e.activation(
                      dst, bank[:, :], AF.Copy, scale=scale),
                  ("sC", 1))
        kb.set_guard(bname, "sC", v)
        return v

    if layer == 0:
        def ctx_evac(bname, bank, fc, sc, tv):
            v = scalar_evac(cx[:, sc * 512:(sc + 1) * 512], bank, bname, tv,
                            float(np.sqrt(0.1)))
            if sc == NSC - 1:
                kb.mark("cx", "sC", v)
        proj_phase(w["we"], ctx_evac, nfc=1)

    def k_evac(bname, bank, fc, sc, tv):
        v = scalar_evac(kt[:, fc * S + sc * 512: fc * S + (sc + 1) * 512], bank, bname, tv, 1.0)
        if fc == NDC - 1 and sc == NSC - 1:
            kb.mark("kt", "sC", v)
    proj_phase(w["wk"], k_evac)

    def v_evac(bname, bank, fc, sc, tv):
        v = scalar_evac(qt[:, fc * S + sc * 512: fc * S + (sc + 1) * 512], bank, bname, tv, 1.0)
        if fc == NDC - 1 and sc == NSC - 1:
            kb.mark("qtv", "sC", v)
    proj_phase(w["wv"], v_evac)

    for fc in range(NDC):
        for half in range(2):
            tv = None
            for stl in range(8):
                st = half * 8 + stl
                waits = kb.need("qtv", "consts") + (kb.get_guard("pvB") if stl == 0 else [])
                tv = kb.op("tensor", waits,
                           lambda e, fc=fc, st=st, stl=stl: e.transpose(
                               pvB[:, stl * 128:(stl + 1) * 128],
                               qt[:, fc * S + st * 128: fc * S + (st + 1) * 128],
                               B["ident_b"][:, :]),
                           ("tP", 1))
            src = pvB[:, 0:1024].rearrange("p (st h c) -> p st h c", h=2, c=64)
            dst = vv[:, :].rearrange("p (st x) -> p st x", x=1040)[
                :, half * 8:(half + 1) * 8, 2 * fc * 65: 2 * fc * 65 + 130] \
                .rearrange("p st (h c) -> p st h c", c=65)[:, :, :, 0:64]
            vcv = kb.op("vector", [("tP", tv)],
                        lambda e, src=src, dst=dst: e.tensor_copy(dst, src),
                        ("vC", 1))
            kb.set_guard("pvB", "vC", vcv)
    kb.set_guard("qtbuf", "tP", kb.now("tP"))
    kb.mark("vv", "vC", kb.now("vC"))

    if dbg:
        dbg(kb, nc_, B, "cxD", cx[:, :], kb.need("cx"))
        dbg(kb, nc_, B, "ktD", kt[:, :], kb.need("kt"))
        dbg(kb, nc_, B, "qtvD", qt[:, :], kb.need("qtv"), guard_res="qtbuf")
        dbg(kb, nc_, B, "vvD", vv[:, :], kb.need("vv"))

    def q_evac(bname, bank, fc, sc, tv):
        waits_extra = kb.get_guard("qtbuf")
        v = kb.op("scalar", [("tP", tv)] + waits_extra,
                  lambda e, fc=fc, sc=sc, bank=bank: e.activation(
                      qt[:, fc * S + sc * 512: fc * S + (sc + 1) * 512],
                      bank[:, :], AF.Copy, scale=SCALE),
                  ("sC", 1))
        kb.set_guard(bname, "sC", v)
        if fc == NDC - 1 and sc == NSC - 1:
            kb.mark("qt", "sC", v)
    proj_phase(w["wq"], q_evac)

    if dbg:
        dbg(kb, nc_, B, "qtD", qt[:, :], kb.need("qt"))

    for qc in range(NSC):
        for hp in range(NDC):
            hpo = hp * S
            qco = qc * 512
            avdone = [None, None]
            nkt = NST if layer == 0 else 4 * qc + 4
            for ktile in range(nkt):
                pr = (ktile % 2) * 2
                off = 0
                if layer == 1:
                    r = ktile - 4 * qc
                    off = max(0, r) * 128
                expv = [None, None]
                for par in (0, 1):
                    bname, bank = psn[pr + par], ps[pr + par]
                    r0, r1 = par * 64, par * 64 + 64
                    waits = kb.get_guard(bname) + kb.need("kt", "qt")
                    kb.op("tensor", waits,
                          lambda e, bank=bank, hpo=hpo, qco=qco, kt0=ktile * 128,
                                 r0=r0, r1=r1, off=off: e.matmul(
                              bank[:, off:512], kt[r0:r1, hpo + kt0: hpo + kt0 + 128],
                              qt[r0:r1, hpo + qco + off: hpo + qco + 512],
                              start=True, stop=(layer == 1),
                              tile_position=(r0, 0)),
                          ("tP", 1) if layer == 1 else None)
                tvs = [None, None]
                if layer == 0:
                    for par in (0, 1):
                        bname, bank = psn[pr + par], ps[pr + par]
                        tvs[par] = kb.op("tensor", kb.need("cx"),
                                         lambda e, bank=bank, qco=qco, kt0=ktile * 128: e.matmul(
                                             bank[:, :], cx[:, kt0: kt0 + 128],
                                             cx[:, qco: qco + 512],
                                             start=False, stop=True),
                                         ("tP", 1))
                else:
                    tvs = [kb.now("tP"), kb.now("tP")]
                for par in (0, 1):
                    bname, bank = psn[pr + par], ps[pr + par]
                    slot = pr + par
                    waits = [("tP", tvs[par])] + kb.get_guard(("at", slot))
                    ev = kb.op("scalar", waits,
                               lambda e, bank=bank, slot=slot, off=off: e.activation(
                                   at[:, slot * 512 + off:(slot + 1) * 512],
                                   bank[:, off:512], AF.Exp),
                               ("sC", 1))
                    kb.set_guard(bname, "sC", ev)
                    expv[par] = ev
                for par in (0, 1):
                    slot = pr + par
                    h = 2 * hp + par
                    waits = [("sC", expv[par])] + kb.need("vv", "consts")
                    if ktile == 0:
                        waits = waits + kb.get_guard(pon[par])
                    tv = kb.op("tensor", waits,
                               lambda e, par=par, slot=slot, h=h, kt0=ktile, off=off,
                                      last=(ktile == nkt - 1): e.matmul(
                                   po[par][0:65, off:512],
                                   vv[:, kt0 * 1040 + h * 65: kt0 * 1040 + (h + 1) * 65],
                                   at[:, slot * 512 + off:(slot + 1) * 512],
                                   start=(kt0 == 0), stop=last),
                               ("tP", 1))
                    kb.set_guard(("at", slot), "tP", tv)
                    avdone[par] = tv
            for par in (0, 1):
                h = 2 * hp + par
                waits = [("tP", avdone[par])] + kb.get_guard("rr")
                rv = kb.op("vector", waits,
                           lambda e, par=par: e.reciprocal(rr[:, :], po[par][64:65, :]),
                           ("vC", 1))
                bw = kb.get_guard("pu") + [("vC", rv)] + kb.need("consts")
                bv = kb.op("tensor", bw,
                           lambda e: e.matmul(pu[0:64, :], B["ones1"][0:1, 0:64], rr[:, :],
                                              start=True, stop=True),
                           ("tP", 1))
                kb.set_guard("rr", "tP", bv)
                cv = kb.op("vector", [("tP", bv)],
                           lambda e: e.tensor_copy(br[:, :], pu[0:64, :]),
                           ("vC", 1))
                kb.set_guard("pu", "vC", cv)
                dst = ao[par * 64: par * 64 + 64, hp * 512:(hp + 1) * 512]
                mv = kb.op("vector", [("vC", cv)] + kb.get_guard(("ao", hp, par)),
                           lambda e, par=par, dst=dst: e.tensor_tensor(
                               dst, po[par][0:64, :], br[:, :], ALU.mult),
                           ("vC", 1))
                kb.set_guard(pon[par], "vC", mv)
                kb.set_guard("br", "vC", mv)
                kb.mark(("ao", hp, par), "vC", mv)

        if dbg and qc == 0:
            v_ = dbg(kb, nc_, B, "aoD", ao[:, :],
                     kb.need(*[("ao", d_, p_) for d_ in range(NDC) for p_ in (0, 1)]))
            for d_ in range(NDC):
                for p_ in (0, 1):
                    kb.set_guard(("ao", d_, p_), "dG", v_)

        util = [("pu", pu), ("po1", po1b)]
        for fc in range(NDC):
            slot = fc % 2
            v = kb.op("sync", kb.get_guard(("wa", slot)),
                      lambda e, fc=fc, slot=slot: e.dma_start(
                          out=wdst_ap(wa, slot * 1024), in_=wcol_ap(w["wo"], fc)),
                      ("dA%d" % slot, 16))
            kb.mark(("wa", slot), "dA%d" % slot, v)
            vcw = kb.op("vector", [("dA%d" % slot, v)] + kb.get_guard(("wc", slot)),
                        lambda e, slot=slot: e.tensor_copy(
                            wc[:, slot * 1024:(slot + 1) * 1024],
                            wa[:, slot * 1024:(slot + 1) * 1024].bitcast(F32)),
                        ("vC", 1))
            kb.set_guard(("wa", slot), "vC", vcw)
            kb.mark(("wc", slot), "vC", vcw)
            bname, bank = util[fc % 2]
            tv = None
            for dc in range(NDC):
                waits = kb.need(("ao", dc, 0), ("ao", dc, 1))
                if dc == 0:
                    waits = waits + kb.get_guard(bname) + kb.need(("wc", slot))
                lhs = wc[:, slot * 1024 + dc * 128: slot * 1024 + (dc + 1) * 128]
                rhs = ao[:, dc * 512:(dc + 1) * 512]
                last = dc == NDC - 1
                tv = kb.op("tensor", waits,
                           lambda e, bank=bank, lhs=lhs, rhs=rhs, dc=dc, last=last: e.matmul(
                               bank[:, :], lhs, rhs, start=(dc == 0), stop=last),
                           ("tP", 1) if last else None)
            kb.set_guard(("wc", slot), "tP", tv)
            dst = sxsl(fc, qc * 512, 512)
            v = kb.op("vector", [("tP", tv)],
                      lambda e, dst=dst, bank=bank: e.tensor_tensor(
                          dst, bank[:, :], dst.bitcast(F32), ALU.add),
                      ("vC", 1))
            kb.set_guard(bname, "vC", v)
            kb.mark(("sx", fc, "r", qc), "vC", v)
        for dc in range(NDC):
            for par in (0, 1):
                kb.set_guard(("ao", dc, par), "tP", kb.now("tP"))

    tnow = kb.now("tP")
    for rname in ("vv", "cx", "kt", "qt"):
        kb.set_guard(rname, "tP", tnow)

    if skip_ln:
        for dc in range(NDC):
            lv = max(kb.ready[("sx", dc, "r", qc)][1] for qc in range(NSC))
            kb.mark(("sx", dc), "vC", lv)
            kb.mark(("sx", dc, "ln"), "vC", lv)
    else:
        emit_ln(kb, B, rdeps=None)

    if layer == 1:
        return
    last = None
    for dc in range(NDC):
        src = sx[:, dc * S:(dc + 1) * S]
        dst = B["am8"][:, dc:dc + 1]
        last = kb.op("vector", kb.need(("sx", dc, "ln")) +
                     ([("vC", last)] if last else []),
                     lambda e, src=src, dst=dst: e.reduce_max(
                         dst, src.bitcast(F32), axis=AX.X, apply_absolute_value=True),
                     ("vC", 1))
        kb.set_guard(("sx", dc), "vC", last)
    v = kb.vchain([
        lambda e, b=b: e.reduce_max(B["am2"][:, b:b + 1], B["am8"][:, :],
                                    axis=AX.X, apply_absolute_value=True),
        lambda e: e.engine_nop(),
    ], waits0=[("vC", last)])
    kb.mark(("am2", b), "vC", v)


"""L0 FFN sublayer: global-amax fp8(e4m3fn) quantized linear + bias, bf16
round, residual, LN.   x_out = LN(x + bf16(acc*osc + bias)) in place on sx.

Buffer reuse during this phase: vv (bitcast f32r) holds the staged weight,
qt holds the quantized weight (bf16), kt holds quantized x (bf16),
wc/at (bitcast) are elementwise scratch, ao holds bf16 ffn outputs.
scl [128, 4] f32: col0 = w-scale, col1 = x-scale, col2 = osc.
"""


def emit_fp8_quant(kb, B, src, dst, scl_col, n, extra=()):
    sqf = B["ao"][:, 0:2 * n].bitcast(F32)[:, 0:n]
    sqi = B["ao"][:, 0:2 * n].bitcast(I32)[:, 0:n]
    wcf = B["wc"][:, 0:2 * n].bitcast(F32)[:, 0:n]
    msk = B["at"][:, 0:2 * n].bitcast(I32)[:, 0:n]
    scl = B["scl"]
    g = kb.get_guard(("aoq", 0)) + kb.get_guard(("aoq", 1)) + \
        kb.get_guard("wcscr") + kb.get_guard("atscr") + list(extra)
    kb.op("vector", g + kb.need("scl"),
          lambda e: e.tensor_scalar(sqf, src.bitcast(F32),
                                    scl[:, scl_col:scl_col + 1], None, ALU.mult), None)
    kb.op("vector", [],
          lambda e: e.tensor_scalar(sqf, sqf, 448.0, None, ALU.min), None)
    kb.op("vector", [],
          lambda e: e.tensor_scalar(sqf, sqf, -448.0, None, ALU.max), None)
    kb.op("vector", [],
          lambda e: e.tensor_scalar(wcf, sqf, 24576.0, None, ALU.add), None)
    kb.op("vector", [],
          lambda e: e.tensor_scalar(wcf, wcf, 24576.0, None, ALU.subtract), None)
    kb.op("vector", [],
          lambda e: e.tensor_scalar(msk, sqi, 0x7FFFFFFF, None, ALU.bitwise_and), None)
    kb.op("vector", [],
          lambda e: e.tensor_scalar(msk, msk, 0x3C800000, None, ALU.is_lt), None)
    kb.op("vector", [],
          lambda e: e.tensor_scalar(sqi, sqi, 0x80000, None, ALU.add), None)
    kb.op("vector", [],
          lambda e: e.tensor_scalar(sqi, sqi, ~0xFFFFF, None, ALU.bitwise_and), None)
    kb.op("vector", [],
          lambda e: e.select(sqf, msk, wcf, sqf), None)
    v = kb.op("vector", [],
              lambda e: e.tensor_copy(dst, sqf), ("vC", 1))
    for r in (("aoq", 0), ("aoq", 1), "wcscr", "atscr"):
        kb.set_guard(r, "vC", v)
    return v


def emit_scale_rows(kb, B, col_amax, col_out, col_tmp=152):
    srow = B["srow"]
    return kb.vchain([
        lambda e: e.tensor_scalar(srow[0:1, col_tmp:col_tmp + 1],
                                  srow[0:1, col_amax:col_amax + 1],
                                  1e-12, None, ALU.max),
        lambda e: e.reciprocal(srow[0:1, col_tmp + 1:col_tmp + 2],
                               srow[0:1, col_tmp:col_tmp + 1]),
        lambda e: e.tensor_scalar(srow[0:1, col_out:col_out + 1],
                                  srow[0:1, col_tmp + 1:col_tmp + 2],
                                  448.0, None, ALU.mult),
        lambda e: e.tensor_scalar(srow[0:1, col_out:col_out + 1],
                                  srow[0:1, col_out:col_out + 1],
                                  448.0, None, ALU.min),
    ])


def emit_l0ffn_prep(kb, B, w8_dram, bias_dram, amaxes_dram, amscr_dram, sclscr_dram, fused_cc=None):
    nc = kb.nc
    banks = B["banks"]
    vvr = B["vv"][:, 0:16384].bitcast(F32R)
    qt, kt, sq, srow, scl = B["qt"], B["kt"], B["sq"], B["srow"], B["scl"]

    for fc in range(NDC):
        v = kb.op("sync", kb.get_guard("vv"),
                  lambda e, fc=fc: e.dma_start(
                      out=wdst_ap(vvr, fc * 1024), in_=wcol_ap(w8_dram, fc)),
                  ("dS", 16))
    kb.mark("w8stage", "dS", v)

    last = None
    for fc in range(NDC):
        last = kb.op("vector", kb.need("w8stage") + kb.get_guard("am8") +
                     ([("vC", last)] if last else []),
                     lambda e, fc=fc: e.reduce_max(
                         B["am8"][:, fc:fc + 1], vvr[:, fc * 1024:(fc + 1) * 1024].bitcast(F32),
                         axis=AX.X, apply_absolute_value=True),
                     ("vC", 1))
    v = kb.vchain([
        lambda e: e.reduce_max(sq[:, 0:1], B["am8"][:, :], axis=AX.X,
                               apply_absolute_value=True),
        lambda e: e.engine_nop(),
    ], waits0=[("vC", last)])
    v = kb.op("gpsimd", [("vC", v)],
              lambda e: e.dma_start(out=amscr_dram, in_=sq[:, 0:1].bitcast(F32)),
              ("dM", 16))
    v = kb.op("sync", [("dM", v)],
              lambda e: e.dma_start(out=srow[0:1, 0:128],
                                    in_=amscr_dram.rearrange("p o -> o p")),
              ("dR", 16))
    v = kb.vchain([
        lambda e: e.reduce_max(srow[0:1, 129:130], srow[0:1, 0:128],
                               axis=AX.X, apply_absolute_value=False),
        lambda e: e.engine_nop(),
    ], waits0=[("dR", v)])
    v_w = v

    if amaxes_dram is not None:
        v2 = kb.op("sync", [],
                   lambda e: e.dma_start(out=srow[0:1, 144:152], in_=amaxes_dram),
                   ("dR2", 16))
        v2 = kb.vchain([
            lambda e: e.reduce_max(srow[0:1, 128:129], srow[0:1, 144:152],
                                   axis=AX.X, apply_absolute_value=False),
            lambda e: e.engine_nop(),
        ], waits0=[("dR2", v2)])
    else:
        cc_in_t, cc_out_t, amscr2_dram = fused_cc
        v2 = kb.vchain([
            lambda e: e.tensor_tensor(sq[:, 1:2], B["am2"][:, 0:1],
                                      B["am2"][:, 1:2], ALU.max),
            lambda e: e.engine_nop(),
        ], waits0=kb.need(("am2", 0), ("am2", 1)))
        v2 = kb.op("gpsimd", [("vC", v2)],
                   lambda e: e.dma_start(out=amscr2_dram, in_=sq[:, 1:2].bitcast(F32)),
                   ("dM", 16))
        v2 = kb.op("sync", [("dM", v2), ("vC", v_w)],
                   lambda e: e.dma_start(out=srow[0:1, 0:128],
                                         in_=amscr2_dram.rearrange("p o -> o p")),
                   ("dR", 16))
        v2 = kb.op("vector", [("dR", v2)],
                   lambda e: e.reduce_max(srow[0:1, 130:131], srow[0:1, 0:128],
                                          axis=AX.X, apply_absolute_value=False),
                   ("vC", 1))
        v2 = kb.op("gpsimd", [("vC", v2)],
                   lambda e: e.dma_start(out=cc_in_t.ap(), in_=srow[0:1, 130:138]),
                   ("dM", 16))
        v2 = kb.op("gpsimd", [("dM", v2)],
                   lambda e: e.collective_compute(
                       "AllReduce", ALU.max, replica_groups=[list(range(8))],
                       ins=[cc_in_t.ap()], outs=[cc_out_t.ap()]),
                   ("cc", 1))
        v2 = kb.op("sync", [("cc", v2)],
                   lambda e: e.dma_start(out=srow[0:1, 128:129],
                                         in_=cc_out_t.ap()[0:1, 0:1]),
                   ("dR", 16))
        v2 = kb.op("vector", [("dR", v2)], lambda e: e.engine_nop(), ("vC", 1))

    emit_scale_rows(kb, B, 129, 131, 152)
    emit_scale_rows(kb, B, 128, 132, 154)
    kb.vchain([
        lambda e: e.reciprocal(srow[0:1, 134:135], srow[0:1, 131:132]),
        lambda e: e.reciprocal(srow[0:1, 135:136], srow[0:1, 132:133]),
        lambda e: e.tensor_tensor(srow[0:1, 133:134], srow[0:1, 134:135],
                                  srow[0:1, 135:136], ALU.mult),
    ])
    v3 = kb.op("vector", [], lambda e: e.engine_nop(), ("vC", 1))
    v3 = kb.op("gpsimd", [("vC", v3)],
               lambda e: e.dma_start(out=sclscr_dram, in_=srow[0:1, 131:135]),
               ("dM", 16))
    v3 = kb.op("sync", [("dM", v3)],
               lambda e: e.dma_start(out=scl[:, 0:4],
                                     in_=sclscr_dram.broadcast_to([128, 4])),
               ("dR", 16))
    kb.mark("scl", "dR", v3)

    for fc in range(NDC):
        v = emit_fp8_quant(kb, B, vvr[:, fc * 1024:(fc + 1) * 1024],
                           qt[:, fc * 1024:(fc + 1) * 1024], 0, 1024,
                           extra=kb.get_guard("qt"))
    kb.mark("wq8", "vC", v)
    kb.set_guard("vv", "vC", v)

    v = kb.op("sync", [],
              lambda e: e.dma_start(out=B["biasb"][:, :], in_=bias_dram),
              ("dB", 16))
    kb.mark("biasb", "dB", v)


def emit_l0ffn(kb, B, b, skip_ln=False, ln_dbg=None):
    sx, kt, qt, ao = B["sx"], B["kt"], B["qt"], B["ao"]
    banks = B["banks"]
    util = [("pu", banks["pu"]), ("po1", banks["po1"])]

    def sxsl(dc, c0, nn):
        return sx[:, dc * S + c0: dc * S + c0 + nn]

    for dc in range(NDC):
        for hf in range(2):
            v = emit_fp8_quant(
                kb, B, sxsl(dc, hf * 1024, 1024),
                kt[:, dc * S + hf * 1024: dc * S + (hf + 1) * 1024], 1, 1024,
                extra=kb.get_guard("kt") + kb.need(("sx", dc)))
    kb.mark("xq8", "vC", v)

    for fc in range(NDC):
        for sc in range(NSC):
            j = fc * NSC + sc
            bname, bank = util[j % 2]
            tv = None
            for dc in range(NDC):
                waits = kb.need("xq8", "wq8") + (kb.get_guard(bname) if dc == 0 else [])
                tv = kb.op("tensor", waits,
                           lambda e, bank=bank, fc=fc, sc=sc, dc=dc: e.matmul(
                               bank[:, :],
                               qt[:, fc * 1024 + dc * 128: fc * 1024 + (dc + 1) * 128],
                               kt[:, dc * S + sc * 512: dc * S + (sc + 1) * 512],
                               start=(dc == 0), stop=(dc == NDC - 1)),
                           ("tP", 1) if dc == NDC - 1 else None)
            dst_b = ao[:, (j % 4) * 512: (j % 4 + 1) * 512]
            v = kb.op("vector", [("tP", tv)] + kb.need("scl", "biasb") + kb.get_guard(("aoq", j % 4)),
                      lambda e, bank=bank, dst_b=dst_b, fc=fc: e.tensor_scalar(
                          dst_b, bank[:, :], B["scl"][:, 2:3], B["biasb"][:, fc:fc + 1],
                          ALU.mult, ALU.add),
                      ("vC", 1))
            kb.set_guard(bname, "vC", v)
            dst = sxsl(fc, sc * 512, 512)
            v = kb.op("vector", [],
                      lambda e, dst=dst, dst_b=dst_b: e.tensor_tensor(
                          dst, dst_b, dst.bitcast(F32), ALU.add),
                      ("vC", 1))
            kb.set_guard(("aoq", j % 4), "vC", v)
            kb.mark(("sx", fc, "r", sc), "vC", v)
    if skip_ln:
        for dc in range(NDC):
            lv = max(kb.ready[("sx", dc, "r", qq)][1] for qq in range(NSC))
            kb.mark(("sx", dc), "vC", lv)
            kb.mark(("sx", dc, "ln"), "vC", lv)
    else:
        emit_ln(kb, B, rdeps=None, dbg=ln_dbg)


"""L1 FFN: blockwise-fp4 (E2M1, block 16) quant-dequant SwiGLU MLP.

out = fp4linear(silu(fp4linear(x,wg)) * fp4linear(x,wu), wd);
x_out = LN(x + out) in place on sx.

Quantized+transposed weights are spilled to HBM scratch once (prep) and
streamed back during the matmuls (they don't fit in SBUF).

Buffer roles in this phase:
  kt  = xq4T (quantized x, transposed back to [d,seq])     [128, 8*2048] bf16
  qt  = hqT tiles (cols 0:11264) + int scratch + wdq stream
  vv  = weight staging (prep, f32r view) / gate+up rhs stream slots (4x4096)
  cx  = natural quantized staging (x / h tiles)
  ao(f32 view) = src + value scratch ; wc(f32 view) = acc scratch
  at  = silu output tiles (bf16)
  sq  = blockmax / scale / rcp rows (f32r)
"""

MIDS = [0.25, 0.75, 1.25, 1.75, 2.5, 2.5, 3.5, 3.5, 5.0, 5.0, 5.0, 5.0]


def emit_fp4_quant(kb, B, srcf, srci, dst, n, c_i32, waits0=()):
    nb = n // 16
    sq = B["sq"]
    bm = sq[:, 0:nb]
    s_c = sq[:, 64:64 + nb]
    rcp = sq[:, 128:128 + nb]
    s_h = sq[:, 192:192 + nb]
    A = B["ao"][:, 2048:2048 + 2 * n].bitcast(F32)[:, 0:n]
    A_i = B["ao"][:, 2048:2048 + 2 * n].bitcast(I32)[:, 0:n]
    AC = B["wc"][:, 0:2 * n].bitcast(F32)[:, 0:n]
    g = kb.get_guard("fp4scr") + list(waits0)
    rcp_b = rcp.rearrange("p (b o) -> p b o", o=1).broadcast_to([128, nb, 16])
    kb.vchain([
        lambda e: e.reduce_max(bm, srcf.rearrange("p (b w) -> p b w", w=16),
                               axis=AX.X, apply_absolute_value=True),
        lambda e: e.tensor_scalar(s_c, bm, 1.0 / 6.0, None, ALU.mult),
        lambda e: e.tensor_scalar(s_c, s_c, 1e-12, None, ALU.max),
        lambda e: e.reciprocal(rcp, s_c),
        lambda e: e.tensor_scalar(s_h, bm, 1.0 / 12.0, None, ALU.mult),
        lambda e: e.tensor_scalar(s_h, s_h, 0.5e-12, None, ALU.max),
        lambda e: e.tensor_scalar(c_i32, srci, 0x7FFFFFFF, None, ALU.bitwise_and),
        lambda e: e.tensor_tensor(A.rearrange("p (b w) -> p b w", w=16),
                                  c_i32.bitcast(F32).rearrange("p (b w) -> p b w", w=16),
                                  rcp_b, ALU.mult),
    ], waits0=g)
    kb.op("vector", [],
          lambda e: e.tensor_scalar(AC, A, MIDS[0], None, ALU.is_gt), None)
    for m in MIDS[1:]:
        kb.op("vector", [],
              lambda e, m=m: e.scalar_tensor_tensor(AC, A, m, AC, ALU.is_gt, ALU.add), None)
    sh_b = s_h.rearrange("p (b o) -> p b o", o=1).broadcast_to([128, nb, 16])
    kb.op("vector", [],
          lambda e: e.tensor_tensor(A.rearrange("p (b w) -> p b w", w=16),
                                    AC.rearrange("p (b w) -> p b w", w=16),
                                    sh_b, ALU.mult), None)
    kb.op("vector", [],
          lambda e: e.tensor_scalar(c_i32, srci, -0x80000000, None, ALU.bitwise_and), None)
    kb.op("vector", [],
          lambda e: e.tensor_tensor(A_i, A_i, c_i32, ALU.bitwise_or), None)
    v = kb.op("vector", [], lambda e: e.tensor_copy(dst, A), ("vC", 1))
    kb.set_guard("fp4scr", "vC", v)
    return v





def emit_l1ffn_prep(kb, B, wg_dram, wu_dram, wd_dram, spills):
    nc = kb.nc
    vv, kt, qt, pvB = B["vv"], B["kt"], B["qt"], B["banks"]["pvB"]
    vvr = vv[:, 0:16640].bitcast(F32R)

    def quant_transpose_spill(src_slot_f32r, n, dst_stage, spill_ap, qdst, skey):
        nblk = n // 128
        off = 0
        lastq = None
        while off < n:
            m = min(1024, n - off)
            lastq = emit_fp4_quant(
                kb, B, src_slot_f32r[:, off:off + m].bitcast(F32),
                src_slot_f32r[:, off:off + m].bitcast(I32),
                qdst[:, off:off + m], m,
                B["qt"][:, 11264:11264 + 2 * m].bitcast(I32)[:, 0:m])
            off += m
        done = 0
        while done < nblk:
            cnt = min(8, nblk - done)
            tv = None
            for j in range(cnt):
                waits = ([("vC", lastq)] + kb.get_guard("pvB") + kb.need("consts")) if j == 0 else []
                tv = kb.op("tensor", waits,
                           lambda e, j=j, done=done, qdst=qdst: e.transpose(
                               pvB[:, j * 128:(j + 1) * 128],
                               qdst[:, (done + j) * 128:(done + j + 1) * 128],
                               B["ident_b"][:, :]),
                           ("tP", 1))
            v = kb.op("vector", [("tP", tv)] + kb.get_guard(skey),
                      lambda e, cnt=cnt, done=done: e.tensor_copy(
                          dst_stage[:, done * 128:(done + cnt) * 128],
                          pvB[:, 0:cnt * 128]),
                      ("vC", 1))
            kb.set_guard("pvB", "vC", v)
            done += cnt
        kb.set_guard("fp4scr", "tP", kb.now("tP"))
        sv = None
        for dst_ap, c0, c1 in spill_ap:
            sv = kb.op("gpsimd", [("vC", v)],
                       lambda e, dst_ap=dst_ap, c0=c0, c1=c1: e.dma_start(
                           out=dst_ap, in_=dst_stage[:, c0:c1]),
                       ("dW", 16))
        kb.set_guard(skey, "dW", sv)
        return sv

    sv = None
    for name, wdram in (("wg", wg_dram), ("wu", wu_dram)):
        for nt in range(22):
            slot = nt % 2
            lv = kb.op("sync", kb.get_guard(("vvs", slot)),
                       lambda e, wdram=wdram, nt=nt, slot=slot: e.dma_start(
                           out=vvr[:, slot * 1024:(slot + 1) * 1024], in_=wdram[nt]),
                       ("dS%d" % slot, 16))
            src = vvr[:, slot * 1024:(slot + 1) * 1024]
            stage = kt[:, 2048 + slot * 1024: 2048 + (slot + 1) * 1024]
            qdst = kt[:, slot * 1024:(slot + 1) * 1024]
            kb.set_guard("fp4scr", "dS%d" % slot, lv)
            sv = quant_transpose_spill(
                src, 1024, stage,
                [(spills[name][j, :, nt * 128:(nt + 1) * 128], j * 128, (j + 1) * 128)
                 for j in range(8)],
                qdst, ("stg", slot))
            kb.set_guard(("vvs", slot), "vC", kb.now("vC"))
    kb.set_guard("fp4scr", "dW", kb.now("dW"))
    kb.set_guard(("vvs2", 0), "vC", kb.now("vC"))
    kb.set_guard(("vvs2", 1), "vC", kb.now("vC"))
    for dc in range(8):
        slot = dc % 2
        lv = kb.op("sync", kb.get_guard(("vvs2", slot)),
                   lambda e, dc=dc, slot=slot: e.dma_start(
                       out=vvr[:, slot * 2816:(slot + 1) * 2816], in_=wd_dram[dc]),
                   ("dS%d" % slot, 16))
        src = vvr[:, slot * 2816:(slot + 1) * 2816]
        stage = kt[:, 8192 + slot * 2816: 8192 + (slot + 1) * 2816]
        qdst = kt[:, slot * 2816 + 0: slot * 2816 + 2816]
        kb.set_guard("fp4scr", "dS%d" % slot, lv)
        sv = quant_transpose_spill(src, 2816, stage,
                                   [(spills["wd"][dc], 0, 2816)], qdst,
                                   ("stg2", slot))
        kb.set_guard(("vvs2", slot), "vC", kb.now("vC"))
    kb.mark("spilled", "dW", kb.now("dW"))
    tnow = kb.now("vC")
    for r in ("vv", "kt", "qt"):
        kb.set_guard(r, "vC", tnow)


def emit_l1ffn(kb, B, spills, b, dbg=None, sc_list=None, skip_ln=False):
    nc = kb.nc
    sx, kt, qt, vv, cx, at, ao = B["sx"], B["kt"], B["qt"], B["vv"], B["cx"], B["at"], B["ao"]
    for r in ("cx", "kt", "qt", "vv", "at"):
        for (sn, gv) in kb.get_guard(r):
            kb.set_guard("fp4scr", sn, gv)
    for dcq in range(NDC):
        for par in (0, 1):
            for (sn, gv) in kb.get_guard(("ao", dcq, par)):
                kb.set_guard("xnat", sn, gv)
                kb.set_guard("fp4scr", sn, gv)
    for (sn, gv) in kb.get_guard("qt"):
        for ntc_ in range(6):
            for st_ in range(4):
                kb.set_guard(("hqT", ntc_, st_), sn, gv)
        kb.set_guard("wdqs", sn, gv)
    banks = B["banks"]
    pvB = banks["pvB"]
    pu, po1 = banks["pu"], banks["po1"]
    gb = [banks["ps0a"], banks["ps0b"]]
    gbn = ["ps0a", "ps0b"]
    ub = [banks["ps1a"], banks["ps1b"]]
    ubn = ["ps1a", "ps1b"]

    def sxsl(dc, c0, nn):
        return sx[:, dc * S + c0: dc * S + c0 + nn]

    for st in range(NST):
        for half in range(2):
            bank = pu if half == 0 else po1
            bname = "pu" if half == 0 else "po1"
            bview = bank[:, :].bitcast(F32R)
            tv = None
            for j in range(4):
                dc = half * 4 + j
                waits = (kb.get_guard(bname) + kb.need(("sx", dc), "consts")) if j == 0 else kb.need(("sx", dc))
                tv = kb.op("tensor", waits,
                           lambda e, dc=dc, st=st, j=j, bview=bview: e.transpose(
                               bview[:, j * 128:(j + 1) * 128],
                               sx[:, dc * S + st * 128: dc * S + (st + 1) * 128],
                               B["ident_r"][:, :]),
                           ("tP", 1))
            v = kb.op("vector", [("tP", tv)] + (kb.get_guard("xnat") if half == 0 else []),
                      lambda e, half=half, bview=bview: e.tensor_copy(
                          ao[:, 0:2048].bitcast(F32)[:, half * 512:(half + 1) * 512], bview),
                      ("vC", 1))
            kb.set_guard(bname, "vC", v)
        xnat_f = ao[:, 0:2048].bitcast(F32)
        xnat_i = ao[:, 0:2048].bitcast(I32)
        qv = emit_fp4_quant(kb, B, xnat_f, xnat_i, cx[:, 0:1024], 1024,
                            qt[:, 11264:13312].bitcast(I32)[:, 0:1024])
        kb.set_guard("xnat", "vC", qv)
        tv = None
        for j in range(8):
            waits = ([("vC", qv)] + kb.get_guard("pvB")) if j == 0 else []
            tv = kb.op("tensor", waits,
                       lambda e, j=j: e.transpose(
                           pvB[:, j * 128:(j + 1) * 128],
                           cx[:, j * 128:(j + 1) * 128], B["ident_b"][:, :]),
                       ("tP", 1))
        dst = kt[:, :].rearrange("p (dc s) -> p dc s", s=S)[:, :, st * 128:(st + 1) * 128]
        v = kb.op("vector", [("tP", tv)] + (kb.get_guard("kt") if st == 0 else []),
                  lambda e, dst=dst: e.tensor_copy(dst, pvB[:, 0:1024].rearrange(
                      "p (dc s) -> p dc s", s=128)),
                  ("vC", 1))
        kb.set_guard("pvB", "vC", v)
        kb.set_guard("cxq", "vC", v)
    kb.mark("xq4", "vC", kb.now("vC"))

    NTC = [512, 512, 512, 512, 512, 256]
    for sc in (sc_list if sc_list is not None else range(NSC)):
        hq_done = None
        for ntc in range(6):
            nw = NTC[ntc]
            nnt = nw // 128
            gslot = ntc % 2
            uslot = 2 + ntc % 2
            for name, slot in (("wg", gslot), ("wu", uslot)):
                lv = None
                for kc in range(NDC):
                    w0 = (kb.get_guard(("vvq", slot)) + kb.get_guard("vv")
                          + kb.need("spilled")) if kc == 0 else []
                    lv = kb.op("sync", w0,
                               lambda e, name=name, slot=slot, ntc=ntc,
                                      kc=kc, nw=nw: e.dma_start(
                                   out=vv[:, slot * 4096 + kc * nw: slot * 4096 + (kc + 1) * nw],
                                   in_=spills[name][kc, :, ntc * 512: ntc * 512 + nw]),
                               ("dV%d" % slot, 16))
                kb.mark(("vvq", slot), "dV%d" % slot, lv)
            for st in range(4):
                stg = sc * 4 + st
                gbank, gname = gb[st % 2], gbn[st % 2]
                ubank, uname = ub[st % 2], ubn[st % 2]
                tg = tu = None
                for kc in range(NDC):
                    waits = kb.need("xq4", ("vvq", gslot)) + (kb.get_guard(gname) if kc == 0 else [])
                    tg = kb.op("tensor", waits,
                               lambda e, kc=kc, stg=stg, gslot=gslot, nw=nw, gbank=gbank: e.matmul(
                                   gbank[:, 0:nw],
                                   kt[:, kc * S + stg * 128: kc * S + (stg + 1) * 128],
                                   vv[:, gslot * 4096 + kc * nw: gslot * 4096 + (kc + 1) * nw],
                                   start=(kc == 0), stop=(kc == NDC - 1)),
                               ("tP", 1) if kc == NDC - 1 else None)
                for kc in range(NDC):
                    waits = kb.need(("vvq", uslot)) + (kb.get_guard(uname) if kc == 0 else [])
                    tu = kb.op("tensor", waits,
                               lambda e, kc=kc, stg=stg, uslot=uslot, nw=nw, ubank=ubank: e.matmul(
                                   ubank[:, 0:nw],
                                   kt[:, kc * S + stg * 128: kc * S + (stg + 1) * 128],
                                   vv[:, uslot * 4096 + kc * nw: uslot * 4096 + (kc + 1) * nw],
                                   start=(kc == 0), stop=(kc == NDC - 1)),
                               ("tP", 1) if kc == NDC - 1 else None)
                atsl = ao[:, 2048:2048 + 2 * nw].bitcast(F32)[:, 0:nw]
                hv = kb.op("scalar", [("tP", tg)] + kb.get_guard("xnat"),
                           lambda e, gbank=gbank, atsl=atsl, nw=nw: e.activation(
                               atsl, gbank[:, 0:nw], AF.Silu),
                           ("sC", 1))
                kb.set_guard(gname, "sC", hv)
                srcsl = ao[:, 0:2048].bitcast(F32)[:, 0:nw]
                hv2 = kb.op("vector", [("sC", hv), ("tP", tu)] + kb.get_guard("xnat"),
                            lambda e, atsl=atsl, srcsl=srcsl, ubank=ubank, nw=nw: e.tensor_tensor(
                                srcsl, atsl, ubank[:, 0:nw], ALU.mult),
                            ("vC", 1))
                kb.set_guard(uname, "vC", hv2)
                kb.set_guard("xnat", "vC", hv2)
                qv = emit_fp4_quant(kb, B, srcsl, ao[:, 0:2048].bitcast(I32)[:, 0:nw],
                                    cx[:, 0:nw], nw,
                                    qt[:, 11264:13312].bitcast(I32)[:, 0:nw],
                                    waits0=[("vC", hv2)])
                kb.set_guard("xnat", "vC", qv)
                tv = None
                for j in range(nnt):
                    waits = ([("vC", qv)] + kb.get_guard("pvB")) if j == 0 else []
                    tv = kb.op("tensor", waits,
                               lambda e, j=j: e.transpose(
                                   pvB[:, j * 128:(j + 1) * 128],
                                   cx[:, j * 128:(j + 1) * 128], B["ident_b"][:, :]),
                               ("tP", 1))
                dst = qt[:, :].rearrange("p (kc s) -> p kc s", s=512)[
                    :, ntc * 4: ntc * 4 + nnt, st * 128:(st + 1) * 128]
                hq_done = kb.op("vector", [("tP", tv)] + kb.get_guard(("hqT", ntc, st)),
                                lambda e, dst=dst, nnt=nnt: e.tensor_copy(
                                    dst, pvB[:, 0:nnt * 128].rearrange(
                                        "p (kc s) -> p kc s", s=128)),
                                ("vC", 1))
                kb.set_guard("pvB", "vC", hq_done)
                kb.mark(("hqT", ntc, st), "vC", hq_done)
                if st == 3:
                    kb.set_guard(("vvq", gslot), "tP", kb.now("tP"))
                    kb.set_guard(("vvq", uslot), "tP", kb.now("tP"))
        if dbg is not None and sc == 0:
            dbg(kb, "hq0D", qt[:, 0:11264],
                kb.need(*[("hqT", n_, s_) for n_ in range(6) for s_ in range(4)]),
                [("hqT", n_, s_) for n_ in range(6) for s_ in range(4)])
        util = [("pu", pu), ("po1", po1)]
        for fc in range(NDC):
            lv = kb.op("sync", kb.get_guard("wdqs") + kb.need("spilled"),
                       lambda e, fc=fc: e.dma_start(
                           out=qt[:, 13312:16128], in_=spills["wd"][fc]),
                       ("dD", 16))
            bname, bank = util[fc % 2]
            tv = None
            for kc in range(22):
                waits = [("dD", lv)] + kb.need(*[("hqT", kc // 4, stq) for stq in range(4)])
                if kc == 0:
                    waits = waits + kb.get_guard(bname)
                tv = kb.op("tensor", waits,
                           lambda e, kc=kc, bank=bank: e.matmul(
                               bank[:, :],
                               qt[:, 13312 + kc * 128: 13312 + (kc + 1) * 128],
                               qt[:, kc * 512:(kc + 1) * 512],
                               start=(kc == 0), stop=(kc == 21)),
                           ("tP", 1) if kc == 21 else None)
            kb.set_guard("wdqs", "tP", tv)
            dst = sxsl(fc, sc * 512, 512)
            v = kb.op("vector", [("tP", tv)],
                      lambda e, dst=dst, bank=bank: e.tensor_tensor(
                          dst, bank[:, :], dst.bitcast(F32), ALU.add),
                      ("vC", 1))
            kb.set_guard(bname, "vC", v)
            kb.mark(("sx", fc, "r", sc), "vC", v)
        if dbg is not None and sc == 0:
            dbg(kb, "r0D", sx[:, :],
                kb.need(*[("sx", f_, "r", 0) for f_ in range(NDC)]),
                [("sx", f_) for f_ in range(NDC)])
        for ntc in range(6):
            for st in range(4):
                kb.set_guard(("hqT", ntc, st), "tP", kb.now("tP"))
    if skip_ln:
        for dc in range(NDC):
            vals = [kb.ready[("sx", dc, "r", qq)][1] for qq in (sc_list or range(NSC))
                    if ("sx", dc, "r", qq) in kb.ready]
            kb.mark(("sx", dc), "vC", max(vals))
            kb.mark(("sx", dc, "ln"), "vC", max(vals))
    else:
        vnow = kb.now("vC")
        kb.set_guard(("sq", 0), "vC", vnow)
        kb.set_guard(("sq", 1), "vC", vnow)
        emit_ln(kb, B, rdeps=None)


"""Standalone single-sublayer launch builders (for incremental validation,
and for the UNFUSED multi-launch kernel checkpoint)."""
from contextlib import ExitStack
import numpy as np
import concourse.bass as bass


def prep_l0attn_inputs(X, Wq, Wk, Wv, Wo, We, core, n_cores=8):
    bpc = X.shape[0] // n_cores
    xs = X[core * bpc:(core + 1) * bpc]
    xt = np.ascontiguousarray(xs.transpose(0, 2, 1))
    xt = np.ascontiguousarray(xt.reshape(bpc, NDC, 128, S))
    return {
        "xt": xt,
        "wq": chunk_T(Wq.T), "wk": chunk_T(Wk.T),
        "wv": chunk_T(Wv.T), "wo": chunk_T(Wo.T), "we": chunk_T(We.T),
    }


def emit_store_sx(kb, B, xo_dram, b):
    for dc in range(NDC):
        v = kb.op("gpsimd", kb.need(("sx", dc)),
                  lambda e, dc=dc: e.dma_start(
                      out=xo_dram[dc], in_=B["sx"][:, dc * S:(dc + 1) * S].bitcast(F32)),
                  ("dG", 16))
    for dc in range(NDC):
        kb.set_guard(("sx", dc), "dG", v)
    return v


def emit_amax_local(kb, B, amscr_dram):
    sq = B["sq"]
    v = kb.vchain([
        lambda e: e.tensor_tensor(sq[:, 0:1], B["am2"][:, 0:1], B["am2"][:, 1:2], ALU.max),
        lambda e: e.engine_nop(),
    ], waits0=kb.need(("am2", 0), ("am2", 1)))
    v = kb.op("gpsimd", [("vC", v)],
              lambda e: e.dma_start(out=amscr_dram, in_=sq[:, 0:1].bitcast(F32)),
              ("dM", 16))
    v = kb.op("sync", [("dM", v)],
              lambda e: e.dma_start(out=B["srow"][0:1, 0:128],
                                    in_=amscr_dram.rearrange("p o -> o p")),
              ("dR", 16))
    v = kb.vchain([
        lambda e: e.reduce_max(B["srow"][0:1, 128:129], B["srow"][0:1, 0:128],
                               axis=AX.X, apply_absolute_value=False),
        lambda e: e.engine_nop(),
    ], waits0=[("dR", v)])
    kb.mark("amax_local", "vC", v)
    return v


def emit_fence(kb):
    sems = [(n, kb.cnt[n]) for n in list(kb.cnt) if kb.cnt[n] > 0]
    for eng in ("tensor", "vector", "scalar", "sync", "gpsimd"):
        kb.op(eng, sems, lambda e: None, None)


def emit_spill_sx_r(kb, B, dst_dram):
    v = None
    for dc in range(NDC):
        v = kb.op("gpsimd", kb.need(("sx", dc)) + kb.get_guard(("sxsp", dc)),
                  lambda e, dc=dc: e.dma_start(
                      out=dst_dram[dc], in_=B["sx"][:, dc * S:(dc + 1) * S]),
                  ("dG", 16))
    for dc in range(NDC):
        kb.set_guard(("sx", dc), "dG", v)
    return v


def emit_load_sx_r(kb, B, src_dram):
    v = None
    for dc in range(NDC):
        v = kb.op("sync", kb.get_guard(("sx", dc)),
                  lambda e, dc=dc: e.dma_start(
                      out=B["sx"][:, dc * S:(dc + 1) * S], in_=src_dram[dc]),
                  ("dX", 16))
    for dc in range(NDC):
        kb.mark(("sx", dc), "dX", v)
    return v


def emit_tap(kb, nc, B, name, src_ap, waits, guard_res=None):
    shp = list(src_ap.shape)
    t = nc.dram_tensor(name, shp, src_ap.dtype, kind="ExternalOutput").ap()
    v = kb.op("gpsimd", waits,
              lambda e, t=t, src_ap=src_ap: e.dma_start(out=t, in_=src_ap),
              ("dG", 16))
    if guard_res:
        kb.set_guard(guard_res, "dG", v)
    return v


def build_l0attn_standalone(bpc=2, debug=False, skip_ln=False):
    nc = bass.Bass("TRN2", target_bir_lowering=False, dynamic_dma_scratch_size=16384,
                   detect_race_conditions=False)
    xt = nc.dram_tensor("xt", [bpc, NDC, 128, S], F32R, kind="ExternalInput").ap()
    w = {}
    for n in ("wq", "wk", "wv", "we"):
        shp = [1, NDC, 128, 128] if n == "we" else [NDC, NDC, 128, 128]
        w[n] = nc.dram_tensor(n, shp, F32R, kind="ExternalInput").ap()
    w["wo"] = nc.dram_tensor("wo", [NDC, NDC, 128, 128], F32R, kind="ExternalInput").ap()
    xo = nc.dram_tensor("xo", [bpc, NDC, 128, S], F32, kind="ExternalOutput").ap()
    am = nc.dram_tensor("am", [1, 1], F32, kind="ExternalOutput").ap()
    amscr = nc.dram_tensor("amscr", [128, 1], F32).ap()

    es = ExitStack()
    kb = KB(nc)
    B = alloc_bufs(kb, es)
    emit_setup(kb, B)
    for b in range(bpc):
        emit_l0attn(kb, B, [xt[b, dc] for dc in range(NDC)], w, b,
                    dbg=(emit_tap if (debug and b == 0) else None), nc_=nc,
                    skip_ln=skip_ln)
        emit_store_sx(kb, B, [xo[b, dc] for dc in range(NDC)], b)
    emit_amax_local(kb, B, amscr)
    v = kb.op("gpsimd", kb.need("amax_local"),
              lambda e: e.dma_start(out=am, in_=B["srow"][0:1, 128:129]),
              ("dG", 16))
    kb.op("gpsimd", [(n, kb.now(n)) for n in
                     ("dG", "dX", "dA0", "dA1", "dS", "dR", "dR2", "dM", "dB") if kb.cnt.get(n)],
          lambda e: e.engine_nop(), None)
    kb.finalize()
    es.close()
    return nc


def prep_l0ffn_inputs(X1, weight, bias, amaxes, core, n_cores=8):
    bpc = X1.shape[0] // n_cores
    xt = X1[core * bpc:(core + 1) * bpc]
    return {
        "xt": np.ascontiguousarray(xt),
        "w8": chunk_T(np.ascontiguousarray(weight.T)),
        "bias": np.ascontiguousarray(bias.reshape(8, 128).T),
        "amaxes": np.ascontiguousarray(amaxes.reshape(1, 8).astype(np.float32)),
    }


def build_l0ffn_standalone(bpc=2, skip_ln=False, debug=False):
    nc = bass.Bass("TRN2", target_bir_lowering=False, dynamic_dma_scratch_size=16384,
                   detect_race_conditions=False)
    xt = nc.dram_tensor("xt", [bpc, NDC, 128, S], F32R, kind="ExternalInput").ap()
    w8 = nc.dram_tensor("w8", [NDC, NDC, 128, 128], F32R, kind="ExternalInput").ap()
    bias = nc.dram_tensor("bias", [128, 8], F32, kind="ExternalInput").ap()
    amaxes = nc.dram_tensor("amaxes", [1, 8], F32, kind="ExternalInput").ap()
    xo = nc.dram_tensor("xo", [bpc, NDC, 128, S], F32, kind="ExternalOutput").ap()
    amscr = nc.dram_tensor("amscr", [128, 1], F32).ap()
    sclscr = nc.dram_tensor("sclscr", [1, 4], F32).ap()

    es = ExitStack()
    kb = KB(nc)
    B = alloc_bufs(kb, es)
    emit_setup(kb, B)
    emit_l0ffn_prep(kb, B, w8, bias, amaxes, amscr, sclscr)
    for b in range(bpc):
        for dc in range(NDC):
            v = kb.op("sync", kb.get_guard(("sx", dc)),
                      lambda e, b=b, dc=dc: e.dma_start(
                          out=B["sx"][:, dc * S:(dc + 1) * S], in_=xt[b, dc]),
                      ("dX", 16))
        for dc in range(NDC):
            kb.mark(("sx", dc), "dX", v)
        rowtap = None
        if debug and b == 0:
            rt_t = nc.dram_tensor("rowsD", [1, 1536], F32, kind="ExternalOutput").ap()
            def rowtap(kb_, mv, rt_t=rt_t):
                vv2 = kb_.op("gpsimd", [("vC", mv)],
                       lambda e: e.dma_start(out=rt_t[0:1, 0:1536],
                                             in_=B["rows"][0:1, :].bitcast(F32)),
                       ("dG", 16))
                kb_.set_guard("rows", "dG", vv2)
        emit_l0ffn(kb, B, b, skip_ln=skip_ln, ln_dbg=rowtap)
        if debug and b == 0:
            emit_tap(kb, nc, B, "ktD", B["kt"][:, :], kb.need("xq8"), guard_res="kt")
            emit_tap(kb, nc, B, "qtD", B["qt"][:, 0:8192], kb.need("wq8"), guard_res="qt")
            emit_tap(kb, nc, B, "sclD", B["scl"][:, 0:3], kb.need("scl"))
            emit_tap(kb, nc, B, "srowD", B["srow"][0:1, 128:130], kb.need("scl"))
        emit_store_sx(kb, B, [xo[b, dc] for dc in range(NDC)], b)
    kb.op("gpsimd", [(n, kb.now(n)) for n in
                     ("dG", "dX", "dA0", "dA1", "dS", "dR", "dR2", "dM", "dB") if kb.cnt.get(n)],
          lambda e: e.engine_nop(), None)
    kb.finalize()
    es.close()
    return nc


def prep_l1attn_inputs(X1, Wq, Wk, Wv, Wo, core, n_cores=8):
    bpc = X1.shape[0] // n_cores
    return {
        "xt": np.ascontiguousarray(X1[core * bpc:(core + 1) * bpc]),
        "wq": chunk_T(Wq), "wk": chunk_T(Wk), "wv": chunk_T(Wv), "wo": chunk_T(Wo),
    }


def build_l1attn_standalone(bpc=2, skip_ln=False):
    nc = bass.Bass("TRN2", target_bir_lowering=False, dynamic_dma_scratch_size=16384,
                   detect_race_conditions=False)
    xt = nc.dram_tensor("xt", [bpc, NDC, 128, S], F32R, kind="ExternalInput").ap()
    w = {}
    for n in ("wq", "wk", "wv", "wo"):
        w[n] = nc.dram_tensor(n, [NDC, NDC, 128, 128], F32R, kind="ExternalInput").ap()
    xo = nc.dram_tensor("xo", [bpc, NDC, 128, S], F32, kind="ExternalOutput").ap()

    es = ExitStack()
    kb = KB(nc)
    B = alloc_bufs(kb, es)
    emit_setup(kb, B)
    for b in range(bpc):
        emit_l0attn(kb, B, [xt[b, dc] for dc in range(NDC)], w, b,
                    skip_ln=skip_ln, layer=1)
        emit_store_sx(kb, B, [xo[b, dc] for dc in range(NDC)], b)
    kb.op("gpsimd", [(n, kb.now(n)) for n in
                     ("dG", "dX", "dA0", "dA1", "dS", "dR", "dR2", "dM", "dB") if kb.cnt.get(n)],
          lambda e: e.engine_nop(), None)
    kb.finalize()
    es.close()
    return nc


def build_full_fused(bpc=2, debug=False):
    nc = bass.Bass("TRN2", target_bir_lowering=False, dynamic_dma_scratch_size=16384,
                   detect_race_conditions=False)
    xt = nc.dram_tensor("xt", [bpc, NDC, 128, S], F32R, kind="ExternalInput").ap()
    w0 = {}
    for n in ("wq", "wk", "wv", "wo", "we"):
        shp = [1, NDC, 128, 128] if n == "we" else [NDC, NDC, 128, 128]
        w0[n] = nc.dram_tensor("w0" + n, shp, F32R, kind="ExternalInput").ap()
    w8 = nc.dram_tensor("w8", [NDC, NDC, 128, 128], F32R, kind="ExternalInput").ap()
    bias = nc.dram_tensor("bias", [128, 8], F32, kind="ExternalInput").ap()
    w1 = {n: nc.dram_tensor("w1" + n, [NDC, NDC, 128, 128], F32R,
                            kind="ExternalInput").ap()
          for n in ("wq", "wk", "wv", "wo")}
    wg = nc.dram_tensor("wg", [22, 128, 1024], F32R, kind="ExternalInput").ap()
    wu = nc.dram_tensor("wu", [22, 128, 1024], F32R, kind="ExternalInput").ap()
    wd = nc.dram_tensor("wd", [8, 128, 2816], F32R, kind="ExternalInput").ap()
    xo = nc.dram_tensor("xo", [bpc, NDC, 128, S], F32, kind="ExternalOutput").ap()
    dbgk = "ExternalOutput" if debug else "Internal"
    x1scr = nc.dram_tensor("x1scr", [NDC, 128, S], F32R, kind=dbgk).ap()
    x2scr = nc.dram_tensor("x2scr", [NDC, 128, S], F32R, kind=dbgk).ap()
    if debug:
        xd2b0 = nc.dram_tensor("xd2b0", [NDC, 128, S], F32R, kind="ExternalOutput").ap()
        xd3b0 = nc.dram_tensor("xd3b0", [NDC, 128, S], F32R, kind="ExternalOutput").ap()
        xd3b1 = nc.dram_tensor("xd3b1", [NDC, 128, S], F32R, kind="ExternalOutput").ap()
    amscr = nc.dram_tensor("amscr", [128, 1], F32).ap()
    amscr2 = nc.dram_tensor("amscr2", [128, 1], F32).ap()
    sclscr = nc.dram_tensor("sclscr", [1, 4], F32).ap()
    cc_in_t = nc.dram_tensor("cc_in", [1, 8], F32)
    cc_out_t = nc.dram_tensor("cc_out", [1, 8], F32)
    spills = {n: nc.dram_tensor("sp_" + n, [8, 128, 2816], BF16).ap()
              for n in ("wg", "wu", "wd")}

    es = ExitStack()
    kb = KB(nc)
    B = alloc_bufs(kb, es)
    emit_setup(kb, B)
    emit_l1ffn_prep(kb, B, wg, wu, wd, spills)
    emit_fence(kb)
    emit_l0attn(kb, B, [xt[0, dc] for dc in range(NDC)], w0, 0)
    emit_spill_sx_r(kb, B, x1scr)
    emit_l0attn(kb, B, [xt[1, dc] for dc in range(NDC)], w0, 1)
    emit_fence(kb)
    emit_l0ffn_prep(kb, B, w8, bias, None, amscr, sclscr,
                    fused_cc=(cc_in_t, cc_out_t, amscr2))
    emit_fence(kb)
    emit_l0ffn(kb, B, 1)
    emit_fence(kb)
    emit_spill_sx_r(kb, B, x2scr)
    emit_fence(kb)
    emit_load_sx_r(kb, B, x1scr)
    emit_l0ffn(kb, B, 0)
    emit_fence(kb)
    if debug:
        emit_spill_sx_r(kb, B, xd2b0)
        emit_fence(kb)
    emit_l0attn(kb, B, None, w1, 0, layer=1)
    emit_fence(kb)
    if debug:
        emit_spill_sx_r(kb, B, xd3b0)
        emit_fence(kb)
    emit_l1ffn(kb, B, spills, 0)
    emit_fence(kb)
    emit_store_sx(kb, B, [xo[0, dc] for dc in range(NDC)], 0)
    emit_fence(kb)
    emit_load_sx_r(kb, B, x2scr)
    if debug:
        emit_tap(kb, nc, B, "sxldD", B["sx"][:, :].bitcast(F32),
                 kb.need(*[("sx", dc) for dc in range(NDC)]))
        emit_fence(kb)
    emit_l0attn(kb, B, None, w1, 1, layer=1,
                dbg=(emit_tap if debug else None), nc_=nc)
    emit_fence(kb)
    if debug:
        emit_spill_sx_r(kb, B, xd3b1)
        emit_fence(kb)
    emit_l1ffn(kb, B, spills, 1)
    emit_fence(kb)
    emit_store_sx(kb, B, [xo[1, dc] for dc in range(NDC)], 1)
    emit_fence(kb)
    kb.finalize()
    es.close()
    return nc


def prep_l1ffn_inputs(X1, wg, wu, wd, core, n_cores=8):
    bpc = X1.shape[0] // n_cores
    return {
        "xt": np.ascontiguousarray(X1[core * bpc:(core + 1) * bpc]),
        "wg": np.ascontiguousarray(wg.reshape(22, 128, 1024)),
        "wu": np.ascontiguousarray(wu.reshape(22, 128, 1024)),
        "wd": np.ascontiguousarray(wd.reshape(8, 128, 2816)),
    }


def build_l1ffn_standalone(bpc=2, skip_ln=False, debug_spills=False, debug_taps=False, sc_list=None):
    nc = bass.Bass("TRN2", target_bir_lowering=False, dynamic_dma_scratch_size=16384,
                   detect_race_conditions=False)
    xt = nc.dram_tensor("xt", [bpc, NDC, 128, S], F32R, kind="ExternalInput").ap()
    wg = nc.dram_tensor("wg", [22, 128, 1024], F32R, kind="ExternalInput").ap()
    wu = nc.dram_tensor("wu", [22, 128, 1024], F32R, kind="ExternalInput").ap()
    wd = nc.dram_tensor("wd", [8, 128, 2816], F32R, kind="ExternalInput").ap()
    xo = nc.dram_tensor("xo", [bpc, NDC, 128, S], F32, kind="ExternalOutput").ap()
    spills = {n: nc.dram_tensor("sp_" + n, [8, 128, 2816], BF16,
                                kind=("ExternalOutput" if debug_spills else "Internal")).ap()
              for n in ("wg", "wu", "wd")}

    es = ExitStack()
    kb = KB(nc)
    B = alloc_bufs(kb, es)
    emit_setup(kb, B)
    emit_l1ffn_prep(kb, B, wg, wu, wd, spills)
    for b in range(bpc):
        for dc in range(NDC):
            v = kb.op("sync", kb.get_guard(("sx", dc)),
                      lambda e, b=b, dc=dc: e.dma_start(
                          out=B["sx"][:, dc * S:(dc + 1) * S], in_=xt[b, dc]),
                      ("dX", 16))
        for dc in range(NDC):
            kb.mark(("sx", dc), "dX", v)
        dbgfn = None
        if debug_taps and b == 0:
            def dbgfn(kb_, name, ap, waits, guard_keys):
                t = nc.dram_tensor(name, list(ap.shape), ap.dtype, kind="ExternalOutput").ap()
                vv_ = kb_.op("gpsimd", waits,
                             lambda e, t=t, ap=ap: e.dma_start(out=t, in_=ap),
                             ("dG", 16))
                for kk in guard_keys:
                    kb_.set_guard(kk, "dG", vv_)
        emit_l1ffn(kb, B, spills, b, dbg=dbgfn, sc_list=sc_list, skip_ln=skip_ln)
        emit_store_sx(kb, B, [xo[b, dc] for dc in range(NDC)], b)
    kb.op("gpsimd", [(n, kb.now(n)) for n in
                     ("dG", "dX", "dA0", "dA1", "dS0", "dS1", "dR", "dR2", "dM", "dB",
                      "dW", "dV0", "dV1", "dV2", "dV3", "dD") if kb.cnt.get(n)],
          lambda e: e.engine_nop(), None)
    kb.finalize()
    es.close()
    return nc


def kernel(L0_attn_X, L0_attn_Wq, L0_attn_Wk, L0_attn_Wv, L0_attn_Wo, L0_attn_We,
           L0_ffn_weight, L0_ffn_bias, L1_attn_Wq, L1_attn_Wk, L1_attn_Wv,
           L1_attn_Wo, L1_ffn_w_gate, L1_ffn_w_up, L1_ffn_w_down):
    from concourse.bass_utils import run_bass_kernel_spmd
    n_cores = 8
    cores = list(range(n_cores))

    nc1 = build_full_fused(bpc=2)
    shared = {
        "w0wq": chunk_T(L0_attn_Wq.T), "w0wk": chunk_T(L0_attn_Wk.T),
        "w0wv": chunk_T(L0_attn_Wv.T), "w0wo": chunk_T(L0_attn_Wo.T),
        "w0we": chunk_T(L0_attn_We.T),
        "w8": chunk_T(np.ascontiguousarray(L0_ffn_weight.T)),
        "bias": np.ascontiguousarray(L0_ffn_bias.reshape(8, 128).T),
        "w1wq": chunk_T(L1_attn_Wq), "w1wk": chunk_T(L1_attn_Wk),
        "w1wv": chunk_T(L1_attn_Wv), "w1wo": chunk_T(L1_attn_Wo),
        "wg": np.ascontiguousarray(L1_ffn_w_gate.reshape(22, 128, 1024)),
        "wu": np.ascontiguousarray(L1_ffn_w_up.reshape(22, 128, 1024)),
        "wd": np.ascontiguousarray(L1_ffn_w_down.reshape(8, 128, 2816)),
    }
    ins = []
    for c in cores:
        xs = L0_attn_X[c * 2:(c + 1) * 2]
        xtc = np.ascontiguousarray(xs.transpose(0, 2, 1)).reshape(2, NDC, 128, S)
        d = {"xt": np.ascontiguousarray(xtc)}
        d.update(shared)
        ins.append(d)
    r = run_bass_kernel_spmd(nc1, ins, core_ids=cores).results

    out = np.empty((16, 2048, 1024), dtype=np.float32)
    for c in cores:
        xc = np.asarray(r[c]["xo"]).reshape(2, 1024, 2048)
        out[c * 2] = xc[0].T
        out[c * 2 + 1] = xc[1].T
    return out


# version 07
# speedup vs baseline: 1.0792x; 1.0792x over previous
import numpy as np
import concourse.bass as bass
import concourse.mybir as mybir

F32 = mybir.dt.float32
F32R = mybir.dt.float32r
BF16 = mybir.dt.bfloat16
I32 = mybir.dt.int32
AF = mybir.ActivationFunctionType
ALU = mybir.AluOpType
AX = mybir.AxisListType

D = 1024
S = 2048
H = 16
DK = 64
DE = 128
NDC = D // 128
NSC = S // 512
NST = S // 128
FFI = 2816
SCALE = 0.125
EPS = 1e-5


class KB:

    def __init__(self, nc):
        self.nc = nc
        self.items = []
        self.sems = {}
        self.cnt = {}
        self.guard = {}
        self.ready = {}
        self._waited = {}
        self._es = None

    def sem(self, name):
        if name not in self.sems:
            h = self.nc.semaphore(name)
            self.sems[name] = h.__enter__()
            self.cnt[name] = 0
        return self.sems[name]

    def inc(self, name, amount=1):
        self.sem(name)
        self.cnt[name] += amount
        return self.cnt[name]

    def now(self, name):
        self.sem(name)
        return self.cnt[name]

    def add(self, engine, fn):
        self.items.append((engine, fn))

    def set_guard(self, res, semname, val):
        d = self.guard.setdefault(res, {})
        d[semname] = max(d.get(semname, 0), val)

    def clear_guard(self, res):
        self.guard.pop(res, None)

    def get_guard(self, res):
        return list(self.guard.get(res, {}).items())

    def mark(self, res, semname, val):
        self.ready[res] = (semname, val)

    def need(self, *res):
        out = []
        for r in res:
            g = self.ready.get(r)
            if g:
                out.append(g)
        return out

    def vchain(self, steps, waits0=()):
        prev = None
        for i, emit in enumerate(steps):
            w = list(waits0) if i == 0 else [("vC", prev)]
            prev = self.op("vector", w, emit, ("vC", 1))
        return prev

    def op(self, engine, waits, emit, inc=None):
        newval = None
        incsem = None
        incamt = 0
        if inc is not None:
            name, incamt = (inc[0], inc[1]) if len(inc) > 1 else (inc[0], 1)
            newval = self.inc(name, incamt)
            incsem = self.sems[name]
        wl = []
        wmax = self._waited.setdefault(engine, {})
        for n, v in waits:
            if v and v > wmax.get(n, 0):
                wmax[n] = v
                wl.append((self.sem(n), v))

        def fn(eng, wl=wl, emit=emit, incsem=incsem, incamt=incamt):
            for s, v in wl:
                eng.wait_ge(s, v)
            ins = emit(eng)
            if incsem is not None and ins is not None:
                ins.then_inc(incsem, incamt)
        self.add(engine, fn)
        return newval

    def finalize(self):
        self._lp = self.nc.allow_low_precision("f32r intermediates are intentional")
        self._lp.__enter__()
        block = self.nc.Block()
        block.__enter__()
        by_eng = {}
        for e, fn in self.items:
            by_eng.setdefault(e, []).append(fn)
        for e, fns in by_eng.items():
            def body(engine, fns=fns):
                for f in fns:
                    f(engine)
            getattr(block, e)(body)
        block.__exit__(None, None, None)
        self._lp.__exit__(None, None, None)


def fn_engine(e):
    return e


def ceil_div(a, b):
    return (a + b - 1) // b



def chunk_T(w):
    w = np.asarray(w)
    d, f = w.shape
    t = w.reshape(d // 128, 128, f // 128, 128)
    t = t.transpose(2, 0, 1, 3)
    return np.ascontiguousarray(t)


"""Static SBUF / PSUM allocation shared by all sublayer emitters."""


def alloc_bufs(kb, es):
    nc = kb.nc
    B = {}

    def sb(name, shape, dtype):
        h = es.enter_context(nc.sbuf_tensor(name, shape, dtype))
        B[name] = h
        return h

    sb("sx", [128, NDC * S], F32R)
    sb("kt", [128, NDC * S], BF16)
    sb("qt", [128, NDC * S], BF16)
    sb("vv", [128, NST * (H * 65 + 0)], BF16)
    sb("cx", [128, S], BF16)
    sb("ao", [128, NDC * 512], BF16)
    sb("at", [128, 4 * 512], BF16)
    sb("wa", [128, 2 * 1024], F32R)
    sb("wc", [128, 2 * 1024], BF16)
    sb("sq", [128, 2 * 512], F32R)
    sb("rr", [1, 512], F32R)
    sb("br", [64, 512], F32R)
    sb("rows", [1, 3 * 512], F32R)
    sb("onesc", [128, 1], F32R)
    sb("ones1", [1, 128], F32R)
    sb("ident_b", [128, 128], BF16)
    sb("ident_r", [128, 128], F32R)
    sb("ident_f", [128, 128], F32)
    sb("am8", [128, 8], F32)
    sb("onesb", [128, 16], BF16)
    sb("scl", [128, 4], F32)
    sb("biasb", [128, 8], F32)
    sb("am2", [128, 2], F32)
    sb("srow", [1, 160], F32)

    banks = {}
    for n in ("ps0a", "ps1a", "ps0b", "ps1b", "po0", "po1", "pu"):
        banks[n] = es.enter_context(nc.psum_tensor(n, [128, 512], F32))
    banks["pvB"] = es.enter_context(nc.psum_tensor("pvB", [128, 1024], BF16))
    B["banks"] = banks
    return B


def emit_vv_ones(kb, B):
    v1 = None
    for st in range(NST):
        dst = B["vv"][:, st * 1040: (st + 1) * 1040]             .rearrange("p (h c) -> p h c", c=65)[:, :, 64:65]
        src = B["onesb"][:, :].rearrange("p (h o) -> p h o", o=1)
        v1 = kb.op("vector", kb.get_guard("vv") if st == 0 else [],
                   lambda e, dst=dst, src=src: e.tensor_copy(dst, src),
                   ("vC", 1) if st == NST - 1 else None)
    return v1


def emit_setup(kb, B):
    gv = kb.op("gpsimd", [], lambda e: e.memset(B["ident_f"][:, :], 1.0), ("gP", 1))
    v1 = kb.op("vector", [("gP", gv)],
               lambda e: e.tensor_copy(B["onesb"][:, :], B["ident_f"][:, 0:16]), None)
    v1 = kb.op("vector", [],
               lambda e: e.tensor_copy(B["onesc"][:, :], B["ident_f"][:, 0:1]), None)
    v1 = kb.op("vector", [],
               lambda e: e.tensor_copy(B["ones1"][:, :], B["ident_f"][0:1, :]), ("vC", 1))
    v1 = emit_vv_ones(kb, B)
    gv = kb.op("gpsimd", [("vC", v1)], lambda e: e.memset(B["ident_f"][:, :], 0.0), None)
    gv = kb.op("gpsimd", [],
               lambda e: e.affine_select(
                   out=B["ident_f"][:, :], in_=B["ident_f"][:, :],
                   compare_op=ALU.not_equal, fill=1.0, base=0,
                   pattern=[[-1, 128]], channel_multiplier=1),
               ("gP", 1))
    v1 = kb.op("vector", [("gP", gv)],
               lambda e: e.tensor_copy(B["ident_b"][:, :], B["ident_f"][:, :]), None)
    v1 = kb.op("vector", [],
               lambda e: e.tensor_copy(B["ident_r"][:, :], B["ident_f"][:, :]), ("vC", 1))
    kb.mark("consts", "vC", v1)
    kb.mark("consts_g", "gP", gv)


"""LayerNorm over the feature dim, in place on bufs['sx'] ([d,seq] layout).

Stats via ones-matmuls over partitions (accumulated across the 8 d-chunks),
row math on [1,512] vectors, rstd = Exp(-0.5*Ln(var+eps)) (one act table),
broadcast back via rank-1 matmuls, normalize with two TT passes per chunk.
Marks ("sx", dc) and ("sx", dc, "ln") when chunk dc is fully normalized.
"""


def emit_ln(kb, B, rdeps, dbg=None):
    sx, sq = B["sx"], B["sq"]
    rows = B["rows"]
    banks = B["banks"]
    st_name, st = "po1", banks["po1"]
    s2_name, st2 = "pu", banks["pu"]
    b1n, b1 = "ps0a", banks["ps0a"]
    b2n, b2 = "ps1a", banks["ps1a"]
    onesc, ones1 = B["onesc"], B["ones1"]
    lastv = {}

    def sxsl(dc, c0, n):
        return sx[:, dc * S + c0: dc * S + c0 + n]

    for sc in range(NSC):
        tv = None
        for dc in range(NDC):
            waits = [w for w in [kb.ready.get(("sx", dc, "r", sc))] if w]
            waits = [(n, v) for (n, v) in waits]
            if dc == 0:
                waits = waits + kb.get_guard(st_name) + kb.need("consts")
            tv = kb.op("tensor", waits,
                       lambda e, dc=dc, sc=sc: e.matmul(
                           st[0:1, :], onesc[:, 0:1], sxsl(dc, sc * 512, 512),
                           start=(dc == 0), stop=(dc == NDC - 1)),
                       ("tP", 1) if dc == NDC - 1 else None)
        for dc in range(NDC):
            slot = dc % 2
            waits = kb.get_guard(("sq", slot))
            w2 = kb.ready.get(("sx", dc, "r", sc))
            if w2:
                waits = waits + [w2]
            sv = kb.op("scalar", waits,
                       lambda e, dc=dc, sc=sc, slot=slot: e.activation(
                           sq[:, slot * 512:(slot + 1) * 512],
                           sxsl(dc, sc * 512, 512).bitcast(F32), AF.Square),
                       ("sC", 1))
            w3 = [("sC", sv)] + (kb.get_guard(s2_name) if dc == 0 else [])
            tv = kb.op("tensor", w3,
                       lambda e, slot=slot, dc=dc: e.matmul(
                           st2[0:1, :], onesc[:, 0:1], sq[:, slot * 512:(slot + 1) * 512],
                           start=(dc == 0), stop=(dc == NDC - 1)),
                       ("tP", 1) if dc == NDC - 1 else None)
            kb.set_guard(("sq", slot), "tP", tv if tv is not None else kb.now("tP"))
        r_m = rows[:, 0:512]
        r_v = rows[:, 512:1024]
        r_t = rows[:, 1024:1536]
        rr_s = rows[:, 512:1024]
        rr_mr = rows[:, 1024:1536]
        kb.op("vector", [("tP", tv)] + kb.get_guard("rows"),
              lambda e: e.tensor_scalar(r_m, st[0:1, :], 1.0 / D, None, ALU.mult), None)
        kb.op("vector", [],
              lambda e: e.tensor_scalar(r_v, st2[0:1, :], 1.0 / D, None, ALU.mult), None)
        kb.op("vector", [], lambda e: e.tensor_tensor(r_t, r_m, r_m, ALU.mult), None)
        kb.op("vector", [], lambda e: e.tensor_tensor(r_v, r_v, r_t, ALU.subtract), None)
        vv_ = kb.op("vector", [],
                    lambda e: e.tensor_scalar(r_v, r_v, EPS, None, ALU.add), ("vC", 1))
        kb.set_guard(st_name, "vC", vv_)
        kb.set_guard(s2_name, "vC", vv_)
        sv = kb.op("scalar", [("vC", vv_)],
                   lambda e: e.activation(r_t, r_v.bitcast(F32), AF.Ln), None)
        sv = kb.op("scalar", [],
                   lambda e: e.activation(rr_s, r_t.bitcast(F32), AF.Exp, scale=-0.5),
                   ("sC", 1))
        mv = kb.op("vector", [("sC", sv)],
                   lambda e: e.tensor_tensor(rr_mr, r_m, rr_s, ALU.mult), ("vC", 1))
        if dbg is not None and sc == 0:
            dbg(kb, mv)
        bv1 = kb.op("tensor", kb.get_guard(b1n) + [("sC", sv)],
                    lambda e: e.matmul(b1[:, :], ones1[0:1, :], rr_s,
                                       start=True, stop=True), ("tP", 1))
        bv2 = kb.op("tensor", kb.get_guard(b2n) + [("vC", mv)],
                    lambda e: e.matmul(b2[:, :], ones1[0:1, :], rr_mr,
                                       start=True, stop=True), ("tP", 1))
        kb.set_guard("rows", "tP", bv2)
        nv = None
        for dc in range(NDC):
            slot = dc % 2
            waits = [("tP", bv2)] + kb.get_guard(("sq", slot))
            kb.op("vector", waits,
                  lambda e, dc=dc, sc=sc, slot=slot: e.tensor_tensor(
                      sq[:, slot * 512:(slot + 1) * 512],
                      sxsl(dc, sc * 512, 512).bitcast(F32), b1[:, :], ALU.mult),
                  None)
            nv = kb.op("vector", [],
                       lambda e, dc=dc, sc=sc, slot=slot: e.tensor_tensor(
                           sxsl(dc, sc * 512, 512),
                           sq[:, slot * 512:(slot + 1) * 512].bitcast(F32),
                           b2[:, :], ALU.subtract),
                       ("vC", 1))
            kb.set_guard(("sq", slot), "vC", nv)
            lastv[dc] = nv
        kb.set_guard(b1n, "vC", nv)
        kb.set_guard(b2n, "vC", nv)
    for dc in range(NDC):
        kb.mark(("sx", dc), "vC", lastv[dc])
        kb.mark(("sx", dc, "ln"), "vC", lastv[dc])


"""L0 attention sublayer emitter (softmax attn + ctx-influence bias).

x_out = LN(x + Attn(x)) written in place into bufs['sx'];
per-batch |x_out| partition-max column into bufs['am2'][:, b].

See common.py for layouts. psum banks (f32 [128,512] unless noted):
ps0a ps1a ps0b ps1b po0 po1 pu, pvB ([128,1024] bf16).
"""


def wcol_ap(wdram, fc):
    return wdram[fc].transpose([1, 0, 2])


def wdst_ap(buf, off, n=1024):
    return buf[:, off: off + n].rearrange("p (c k) -> p c k", k=128)


def emit_l0attn(kb, B, xt_dram, w, b, dbg=None, nc_=None, skip_ln=False, layer=0):
    sx, kt, qt, vv, cx, ao, at = B["sx"], B["kt"], B["qt"], B["vv"], B["cx"], B["ao"], B["at"]
    wa, wc, rr, br = B["wa"], B["wc"], B["rr"], B["br"]
    banks = B["banks"]
    ps = [banks[n] for n in ("ps0a", "ps1a", "ps0b", "ps1b")]
    psn = ["ps0a", "ps1a", "ps0b", "ps1b"]
    po = [banks["po0"], banks["po1"]]
    pon = ["po0", "po1"]
    pu, po1b, pvB = banks["pu"], banks["po1"], banks["pvB"]

    def sxsl(dc, c0, n):
        return sx[:, dc * S + c0: dc * S + c0 + n]

    emit_vv_ones(kb, B)

    if xt_dram is not None:
        for dc in range(NDC):
            v = kb.op("sync", kb.get_guard(("sx", dc)),
                      lambda e, dc=dc: e.dma_start(
                          out=sx[:, dc * S:(dc + 1) * S], in_=xt_dram[dc]),
                      ("dX", 16))
        for dc in range(NDC):
            kb.mark(("sx", dc), "dX", v)

    def proj_phase(wdram, evac, nfc=NDC):
        util = [("pu", pu), ("po1", po1b)]
        for fc in range(nfc):
            slot = fc % 2
            v = kb.op("sync", kb.get_guard(("wa", slot)),
                      lambda e, wdram=wdram, fc=fc, slot=slot: e.dma_start(
                          out=wdst_ap(wa, slot * 1024), in_=wcol_ap(wdram, fc)),
                      ("dA%d" % slot, 16))
            kb.mark(("wa", slot), "dA%d" % slot, v)
            for sc in range(NSC):
                j = fc * NSC + sc
                bname, bank = util[j % 2]
                tv = None
                for dc in range(NDC):
                    waits = []
                    if dc == 0:
                        waits = kb.get_guard(bname) + kb.need(("wa", slot))
                    waits = waits + kb.need(("sx", dc))
                    lhs = wa[:, slot * 1024 + dc * 128: slot * 1024 + (dc + 1) * 128]
                    rhs = sxsl(dc, sc * 512, 512)
                    last = dc == NDC - 1
                    tv = kb.op("tensor", waits,
                               lambda e, bank=bank, lhs=lhs, rhs=rhs, dc=dc, last=last: e.matmul(
                                   bank[:, :], lhs, rhs, start=(dc == 0), stop=last),
                               ("tP", 1) if last else None)
                if sc == NSC - 1:
                    kb.set_guard(("wa", slot), "tP", tv)
                evac(bname, bank, fc, sc, tv)

    def scalar_evac(dst_ap, bank, bname, tv, scale):
        v = kb.op("scalar", [("tP", tv)],
                  lambda e, dst=dst_ap, bank=bank, scale=scale: e.activation(
                      dst, bank[:, :], AF.Copy, scale=scale),
                  ("sC", 1))
        kb.set_guard(bname, "sC", v)
        return v

    if layer == 0:
        def ctx_evac(bname, bank, fc, sc, tv):
            v = scalar_evac(cx[:, sc * 512:(sc + 1) * 512], bank, bname, tv,
                            float(np.sqrt(0.1)))
            if sc == NSC - 1:
                kb.mark("cx", "sC", v)
        proj_phase(w["we"], ctx_evac, nfc=1)

    def k_evac(bname, bank, fc, sc, tv):
        v = scalar_evac(kt[:, fc * S + sc * 512: fc * S + (sc + 1) * 512], bank, bname, tv, 1.0)
        if fc == NDC - 1 and sc == NSC - 1:
            kb.mark("kt", "sC", v)
    proj_phase(w["wk"], k_evac)

    def v_evac(bname, bank, fc, sc, tv):
        v = scalar_evac(qt[:, fc * S + sc * 512: fc * S + (sc + 1) * 512], bank, bname, tv, 1.0)
        if fc == NDC - 1 and sc == NSC - 1:
            kb.mark("qtv", "sC", v)
    proj_phase(w["wv"], v_evac)

    for fc in range(NDC):
        for half in range(2):
            tv = None
            for stl in range(8):
                st = half * 8 + stl
                waits = kb.need("qtv", "consts") + (kb.get_guard("pvB") if stl == 0 else [])
                tv = kb.op("tensor", waits,
                           lambda e, fc=fc, st=st, stl=stl: e.transpose(
                               pvB[:, stl * 128:(stl + 1) * 128],
                               qt[:, fc * S + st * 128: fc * S + (st + 1) * 128],
                               B["ident_b"][:, :]),
                           ("tP", 1))
            src = pvB[:, 0:1024].rearrange("p (st h c) -> p st h c", h=2, c=64)
            dst = vv[:, :].rearrange("p (st x) -> p st x", x=1040)[
                :, half * 8:(half + 1) * 8, 2 * fc * 65: 2 * fc * 65 + 130] \
                .rearrange("p st (h c) -> p st h c", c=65)[:, :, :, 0:64]
            vcv = kb.op("vector", [("tP", tv)],
                        lambda e, src=src, dst=dst: e.tensor_copy(dst, src),
                        ("vC", 1))
            kb.set_guard("pvB", "vC", vcv)
    kb.set_guard("qtbuf", "tP", kb.now("tP"))
    kb.mark("vv", "vC", kb.now("vC"))

    if dbg:
        dbg(kb, nc_, B, "cxD", cx[:, :], kb.need("cx"))
        dbg(kb, nc_, B, "ktD", kt[:, :], kb.need("kt"))
        dbg(kb, nc_, B, "qtvD", qt[:, :], kb.need("qtv"), guard_res="qtbuf")
        dbg(kb, nc_, B, "vvD", vv[:, :], kb.need("vv"))

    def q_evac(bname, bank, fc, sc, tv):
        waits_extra = kb.get_guard("qtbuf")
        v = kb.op("scalar", [("tP", tv)] + waits_extra,
                  lambda e, fc=fc, sc=sc, bank=bank: e.activation(
                      qt[:, fc * S + sc * 512: fc * S + (sc + 1) * 512],
                      bank[:, :], AF.Copy, scale=SCALE),
                  ("sC", 1))
        kb.set_guard(bname, "sC", v)
        if fc == NDC - 1 and sc == NSC - 1:
            kb.mark("qt", "sC", v)
    proj_phase(w["wq"], q_evac)

    if dbg:
        dbg(kb, nc_, B, "qtD", qt[:, :], kb.need("qt"))

    for qc in range(NSC):
        for hp in range(NDC):
            hpo = hp * S
            qco = qc * 512
            avdone = [None, None]
            nkt = NST if layer == 0 else 4 * qc + 4
            for ktile in range(nkt):
                pr = (ktile % 2) * 2
                off = 0
                if layer == 1:
                    r = ktile - 4 * qc
                    off = max(0, r) * 128
                expv = [None, None]
                for par in (0, 1):
                    bname, bank = psn[pr + par], ps[pr + par]
                    r0, r1 = par * 64, par * 64 + 64
                    waits = kb.get_guard(bname) + kb.need("kt", "qt")
                    kb.op("tensor", waits,
                          lambda e, bank=bank, hpo=hpo, qco=qco, kt0=ktile * 128,
                                 r0=r0, r1=r1, off=off: e.matmul(
                              bank[:, off:512], kt[r0:r1, hpo + kt0: hpo + kt0 + 128],
                              qt[r0:r1, hpo + qco + off: hpo + qco + 512],
                              start=True, stop=(layer == 1),
                              tile_position=(r0, 0)),
                          ("tP", 1) if layer == 1 else None)
                tvs = [None, None]
                if layer == 0:
                    for par in (0, 1):
                        bname, bank = psn[pr + par], ps[pr + par]
                        tvs[par] = kb.op("tensor", kb.need("cx"),
                                         lambda e, bank=bank, qco=qco, kt0=ktile * 128: e.matmul(
                                             bank[:, :], cx[:, kt0: kt0 + 128],
                                             cx[:, qco: qco + 512],
                                             start=False, stop=True),
                                         ("tP", 1))
                else:
                    tvs = [kb.now("tP"), kb.now("tP")]
                for par in (0, 1):
                    bname, bank = psn[pr + par], ps[pr + par]
                    slot = pr + par
                    waits = [("tP", tvs[par])] + kb.get_guard(("at", slot))
                    ev = kb.op("scalar", waits,
                               lambda e, bank=bank, slot=slot, off=off: e.activation(
                                   at[:, slot * 512 + off:(slot + 1) * 512],
                                   bank[:, off:512], AF.Exp),
                               ("sC", 1))
                    kb.set_guard(bname, "sC", ev)
                    expv[par] = ev
                for par in (0, 1):
                    slot = pr + par
                    h = 2 * hp + par
                    waits = [("sC", expv[par])] + kb.need("vv", "consts")
                    if ktile == 0:
                        waits = waits + kb.get_guard(pon[par])
                    tv = kb.op("tensor", waits,
                               lambda e, par=par, slot=slot, h=h, kt0=ktile, off=off,
                                      last=(ktile == nkt - 1): e.matmul(
                                   po[par][0:65, off:512],
                                   vv[:, kt0 * 1040 + h * 65: kt0 * 1040 + (h + 1) * 65],
                                   at[:, slot * 512 + off:(slot + 1) * 512],
                                   start=(kt0 == 0), stop=last),
                               ("tP", 1))
                    kb.set_guard(("at", slot), "tP", tv)
                    avdone[par] = tv
            for par in (0, 1):
                h = 2 * hp + par
                waits = [("tP", avdone[par])] + kb.get_guard("rr")
                rv = kb.op("vector", waits,
                           lambda e, par=par: e.reciprocal(rr[:, :], po[par][64:65, :]),
                           ("vC", 1))
                bw = kb.get_guard("pu") + [("vC", rv)] + kb.need("consts")
                bv = kb.op("tensor", bw,
                           lambda e: e.matmul(pu[0:64, :], B["ones1"][0:1, 0:64], rr[:, :],
                                              start=True, stop=True),
                           ("tP", 1))
                kb.set_guard("rr", "tP", bv)
                cv = kb.op("vector", [("tP", bv)],
                           lambda e: e.tensor_copy(br[:, :], pu[0:64, :]),
                           ("vC", 1))
                kb.set_guard("pu", "vC", cv)
                dst = ao[par * 64: par * 64 + 64, hp * 512:(hp + 1) * 512]
                mv = kb.op("vector", [("vC", cv)] + kb.get_guard(("ao", hp, par)),
                           lambda e, par=par, dst=dst: e.tensor_tensor(
                               dst, po[par][0:64, :], br[:, :], ALU.mult),
                           ("vC", 1))
                kb.set_guard(pon[par], "vC", mv)
                kb.set_guard("br", "vC", mv)
                kb.mark(("ao", hp, par), "vC", mv)

        if dbg and qc == 0:
            v_ = dbg(kb, nc_, B, "aoD", ao[:, :],
                     kb.need(*[("ao", d_, p_) for d_ in range(NDC) for p_ in (0, 1)]))
            for d_ in range(NDC):
                for p_ in (0, 1):
                    kb.set_guard(("ao", d_, p_), "dG", v_)

        util = [("pu", pu), ("po1", po1b)]
        for fc in range(NDC):
            slot = fc % 2
            v = kb.op("sync", kb.get_guard(("wa", slot)),
                      lambda e, fc=fc, slot=slot: e.dma_start(
                          out=wdst_ap(wa, slot * 1024), in_=wcol_ap(w["wo"], fc)),
                      ("dA%d" % slot, 16))
            kb.mark(("wa", slot), "dA%d" % slot, v)
            vcw = kb.op("vector", [("dA%d" % slot, v)] + kb.get_guard(("wc", slot)),
                        lambda e, slot=slot: e.tensor_copy(
                            wc[:, slot * 1024:(slot + 1) * 1024],
                            wa[:, slot * 1024:(slot + 1) * 1024].bitcast(F32)),
                        ("vC", 1))
            kb.set_guard(("wa", slot), "vC", vcw)
            kb.mark(("wc", slot), "vC", vcw)
            bname, bank = util[fc % 2]
            tv = None
            for dc in range(NDC):
                waits = kb.need(("ao", dc, 0), ("ao", dc, 1))
                if dc == 0:
                    waits = waits + kb.get_guard(bname) + kb.need(("wc", slot))
                lhs = wc[:, slot * 1024 + dc * 128: slot * 1024 + (dc + 1) * 128]
                rhs = ao[:, dc * 512:(dc + 1) * 512]
                last = dc == NDC - 1
                tv = kb.op("tensor", waits,
                           lambda e, bank=bank, lhs=lhs, rhs=rhs, dc=dc, last=last: e.matmul(
                               bank[:, :], lhs, rhs, start=(dc == 0), stop=last),
                           ("tP", 1) if last else None)
            kb.set_guard(("wc", slot), "tP", tv)
            dst = sxsl(fc, qc * 512, 512)
            v = kb.op("vector", [("tP", tv)],
                      lambda e, dst=dst, bank=bank: e.tensor_tensor(
                          dst, bank[:, :], dst.bitcast(F32), ALU.add),
                      ("vC", 1))
            kb.set_guard(bname, "vC", v)
            kb.mark(("sx", fc, "r", qc), "vC", v)
        for dc in range(NDC):
            for par in (0, 1):
                kb.set_guard(("ao", dc, par), "tP", kb.now("tP"))

    tnow = kb.now("tP")
    for rname in ("vv", "cx", "kt", "qt"):
        kb.set_guard(rname, "tP", tnow)

    if skip_ln:
        for dc in range(NDC):
            lv = max(kb.ready[("sx", dc, "r", qc)][1] for qc in range(NSC))
            kb.mark(("sx", dc), "vC", lv)
            kb.mark(("sx", dc, "ln"), "vC", lv)
    else:
        emit_ln(kb, B, rdeps=None)

    if layer == 1:
        return
    last = None
    for dc in range(NDC):
        src = sx[:, dc * S:(dc + 1) * S]
        dst = B["am8"][:, dc:dc + 1]
        last = kb.op("vector", kb.need(("sx", dc, "ln")) +
                     ([("vC", last)] if last else []),
                     lambda e, src=src, dst=dst: e.reduce_max(
                         dst, src.bitcast(F32), axis=AX.X, apply_absolute_value=True),
                     ("vC", 1))
        kb.set_guard(("sx", dc), "vC", last)
    v = kb.vchain([
        lambda e, b=b: e.reduce_max(B["am2"][:, b:b + 1], B["am8"][:, :],
                                    axis=AX.X, apply_absolute_value=True),
        lambda e: e.engine_nop(),
    ], waits0=[("vC", last)])
    kb.mark(("am2", b), "vC", v)


"""L0 FFN sublayer: global-amax fp8(e4m3fn) quantized linear + bias, bf16
round, residual, LN.   x_out = LN(x + bf16(acc*osc + bias)) in place on sx.

Buffer reuse during this phase: vv (bitcast f32r) holds the staged weight,
qt holds the quantized weight (bf16), kt holds quantized x (bf16),
wc/at (bitcast) are elementwise scratch, ao holds bf16 ffn outputs.
scl [128, 4] f32: col0 = w-scale, col1 = x-scale, col2 = osc.
"""


def emit_fp8_quant(kb, B, src, dst, scl_col, n, extra=()):
    sqf = B["ao"][:, 0:2 * n].bitcast(F32)[:, 0:n]
    sqi = B["ao"][:, 0:2 * n].bitcast(I32)[:, 0:n]
    wcf = B["wc"][:, 0:2 * n].bitcast(F32)[:, 0:n]
    msk = B["at"][:, 0:2 * n].bitcast(I32)[:, 0:n]
    scl = B["scl"]
    g = kb.get_guard(("aoq", 0)) + kb.get_guard(("aoq", 1)) + \
        kb.get_guard("wcscr") + kb.get_guard("atscr") + list(extra)
    kb.op("vector", g + kb.need("scl"),
          lambda e: e.tensor_scalar(sqf, src.bitcast(F32),
                                    scl[:, scl_col:scl_col + 1], None, ALU.mult), None)
    kb.op("vector", [],
          lambda e: e.tensor_scalar(sqf, sqf, 448.0, None, ALU.min), None)
    kb.op("vector", [],
          lambda e: e.tensor_scalar(sqf, sqf, -448.0, None, ALU.max), None)
    kb.op("vector", [],
          lambda e: e.tensor_scalar(wcf, sqf, 24576.0, None, ALU.add), None)
    kb.op("vector", [],
          lambda e: e.tensor_scalar(wcf, wcf, 24576.0, None, ALU.subtract), None)
    kb.op("vector", [],
          lambda e: e.tensor_scalar(msk, sqi, 0x7FFFFFFF, None, ALU.bitwise_and), None)
    kb.op("vector", [],
          lambda e: e.tensor_scalar(msk, msk, 0x3C800000, None, ALU.is_lt), None)
    kb.op("vector", [],
          lambda e: e.tensor_scalar(sqi, sqi, 0x80000, None, ALU.add), None)
    kb.op("vector", [],
          lambda e: e.tensor_scalar(sqi, sqi, ~0xFFFFF, None, ALU.bitwise_and), None)
    kb.op("vector", [],
          lambda e: e.select(sqf, msk, wcf, sqf), None)
    v = kb.op("vector", [],
              lambda e: e.tensor_copy(dst, sqf), ("vC", 1))
    for r in (("aoq", 0), ("aoq", 1), "wcscr", "atscr"):
        kb.set_guard(r, "vC", v)
    return v


def emit_scale_rows(kb, B, col_amax, col_out, col_tmp=152):
    srow = B["srow"]
    return kb.vchain([
        lambda e: e.tensor_scalar(srow[0:1, col_tmp:col_tmp + 1],
                                  srow[0:1, col_amax:col_amax + 1],
                                  1e-12, None, ALU.max),
        lambda e: e.reciprocal(srow[0:1, col_tmp + 1:col_tmp + 2],
                               srow[0:1, col_tmp:col_tmp + 1]),
        lambda e: e.tensor_scalar(srow[0:1, col_out:col_out + 1],
                                  srow[0:1, col_tmp + 1:col_tmp + 2],
                                  448.0, None, ALU.mult),
        lambda e: e.tensor_scalar(srow[0:1, col_out:col_out + 1],
                                  srow[0:1, col_out:col_out + 1],
                                  448.0, None, ALU.min),
    ])


def emit_l0ffn_prep(kb, B, w8_dram, bias_dram, amaxes_dram, amscr_dram, sclscr_dram, fused_cc=None):
    nc = kb.nc
    banks = B["banks"]
    vvr = B["vv"][:, 0:16384].bitcast(F32R)
    qt, kt, sq, srow, scl = B["qt"], B["kt"], B["sq"], B["srow"], B["scl"]

    for fc in range(NDC):
        v = kb.op("sync", kb.get_guard("vv"),
                  lambda e, fc=fc: e.dma_start(
                      out=wdst_ap(vvr, fc * 1024), in_=wcol_ap(w8_dram, fc)),
                  ("dS", 16))
    kb.mark("w8stage", "dS", v)

    last = None
    for fc in range(NDC):
        last = kb.op("vector", kb.need("w8stage") + kb.get_guard("am8") +
                     ([("vC", last)] if last else []),
                     lambda e, fc=fc: e.reduce_max(
                         B["am8"][:, fc:fc + 1], vvr[:, fc * 1024:(fc + 1) * 1024].bitcast(F32),
                         axis=AX.X, apply_absolute_value=True),
                     ("vC", 1))
    v = kb.vchain([
        lambda e: e.reduce_max(sq[:, 0:1], B["am8"][:, :], axis=AX.X,
                               apply_absolute_value=True),
        lambda e: e.engine_nop(),
    ], waits0=[("vC", last)])
    v = kb.op("gpsimd", [("vC", v)],
              lambda e: e.dma_start(out=amscr_dram, in_=sq[:, 0:1].bitcast(F32)),
              ("dM", 16))
    v = kb.op("sync", [("dM", v)],
              lambda e: e.dma_start(out=srow[0:1, 0:128],
                                    in_=amscr_dram.rearrange("p o -> o p")),
              ("dR", 16))
    v = kb.vchain([
        lambda e: e.reduce_max(srow[0:1, 129:130], srow[0:1, 0:128],
                               axis=AX.X, apply_absolute_value=False),
        lambda e: e.engine_nop(),
    ], waits0=[("dR", v)])
    v_w = v

    if amaxes_dram is not None:
        v2 = kb.op("sync", [],
                   lambda e: e.dma_start(out=srow[0:1, 144:152], in_=amaxes_dram),
                   ("dR2", 16))
        v2 = kb.vchain([
            lambda e: e.reduce_max(srow[0:1, 128:129], srow[0:1, 144:152],
                                   axis=AX.X, apply_absolute_value=False),
            lambda e: e.engine_nop(),
        ], waits0=[("dR2", v2)])
    else:
        cc_in_t, cc_out_t, amscr2_dram = fused_cc
        v2 = kb.vchain([
            lambda e: e.tensor_tensor(sq[:, 1:2], B["am2"][:, 0:1],
                                      B["am2"][:, 1:2], ALU.max),
            lambda e: e.engine_nop(),
        ], waits0=kb.need(("am2", 0), ("am2", 1)))
        v2 = kb.op("gpsimd", [("vC", v2)],
                   lambda e: e.dma_start(out=amscr2_dram, in_=sq[:, 1:2].bitcast(F32)),
                   ("dM", 16))
        v2 = kb.op("sync", [("dM", v2), ("vC", v_w)],
                   lambda e: e.dma_start(out=srow[0:1, 0:128],
                                         in_=amscr2_dram.rearrange("p o -> o p")),
                   ("dR", 16))
        v2 = kb.op("vector", [("dR", v2)],
                   lambda e: e.reduce_max(srow[0:1, 130:131], srow[0:1, 0:128],
                                          axis=AX.X, apply_absolute_value=False),
                   ("vC", 1))
        v2 = kb.op("gpsimd", [("vC", v2)],
                   lambda e: e.dma_start(out=cc_in_t.ap(), in_=srow[0:1, 130:138]),
                   ("dM", 16))
        v2 = kb.op("gpsimd", [("dM", v2)],
                   lambda e: e.collective_compute(
                       "AllReduce", ALU.max, replica_groups=[list(range(8))],
                       ins=[cc_in_t.ap()], outs=[cc_out_t.ap()]),
                   ("cc", 1))
        v2 = kb.op("sync", [("cc", v2)],
                   lambda e: e.dma_start(out=srow[0:1, 128:129],
                                         in_=cc_out_t.ap()[0:1, 0:1]),
                   ("dR", 16))
        v2 = kb.op("vector", [("dR", v2)], lambda e: e.engine_nop(), ("vC", 1))

    emit_scale_rows(kb, B, 129, 131, 152)
    emit_scale_rows(kb, B, 128, 132, 154)
    kb.vchain([
        lambda e: e.reciprocal(srow[0:1, 134:135], srow[0:1, 131:132]),
        lambda e: e.reciprocal(srow[0:1, 135:136], srow[0:1, 132:133]),
        lambda e: e.tensor_tensor(srow[0:1, 133:134], srow[0:1, 134:135],
                                  srow[0:1, 135:136], ALU.mult),
    ])
    v3 = kb.op("vector", [], lambda e: e.engine_nop(), ("vC", 1))
    v3 = kb.op("gpsimd", [("vC", v3)],
               lambda e: e.dma_start(out=sclscr_dram, in_=srow[0:1, 131:135]),
               ("dM", 16))
    v3 = kb.op("sync", [("dM", v3)],
               lambda e: e.dma_start(out=scl[:, 0:4],
                                     in_=sclscr_dram.broadcast_to([128, 4])),
               ("dR", 16))
    kb.mark("scl", "dR", v3)

    for fc in range(NDC):
        v = emit_fp8_quant(kb, B, vvr[:, fc * 1024:(fc + 1) * 1024],
                           qt[:, fc * 1024:(fc + 1) * 1024], 0, 1024,
                           extra=kb.get_guard("qt"))
    kb.mark("wq8", "vC", v)
    kb.set_guard("vv", "vC", v)

    v = kb.op("sync", [],
              lambda e: e.dma_start(out=B["biasb"][:, :], in_=bias_dram),
              ("dB", 16))
    kb.mark("biasb", "dB", v)


def emit_l0ffn(kb, B, b, skip_ln=False, ln_dbg=None):
    sx, kt, qt, ao = B["sx"], B["kt"], B["qt"], B["ao"]
    banks = B["banks"]
    util = [("pu", banks["pu"]), ("po1", banks["po1"])]

    def sxsl(dc, c0, nn):
        return sx[:, dc * S + c0: dc * S + c0 + nn]

    for dc in range(NDC):
        for hf in range(2):
            v = emit_fp8_quant(
                kb, B, sxsl(dc, hf * 1024, 1024),
                kt[:, dc * S + hf * 1024: dc * S + (hf + 1) * 1024], 1, 1024,
                extra=kb.get_guard("kt") + kb.need(("sx", dc)))
    kb.mark("xq8", "vC", v)

    for fc in range(NDC):
        for sc in range(NSC):
            j = fc * NSC + sc
            bname, bank = util[j % 2]
            tv = None
            for dc in range(NDC):
                waits = kb.need("xq8", "wq8") + (kb.get_guard(bname) if dc == 0 else [])
                tv = kb.op("tensor", waits,
                           lambda e, bank=bank, fc=fc, sc=sc, dc=dc: e.matmul(
                               bank[:, :],
                               qt[:, fc * 1024 + dc * 128: fc * 1024 + (dc + 1) * 128],
                               kt[:, dc * S + sc * 512: dc * S + (sc + 1) * 512],
                               start=(dc == 0), stop=(dc == NDC - 1)),
                           ("tP", 1) if dc == NDC - 1 else None)
            dst_b = ao[:, (j % 4) * 512: (j % 4 + 1) * 512]
            v = kb.op("vector", [("tP", tv)] + kb.need("scl", "biasb") + kb.get_guard(("aoq", j % 4)),
                      lambda e, bank=bank, dst_b=dst_b, fc=fc: e.tensor_scalar(
                          dst_b, bank[:, :], B["scl"][:, 2:3], B["biasb"][:, fc:fc + 1],
                          ALU.mult, ALU.add),
                      ("vC", 1))
            kb.set_guard(bname, "vC", v)
            dst = sxsl(fc, sc * 512, 512)
            v = kb.op("vector", [],
                      lambda e, dst=dst, dst_b=dst_b: e.tensor_tensor(
                          dst, dst_b, dst.bitcast(F32), ALU.add),
                      ("vC", 1))
            kb.set_guard(("aoq", j % 4), "vC", v)
            kb.mark(("sx", fc, "r", sc), "vC", v)
    if skip_ln:
        for dc in range(NDC):
            lv = max(kb.ready[("sx", dc, "r", qq)][1] for qq in range(NSC))
            kb.mark(("sx", dc), "vC", lv)
            kb.mark(("sx", dc, "ln"), "vC", lv)
    else:
        emit_ln(kb, B, rdeps=None, dbg=ln_dbg)


"""L1 FFN: blockwise-fp4 (E2M1, block 16) quant-dequant SwiGLU MLP.

out = fp4linear(silu(fp4linear(x,wg)) * fp4linear(x,wu), wd);
x_out = LN(x + out) in place on sx.

Quantized+transposed weights are spilled to HBM scratch once (prep) and
streamed back during the matmuls (they don't fit in SBUF).

Buffer roles in this phase:
  kt  = xq4T (quantized x, transposed back to [d,seq])     [128, 8*2048] bf16
  qt  = hqT tiles (cols 0:11264) + int scratch + wdq stream
  vv  = weight staging (prep, f32r view) / gate+up rhs stream slots (4x4096)
  cx  = natural quantized staging (x / h tiles)
  ao(f32 view) = src + value scratch ; wc(f32 view) = acc scratch
  at  = silu output tiles (bf16)
  sq  = blockmax / scale / rcp rows (f32r)
"""

MIDS = [0.25, 0.75, 1.25, 1.75, 2.5, 2.5, 3.5, 3.5, 5.0, 5.0, 5.0, 5.0]


def emit_fp4_quant(kb, B, srcf, srci, dst, n, c_i32, waits0=()):
    nb = n // 16
    sq = B["sq"]
    bm = sq[:, 0:nb]
    s_c = sq[:, 64:64 + nb]
    rcp = sq[:, 128:128 + nb]
    A = B["ao"][:, 2048:2048 + 2 * n].bitcast(F32)[:, 0:n]
    A_i = B["ao"][:, 2048:2048 + 2 * n].bitcast(I32)[:, 0:n]
    AC = B["wc"][:, 0:2 * n].bitcast(F32)[:, 0:n]
    T_f = B["at"][:, 0:2 * n].bitcast(F32)[:, 0:n]
    T_i = B["at"][:, 0:2 * n].bitcast(I32)[:, 0:n]
    g = kb.get_guard("fp4scr") + list(waits0)
    rcp_b = rcp.rearrange("p (b o) -> p b o", o=1).broadcast_to([128, nb, 16])
    kb.vchain([
        lambda e: e.reduce_max(bm, srcf.rearrange("p (b w) -> p b w", w=16),
                               axis=AX.X, apply_absolute_value=True),
        lambda e: e.tensor_scalar(s_c, bm, 1.0 / 6.0, None, ALU.mult),
        lambda e: e.tensor_scalar(s_c, s_c, 1e-12, None, ALU.max),
        lambda e: e.reciprocal(rcp, s_c),
        lambda e: e.tensor_scalar(c_i32, srci, 0x7FFFFFFF, None, ALU.bitwise_and),
        lambda e: e.tensor_tensor(A.rearrange("p (b w) -> p b w", w=16),
                                  c_i32.bitcast(F32).rearrange("p (b w) -> p b w", w=16),
                                  rcp_b, ALU.mult),
    ], waits0=g)
    kb.op("vector", [],
          lambda e: e.tensor_scalar(c_i32, A_i, 0x3FFFFF, None, ALU.bitwise_and), None)
    kb.op("vector", [],
          lambda e: e.tensor_scalar(c_i32, c_i32, 0x200000, None, ALU.is_gt), None)
    kb.op("vector", [],
          lambda e: e.tensor_scalar(T_i, A_i, ~0x3FFFFF, None, ALU.bitwise_and), None)
    kb.op("vector", [],
          lambda e: e.scalar_tensor_tensor(T_i, c_i32, 0x400000, T_i,
                                           ALU.mult, ALU.add), None)
    kb.op("vector", [],
          lambda e: e.tensor_scalar(T_i, T_i, ~0x3FFFFF, None, ALU.bitwise_and), None)
    kb.op("vector", [],
          lambda e: e.tensor_scalar(AC, A, 0.25, None, ALU.is_gt), None)
    kb.op("vector", [],
          lambda e: e.scalar_tensor_tensor(AC, A, 0.75, AC, ALU.is_gt, ALU.add), None)
    kb.op("vector", [],
          lambda e: e.tensor_scalar(AC, AC, 0.5, None, ALU.mult), None)
    kb.op("vector", [],
          lambda e: e.tensor_scalar(c_i32, A_i, 0x3F800000, None, ALU.is_lt), None)
    kb.op("vector", [],
          lambda e: e.select(A, c_i32, AC, T_f), None)
    sc_b = s_c.rearrange("p (b o) -> p b o", o=1).broadcast_to([128, nb, 16])
    kb.op("vector", [],
          lambda e: e.tensor_tensor(T_f.rearrange("p (b w) -> p b w", w=16),
                                    A.rearrange("p (b w) -> p b w", w=16),
                                    sc_b, ALU.mult), None)
    kb.op("vector", [],
          lambda e: e.tensor_scalar(c_i32, srci, -0x80000000, None, ALU.bitwise_and), None)
    kb.op("vector", [],
          lambda e: e.tensor_tensor(T_i, T_i, c_i32, ALU.bitwise_or), None)
    v = kb.op("vector", [], lambda e: e.tensor_copy(dst, T_f), ("vC", 1))
    kb.set_guard("fp4scr", "vC", v)
    return v





def emit_l1ffn_prep(kb, B, wg_dram, wu_dram, wd_dram, spills):
    nc = kb.nc
    vv, kt, qt, pvB = B["vv"], B["kt"], B["qt"], B["banks"]["pvB"]
    vvr = vv[:, 0:16640].bitcast(F32R)

    def quant_transpose_spill(src_slot_f32r, n, dst_stage, spill_ap, qdst, skey):
        nblk = n // 128
        off = 0
        lastq = None
        while off < n:
            m = min(1024, n - off)
            lastq = emit_fp4_quant(
                kb, B, src_slot_f32r[:, off:off + m].bitcast(F32),
                src_slot_f32r[:, off:off + m].bitcast(I32),
                qdst[:, off:off + m], m,
                B["qt"][:, 11264:11264 + 2 * m].bitcast(I32)[:, 0:m])
            off += m
        done = 0
        while done < nblk:
            cnt = min(8, nblk - done)
            tv = None
            for j in range(cnt):
                waits = ([("vC", lastq)] + kb.get_guard("pvB") + kb.need("consts")) if j == 0 else []
                tv = kb.op("tensor", waits,
                           lambda e, j=j, done=done, qdst=qdst: e.transpose(
                               pvB[:, j * 128:(j + 1) * 128],
                               qdst[:, (done + j) * 128:(done + j + 1) * 128],
                               B["ident_b"][:, :]),
                           ("tP", 1))
            v = kb.op("vector", [("tP", tv)] + kb.get_guard(skey),
                      lambda e, cnt=cnt, done=done: e.tensor_copy(
                          dst_stage[:, done * 128:(done + cnt) * 128],
                          pvB[:, 0:cnt * 128]),
                      ("vC", 1))
            kb.set_guard("pvB", "vC", v)
            done += cnt
        kb.set_guard("fp4scr", "tP", kb.now("tP"))
        sv = None
        for dst_ap, c0, c1 in spill_ap:
            sv = kb.op("gpsimd", [("vC", v)],
                       lambda e, dst_ap=dst_ap, c0=c0, c1=c1: e.dma_start(
                           out=dst_ap, in_=dst_stage[:, c0:c1]),
                       ("dW", 16))
        kb.set_guard(skey, "dW", sv)
        return sv

    sv = None
    for name, wdram in (("wg", wg_dram), ("wu", wu_dram)):
        for nt in range(22):
            slot = nt % 2
            lv = kb.op("sync", kb.get_guard(("vvs", slot)),
                       lambda e, wdram=wdram, nt=nt, slot=slot: e.dma_start(
                           out=vvr[:, slot * 1024:(slot + 1) * 1024], in_=wdram[nt]),
                       ("dS%d" % slot, 16))
            src = vvr[:, slot * 1024:(slot + 1) * 1024]
            stage = kt[:, 2048 + slot * 1024: 2048 + (slot + 1) * 1024]
            qdst = kt[:, slot * 1024:(slot + 1) * 1024]
            kb.set_guard("fp4scr", "dS%d" % slot, lv)
            sv = quant_transpose_spill(
                src, 1024, stage,
                [(spills[name][j, :, nt * 128:(nt + 1) * 128], j * 128, (j + 1) * 128)
                 for j in range(8)],
                qdst, ("stg", slot))
            kb.set_guard(("vvs", slot), "vC", kb.now("vC"))
    kb.set_guard("fp4scr", "dW", kb.now("dW"))
    kb.set_guard(("vvs2", 0), "vC", kb.now("vC"))
    kb.set_guard(("vvs2", 1), "vC", kb.now("vC"))
    for dc in range(8):
        slot = dc % 2
        lv = kb.op("sync", kb.get_guard(("vvs2", slot)),
                   lambda e, dc=dc, slot=slot: e.dma_start(
                       out=vvr[:, slot * 2816:(slot + 1) * 2816], in_=wd_dram[dc]),
                   ("dS%d" % slot, 16))
        src = vvr[:, slot * 2816:(slot + 1) * 2816]
        stage = kt[:, 8192 + slot * 2816: 8192 + (slot + 1) * 2816]
        qdst = kt[:, slot * 2816 + 0: slot * 2816 + 2816]
        kb.set_guard("fp4scr", "dS%d" % slot, lv)
        sv = quant_transpose_spill(src, 2816, stage,
                                   [(spills["wd"][dc], 0, 2816)], qdst,
                                   ("stg2", slot))
        kb.set_guard(("vvs2", slot), "vC", kb.now("vC"))
    kb.mark("spilled", "dW", kb.now("dW"))
    tnow = kb.now("vC")
    for r in ("vv", "kt", "qt"):
        kb.set_guard(r, "vC", tnow)


def emit_l1ffn(kb, B, spills, b, dbg=None, sc_list=None, skip_ln=False):
    nc = kb.nc
    sx, kt, qt, vv, cx, at, ao = B["sx"], B["kt"], B["qt"], B["vv"], B["cx"], B["at"], B["ao"]
    for r in ("cx", "kt", "qt", "vv", "at"):
        for (sn, gv) in kb.get_guard(r):
            kb.set_guard("fp4scr", sn, gv)
    for dcq in range(NDC):
        for par in (0, 1):
            for (sn, gv) in kb.get_guard(("ao", dcq, par)):
                kb.set_guard("xnat", sn, gv)
                kb.set_guard("fp4scr", sn, gv)
    for (sn, gv) in kb.get_guard("qt"):
        for ntc_ in range(6):
            for st_ in range(4):
                kb.set_guard(("hqT", ntc_, st_), sn, gv)
        kb.set_guard("wdqs", sn, gv)
    banks = B["banks"]
    pvB = banks["pvB"]
    pu, po1 = banks["pu"], banks["po1"]
    gb = [banks["ps0a"], banks["ps0b"]]
    gbn = ["ps0a", "ps0b"]
    ub = [banks["ps1a"], banks["ps1b"]]
    ubn = ["ps1a", "ps1b"]

    def sxsl(dc, c0, nn):
        return sx[:, dc * S + c0: dc * S + c0 + nn]

    for st in range(NST):
        for half in range(2):
            bank = pu if half == 0 else po1
            bname = "pu" if half == 0 else "po1"
            bview = bank[:, :].bitcast(F32R)
            tv = None
            for j in range(4):
                dc = half * 4 + j
                waits = (kb.get_guard(bname) + kb.need(("sx", dc), "consts")) if j == 0 else kb.need(("sx", dc))
                tv = kb.op("tensor", waits,
                           lambda e, dc=dc, st=st, j=j, bview=bview: e.transpose(
                               bview[:, j * 128:(j + 1) * 128],
                               sx[:, dc * S + st * 128: dc * S + (st + 1) * 128],
                               B["ident_r"][:, :]),
                           ("tP", 1))
            v = kb.op("vector", [("tP", tv)] + (kb.get_guard("xnat") if half == 0 else []),
                      lambda e, half=half, bview=bview: e.tensor_copy(
                          ao[:, 0:2048].bitcast(F32)[:, half * 512:(half + 1) * 512], bview),
                      ("vC", 1))
            kb.set_guard(bname, "vC", v)
        xnat_f = ao[:, 0:2048].bitcast(F32)
        xnat_i = ao[:, 0:2048].bitcast(I32)
        qv = emit_fp4_quant(kb, B, xnat_f, xnat_i, cx[:, 0:1024], 1024,
                            qt[:, 11264:13312].bitcast(I32)[:, 0:1024])
        kb.set_guard("xnat", "vC", qv)
        tv = None
        for j in range(8):
            waits = ([("vC", qv)] + kb.get_guard("pvB")) if j == 0 else []
            tv = kb.op("tensor", waits,
                       lambda e, j=j: e.transpose(
                           pvB[:, j * 128:(j + 1) * 128],
                           cx[:, j * 128:(j + 1) * 128], B["ident_b"][:, :]),
                       ("tP", 1))
        dst = kt[:, :].rearrange("p (dc s) -> p dc s", s=S)[:, :, st * 128:(st + 1) * 128]
        v = kb.op("vector", [("tP", tv)] + (kb.get_guard("kt") if st == 0 else []),
                  lambda e, dst=dst: e.tensor_copy(dst, pvB[:, 0:1024].rearrange(
                      "p (dc s) -> p dc s", s=128)),
                  ("vC", 1))
        kb.set_guard("pvB", "vC", v)
        kb.set_guard("cxq", "vC", v)
    kb.mark("xq4", "vC", kb.now("vC"))

    NTC = [512, 512, 512, 512, 512, 256]
    for sc in (sc_list if sc_list is not None else range(NSC)):
        hq_done = None
        for ntc in range(6):
            nw = NTC[ntc]
            nnt = nw // 128
            gslot = ntc % 2
            uslot = 2 + ntc % 2
            for name, slot in (("wg", gslot), ("wu", uslot)):
                lv = None
                for kc in range(NDC):
                    w0 = (kb.get_guard(("vvq", slot)) + kb.get_guard("vv")
                          + kb.need("spilled")) if kc == 0 else []
                    lv = kb.op("sync", w0,
                               lambda e, name=name, slot=slot, ntc=ntc,
                                      kc=kc, nw=nw: e.dma_start(
                                   out=vv[:, slot * 4096 + kc * nw: slot * 4096 + (kc + 1) * nw],
                                   in_=spills[name][kc, :, ntc * 512: ntc * 512 + nw]),
                               ("dV%d" % slot, 16))
                kb.mark(("vvq", slot), "dV%d" % slot, lv)
            for st in range(4):
                stg = sc * 4 + st
                gbank, gname = gb[st % 2], gbn[st % 2]
                ubank, uname = ub[st % 2], ubn[st % 2]
                tg = tu = None
                for kc in range(NDC):
                    waits = kb.need("xq4", ("vvq", gslot)) + (kb.get_guard(gname) if kc == 0 else [])
                    tg = kb.op("tensor", waits,
                               lambda e, kc=kc, stg=stg, gslot=gslot, nw=nw, gbank=gbank: e.matmul(
                                   gbank[:, 0:nw],
                                   kt[:, kc * S + stg * 128: kc * S + (stg + 1) * 128],
                                   vv[:, gslot * 4096 + kc * nw: gslot * 4096 + (kc + 1) * nw],
                                   start=(kc == 0), stop=(kc == NDC - 1)),
                               ("tP", 1) if kc == NDC - 1 else None)
                for kc in range(NDC):
                    waits = kb.need(("vvq", uslot)) + (kb.get_guard(uname) if kc == 0 else [])
                    tu = kb.op("tensor", waits,
                               lambda e, kc=kc, stg=stg, uslot=uslot, nw=nw, ubank=ubank: e.matmul(
                                   ubank[:, 0:nw],
                                   kt[:, kc * S + stg * 128: kc * S + (stg + 1) * 128],
                                   vv[:, uslot * 4096 + kc * nw: uslot * 4096 + (kc + 1) * nw],
                                   start=(kc == 0), stop=(kc == NDC - 1)),
                               ("tP", 1) if kc == NDC - 1 else None)
                atsl = ao[:, 2048:2048 + 2 * nw].bitcast(F32)[:, 0:nw]
                hv = kb.op("scalar", [("tP", tg)] + kb.get_guard("xnat"),
                           lambda e, gbank=gbank, atsl=atsl, nw=nw: e.activation(
                               atsl, gbank[:, 0:nw], AF.Silu),
                           ("sC", 1))
                kb.set_guard(gname, "sC", hv)
                srcsl = ao[:, 0:2048].bitcast(F32)[:, 0:nw]
                hv2 = kb.op("vector", [("sC", hv), ("tP", tu)] + kb.get_guard("xnat"),
                            lambda e, atsl=atsl, srcsl=srcsl, ubank=ubank, nw=nw: e.tensor_tensor(
                                srcsl, atsl, ubank[:, 0:nw], ALU.mult),
                            ("vC", 1))
                kb.set_guard(uname, "vC", hv2)
                kb.set_guard("xnat", "vC", hv2)
                qv = emit_fp4_quant(kb, B, srcsl, ao[:, 0:2048].bitcast(I32)[:, 0:nw],
                                    cx[:, 0:nw], nw,
                                    qt[:, 11264:13312].bitcast(I32)[:, 0:nw],
                                    waits0=[("vC", hv2)])
                kb.set_guard("xnat", "vC", qv)
                tv = None
                for j in range(nnt):
                    waits = ([("vC", qv)] + kb.get_guard("pvB")) if j == 0 else []
                    tv = kb.op("tensor", waits,
                               lambda e, j=j: e.transpose(
                                   pvB[:, j * 128:(j + 1) * 128],
                                   cx[:, j * 128:(j + 1) * 128], B["ident_b"][:, :]),
                               ("tP", 1))
                dst = qt[:, :].rearrange("p (kc s) -> p kc s", s=512)[
                    :, ntc * 4: ntc * 4 + nnt, st * 128:(st + 1) * 128]
                hq_done = kb.op("vector", [("tP", tv)] + kb.get_guard(("hqT", ntc, st)),
                                lambda e, dst=dst, nnt=nnt: e.tensor_copy(
                                    dst, pvB[:, 0:nnt * 128].rearrange(
                                        "p (kc s) -> p kc s", s=128)),
                                ("vC", 1))
                kb.set_guard("pvB", "vC", hq_done)
                kb.mark(("hqT", ntc, st), "vC", hq_done)
                if st == 3:
                    kb.set_guard(("vvq", gslot), "tP", kb.now("tP"))
                    kb.set_guard(("vvq", uslot), "tP", kb.now("tP"))
        if dbg is not None and sc == 0:
            dbg(kb, "hq0D", qt[:, 0:11264],
                kb.need(*[("hqT", n_, s_) for n_ in range(6) for s_ in range(4)]),
                [("hqT", n_, s_) for n_ in range(6) for s_ in range(4)])
        util = [("pu", pu), ("po1", po1)]
        for fc in range(NDC):
            lv = kb.op("sync", kb.get_guard("wdqs") + kb.need("spilled"),
                       lambda e, fc=fc: e.dma_start(
                           out=qt[:, 13312:16128], in_=spills["wd"][fc]),
                       ("dD", 16))
            bname, bank = util[fc % 2]
            tv = None
            for kc in range(22):
                waits = [("dD", lv)] + kb.need(*[("hqT", kc // 4, stq) for stq in range(4)])
                if kc == 0:
                    waits = waits + kb.get_guard(bname)
                tv = kb.op("tensor", waits,
                           lambda e, kc=kc, bank=bank: e.matmul(
                               bank[:, :],
                               qt[:, 13312 + kc * 128: 13312 + (kc + 1) * 128],
                               qt[:, kc * 512:(kc + 1) * 512],
                               start=(kc == 0), stop=(kc == 21)),
                           ("tP", 1) if kc == 21 else None)
            kb.set_guard("wdqs", "tP", tv)
            dst = sxsl(fc, sc * 512, 512)
            v = kb.op("vector", [("tP", tv)],
                      lambda e, dst=dst, bank=bank: e.tensor_tensor(
                          dst, bank[:, :], dst.bitcast(F32), ALU.add),
                      ("vC", 1))
            kb.set_guard(bname, "vC", v)
            kb.mark(("sx", fc, "r", sc), "vC", v)
        if dbg is not None and sc == 0:
            dbg(kb, "r0D", sx[:, :],
                kb.need(*[("sx", f_, "r", 0) for f_ in range(NDC)]),
                [("sx", f_) for f_ in range(NDC)])
        for ntc in range(6):
            for st in range(4):
                kb.set_guard(("hqT", ntc, st), "tP", kb.now("tP"))
    if skip_ln:
        for dc in range(NDC):
            vals = [kb.ready[("sx", dc, "r", qq)][1] for qq in (sc_list or range(NSC))
                    if ("sx", dc, "r", qq) in kb.ready]
            kb.mark(("sx", dc), "vC", max(vals))
            kb.mark(("sx", dc, "ln"), "vC", max(vals))
    else:
        vnow = kb.now("vC")
        kb.set_guard(("sq", 0), "vC", vnow)
        kb.set_guard(("sq", 1), "vC", vnow)
        emit_ln(kb, B, rdeps=None)


"""Standalone single-sublayer launch builders (for incremental validation,
and for the UNFUSED multi-launch kernel checkpoint)."""
from contextlib import ExitStack
import numpy as np
import concourse.bass as bass


def prep_l0attn_inputs(X, Wq, Wk, Wv, Wo, We, core, n_cores=8):
    bpc = X.shape[0] // n_cores
    xs = X[core * bpc:(core + 1) * bpc]
    xt = np.ascontiguousarray(xs.transpose(0, 2, 1))
    xt = np.ascontiguousarray(xt.reshape(bpc, NDC, 128, S))
    return {
        "xt": xt,
        "wq": chunk_T(Wq.T), "wk": chunk_T(Wk.T),
        "wv": chunk_T(Wv.T), "wo": chunk_T(Wo.T), "we": chunk_T(We.T),
    }


def emit_store_sx(kb, B, xo_dram, b):
    for dc in range(NDC):
        v = kb.op("gpsimd", kb.need(("sx", dc)),
                  lambda e, dc=dc: e.dma_start(
                      out=xo_dram[dc], in_=B["sx"][:, dc * S:(dc + 1) * S].bitcast(F32)),
                  ("dG", 16))
    for dc in range(NDC):
        kb.set_guard(("sx", dc), "dG", v)
    return v


def emit_amax_local(kb, B, amscr_dram):
    sq = B["sq"]
    v = kb.vchain([
        lambda e: e.tensor_tensor(sq[:, 0:1], B["am2"][:, 0:1], B["am2"][:, 1:2], ALU.max),
        lambda e: e.engine_nop(),
    ], waits0=kb.need(("am2", 0), ("am2", 1)))
    v = kb.op("gpsimd", [("vC", v)],
              lambda e: e.dma_start(out=amscr_dram, in_=sq[:, 0:1].bitcast(F32)),
              ("dM", 16))
    v = kb.op("sync", [("dM", v)],
              lambda e: e.dma_start(out=B["srow"][0:1, 0:128],
                                    in_=amscr_dram.rearrange("p o -> o p")),
              ("dR", 16))
    v = kb.vchain([
        lambda e: e.reduce_max(B["srow"][0:1, 128:129], B["srow"][0:1, 0:128],
                               axis=AX.X, apply_absolute_value=False),
        lambda e: e.engine_nop(),
    ], waits0=[("dR", v)])
    kb.mark("amax_local", "vC", v)
    return v


def emit_fence(kb):
    sems = [(n, kb.cnt[n]) for n in list(kb.cnt) if kb.cnt[n] > 0]
    for eng in ("tensor", "vector", "scalar", "sync", "gpsimd"):
        kb.op(eng, sems, lambda e: None, None)


def emit_spill_sx_r(kb, B, dst_dram):
    v = None
    for dc in range(NDC):
        v = kb.op("gpsimd", kb.need(("sx", dc)) + kb.get_guard(("sxsp", dc)),
                  lambda e, dc=dc: e.dma_start(
                      out=dst_dram[dc], in_=B["sx"][:, dc * S:(dc + 1) * S]),
                  ("dG", 16))
    for dc in range(NDC):
        kb.set_guard(("sx", dc), "dG", v)
    return v


def emit_load_sx_r(kb, B, src_dram):
    v = None
    for dc in range(NDC):
        v = kb.op("sync", kb.get_guard(("sx", dc)),
                  lambda e, dc=dc: e.dma_start(
                      out=B["sx"][:, dc * S:(dc + 1) * S], in_=src_dram[dc]),
                  ("dX", 16))
    for dc in range(NDC):
        kb.mark(("sx", dc), "dX", v)
    return v


def emit_tap(kb, nc, B, name, src_ap, waits, guard_res=None):
    shp = list(src_ap.shape)
    t = nc.dram_tensor(name, shp, src_ap.dtype, kind="ExternalOutput").ap()
    v = kb.op("gpsimd", waits,
              lambda e, t=t, src_ap=src_ap: e.dma_start(out=t, in_=src_ap),
              ("dG", 16))
    if guard_res:
        kb.set_guard(guard_res, "dG", v)
    return v


def build_l0attn_standalone(bpc=2, debug=False, skip_ln=False):
    nc = bass.Bass("TRN2", target_bir_lowering=False, dynamic_dma_scratch_size=16384,
                   detect_race_conditions=False)
    xt = nc.dram_tensor("xt", [bpc, NDC, 128, S], F32R, kind="ExternalInput").ap()
    w = {}
    for n in ("wq", "wk", "wv", "we"):
        shp = [1, NDC, 128, 128] if n == "we" else [NDC, NDC, 128, 128]
        w[n] = nc.dram_tensor(n, shp, F32R, kind="ExternalInput").ap()
    w["wo"] = nc.dram_tensor("wo", [NDC, NDC, 128, 128], F32R, kind="ExternalInput").ap()
    xo = nc.dram_tensor("xo", [bpc, NDC, 128, S], F32, kind="ExternalOutput").ap()
    am = nc.dram_tensor("am", [1, 1], F32, kind="ExternalOutput").ap()
    amscr = nc.dram_tensor("amscr", [128, 1], F32).ap()

    es = ExitStack()
    kb = KB(nc)
    B = alloc_bufs(kb, es)
    emit_setup(kb, B)
    for b in range(bpc):
        emit_l0attn(kb, B, [xt[b, dc] for dc in range(NDC)], w, b,
                    dbg=(emit_tap if (debug and b == 0) else None), nc_=nc,
                    skip_ln=skip_ln)
        emit_store_sx(kb, B, [xo[b, dc] for dc in range(NDC)], b)
    emit_amax_local(kb, B, amscr)
    v = kb.op("gpsimd", kb.need("amax_local"),
              lambda e: e.dma_start(out=am, in_=B["srow"][0:1, 128:129]),
              ("dG", 16))
    kb.op("gpsimd", [(n, kb.now(n)) for n in
                     ("dG", "dX", "dA0", "dA1", "dS", "dR", "dR2", "dM", "dB") if kb.cnt.get(n)],
          lambda e: e.engine_nop(), None)
    kb.finalize()
    es.close()
    return nc


def prep_l0ffn_inputs(X1, weight, bias, amaxes, core, n_cores=8):
    bpc = X1.shape[0] // n_cores
    xt = X1[core * bpc:(core + 1) * bpc]
    return {
        "xt": np.ascontiguousarray(xt),
        "w8": chunk_T(np.ascontiguousarray(weight.T)),
        "bias": np.ascontiguousarray(bias.reshape(8, 128).T),
        "amaxes": np.ascontiguousarray(amaxes.reshape(1, 8).astype(np.float32)),
    }


def build_l0ffn_standalone(bpc=2, skip_ln=False, debug=False):
    nc = bass.Bass("TRN2", target_bir_lowering=False, dynamic_dma_scratch_size=16384,
                   detect_race_conditions=False)
    xt = nc.dram_tensor("xt", [bpc, NDC, 128, S], F32R, kind="ExternalInput").ap()
    w8 = nc.dram_tensor("w8", [NDC, NDC, 128, 128], F32R, kind="ExternalInput").ap()
    bias = nc.dram_tensor("bias", [128, 8], F32, kind="ExternalInput").ap()
    amaxes = nc.dram_tensor("amaxes", [1, 8], F32, kind="ExternalInput").ap()
    xo = nc.dram_tensor("xo", [bpc, NDC, 128, S], F32, kind="ExternalOutput").ap()
    amscr = nc.dram_tensor("amscr", [128, 1], F32).ap()
    sclscr = nc.dram_tensor("sclscr", [1, 4], F32).ap()

    es = ExitStack()
    kb = KB(nc)
    B = alloc_bufs(kb, es)
    emit_setup(kb, B)
    emit_l0ffn_prep(kb, B, w8, bias, amaxes, amscr, sclscr)
    for b in range(bpc):
        for dc in range(NDC):
            v = kb.op("sync", kb.get_guard(("sx", dc)),
                      lambda e, b=b, dc=dc: e.dma_start(
                          out=B["sx"][:, dc * S:(dc + 1) * S], in_=xt[b, dc]),
                      ("dX", 16))
        for dc in range(NDC):
            kb.mark(("sx", dc), "dX", v)
        rowtap = None
        if debug and b == 0:
            rt_t = nc.dram_tensor("rowsD", [1, 1536], F32, kind="ExternalOutput").ap()
            def rowtap(kb_, mv, rt_t=rt_t):
                vv2 = kb_.op("gpsimd", [("vC", mv)],
                       lambda e: e.dma_start(out=rt_t[0:1, 0:1536],
                                             in_=B["rows"][0:1, :].bitcast(F32)),
                       ("dG", 16))
                kb_.set_guard("rows", "dG", vv2)
        emit_l0ffn(kb, B, b, skip_ln=skip_ln, ln_dbg=rowtap)
        if debug and b == 0:
            emit_tap(kb, nc, B, "ktD", B["kt"][:, :], kb.need("xq8"), guard_res="kt")
            emit_tap(kb, nc, B, "qtD", B["qt"][:, 0:8192], kb.need("wq8"), guard_res="qt")
            emit_tap(kb, nc, B, "sclD", B["scl"][:, 0:3], kb.need("scl"))
            emit_tap(kb, nc, B, "srowD", B["srow"][0:1, 128:130], kb.need("scl"))
        emit_store_sx(kb, B, [xo[b, dc] for dc in range(NDC)], b)
    kb.op("gpsimd", [(n, kb.now(n)) for n in
                     ("dG", "dX", "dA0", "dA1", "dS", "dR", "dR2", "dM", "dB") if kb.cnt.get(n)],
          lambda e: e.engine_nop(), None)
    kb.finalize()
    es.close()
    return nc


def prep_l1attn_inputs(X1, Wq, Wk, Wv, Wo, core, n_cores=8):
    bpc = X1.shape[0] // n_cores
    return {
        "xt": np.ascontiguousarray(X1[core * bpc:(core + 1) * bpc]),
        "wq": chunk_T(Wq), "wk": chunk_T(Wk), "wv": chunk_T(Wv), "wo": chunk_T(Wo),
    }


def build_l1attn_standalone(bpc=2, skip_ln=False):
    nc = bass.Bass("TRN2", target_bir_lowering=False, dynamic_dma_scratch_size=16384,
                   detect_race_conditions=False)
    xt = nc.dram_tensor("xt", [bpc, NDC, 128, S], F32R, kind="ExternalInput").ap()
    w = {}
    for n in ("wq", "wk", "wv", "wo"):
        w[n] = nc.dram_tensor(n, [NDC, NDC, 128, 128], F32R, kind="ExternalInput").ap()
    xo = nc.dram_tensor("xo", [bpc, NDC, 128, S], F32, kind="ExternalOutput").ap()

    es = ExitStack()
    kb = KB(nc)
    B = alloc_bufs(kb, es)
    emit_setup(kb, B)
    for b in range(bpc):
        emit_l0attn(kb, B, [xt[b, dc] for dc in range(NDC)], w, b,
                    skip_ln=skip_ln, layer=1)
        emit_store_sx(kb, B, [xo[b, dc] for dc in range(NDC)], b)
    kb.op("gpsimd", [(n, kb.now(n)) for n in
                     ("dG", "dX", "dA0", "dA1", "dS", "dR", "dR2", "dM", "dB") if kb.cnt.get(n)],
          lambda e: e.engine_nop(), None)
    kb.finalize()
    es.close()
    return nc


def build_full_fused(bpc=2, debug=False):
    nc = bass.Bass("TRN2", target_bir_lowering=False, dynamic_dma_scratch_size=16384,
                   detect_race_conditions=False)
    xt = nc.dram_tensor("xt", [bpc, NDC, 128, S], F32R, kind="ExternalInput").ap()
    w0 = {}
    for n in ("wq", "wk", "wv", "wo", "we"):
        shp = [1, NDC, 128, 128] if n == "we" else [NDC, NDC, 128, 128]
        w0[n] = nc.dram_tensor("w0" + n, shp, F32R, kind="ExternalInput").ap()
    w8 = nc.dram_tensor("w8", [NDC, NDC, 128, 128], F32R, kind="ExternalInput").ap()
    bias = nc.dram_tensor("bias", [128, 8], F32, kind="ExternalInput").ap()
    w1 = {n: nc.dram_tensor("w1" + n, [NDC, NDC, 128, 128], F32R,
                            kind="ExternalInput").ap()
          for n in ("wq", "wk", "wv", "wo")}
    wg = nc.dram_tensor("wg", [22, 128, 1024], F32R, kind="ExternalInput").ap()
    wu = nc.dram_tensor("wu", [22, 128, 1024], F32R, kind="ExternalInput").ap()
    wd = nc.dram_tensor("wd", [8, 128, 2816], F32R, kind="ExternalInput").ap()
    xo = nc.dram_tensor("xo", [bpc, NDC, 128, S], F32, kind="ExternalOutput").ap()
    dbgk = "ExternalOutput" if debug else "Internal"
    x1scr = nc.dram_tensor("x1scr", [NDC, 128, S], F32R, kind=dbgk).ap()
    x2scr = nc.dram_tensor("x2scr", [NDC, 128, S], F32R, kind=dbgk).ap()
    if debug:
        xd2b0 = nc.dram_tensor("xd2b0", [NDC, 128, S], F32R, kind="ExternalOutput").ap()
        xd3b0 = nc.dram_tensor("xd3b0", [NDC, 128, S], F32R, kind="ExternalOutput").ap()
        xd3b1 = nc.dram_tensor("xd3b1", [NDC, 128, S], F32R, kind="ExternalOutput").ap()
    amscr = nc.dram_tensor("amscr", [128, 1], F32).ap()
    amscr2 = nc.dram_tensor("amscr2", [128, 1], F32).ap()
    sclscr = nc.dram_tensor("sclscr", [1, 4], F32).ap()
    cc_in_t = nc.dram_tensor("cc_in", [1, 8], F32)
    cc_out_t = nc.dram_tensor("cc_out", [1, 8], F32)
    spills = {n: nc.dram_tensor("sp_" + n, [8, 128, 2816], BF16).ap()
              for n in ("wg", "wu", "wd")}

    es = ExitStack()
    kb = KB(nc)
    B = alloc_bufs(kb, es)
    emit_setup(kb, B)
    emit_l1ffn_prep(kb, B, wg, wu, wd, spills)
    emit_fence(kb)
    emit_l0attn(kb, B, [xt[0, dc] for dc in range(NDC)], w0, 0)
    emit_spill_sx_r(kb, B, x1scr)
    emit_l0attn(kb, B, [xt[1, dc] for dc in range(NDC)], w0, 1)
    emit_fence(kb)
    emit_l0ffn_prep(kb, B, w8, bias, None, amscr, sclscr,
                    fused_cc=(cc_in_t, cc_out_t, amscr2))
    emit_fence(kb)
    emit_l0ffn(kb, B, 1)
    emit_fence(kb)
    emit_spill_sx_r(kb, B, x2scr)
    emit_fence(kb)
    emit_load_sx_r(kb, B, x1scr)
    emit_l0ffn(kb, B, 0)
    emit_fence(kb)
    if debug:
        emit_spill_sx_r(kb, B, xd2b0)
        emit_fence(kb)
    emit_l0attn(kb, B, None, w1, 0, layer=1)
    emit_fence(kb)
    if debug:
        emit_spill_sx_r(kb, B, xd3b0)
        emit_fence(kb)
    emit_l1ffn(kb, B, spills, 0)
    emit_fence(kb)
    emit_store_sx(kb, B, [xo[0, dc] for dc in range(NDC)], 0)
    emit_fence(kb)
    emit_load_sx_r(kb, B, x2scr)
    if debug:
        emit_tap(kb, nc, B, "sxldD", B["sx"][:, :].bitcast(F32),
                 kb.need(*[("sx", dc) for dc in range(NDC)]))
        emit_fence(kb)
    emit_l0attn(kb, B, None, w1, 1, layer=1,
                dbg=(emit_tap if debug else None), nc_=nc)
    emit_fence(kb)
    if debug:
        emit_spill_sx_r(kb, B, xd3b1)
        emit_fence(kb)
    emit_l1ffn(kb, B, spills, 1)
    emit_fence(kb)
    emit_store_sx(kb, B, [xo[1, dc] for dc in range(NDC)], 1)
    emit_fence(kb)
    kb.finalize()
    es.close()
    return nc


def prep_l1ffn_inputs(X1, wg, wu, wd, core, n_cores=8):
    bpc = X1.shape[0] // n_cores
    return {
        "xt": np.ascontiguousarray(X1[core * bpc:(core + 1) * bpc]),
        "wg": np.ascontiguousarray(wg.reshape(22, 128, 1024)),
        "wu": np.ascontiguousarray(wu.reshape(22, 128, 1024)),
        "wd": np.ascontiguousarray(wd.reshape(8, 128, 2816)),
    }


def build_l1ffn_standalone(bpc=2, skip_ln=False, debug_spills=False, debug_taps=False, sc_list=None):
    nc = bass.Bass("TRN2", target_bir_lowering=False, dynamic_dma_scratch_size=16384,
                   detect_race_conditions=False)
    xt = nc.dram_tensor("xt", [bpc, NDC, 128, S], F32R, kind="ExternalInput").ap()
    wg = nc.dram_tensor("wg", [22, 128, 1024], F32R, kind="ExternalInput").ap()
    wu = nc.dram_tensor("wu", [22, 128, 1024], F32R, kind="ExternalInput").ap()
    wd = nc.dram_tensor("wd", [8, 128, 2816], F32R, kind="ExternalInput").ap()
    xo = nc.dram_tensor("xo", [bpc, NDC, 128, S], F32, kind="ExternalOutput").ap()
    spills = {n: nc.dram_tensor("sp_" + n, [8, 128, 2816], BF16,
                                kind=("ExternalOutput" if debug_spills else "Internal")).ap()
              for n in ("wg", "wu", "wd")}

    es = ExitStack()
    kb = KB(nc)
    B = alloc_bufs(kb, es)
    emit_setup(kb, B)
    emit_l1ffn_prep(kb, B, wg, wu, wd, spills)
    for b in range(bpc):
        for dc in range(NDC):
            v = kb.op("sync", kb.get_guard(("sx", dc)),
                      lambda e, b=b, dc=dc: e.dma_start(
                          out=B["sx"][:, dc * S:(dc + 1) * S], in_=xt[b, dc]),
                      ("dX", 16))
        for dc in range(NDC):
            kb.mark(("sx", dc), "dX", v)
        dbgfn = None
        if debug_taps and b == 0:
            def dbgfn(kb_, name, ap, waits, guard_keys):
                t = nc.dram_tensor(name, list(ap.shape), ap.dtype, kind="ExternalOutput").ap()
                vv_ = kb_.op("gpsimd", waits,
                             lambda e, t=t, ap=ap: e.dma_start(out=t, in_=ap),
                             ("dG", 16))
                for kk in guard_keys:
                    kb_.set_guard(kk, "dG", vv_)
        emit_l1ffn(kb, B, spills, b, dbg=dbgfn, sc_list=sc_list, skip_ln=skip_ln)
        emit_store_sx(kb, B, [xo[b, dc] for dc in range(NDC)], b)
    kb.op("gpsimd", [(n, kb.now(n)) for n in
                     ("dG", "dX", "dA0", "dA1", "dS0", "dS1", "dR", "dR2", "dM", "dB",
                      "dW", "dV0", "dV1", "dV2", "dV3", "dD") if kb.cnt.get(n)],
          lambda e: e.engine_nop(), None)
    kb.finalize()
    es.close()
    return nc


def kernel(L0_attn_X, L0_attn_Wq, L0_attn_Wk, L0_attn_Wv, L0_attn_Wo, L0_attn_We,
           L0_ffn_weight, L0_ffn_bias, L1_attn_Wq, L1_attn_Wk, L1_attn_Wv,
           L1_attn_Wo, L1_ffn_w_gate, L1_ffn_w_up, L1_ffn_w_down):
    from concourse.bass_utils import run_bass_kernel_spmd
    n_cores = 8
    cores = list(range(n_cores))

    nc1 = build_full_fused(bpc=2)
    shared = {
        "w0wq": chunk_T(L0_attn_Wq.T), "w0wk": chunk_T(L0_attn_Wk.T),
        "w0wv": chunk_T(L0_attn_Wv.T), "w0wo": chunk_T(L0_attn_Wo.T),
        "w0we": chunk_T(L0_attn_We.T),
        "w8": chunk_T(np.ascontiguousarray(L0_ffn_weight.T)),
        "bias": np.ascontiguousarray(L0_ffn_bias.reshape(8, 128).T),
        "w1wq": chunk_T(L1_attn_Wq), "w1wk": chunk_T(L1_attn_Wk),
        "w1wv": chunk_T(L1_attn_Wv), "w1wo": chunk_T(L1_attn_Wo),
        "wg": np.ascontiguousarray(L1_ffn_w_gate.reshape(22, 128, 1024)),
        "wu": np.ascontiguousarray(L1_ffn_w_up.reshape(22, 128, 1024)),
        "wd": np.ascontiguousarray(L1_ffn_w_down.reshape(8, 128, 2816)),
    }
    ins = []
    for c in cores:
        xs = L0_attn_X[c * 2:(c + 1) * 2]
        xtc = np.ascontiguousarray(xs.transpose(0, 2, 1)).reshape(2, NDC, 128, S)
        d = {"xt": np.ascontiguousarray(xtc)}
        d.update(shared)
        ins.append(d)
    r = run_bass_kernel_spmd(nc1, ins, core_ids=cores).results

    out = np.empty((16, 2048, 1024), dtype=np.float32)
    for c in cores:
        xc = np.asarray(r[c]["xo"]).reshape(2, 1024, 2048)
        out[c * 2] = xc[0].T
        out[c * 2 + 1] = xc[1].T
    return out


# version 08
# speedup vs baseline: 1.1336x; 1.0504x over previous
import numpy as np
import concourse.bass as bass
import concourse.mybir as mybir

F32 = mybir.dt.float32
F32R = mybir.dt.float32r
BF16 = mybir.dt.bfloat16
I32 = mybir.dt.int32
AF = mybir.ActivationFunctionType
ALU = mybir.AluOpType
AX = mybir.AxisListType

D = 1024
S = 2048
H = 16
DK = 64
DE = 128
NDC = D // 128
NSC = S // 512
NST = S // 128
FFI = 2816
SCALE = 0.125
EPS = 1e-5


class KB:

    def __init__(self, nc):
        self.nc = nc
        self.items = []
        self.sems = {}
        self.cnt = {}
        self.guard = {}
        self.ready = {}
        self._waited = {}
        self._es = None

    def sem(self, name):
        if name not in self.sems:
            h = self.nc.semaphore(name)
            self.sems[name] = h.__enter__()
            self.cnt[name] = 0
        return self.sems[name]

    def inc(self, name, amount=1):
        self.sem(name)
        self.cnt[name] += amount
        return self.cnt[name]

    def now(self, name):
        self.sem(name)
        return self.cnt[name]

    def add(self, engine, fn):
        self.items.append((engine, fn))

    def set_guard(self, res, semname, val):
        d = self.guard.setdefault(res, {})
        d[semname] = max(d.get(semname, 0), val)

    def clear_guard(self, res):
        self.guard.pop(res, None)

    def get_guard(self, res):
        return list(self.guard.get(res, {}).items())

    def mark(self, res, semname, val):
        self.ready[res] = (semname, val)

    def need(self, *res):
        out = []
        for r in res:
            g = self.ready.get(r)
            if g:
                out.append(g)
        return out

    def vchain(self, steps, waits0=()):
        prev = None
        for i, emit in enumerate(steps):
            w = list(waits0) if i == 0 else [("vC", prev)]
            prev = self.op("vector", w, emit, ("vC", 1))
        return prev

    def op(self, engine, waits, emit, inc=None):
        newval = None
        incsem = None
        incamt = 0
        if inc is not None:
            name, incamt = (inc[0], inc[1]) if len(inc) > 1 else (inc[0], 1)
            newval = self.inc(name, incamt)
            incsem = self.sems[name]
        wl = []
        wmax = self._waited.setdefault(engine, {})
        for n, v in waits:
            if v and v > wmax.get(n, 0):
                wmax[n] = v
                wl.append((self.sem(n), v))

        def fn(eng, wl=wl, emit=emit, incsem=incsem, incamt=incamt):
            for s, v in wl:
                eng.wait_ge(s, v)
            ins = emit(eng)
            if incsem is not None and ins is not None:
                ins.then_inc(incsem, incamt)
        self.add(engine, fn)
        return newval

    def finalize(self):
        self._lp = self.nc.allow_low_precision("f32r intermediates are intentional")
        self._lp.__enter__()
        block = self.nc.Block()
        block.__enter__()
        by_eng = {}
        for e, fn in self.items:
            by_eng.setdefault(e, []).append(fn)
        for e, fns in by_eng.items():
            def body(engine, fns=fns):
                for f in fns:
                    f(engine)
            getattr(block, e)(body)
        block.__exit__(None, None, None)
        self._lp.__exit__(None, None, None)


def fn_engine(e):
    return e


def ceil_div(a, b):
    return (a + b - 1) // b



def chunk_T(w):
    w = np.asarray(w)
    d, f = w.shape
    t = w.reshape(d // 128, 128, f // 128, 128)
    t = t.transpose(2, 0, 1, 3)
    return np.ascontiguousarray(t)


"""Static SBUF / PSUM allocation shared by all sublayer emitters."""


def alloc_bufs(kb, es):
    nc = kb.nc
    B = {}

    def sb(name, shape, dtype):
        h = es.enter_context(nc.sbuf_tensor(name, shape, dtype))
        B[name] = h
        return h

    sb("sx", [128, NDC * S], F32R)
    sb("kt", [128, NDC * S], BF16)
    sb("qt", [128, NDC * S], BF16)
    sb("vv", [128, NST * (H * 65 + 0)], BF16)
    sb("cx", [128, S], BF16)
    sb("ao", [128, NDC * 512], BF16)
    sb("at", [128, 4 * 512], BF16)
    sb("wa", [128, 2 * 1024], F32R)
    sb("wc", [128, 2 * 1024], BF16)
    sb("sq", [128, 2 * 512], F32R)
    sb("rr", [1, 512], F32R)
    sb("br", [64, 512], F32R)
    sb("rows", [1, 3 * 512], F32R)
    sb("onesc", [128, 1], F32R)
    sb("ones1", [1, 128], F32R)
    sb("ident_b", [128, 128], BF16)
    sb("ident_r", [128, 128], F32R)
    sb("ident_f", [128, 128], F32)
    sb("am8", [128, 8], F32)
    sb("onesb", [128, 16], BF16)
    sb("scl", [128, 4], F32)
    sb("biasb", [128, 8], F32)
    sb("am2", [128, 2], F32)
    sb("srow", [1, 160], F32)

    banks = {}
    for n in ("ps0a", "ps1a", "ps0b", "ps1b", "po0", "po1", "pu"):
        banks[n] = es.enter_context(nc.psum_tensor(n, [128, 512], F32))
    banks["pvB"] = es.enter_context(nc.psum_tensor("pvB", [128, 1024], BF16))
    B["banks"] = banks
    return B


def emit_vv_ones(kb, B):
    v1 = None
    for st in range(NST):
        dst = B["vv"][:, st * 1040: (st + 1) * 1040]             .rearrange("p (h c) -> p h c", c=65)[:, :, 64:65]
        src = B["onesb"][:, :].rearrange("p (h o) -> p h o", o=1)
        v1 = kb.op("vector", kb.get_guard("vv") if st == 0 else [],
                   lambda e, dst=dst, src=src: e.tensor_copy(dst, src),
                   ("vC", 1) if st == NST - 1 else None)
    return v1


def emit_setup(kb, B):
    gv = kb.op("gpsimd", [], lambda e: e.memset(B["ident_f"][:, :], 1.0), ("gP", 1))
    v1 = kb.op("vector", [("gP", gv)],
               lambda e: e.tensor_copy(B["onesb"][:, :], B["ident_f"][:, 0:16]), None)
    v1 = kb.op("vector", [],
               lambda e: e.tensor_copy(B["onesc"][:, :], B["ident_f"][:, 0:1]), None)
    v1 = kb.op("vector", [],
               lambda e: e.tensor_copy(B["ones1"][:, :], B["ident_f"][0:1, :]), ("vC", 1))
    v1 = emit_vv_ones(kb, B)
    gv = kb.op("gpsimd", [("vC", v1)], lambda e: e.memset(B["ident_f"][:, :], 0.0), None)
    gv = kb.op("gpsimd", [],
               lambda e: e.affine_select(
                   out=B["ident_f"][:, :], in_=B["ident_f"][:, :],
                   compare_op=ALU.not_equal, fill=1.0, base=0,
                   pattern=[[-1, 128]], channel_multiplier=1),
               ("gP", 1))
    v1 = kb.op("vector", [("gP", gv)],
               lambda e: e.tensor_copy(B["ident_b"][:, :], B["ident_f"][:, :]), None)
    v1 = kb.op("vector", [],
               lambda e: e.tensor_copy(B["ident_r"][:, :], B["ident_f"][:, :]), ("vC", 1))
    kb.mark("consts", "vC", v1)
    kb.mark("consts_g", "gP", gv)


"""LayerNorm over the feature dim, in place on bufs['sx'] ([d,seq] layout).

Stats via ones-matmuls over partitions (accumulated across the 8 d-chunks),
row math on [1,512] vectors, rstd = Exp(-0.5*Ln(var+eps)) (one act table),
broadcast back via rank-1 matmuls, normalize with two TT passes per chunk.
Marks ("sx", dc) and ("sx", dc, "ln") when chunk dc is fully normalized.
"""


def emit_ln(kb, B, rdeps, dbg=None):
    sx, sq = B["sx"], B["sq"]
    rows = B["rows"]
    banks = B["banks"]
    st_name, st = "po1", banks["po1"]
    s2_name, st2 = "pu", banks["pu"]
    b1n, b1 = "ps0a", banks["ps0a"]
    b2n, b2 = "ps1a", banks["ps1a"]
    onesc, ones1 = B["onesc"], B["ones1"]
    lastv = {}

    def sxsl(dc, c0, n):
        return sx[:, dc * S + c0: dc * S + c0 + n]

    for sc in range(NSC):
        tv = None
        for dc in range(NDC):
            waits = [w for w in [kb.ready.get(("sx", dc, "r", sc))] if w]
            waits = [(n, v) for (n, v) in waits]
            if dc == 0:
                waits = waits + kb.get_guard(st_name) + kb.need("consts")
            tv = kb.op("tensor", waits,
                       lambda e, dc=dc, sc=sc: e.matmul(
                           st[0:1, :], onesc[:, 0:1], sxsl(dc, sc * 512, 512),
                           start=(dc == 0), stop=(dc == NDC - 1)),
                       ("tP", 1) if dc == NDC - 1 else None)
        for dc in range(NDC):
            slot = dc % 2
            waits = kb.get_guard(("sq", slot))
            w2 = kb.ready.get(("sx", dc, "r", sc))
            if w2:
                waits = waits + [w2]
            sv = kb.op("scalar", waits,
                       lambda e, dc=dc, sc=sc, slot=slot: e.activation(
                           sq[:, slot * 512:(slot + 1) * 512],
                           sxsl(dc, sc * 512, 512).bitcast(F32), AF.Square),
                       ("sC", 1))
            w3 = [("sC", sv)] + (kb.get_guard(s2_name) if dc == 0 else [])
            tv = kb.op("tensor", w3,
                       lambda e, slot=slot, dc=dc: e.matmul(
                           st2[0:1, :], onesc[:, 0:1], sq[:, slot * 512:(slot + 1) * 512],
                           start=(dc == 0), stop=(dc == NDC - 1)),
                       ("tP", 1) if dc == NDC - 1 else None)
            kb.set_guard(("sq", slot), "tP", tv if tv is not None else kb.now("tP"))
        r_m = rows[:, 0:512]
        r_v = rows[:, 512:1024]
        r_t = rows[:, 1024:1536]
        rr_s = rows[:, 512:1024]
        rr_mr = rows[:, 1024:1536]
        kb.op("vector", [("tP", tv)] + kb.get_guard("rows"),
              lambda e: e.tensor_scalar(r_m, st[0:1, :], 1.0 / D, None, ALU.mult), None)
        kb.op("vector", [],
              lambda e: e.tensor_scalar(r_v, st2[0:1, :], 1.0 / D, None, ALU.mult), None)
        kb.op("vector", [], lambda e: e.tensor_tensor(r_t, r_m, r_m, ALU.mult), None)
        kb.op("vector", [], lambda e: e.tensor_tensor(r_v, r_v, r_t, ALU.subtract), None)
        vv_ = kb.op("vector", [],
                    lambda e: e.tensor_scalar(r_v, r_v, EPS, None, ALU.add), ("vC", 1))
        kb.set_guard(st_name, "vC", vv_)
        kb.set_guard(s2_name, "vC", vv_)
        sv = kb.op("scalar", [("vC", vv_)],
                   lambda e: e.activation(r_t, r_v.bitcast(F32), AF.Ln), None)
        sv = kb.op("scalar", [],
                   lambda e: e.activation(rr_s, r_t.bitcast(F32), AF.Exp, scale=-0.5),
                   ("sC", 1))
        mv = kb.op("vector", [("sC", sv)],
                   lambda e: e.tensor_tensor(rr_mr, r_m, rr_s, ALU.mult), ("vC", 1))
        if dbg is not None and sc == 0:
            dbg(kb, mv)
        bv1 = kb.op("tensor", kb.get_guard(b1n) + [("sC", sv)],
                    lambda e: e.matmul(b1[:, :], ones1[0:1, :], rr_s,
                                       start=True, stop=True), ("tP", 1))
        bv2 = kb.op("tensor", kb.get_guard(b2n) + [("vC", mv)],
                    lambda e: e.matmul(b2[:, :], ones1[0:1, :], rr_mr,
                                       start=True, stop=True), ("tP", 1))
        kb.set_guard("rows", "tP", bv2)
        nv = None
        for dc in range(NDC):
            slot = dc % 2
            waits = [("tP", bv2)] + kb.get_guard(("sq", slot))
            kb.op("vector", waits,
                  lambda e, dc=dc, sc=sc, slot=slot: e.tensor_tensor(
                      sq[:, slot * 512:(slot + 1) * 512],
                      sxsl(dc, sc * 512, 512).bitcast(F32), b1[:, :], ALU.mult),
                  None)
            nv = kb.op("vector", [],
                       lambda e, dc=dc, sc=sc, slot=slot: e.tensor_tensor(
                           sxsl(dc, sc * 512, 512),
                           sq[:, slot * 512:(slot + 1) * 512].bitcast(F32),
                           b2[:, :], ALU.subtract),
                       ("vC", 1))
            kb.set_guard(("sq", slot), "vC", nv)
            lastv[dc] = nv
        kb.set_guard(b1n, "vC", nv)
        kb.set_guard(b2n, "vC", nv)
    for dc in range(NDC):
        kb.mark(("sx", dc), "vC", lastv[dc])
        kb.mark(("sx", dc, "ln"), "vC", lastv[dc])


"""L0 attention sublayer emitter (softmax attn + ctx-influence bias).

x_out = LN(x + Attn(x)) written in place into bufs['sx'];
per-batch |x_out| partition-max column into bufs['am2'][:, b].

See common.py for layouts. psum banks (f32 [128,512] unless noted):
ps0a ps1a ps0b ps1b po0 po1 pu, pvB ([128,1024] bf16).
"""


def wcol_ap(wdram, fc):
    return wdram[fc].transpose([1, 0, 2])


def wdst_ap(buf, off, n=1024):
    return buf[:, off: off + n].rearrange("p (c k) -> p c k", k=128)


def emit_l0attn(kb, B, xt_dram, w, b, dbg=None, nc_=None, skip_ln=False, layer=0):
    sx, kt, qt, vv, cx, ao, at = B["sx"], B["kt"], B["qt"], B["vv"], B["cx"], B["ao"], B["at"]
    wa, wc, rr, br = B["wa"], B["wc"], B["rr"], B["br"]
    banks = B["banks"]
    ps = [banks[n] for n in ("ps0a", "ps1a", "ps0b", "ps1b")]
    psn = ["ps0a", "ps1a", "ps0b", "ps1b"]
    po = [banks["po0"], banks["po1"]]
    pon = ["po0", "po1"]
    pu, po1b, pvB = banks["pu"], banks["po1"], banks["pvB"]

    def sxsl(dc, c0, n):
        return sx[:, dc * S + c0: dc * S + c0 + n]

    emit_vv_ones(kb, B)

    if xt_dram is not None:
        for dc in range(NDC):
            v = kb.op("sync", kb.get_guard(("sx", dc)),
                      lambda e, dc=dc: e.dma_start(
                          out=sx[:, dc * S:(dc + 1) * S], in_=xt_dram[dc]),
                      ("dX", 16))
        for dc in range(NDC):
            kb.mark(("sx", dc), "dX", v)

    def proj_phase(wdram, evac, nfc=NDC):
        util = [("pu", pu), ("po1", po1b)]
        for fc in range(nfc):
            slot = fc % 2
            v = kb.op("sync", kb.get_guard(("wa", slot)),
                      lambda e, wdram=wdram, fc=fc, slot=slot: e.dma_start(
                          out=wdst_ap(wa, slot * 1024), in_=wcol_ap(wdram, fc)),
                      ("dA%d" % slot, 16))
            kb.mark(("wa", slot), "dA%d" % slot, v)
            for sc in range(NSC):
                j = fc * NSC + sc
                bname, bank = util[j % 2]
                tv = None
                for dc in range(NDC):
                    waits = []
                    if dc == 0:
                        waits = kb.get_guard(bname) + kb.need(("wa", slot))
                    waits = waits + kb.need(("sx", dc))
                    lhs = wa[:, slot * 1024 + dc * 128: slot * 1024 + (dc + 1) * 128]
                    rhs = sxsl(dc, sc * 512, 512)
                    last = dc == NDC - 1
                    tv = kb.op("tensor", waits,
                               lambda e, bank=bank, lhs=lhs, rhs=rhs, dc=dc, last=last: e.matmul(
                                   bank[:, :], lhs, rhs, start=(dc == 0), stop=last),
                               ("tP", 1) if last else None)
                if sc == NSC - 1:
                    kb.set_guard(("wa", slot), "tP", tv)
                evac(bname, bank, fc, sc, tv)

    def scalar_evac(dst_ap, bank, bname, tv, scale):
        v = kb.op("scalar", [("tP", tv)],
                  lambda e, dst=dst_ap, bank=bank, scale=scale: e.activation(
                      dst, bank[:, :], AF.Copy, scale=scale),
                  ("sC", 1))
        kb.set_guard(bname, "sC", v)
        return v

    if layer == 0:
        def ctx_evac(bname, bank, fc, sc, tv):
            v = scalar_evac(cx[:, sc * 512:(sc + 1) * 512], bank, bname, tv,
                            float(np.sqrt(0.1)))
            if sc == NSC - 1:
                kb.mark("cx", "sC", v)
        proj_phase(w["we"], ctx_evac, nfc=1)

    def k_evac(bname, bank, fc, sc, tv):
        v = scalar_evac(kt[:, fc * S + sc * 512: fc * S + (sc + 1) * 512], bank, bname, tv, 1.0)
        if fc == NDC - 1 and sc == NSC - 1:
            kb.mark("kt", "sC", v)
    proj_phase(w["wk"], k_evac)

    def v_evac(bname, bank, fc, sc, tv):
        v = scalar_evac(qt[:, fc * S + sc * 512: fc * S + (sc + 1) * 512], bank, bname, tv, 1.0)
        if fc == NDC - 1 and sc == NSC - 1:
            kb.mark("qtv", "sC", v)
    proj_phase(w["wv"], v_evac)

    for fc in range(NDC):
        for half in range(2):
            tv = None
            for stl in range(8):
                st = half * 8 + stl
                waits = kb.need("qtv", "consts") + (kb.get_guard("pvB") if stl == 0 else [])
                tv = kb.op("tensor", waits,
                           lambda e, fc=fc, st=st, stl=stl: e.transpose(
                               pvB[:, stl * 128:(stl + 1) * 128],
                               qt[:, fc * S + st * 128: fc * S + (st + 1) * 128],
                               B["ident_b"][:, :]),
                           ("tP", 1))
            src = pvB[:, 0:1024].rearrange("p (st h c) -> p st h c", h=2, c=64)
            dst = vv[:, :].rearrange("p (st x) -> p st x", x=1040)[
                :, half * 8:(half + 1) * 8, 2 * fc * 65: 2 * fc * 65 + 130] \
                .rearrange("p st (h c) -> p st h c", c=65)[:, :, :, 0:64]
            vcv = kb.op("vector", [("tP", tv)],
                        lambda e, src=src, dst=dst: e.tensor_copy(dst, src),
                        ("vC", 1))
            kb.set_guard("pvB", "vC", vcv)
    kb.set_guard("qtbuf", "tP", kb.now("tP"))
    kb.mark("vv", "vC", kb.now("vC"))

    if dbg:
        dbg(kb, nc_, B, "cxD", cx[:, :], kb.need("cx"))
        dbg(kb, nc_, B, "ktD", kt[:, :], kb.need("kt"))
        dbg(kb, nc_, B, "qtvD", qt[:, :], kb.need("qtv"), guard_res="qtbuf")
        dbg(kb, nc_, B, "vvD", vv[:, :], kb.need("vv"))

    def q_evac(bname, bank, fc, sc, tv):
        waits_extra = kb.get_guard("qtbuf")
        v = kb.op("scalar", [("tP", tv)] + waits_extra,
                  lambda e, fc=fc, sc=sc, bank=bank: e.activation(
                      qt[:, fc * S + sc * 512: fc * S + (sc + 1) * 512],
                      bank[:, :], AF.Copy, scale=SCALE),
                  ("sC", 1))
        kb.set_guard(bname, "sC", v)
        if fc == NDC - 1 and sc == NSC - 1:
            kb.mark("qt", "sC", v)
    proj_phase(w["wq"], q_evac)

    if dbg:
        dbg(kb, nc_, B, "qtD", qt[:, :], kb.need("qt"))

    for qc in range(NSC):
        for hp in range(NDC):
            hpo = hp * S
            qco = qc * 512
            avdone = [None, None]
            nkt = NST if layer == 0 else 4 * qc + 4
            for ktile in range(nkt):
                pr = (ktile % 2) * 2
                off = 0
                if layer == 1:
                    r = ktile - 4 * qc
                    off = max(0, r) * 128
                expv = [None, None]
                for par in (0, 1):
                    bname, bank = psn[pr + par], ps[pr + par]
                    r0, r1 = par * 64, par * 64 + 64
                    waits = kb.get_guard(bname) + kb.need("kt", "qt")
                    kb.op("tensor", waits,
                          lambda e, bank=bank, hpo=hpo, qco=qco, kt0=ktile * 128,
                                 r0=r0, r1=r1, off=off: e.matmul(
                              bank[:, off:512], kt[r0:r1, hpo + kt0: hpo + kt0 + 128],
                              qt[r0:r1, hpo + qco + off: hpo + qco + 512],
                              start=True, stop=(layer == 1),
                              tile_position=(r0, 0)),
                          ("tP", 1) if layer == 1 else None)
                tvs = [None, None]
                if layer == 0:
                    for par in (0, 1):
                        bname, bank = psn[pr + par], ps[pr + par]
                        tvs[par] = kb.op("tensor", kb.need("cx"),
                                         lambda e, bank=bank, qco=qco, kt0=ktile * 128: e.matmul(
                                             bank[:, :], cx[:, kt0: kt0 + 128],
                                             cx[:, qco: qco + 512],
                                             start=False, stop=True),
                                         ("tP", 1))
                else:
                    tvs = [kb.now("tP"), kb.now("tP")]
                for par in (0, 1):
                    bname, bank = psn[pr + par], ps[pr + par]
                    slot = pr + par
                    waits = [("tP", tvs[par])] + kb.get_guard(("at", slot))
                    ev = kb.op("scalar", waits,
                               lambda e, bank=bank, slot=slot, off=off: e.activation(
                                   at[:, slot * 512 + off:(slot + 1) * 512],
                                   bank[:, off:512], AF.Exp),
                               ("sC", 1))
                    kb.set_guard(bname, "sC", ev)
                    expv[par] = ev
                for par in (0, 1):
                    slot = pr + par
                    h = 2 * hp + par
                    waits = [("sC", expv[par])] + kb.need("vv", "consts")
                    if ktile == 0:
                        waits = waits + kb.get_guard(pon[par])
                    tv = kb.op("tensor", waits,
                               lambda e, par=par, slot=slot, h=h, kt0=ktile, off=off,
                                      last=(ktile == nkt - 1): e.matmul(
                                   po[par][0:65, off:512],
                                   vv[:, kt0 * 1040 + h * 65: kt0 * 1040 + (h + 1) * 65],
                                   at[:, slot * 512 + off:(slot + 1) * 512],
                                   start=(kt0 == 0), stop=last),
                               ("tP", 1))
                    kb.set_guard(("at", slot), "tP", tv)
                    avdone[par] = tv
            for par in (0, 1):
                h = 2 * hp + par
                waits = [("tP", avdone[par])] + kb.get_guard("rr")
                rv = kb.op("vector", waits,
                           lambda e, par=par: e.reciprocal(rr[:, :], po[par][64:65, :]),
                           ("vC", 1))
                bw = kb.get_guard("pu") + [("vC", rv)] + kb.need("consts")
                bv = kb.op("tensor", bw,
                           lambda e: e.matmul(pu[0:64, :], B["ones1"][0:1, 0:64], rr[:, :],
                                              start=True, stop=True),
                           ("tP", 1))
                kb.set_guard("rr", "tP", bv)
                cv = kb.op("vector", [("tP", bv)],
                           lambda e: e.tensor_copy(br[:, :], pu[0:64, :]),
                           ("vC", 1))
                kb.set_guard("pu", "vC", cv)
                dst = ao[par * 64: par * 64 + 64, hp * 512:(hp + 1) * 512]
                mv = kb.op("vector", [("vC", cv)] + kb.get_guard(("ao", hp, par)),
                           lambda e, par=par, dst=dst: e.tensor_tensor(
                               dst, po[par][0:64, :], br[:, :], ALU.mult),
                           ("vC", 1))
                kb.set_guard(pon[par], "vC", mv)
                kb.set_guard("br", "vC", mv)
                kb.mark(("ao", hp, par), "vC", mv)

        if dbg and qc == 0:
            v_ = dbg(kb, nc_, B, "aoD", ao[:, :],
                     kb.need(*[("ao", d_, p_) for d_ in range(NDC) for p_ in (0, 1)]))
            for d_ in range(NDC):
                for p_ in (0, 1):
                    kb.set_guard(("ao", d_, p_), "dG", v_)

        util = [("pu", pu), ("po1", po1b)]
        for fc in range(NDC):
            slot = fc % 2
            v = kb.op("sync", kb.get_guard(("wa", slot)),
                      lambda e, fc=fc, slot=slot: e.dma_start(
                          out=wdst_ap(wa, slot * 1024), in_=wcol_ap(w["wo"], fc)),
                      ("dA%d" % slot, 16))
            kb.mark(("wa", slot), "dA%d" % slot, v)
            vcw = kb.op("vector", [("dA%d" % slot, v)] + kb.get_guard(("wc", slot)),
                        lambda e, slot=slot: e.tensor_copy(
                            wc[:, slot * 1024:(slot + 1) * 1024],
                            wa[:, slot * 1024:(slot + 1) * 1024].bitcast(F32)),
                        ("vC", 1))
            kb.set_guard(("wa", slot), "vC", vcw)
            kb.mark(("wc", slot), "vC", vcw)
            bname, bank = util[fc % 2]
            tv = None
            for dc in range(NDC):
                waits = kb.need(("ao", dc, 0), ("ao", dc, 1))
                if dc == 0:
                    waits = waits + kb.get_guard(bname) + kb.need(("wc", slot))
                lhs = wc[:, slot * 1024 + dc * 128: slot * 1024 + (dc + 1) * 128]
                rhs = ao[:, dc * 512:(dc + 1) * 512]
                last = dc == NDC - 1
                tv = kb.op("tensor", waits,
                           lambda e, bank=bank, lhs=lhs, rhs=rhs, dc=dc, last=last: e.matmul(
                               bank[:, :], lhs, rhs, start=(dc == 0), stop=last),
                           ("tP", 1) if last else None)
            kb.set_guard(("wc", slot), "tP", tv)
            dst = sxsl(fc, qc * 512, 512)
            v = kb.op("vector", [("tP", tv)],
                      lambda e, dst=dst, bank=bank: e.tensor_tensor(
                          dst, bank[:, :], dst.bitcast(F32), ALU.add),
                      ("vC", 1))
            kb.set_guard(bname, "vC", v)
            kb.mark(("sx", fc, "r", qc), "vC", v)
        for dc in range(NDC):
            for par in (0, 1):
                kb.set_guard(("ao", dc, par), "tP", kb.now("tP"))

    tnow = kb.now("tP")
    for rname in ("vv", "cx", "kt", "qt"):
        kb.set_guard(rname, "tP", tnow)

    if skip_ln:
        for dc in range(NDC):
            lv = max(kb.ready[("sx", dc, "r", qc)][1] for qc in range(NSC))
            kb.mark(("sx", dc), "vC", lv)
            kb.mark(("sx", dc, "ln"), "vC", lv)
    else:
        emit_ln(kb, B, rdeps=None)

    if layer == 1:
        return
    last = None
    for dc in range(NDC):
        src = sx[:, dc * S:(dc + 1) * S]
        dst = B["am8"][:, dc:dc + 1]
        last = kb.op("vector", kb.need(("sx", dc, "ln")) +
                     ([("vC", last)] if last else []),
                     lambda e, src=src, dst=dst: e.reduce_max(
                         dst, src.bitcast(F32), axis=AX.X, apply_absolute_value=True),
                     ("vC", 1))
        kb.set_guard(("sx", dc), "vC", last)
    v = kb.vchain([
        lambda e, b=b: e.reduce_max(B["am2"][:, b:b + 1], B["am8"][:, :],
                                    axis=AX.X, apply_absolute_value=True),
        lambda e: e.engine_nop(),
    ], waits0=[("vC", last)])
    kb.mark(("am2", b), "vC", v)


"""L0 FFN sublayer: global-amax fp8(e4m3fn) quantized linear + bias, bf16
round, residual, LN.   x_out = LN(x + bf16(acc*osc + bias)) in place on sx.

Buffer reuse during this phase: vv (bitcast f32r) holds the staged weight,
qt holds the quantized weight (bf16), kt holds quantized x (bf16),
wc/at (bitcast) are elementwise scratch, ao holds bf16 ffn outputs.
scl [128, 4] f32: col0 = w-scale, col1 = x-scale, col2 = osc.
"""


def emit_fp8_quant(kb, B, src, dst, scl_col, n, extra=()):
    sqf = B["ao"][:, 0:2 * n].bitcast(F32)[:, 0:n]
    sqi = B["ao"][:, 0:2 * n].bitcast(I32)[:, 0:n]
    wcf = B["wc"][:, 0:2 * n].bitcast(F32)[:, 0:n]
    msk = B["at"][:, 0:2 * n].bitcast(I32)[:, 0:n]
    scl = B["scl"]
    g = kb.get_guard(("aoq", 0)) + kb.get_guard(("aoq", 1)) + \
        kb.get_guard("wcscr") + kb.get_guard("atscr") + list(extra)
    kb.op("vector", g + kb.need("scl"),
          lambda e: e.tensor_scalar(sqf, src.bitcast(F32),
                                    scl[:, scl_col:scl_col + 1], None, ALU.mult), None)
    kb.op("vector", [],
          lambda e: e.tensor_scalar(sqf, sqf, 448.0, None, ALU.min), None)
    kb.op("vector", [],
          lambda e: e.tensor_scalar(sqf, sqf, -448.0, None, ALU.max), None)
    kb.op("vector", [],
          lambda e: e.tensor_scalar(wcf, sqf, 24576.0, None, ALU.add), None)
    kb.op("vector", [],
          lambda e: e.tensor_scalar(wcf, wcf, 24576.0, None, ALU.subtract), None)
    kb.op("vector", [],
          lambda e: e.tensor_scalar(msk, sqi, 0x7FFFFFFF, None, ALU.bitwise_and), None)
    kb.op("vector", [],
          lambda e: e.tensor_scalar(msk, msk, 0x3C800000, None, ALU.is_lt), None)
    kb.op("vector", [],
          lambda e: e.tensor_scalar(sqi, sqi, 0x80000, None, ALU.add), None)
    kb.op("vector", [],
          lambda e: e.tensor_scalar(sqi, sqi, ~0xFFFFF, None, ALU.bitwise_and), None)
    kb.op("vector", [],
          lambda e: e.select(sqf, msk, wcf, sqf), None)
    v = kb.op("vector", [],
              lambda e: e.tensor_copy(dst, sqf), ("vC", 1))
    for r in (("aoq", 0), ("aoq", 1), "wcscr", "atscr"):
        kb.set_guard(r, "vC", v)
    return v


def emit_scale_rows(kb, B, col_amax, col_out, col_tmp=152):
    srow = B["srow"]
    return kb.vchain([
        lambda e: e.tensor_scalar(srow[0:1, col_tmp:col_tmp + 1],
                                  srow[0:1, col_amax:col_amax + 1],
                                  1e-12, None, ALU.max),
        lambda e: e.reciprocal(srow[0:1, col_tmp + 1:col_tmp + 2],
                               srow[0:1, col_tmp:col_tmp + 1]),
        lambda e: e.tensor_scalar(srow[0:1, col_out:col_out + 1],
                                  srow[0:1, col_tmp + 1:col_tmp + 2],
                                  448.0, None, ALU.mult),
        lambda e: e.tensor_scalar(srow[0:1, col_out:col_out + 1],
                                  srow[0:1, col_out:col_out + 1],
                                  448.0, None, ALU.min),
    ])


def emit_l0ffn_prep(kb, B, w8_dram, bias_dram, amaxes_dram, amscr_dram, sclscr_dram, fused_cc=None):
    nc = kb.nc
    banks = B["banks"]
    vvr = B["vv"][:, 0:16384].bitcast(F32R)
    qt, kt, sq, srow, scl = B["qt"], B["kt"], B["sq"], B["srow"], B["scl"]

    for fc in range(NDC):
        v = kb.op("sync", kb.get_guard("vv"),
                  lambda e, fc=fc: e.dma_start(
                      out=wdst_ap(vvr, fc * 1024), in_=wcol_ap(w8_dram, fc)),
                  ("dS", 16))
    kb.mark("w8stage", "dS", v)

    last = None
    for fc in range(NDC):
        last = kb.op("vector", kb.need("w8stage") + kb.get_guard("am8") +
                     ([("vC", last)] if last else []),
                     lambda e, fc=fc: e.reduce_max(
                         B["am8"][:, fc:fc + 1], vvr[:, fc * 1024:(fc + 1) * 1024].bitcast(F32),
                         axis=AX.X, apply_absolute_value=True),
                     ("vC", 1))
    v = kb.vchain([
        lambda e: e.reduce_max(sq[:, 0:1], B["am8"][:, :], axis=AX.X,
                               apply_absolute_value=True),
        lambda e: e.engine_nop(),
    ], waits0=[("vC", last)])
    v = kb.op("gpsimd", [("vC", v)],
              lambda e: e.dma_start(out=amscr_dram, in_=sq[:, 0:1].bitcast(F32)),
              ("dM", 16))
    v = kb.op("sync", [("dM", v)],
              lambda e: e.dma_start(out=srow[0:1, 0:128],
                                    in_=amscr_dram.rearrange("p o -> o p")),
              ("dR", 16))
    v = kb.vchain([
        lambda e: e.reduce_max(srow[0:1, 129:130], srow[0:1, 0:128],
                               axis=AX.X, apply_absolute_value=False),
        lambda e: e.engine_nop(),
    ], waits0=[("dR", v)])
    v_w = v

    if amaxes_dram is not None:
        v2 = kb.op("sync", [],
                   lambda e: e.dma_start(out=srow[0:1, 144:152], in_=amaxes_dram),
                   ("dR2", 16))
        v2 = kb.vchain([
            lambda e: e.reduce_max(srow[0:1, 128:129], srow[0:1, 144:152],
                                   axis=AX.X, apply_absolute_value=False),
            lambda e: e.engine_nop(),
        ], waits0=[("dR2", v2)])
    else:
        cc_in_t, cc_out_t, amscr2_dram = fused_cc
        v2 = kb.vchain([
            lambda e: e.tensor_tensor(sq[:, 1:2], B["am2"][:, 0:1],
                                      B["am2"][:, 1:2], ALU.max),
            lambda e: e.engine_nop(),
        ], waits0=kb.need(("am2", 0), ("am2", 1)))
        v2 = kb.op("gpsimd", [("vC", v2)],
                   lambda e: e.dma_start(out=amscr2_dram, in_=sq[:, 1:2].bitcast(F32)),
                   ("dM", 16))
        v2 = kb.op("sync", [("dM", v2), ("vC", v_w)],
                   lambda e: e.dma_start(out=srow[0:1, 0:128],
                                         in_=amscr2_dram.rearrange("p o -> o p")),
                   ("dR", 16))
        v2 = kb.op("vector", [("dR", v2)],
                   lambda e: e.reduce_max(srow[0:1, 130:131], srow[0:1, 0:128],
                                          axis=AX.X, apply_absolute_value=False),
                   ("vC", 1))
        v2 = kb.op("gpsimd", [("vC", v2)],
                   lambda e: e.dma_start(out=cc_in_t.ap(), in_=srow[0:1, 130:138]),
                   ("dM", 16))
        v2 = kb.op("gpsimd", [("dM", v2)],
                   lambda e: e.collective_compute(
                       "AllReduce", ALU.max, replica_groups=[list(range(8))],
                       ins=[cc_in_t.ap()], outs=[cc_out_t.ap()]),
                   ("cc", 1))
        v2 = kb.op("sync", [("cc", v2)],
                   lambda e: e.dma_start(out=srow[0:1, 128:129],
                                         in_=cc_out_t.ap()[0:1, 0:1]),
                   ("dR", 16))
        v2 = kb.op("vector", [("dR", v2)], lambda e: e.engine_nop(), ("vC", 1))

    emit_scale_rows(kb, B, 129, 131, 152)
    emit_scale_rows(kb, B, 128, 132, 154)
    kb.vchain([
        lambda e: e.reciprocal(srow[0:1, 134:135], srow[0:1, 131:132]),
        lambda e: e.reciprocal(srow[0:1, 135:136], srow[0:1, 132:133]),
        lambda e: e.tensor_tensor(srow[0:1, 133:134], srow[0:1, 134:135],
                                  srow[0:1, 135:136], ALU.mult),
    ])
    v3 = kb.op("vector", [], lambda e: e.engine_nop(), ("vC", 1))
    v3 = kb.op("gpsimd", [("vC", v3)],
               lambda e: e.dma_start(out=sclscr_dram, in_=srow[0:1, 131:135]),
               ("dM", 16))
    v3 = kb.op("sync", [("dM", v3)],
               lambda e: e.dma_start(out=scl[:, 0:4],
                                     in_=sclscr_dram.broadcast_to([128, 4])),
               ("dR", 16))
    kb.mark("scl", "dR", v3)

    for fc in range(NDC):
        v = emit_fp8_quant(kb, B, vvr[:, fc * 1024:(fc + 1) * 1024],
                           qt[:, fc * 1024:(fc + 1) * 1024], 0, 1024,
                           extra=kb.get_guard("qt"))
    kb.mark("wq8", "vC", v)
    kb.set_guard("vv", "vC", v)

    v = kb.op("sync", [],
              lambda e: e.dma_start(out=B["biasb"][:, :], in_=bias_dram),
              ("dB", 16))
    kb.mark("biasb", "dB", v)


def emit_l0ffn(kb, B, b, skip_ln=False, ln_dbg=None):
    sx, kt, qt, ao = B["sx"], B["kt"], B["qt"], B["ao"]
    banks = B["banks"]
    util = [("pu", banks["pu"]), ("po1", banks["po1"])]

    def sxsl(dc, c0, nn):
        return sx[:, dc * S + c0: dc * S + c0 + nn]

    for dc in range(NDC):
        for hf in range(2):
            v = emit_fp8_quant(
                kb, B, sxsl(dc, hf * 1024, 1024),
                kt[:, dc * S + hf * 1024: dc * S + (hf + 1) * 1024], 1, 1024,
                extra=kb.get_guard("kt") + kb.need(("sx", dc)))
    kb.mark("xq8", "vC", v)

    for fc in range(NDC):
        for sc in range(NSC):
            j = fc * NSC + sc
            bname, bank = util[j % 2]
            tv = None
            for dc in range(NDC):
                waits = kb.need("xq8", "wq8") + (kb.get_guard(bname) if dc == 0 else [])
                tv = kb.op("tensor", waits,
                           lambda e, bank=bank, fc=fc, sc=sc, dc=dc: e.matmul(
                               bank[:, :],
                               qt[:, fc * 1024 + dc * 128: fc * 1024 + (dc + 1) * 128],
                               kt[:, dc * S + sc * 512: dc * S + (sc + 1) * 512],
                               start=(dc == 0), stop=(dc == NDC - 1)),
                           ("tP", 1) if dc == NDC - 1 else None)
            dst_b = ao[:, (j % 4) * 512: (j % 4 + 1) * 512]
            v = kb.op("vector", [("tP", tv)] + kb.need("scl", "biasb") + kb.get_guard(("aoq", j % 4)),
                      lambda e, bank=bank, dst_b=dst_b, fc=fc: e.tensor_scalar(
                          dst_b, bank[:, :], B["scl"][:, 2:3], B["biasb"][:, fc:fc + 1],
                          ALU.mult, ALU.add),
                      ("vC", 1))
            kb.set_guard(bname, "vC", v)
            dst = sxsl(fc, sc * 512, 512)
            v = kb.op("vector", [],
                      lambda e, dst=dst, dst_b=dst_b: e.tensor_tensor(
                          dst, dst_b, dst.bitcast(F32), ALU.add),
                      ("vC", 1))
            kb.set_guard(("aoq", j % 4), "vC", v)
            kb.mark(("sx", fc, "r", sc), "vC", v)
    if skip_ln:
        for dc in range(NDC):
            lv = max(kb.ready[("sx", dc, "r", qq)][1] for qq in range(NSC))
            kb.mark(("sx", dc), "vC", lv)
            kb.mark(("sx", dc, "ln"), "vC", lv)
    else:
        emit_ln(kb, B, rdeps=None, dbg=ln_dbg)


"""L1 FFN: blockwise-fp4 (E2M1, block 16) quant-dequant SwiGLU MLP.

out = fp4linear(silu(fp4linear(x,wg)) * fp4linear(x,wu), wd);
x_out = LN(x + out) in place on sx.

Quantized+transposed weights are spilled to HBM scratch once (prep) and
streamed back during the matmuls (they don't fit in SBUF).

Buffer roles in this phase:
  kt  = xq4T (quantized x, transposed back to [d,seq])     [128, 8*2048] bf16
  qt  = hqT tiles (cols 0:11264) + int scratch + wdq stream
  vv  = weight staging (prep, f32r view) / gate+up rhs stream slots (4x4096)
  cx  = natural quantized staging (x / h tiles)
  ao(f32 view) = src + value scratch ; wc(f32 view) = acc scratch
  at  = silu output tiles (bf16)
  sq  = blockmax / scale / rcp rows (f32r)
"""

MIDS = [0.25, 0.75, 1.25, 1.75, 2.5, 2.5, 3.5, 3.5, 5.0, 5.0, 5.0, 5.0]


def emit_fp4_quant(kb, B, srcf, srci, dst, n, c_i32, waits0=()):
    nb = n // 16
    sq = B["sq"]
    bm = sq[:, 0:nb]
    s_c = sq[:, 64:64 + nb]
    rcp = sq[:, 128:128 + nb]
    A = B["ao"][:, 2048:2048 + 2 * n].bitcast(F32)[:, 0:n]
    A_i = B["ao"][:, 2048:2048 + 2 * n].bitcast(I32)[:, 0:n]
    AC = B["wc"][:, 0:2 * n].bitcast(F32)[:, 0:n]
    T_f = B["at"][:, 0:2 * n].bitcast(F32)[:, 0:n]
    T_i = B["at"][:, 0:2 * n].bitcast(I32)[:, 0:n]
    g = kb.get_guard("fp4scr") + list(waits0)
    rcp_b = rcp.rearrange("p (b o) -> p b o", o=1).broadcast_to([128, nb, 16])
    kb.vchain([
        lambda e: e.reduce_max(bm, srcf.rearrange("p (b w) -> p b w", w=16),
                               axis=AX.X, apply_absolute_value=True),
        lambda e: e.tensor_scalar(s_c, bm, 1.0 / 6.0, None, ALU.mult),
        lambda e: e.tensor_scalar(s_c, s_c, 1e-12, None, ALU.max),
        lambda e: e.reciprocal(rcp, s_c),
        lambda e: e.tensor_scalar(c_i32, srci, 0x7FFFFFFF, None, ALU.bitwise_and),
        lambda e: e.tensor_tensor(A.rearrange("p (b w) -> p b w", w=16),
                                  c_i32.bitcast(F32).rearrange("p (b w) -> p b w", w=16),
                                  rcp_b, ALU.mult),
    ], waits0=g)
    kb.op("vector", [],
          lambda e: e.tensor_scalar(c_i32, A_i, 0x3FFFFF, None, ALU.bitwise_and), None)
    kb.op("vector", [],
          lambda e: e.tensor_scalar(c_i32, c_i32, 0x200000, None, ALU.is_gt), None)
    kb.op("vector", [],
          lambda e: e.tensor_scalar(T_i, A_i, ~0x3FFFFF, None, ALU.bitwise_and), None)
    kb.op("vector", [],
          lambda e: e.scalar_tensor_tensor(T_i, c_i32, 0x400000, T_i,
                                           ALU.mult, ALU.add), None)
    kb.op("vector", [],
          lambda e: e.tensor_scalar(T_i, T_i, ~0x3FFFFF, None, ALU.bitwise_and), None)
    kb.op("vector", [],
          lambda e: e.tensor_scalar(AC, A, 0.25, None, ALU.is_gt), None)
    kb.op("vector", [],
          lambda e: e.scalar_tensor_tensor(AC, A, 0.75, AC, ALU.is_gt, ALU.add), None)
    kb.op("vector", [],
          lambda e: e.tensor_scalar(AC, AC, 0.5, None, ALU.mult), None)
    kb.op("vector", [],
          lambda e: e.tensor_scalar(c_i32, A_i, 0x3F800000, None, ALU.is_lt), None)
    kb.op("vector", [],
          lambda e: e.select(A, c_i32, AC, T_f), None)
    sc_b = s_c.rearrange("p (b o) -> p b o", o=1).broadcast_to([128, nb, 16])
    kb.op("vector", [],
          lambda e: e.tensor_tensor(T_f.rearrange("p (b w) -> p b w", w=16),
                                    A.rearrange("p (b w) -> p b w", w=16),
                                    sc_b, ALU.mult), None)
    kb.op("vector", [],
          lambda e: e.tensor_scalar(c_i32, srci, -0x80000000, None, ALU.bitwise_and), None)
    kb.op("vector", [],
          lambda e: e.tensor_tensor(T_i, T_i, c_i32, ALU.bitwise_or), None)
    v = kb.op("vector", [], lambda e: e.tensor_copy(dst, T_f), ("vC", 1))
    kb.set_guard("fp4scr", "vC", v)
    return v





def emit_l1ffn_prep(kb, B, wg_dram, wu_dram, wd_dram, spills):
    nc = kb.nc
    vv, kt, qt, pvB = B["vv"], B["kt"], B["qt"], B["banks"]["pvB"]
    vvr = vv[:, 0:16640].bitcast(F32R)

    def quant_transpose_spill(src_slot_f32r, n, dst_stage, spill_ap, qdst, skey):
        nblk = n // 128
        off = 0
        lastq = None
        while off < n:
            m = min(1024, n - off)
            lastq = emit_fp4_quant(
                kb, B, src_slot_f32r[:, off:off + m].bitcast(F32),
                src_slot_f32r[:, off:off + m].bitcast(I32),
                qdst[:, off:off + m], m,
                B["qt"][:, 11264:11264 + 2 * m].bitcast(I32)[:, 0:m])
            off += m
        done = 0
        while done < nblk:
            cnt = min(8, nblk - done)
            tv = None
            for j in range(cnt):
                waits = ([("vC", lastq)] + kb.get_guard("pvB") + kb.need("consts")) if j == 0 else []
                tv = kb.op("tensor", waits,
                           lambda e, j=j, done=done, qdst=qdst: e.transpose(
                               pvB[:, j * 128:(j + 1) * 128],
                               qdst[:, (done + j) * 128:(done + j + 1) * 128],
                               B["ident_b"][:, :]),
                           ("tP", 1))
            v = kb.op("vector", [("tP", tv)] + kb.get_guard(skey),
                      lambda e, cnt=cnt, done=done: e.tensor_copy(
                          dst_stage[:, done * 128:(done + cnt) * 128],
                          pvB[:, 0:cnt * 128]),
                      ("vC", 1))
            kb.set_guard("pvB", "vC", v)
            done += cnt
        kb.set_guard("fp4scr", "tP", kb.now("tP"))
        sv = None
        for dst_ap, c0, c1 in spill_ap:
            sv = kb.op("gpsimd", [("vC", v)],
                       lambda e, dst_ap=dst_ap, c0=c0, c1=c1: e.dma_start(
                           out=dst_ap, in_=dst_stage[:, c0:c1]),
                       ("dW", 16))
        kb.set_guard(skey, "dW", sv)
        return sv

    sv = None
    for name, wdram in (("wg", wg_dram), ("wu", wu_dram)):
        for nt in range(22):
            slot = nt % 2
            lv = kb.op("sync", kb.get_guard(("vvs", slot)),
                       lambda e, wdram=wdram, nt=nt, slot=slot: e.dma_start(
                           out=vvr[:, slot * 1024:(slot + 1) * 1024], in_=wdram[nt]),
                       ("dS%d" % slot, 16))
            src = vvr[:, slot * 1024:(slot + 1) * 1024]
            stage = kt[:, 2048 + slot * 1024: 2048 + (slot + 1) * 1024]
            qdst = kt[:, slot * 1024:(slot + 1) * 1024]
            kb.set_guard("fp4scr", "dS%d" % slot, lv)
            sv = quant_transpose_spill(
                src, 1024, stage,
                [(spills[name][j, :, nt * 128:(nt + 1) * 128], j * 128, (j + 1) * 128)
                 for j in range(8)],
                qdst, ("stg", slot))
            kb.set_guard(("vvs", slot), "vC", kb.now("vC"))
    kb.set_guard("fp4scr", "dW", kb.now("dW"))
    kb.set_guard(("vvs2", 0), "vC", kb.now("vC"))
    kb.set_guard(("vvs2", 1), "vC", kb.now("vC"))
    for dc in range(8):
        slot = dc % 2
        lv = kb.op("sync", kb.get_guard(("vvs2", slot)),
                   lambda e, dc=dc, slot=slot: e.dma_start(
                       out=vvr[:, slot * 2816:(slot + 1) * 2816], in_=wd_dram[dc]),
                   ("dS%d" % slot, 16))
        src = vvr[:, slot * 2816:(slot + 1) * 2816]
        stage = kt[:, 8192 + slot * 2816: 8192 + (slot + 1) * 2816]
        qdst = kt[:, slot * 2816 + 0: slot * 2816 + 2816]
        kb.set_guard("fp4scr", "dS%d" % slot, lv)
        sv = quant_transpose_spill(src, 2816, stage,
                                   [(spills["wd"][dc], 0, 2816)], qdst,
                                   ("stg2", slot))
        kb.set_guard(("vvs2", slot), "vC", kb.now("vC"))
    kb.mark("spilled", "dW", kb.now("dW"))
    tnow = kb.now("vC")
    for r in ("vv", "kt", "qt"):
        kb.set_guard(r, "vC", tnow)


def emit_l1ffn_prep_sharded(kb, B, wsh_dram, ccg_in, ccg_out, spills):
    nc = kb.nc
    vv, kt, pvB = B["vv"], B["kt"], B["banks"]["pvB"]
    vvr = vv[:, 0:16640].bitcast(F32R)

    sv = None
    for slot in range(7):
        n = 2816 if slot == 6 else 1024
        dslot = slot % 2
        lv = kb.op("sync", kb.get_guard(("vvs", dslot)),
                   lambda e, slot=slot, dslot=dslot, n=n: e.dma_start(
                       out=vvr[:, dslot * 2816: dslot * 2816 + n],
                       in_=wsh_dram[slot][:, 0:n]),
                   ("dS%d" % dslot, 16))
        src = vvr[:, dslot * 2816: dslot * 2816 + n]
        stage = kt[:, 8192 + dslot * 2816: 8192 + dslot * 2816 + n]
        qdst = kt[:, dslot * 2816: dslot * 2816 + n]
        kb.set_guard("fp4scr", "dS%d" % dslot, lv)
        nblk = n // 128
        off = 0
        lastq = None
        while off < n:
            m = min(1024, n - off)
            lastq = emit_fp4_quant(
                kb, B, src[:, off:off + m].bitcast(F32),
                src[:, off:off + m].bitcast(I32),
                qdst[:, off:off + m], m,
                B["qt"][:, 11264:11264 + 2 * m].bitcast(I32)[:, 0:m])
            off += m
        done = 0
        while done < nblk:
            cnt = min(8, nblk - done)
            tv = None
            for j in range(cnt):
                waits = ([("vC", lastq)] + kb.get_guard("pvB") + kb.need("consts")) if j == 0 else []
                tv = kb.op("tensor", waits,
                           lambda e, j=j, done=done, qdst=qdst: e.transpose(
                               pvB[:, j * 128:(j + 1) * 128],
                               qdst[:, (done + j) * 128:(done + j + 1) * 128],
                               B["ident_b"][:, :]),
                           ("tP", 1))
            v = kb.op("vector", [("tP", tv)] + kb.get_guard(("stg", dslot)),
                      lambda e, cnt=cnt, done=done, stage=stage: e.tensor_copy(
                          stage[:, done * 128:(done + cnt) * 128],
                          pvB[:, 0:cnt * 128]),
                      ("vC", 1))
            kb.set_guard("pvB", "vC", v)
            done += cnt
        kb.set_guard("fp4scr", "tP", kb.now("tP"))
        sv = kb.op("gpsimd", [("vC", v)],
                   lambda e, slot=slot, stage=stage, n=n: e.dma_start(
                       out=ccg_in[slot][:, 0:n], in_=stage[:, 0:n]),
                   ("dW", 16))
        kb.set_guard(("stg", dslot), "dW", sv)
        kb.set_guard(("vvs", dslot), "vC", kb.now("vC"))
    cv = kb.op("gpsimd", [("dW", kb.now("dW"))],
               lambda e: e.collective_compute(
                   "AllGather", ALU.bypass, replica_groups=[list(range(8))],
                   ins=[ccg_in], outs=[ccg_out]),
               ("cc", 1))
    def runs(units):
        out = []
        cur = None
        for u in units:
            gi = (u // 6) * 7 + u % 6
            if cur and gi == cur[0] + cur[2]:
                cur[2] += 1
            else:
                cur = [gi, u, 1]
                out.append(cur)
        return out
    rv = None
    for mat, base in (("wg", 0), ("wu", 22)):
        for kc in range(NDC):
            for gi, u0, cnt in runs(range(base, base + 22)):
                nt0 = u0 - base
                rv = kb.op("sync", [("cc", cv)],
                           lambda e, mat=mat, kc=kc, gi=gi, nt0=nt0, cnt=cnt: e.dma_start(
                               out=spills[mat][kc][:, nt0 * 128:(nt0 + cnt) * 128]
                                   .rearrange("p (t c) -> p t c", c=128),
                               in_=ccg_out[gi:gi + cnt, :, kc * 128:(kc + 1) * 128]
                                   .transpose([1, 0, 2])),
                           ("dW", 16))
    for dc in range(NDC):
        rv = kb.op("sync", [("cc", cv)],
                   lambda e, dc=dc: e.dma_start(
                       out=spills["wd"][dc], in_=ccg_out[dc * 7 + 6]),
                   ("dW", 16))
    kb.mark("spilled", "dW", rv)
    tnow = kb.now("vC")
    for r in ("vv", "kt", "qt"):
        kb.set_guard(r, "vC", tnow)


def emit_l1ffn(kb, B, spills, b, dbg=None, sc_list=None, skip_ln=False):
    nc = kb.nc
    sx, kt, qt, vv, cx, at, ao = B["sx"], B["kt"], B["qt"], B["vv"], B["cx"], B["at"], B["ao"]
    for r in ("cx", "kt", "qt", "vv", "at"):
        for (sn, gv) in kb.get_guard(r):
            kb.set_guard("fp4scr", sn, gv)
    for dcq in range(NDC):
        for par in (0, 1):
            for (sn, gv) in kb.get_guard(("ao", dcq, par)):
                kb.set_guard("xnat", sn, gv)
                kb.set_guard("fp4scr", sn, gv)
    for (sn, gv) in kb.get_guard("qt"):
        for ntc_ in range(6):
            for st_ in range(4):
                kb.set_guard(("hqT", ntc_, st_), sn, gv)
        kb.set_guard("wdqs", sn, gv)
    banks = B["banks"]
    pvB = banks["pvB"]
    pu, po1 = banks["pu"], banks["po1"]
    gb = [banks["ps0a"], banks["ps0b"]]
    gbn = ["ps0a", "ps0b"]
    ub = [banks["ps1a"], banks["ps1b"]]
    ubn = ["ps1a", "ps1b"]

    def sxsl(dc, c0, nn):
        return sx[:, dc * S + c0: dc * S + c0 + nn]

    for st in range(NST):
        for half in range(2):
            bank = pu if half == 0 else po1
            bname = "pu" if half == 0 else "po1"
            bview = bank[:, :].bitcast(F32R)
            tv = None
            for j in range(4):
                dc = half * 4 + j
                waits = (kb.get_guard(bname) + kb.need(("sx", dc), "consts")) if j == 0 else kb.need(("sx", dc))
                tv = kb.op("tensor", waits,
                           lambda e, dc=dc, st=st, j=j, bview=bview: e.transpose(
                               bview[:, j * 128:(j + 1) * 128],
                               sx[:, dc * S + st * 128: dc * S + (st + 1) * 128],
                               B["ident_r"][:, :]),
                           ("tP", 1))
            v = kb.op("vector", [("tP", tv)] + (kb.get_guard("xnat") if half == 0 else []),
                      lambda e, half=half, bview=bview: e.tensor_copy(
                          ao[:, 0:2048].bitcast(F32)[:, half * 512:(half + 1) * 512], bview),
                      ("vC", 1))
            kb.set_guard(bname, "vC", v)
        xnat_f = ao[:, 0:2048].bitcast(F32)
        xnat_i = ao[:, 0:2048].bitcast(I32)
        qv = emit_fp4_quant(kb, B, xnat_f, xnat_i, cx[:, 0:1024], 1024,
                            qt[:, 11264:13312].bitcast(I32)[:, 0:1024])
        kb.set_guard("xnat", "vC", qv)
        tv = None
        for j in range(8):
            waits = ([("vC", qv)] + kb.get_guard("pvB")) if j == 0 else []
            tv = kb.op("tensor", waits,
                       lambda e, j=j: e.transpose(
                           pvB[:, j * 128:(j + 1) * 128],
                           cx[:, j * 128:(j + 1) * 128], B["ident_b"][:, :]),
                       ("tP", 1))
        dst = kt[:, :].rearrange("p (dc s) -> p dc s", s=S)[:, :, st * 128:(st + 1) * 128]
        v = kb.op("vector", [("tP", tv)] + (kb.get_guard("kt") if st == 0 else []),
                  lambda e, dst=dst: e.tensor_copy(dst, pvB[:, 0:1024].rearrange(
                      "p (dc s) -> p dc s", s=128)),
                  ("vC", 1))
        kb.set_guard("pvB", "vC", v)
        kb.set_guard("cxq", "vC", v)
    kb.mark("xq4", "vC", kb.now("vC"))

    NTC = [512, 512, 512, 512, 512, 256]
    for sc in (sc_list if sc_list is not None else range(NSC)):
        hq_done = None
        for ntc in range(6):
            nw = NTC[ntc]
            nnt = nw // 128
            gslot = ntc % 2
            uslot = 2 + ntc % 2
            for name, slot in (("wg", gslot), ("wu", uslot)):
                lv = None
                for kc in range(NDC):
                    w0 = (kb.get_guard(("vvq", slot)) + kb.get_guard("vv")
                          + kb.need("spilled")) if kc == 0 else []
                    lv = kb.op("sync", w0,
                               lambda e, name=name, slot=slot, ntc=ntc,
                                      kc=kc, nw=nw: e.dma_start(
                                   out=vv[:, slot * 4096 + kc * nw: slot * 4096 + (kc + 1) * nw],
                                   in_=spills[name][kc, :, ntc * 512: ntc * 512 + nw]),
                               ("dV%d" % slot, 16))
                kb.mark(("vvq", slot), "dV%d" % slot, lv)
            for st in range(4):
                stg = sc * 4 + st
                gbank, gname = gb[st % 2], gbn[st % 2]
                ubank, uname = ub[st % 2], ubn[st % 2]
                tg = tu = None
                for kc in range(NDC):
                    waits = kb.need("xq4", ("vvq", gslot)) + (kb.get_guard(gname) if kc == 0 else [])
                    tg = kb.op("tensor", waits,
                               lambda e, kc=kc, stg=stg, gslot=gslot, nw=nw, gbank=gbank: e.matmul(
                                   gbank[:, 0:nw],
                                   kt[:, kc * S + stg * 128: kc * S + (stg + 1) * 128],
                                   vv[:, gslot * 4096 + kc * nw: gslot * 4096 + (kc + 1) * nw],
                                   start=(kc == 0), stop=(kc == NDC - 1)),
                               ("tP", 1) if kc == NDC - 1 else None)
                for kc in range(NDC):
                    waits = kb.need(("vvq", uslot)) + (kb.get_guard(uname) if kc == 0 else [])
                    tu = kb.op("tensor", waits,
                               lambda e, kc=kc, stg=stg, uslot=uslot, nw=nw, ubank=ubank: e.matmul(
                                   ubank[:, 0:nw],
                                   kt[:, kc * S + stg * 128: kc * S + (stg + 1) * 128],
                                   vv[:, uslot * 4096 + kc * nw: uslot * 4096 + (kc + 1) * nw],
                                   start=(kc == 0), stop=(kc == NDC - 1)),
                               ("tP", 1) if kc == NDC - 1 else None)
                atsl = ao[:, 2048:2048 + 2 * nw].bitcast(F32)[:, 0:nw]
                hv = kb.op("scalar", [("tP", tg)] + kb.get_guard("xnat"),
                           lambda e, gbank=gbank, atsl=atsl, nw=nw: e.activation(
                               atsl, gbank[:, 0:nw], AF.Silu),
                           ("sC", 1))
                kb.set_guard(gname, "sC", hv)
                srcsl = ao[:, 0:2048].bitcast(F32)[:, 0:nw]
                hv2 = kb.op("vector", [("sC", hv), ("tP", tu)] + kb.get_guard("xnat"),
                            lambda e, atsl=atsl, srcsl=srcsl, ubank=ubank, nw=nw: e.tensor_tensor(
                                srcsl, atsl, ubank[:, 0:nw], ALU.mult),
                            ("vC", 1))
                kb.set_guard(uname, "vC", hv2)
                kb.set_guard("xnat", "vC", hv2)
                qv = emit_fp4_quant(kb, B, srcsl, ao[:, 0:2048].bitcast(I32)[:, 0:nw],
                                    cx[:, 0:nw], nw,
                                    qt[:, 11264:13312].bitcast(I32)[:, 0:nw],
                                    waits0=[("vC", hv2)])
                kb.set_guard("xnat", "vC", qv)
                tv = None
                for j in range(nnt):
                    waits = ([("vC", qv)] + kb.get_guard("pvB")) if j == 0 else []
                    tv = kb.op("tensor", waits,
                               lambda e, j=j: e.transpose(
                                   pvB[:, j * 128:(j + 1) * 128],
                                   cx[:, j * 128:(j + 1) * 128], B["ident_b"][:, :]),
                               ("tP", 1))
                dst = qt[:, :].rearrange("p (kc s) -> p kc s", s=512)[
                    :, ntc * 4: ntc * 4 + nnt, st * 128:(st + 1) * 128]
                hq_done = kb.op("vector", [("tP", tv)] + kb.get_guard(("hqT", ntc, st)),
                                lambda e, dst=dst, nnt=nnt: e.tensor_copy(
                                    dst, pvB[:, 0:nnt * 128].rearrange(
                                        "p (kc s) -> p kc s", s=128)),
                                ("vC", 1))
                kb.set_guard("pvB", "vC", hq_done)
                kb.mark(("hqT", ntc, st), "vC", hq_done)
                if st == 3:
                    kb.set_guard(("vvq", gslot), "tP", kb.now("tP"))
                    kb.set_guard(("vvq", uslot), "tP", kb.now("tP"))
        if dbg is not None and sc == 0:
            dbg(kb, "hq0D", qt[:, 0:11264],
                kb.need(*[("hqT", n_, s_) for n_ in range(6) for s_ in range(4)]),
                [("hqT", n_, s_) for n_ in range(6) for s_ in range(4)])
        util = [("pu", pu), ("po1", po1)]
        for fc in range(NDC):
            lv = kb.op("sync", kb.get_guard("wdqs") + kb.need("spilled"),
                       lambda e, fc=fc: e.dma_start(
                           out=qt[:, 13312:16128], in_=spills["wd"][fc]),
                       ("dD", 16))
            bname, bank = util[fc % 2]
            tv = None
            for kc in range(22):
                waits = [("dD", lv)] + kb.need(*[("hqT", kc // 4, stq) for stq in range(4)])
                if kc == 0:
                    waits = waits + kb.get_guard(bname)
                tv = kb.op("tensor", waits,
                           lambda e, kc=kc, bank=bank: e.matmul(
                               bank[:, :],
                               qt[:, 13312 + kc * 128: 13312 + (kc + 1) * 128],
                               qt[:, kc * 512:(kc + 1) * 512],
                               start=(kc == 0), stop=(kc == 21)),
                           ("tP", 1) if kc == 21 else None)
            kb.set_guard("wdqs", "tP", tv)
            dst = sxsl(fc, sc * 512, 512)
            v = kb.op("vector", [("tP", tv)],
                      lambda e, dst=dst, bank=bank: e.tensor_tensor(
                          dst, bank[:, :], dst.bitcast(F32), ALU.add),
                      ("vC", 1))
            kb.set_guard(bname, "vC", v)
            kb.mark(("sx", fc, "r", sc), "vC", v)
        if dbg is not None and sc == 0:
            dbg(kb, "r0D", sx[:, :],
                kb.need(*[("sx", f_, "r", 0) for f_ in range(NDC)]),
                [("sx", f_) for f_ in range(NDC)])
        for ntc in range(6):
            for st in range(4):
                kb.set_guard(("hqT", ntc, st), "tP", kb.now("tP"))
    if skip_ln:
        for dc in range(NDC):
            vals = [kb.ready[("sx", dc, "r", qq)][1] for qq in (sc_list or range(NSC))
                    if ("sx", dc, "r", qq) in kb.ready]
            kb.mark(("sx", dc), "vC", max(vals))
            kb.mark(("sx", dc, "ln"), "vC", max(vals))
    else:
        vnow = kb.now("vC")
        kb.set_guard(("sq", 0), "vC", vnow)
        kb.set_guard(("sq", 1), "vC", vnow)
        emit_ln(kb, B, rdeps=None)


"""Standalone single-sublayer launch builders (for incremental validation,
and for the UNFUSED multi-launch kernel checkpoint)."""
from contextlib import ExitStack
import numpy as np
import concourse.bass as bass


def prep_l0attn_inputs(X, Wq, Wk, Wv, Wo, We, core, n_cores=8):
    bpc = X.shape[0] // n_cores
    xs = X[core * bpc:(core + 1) * bpc]
    xt = np.ascontiguousarray(xs.transpose(0, 2, 1))
    xt = np.ascontiguousarray(xt.reshape(bpc, NDC, 128, S))
    return {
        "xt": xt,
        "wq": chunk_T(Wq.T), "wk": chunk_T(Wk.T),
        "wv": chunk_T(Wv.T), "wo": chunk_T(Wo.T), "we": chunk_T(We.T),
    }


def emit_store_sx(kb, B, xo_dram, b):
    for dc in range(NDC):
        v = kb.op("gpsimd", kb.need(("sx", dc)),
                  lambda e, dc=dc: e.dma_start(
                      out=xo_dram[dc], in_=B["sx"][:, dc * S:(dc + 1) * S].bitcast(F32)),
                  ("dG", 16))
    for dc in range(NDC):
        kb.set_guard(("sx", dc), "dG", v)
    return v


def emit_amax_local(kb, B, amscr_dram):
    sq = B["sq"]
    v = kb.vchain([
        lambda e: e.tensor_tensor(sq[:, 0:1], B["am2"][:, 0:1], B["am2"][:, 1:2], ALU.max),
        lambda e: e.engine_nop(),
    ], waits0=kb.need(("am2", 0), ("am2", 1)))
    v = kb.op("gpsimd", [("vC", v)],
              lambda e: e.dma_start(out=amscr_dram, in_=sq[:, 0:1].bitcast(F32)),
              ("dM", 16))
    v = kb.op("sync", [("dM", v)],
              lambda e: e.dma_start(out=B["srow"][0:1, 0:128],
                                    in_=amscr_dram.rearrange("p o -> o p")),
              ("dR", 16))
    v = kb.vchain([
        lambda e: e.reduce_max(B["srow"][0:1, 128:129], B["srow"][0:1, 0:128],
                               axis=AX.X, apply_absolute_value=False),
        lambda e: e.engine_nop(),
    ], waits0=[("dR", v)])
    kb.mark("amax_local", "vC", v)
    return v


def emit_fence(kb):
    sems = [(n, kb.cnt[n]) for n in list(kb.cnt) if kb.cnt[n] > 0]
    for eng in ("tensor", "vector", "scalar", "sync", "gpsimd"):
        kb.op(eng, sems, lambda e: None, None)


def emit_spill_sx_r(kb, B, dst_dram):
    v = None
    for dc in range(NDC):
        v = kb.op("gpsimd", kb.need(("sx", dc)) + kb.get_guard(("sxsp", dc)),
                  lambda e, dc=dc: e.dma_start(
                      out=dst_dram[dc], in_=B["sx"][:, dc * S:(dc + 1) * S]),
                  ("dG", 16))
    for dc in range(NDC):
        kb.set_guard(("sx", dc), "dG", v)
    return v


def emit_load_sx_r(kb, B, src_dram):
    v = None
    for dc in range(NDC):
        v = kb.op("sync", kb.get_guard(("sx", dc)),
                  lambda e, dc=dc: e.dma_start(
                      out=B["sx"][:, dc * S:(dc + 1) * S], in_=src_dram[dc]),
                  ("dX", 16))
    for dc in range(NDC):
        kb.mark(("sx", dc), "dX", v)
    return v


def emit_tap(kb, nc, B, name, src_ap, waits, guard_res=None):
    shp = list(src_ap.shape)
    t = nc.dram_tensor(name, shp, src_ap.dtype, kind="ExternalOutput").ap()
    v = kb.op("gpsimd", waits,
              lambda e, t=t, src_ap=src_ap: e.dma_start(out=t, in_=src_ap),
              ("dG", 16))
    if guard_res:
        kb.set_guard(guard_res, "dG", v)
    return v


def build_l0attn_standalone(bpc=2, debug=False, skip_ln=False):
    nc = bass.Bass("TRN2", target_bir_lowering=False, dynamic_dma_scratch_size=16384,
                   detect_race_conditions=False)
    xt = nc.dram_tensor("xt", [bpc, NDC, 128, S], F32R, kind="ExternalInput").ap()
    w = {}
    for n in ("wq", "wk", "wv", "we"):
        shp = [1, NDC, 128, 128] if n == "we" else [NDC, NDC, 128, 128]
        w[n] = nc.dram_tensor(n, shp, F32R, kind="ExternalInput").ap()
    w["wo"] = nc.dram_tensor("wo", [NDC, NDC, 128, 128], F32R, kind="ExternalInput").ap()
    xo = nc.dram_tensor("xo", [bpc, NDC, 128, S], F32, kind="ExternalOutput").ap()
    am = nc.dram_tensor("am", [1, 1], F32, kind="ExternalOutput").ap()
    amscr = nc.dram_tensor("amscr", [128, 1], F32).ap()

    es = ExitStack()
    kb = KB(nc)
    B = alloc_bufs(kb, es)
    emit_setup(kb, B)
    for b in range(bpc):
        emit_l0attn(kb, B, [xt[b, dc] for dc in range(NDC)], w, b,
                    dbg=(emit_tap if (debug and b == 0) else None), nc_=nc,
                    skip_ln=skip_ln)
        emit_store_sx(kb, B, [xo[b, dc] for dc in range(NDC)], b)
    emit_amax_local(kb, B, amscr)
    v = kb.op("gpsimd", kb.need("amax_local"),
              lambda e: e.dma_start(out=am, in_=B["srow"][0:1, 128:129]),
              ("dG", 16))
    kb.op("gpsimd", [(n, kb.now(n)) for n in
                     ("dG", "dX", "dA0", "dA1", "dS", "dR", "dR2", "dM", "dB") if kb.cnt.get(n)],
          lambda e: e.engine_nop(), None)
    kb.finalize()
    es.close()
    return nc


def prep_l0ffn_inputs(X1, weight, bias, amaxes, core, n_cores=8):
    bpc = X1.shape[0] // n_cores
    xt = X1[core * bpc:(core + 1) * bpc]
    return {
        "xt": np.ascontiguousarray(xt),
        "w8": chunk_T(np.ascontiguousarray(weight.T)),
        "bias": np.ascontiguousarray(bias.reshape(8, 128).T),
        "amaxes": np.ascontiguousarray(amaxes.reshape(1, 8).astype(np.float32)),
    }


def build_l0ffn_standalone(bpc=2, skip_ln=False, debug=False):
    nc = bass.Bass("TRN2", target_bir_lowering=False, dynamic_dma_scratch_size=16384,
                   detect_race_conditions=False)
    xt = nc.dram_tensor("xt", [bpc, NDC, 128, S], F32R, kind="ExternalInput").ap()
    w8 = nc.dram_tensor("w8", [NDC, NDC, 128, 128], F32R, kind="ExternalInput").ap()
    bias = nc.dram_tensor("bias", [128, 8], F32, kind="ExternalInput").ap()
    amaxes = nc.dram_tensor("amaxes", [1, 8], F32, kind="ExternalInput").ap()
    xo = nc.dram_tensor("xo", [bpc, NDC, 128, S], F32, kind="ExternalOutput").ap()
    amscr = nc.dram_tensor("amscr", [128, 1], F32).ap()
    sclscr = nc.dram_tensor("sclscr", [1, 4], F32).ap()

    es = ExitStack()
    kb = KB(nc)
    B = alloc_bufs(kb, es)
    emit_setup(kb, B)
    emit_l0ffn_prep(kb, B, w8, bias, amaxes, amscr, sclscr)
    for b in range(bpc):
        for dc in range(NDC):
            v = kb.op("sync", kb.get_guard(("sx", dc)),
                      lambda e, b=b, dc=dc: e.dma_start(
                          out=B["sx"][:, dc * S:(dc + 1) * S], in_=xt[b, dc]),
                      ("dX", 16))
        for dc in range(NDC):
            kb.mark(("sx", dc), "dX", v)
        rowtap = None
        if debug and b == 0:
            rt_t = nc.dram_tensor("rowsD", [1, 1536], F32, kind="ExternalOutput").ap()
            def rowtap(kb_, mv, rt_t=rt_t):
                vv2 = kb_.op("gpsimd", [("vC", mv)],
                       lambda e: e.dma_start(out=rt_t[0:1, 0:1536],
                                             in_=B["rows"][0:1, :].bitcast(F32)),
                       ("dG", 16))
                kb_.set_guard("rows", "dG", vv2)
        emit_l0ffn(kb, B, b, skip_ln=skip_ln, ln_dbg=rowtap)
        if debug and b == 0:
            emit_tap(kb, nc, B, "ktD", B["kt"][:, :], kb.need("xq8"), guard_res="kt")
            emit_tap(kb, nc, B, "qtD", B["qt"][:, 0:8192], kb.need("wq8"), guard_res="qt")
            emit_tap(kb, nc, B, "sclD", B["scl"][:, 0:3], kb.need("scl"))
            emit_tap(kb, nc, B, "srowD", B["srow"][0:1, 128:130], kb.need("scl"))
        emit_store_sx(kb, B, [xo[b, dc] for dc in range(NDC)], b)
    kb.op("gpsimd", [(n, kb.now(n)) for n in
                     ("dG", "dX", "dA0", "dA1", "dS", "dR", "dR2", "dM", "dB") if kb.cnt.get(n)],
          lambda e: e.engine_nop(), None)
    kb.finalize()
    es.close()
    return nc


def prep_l1attn_inputs(X1, Wq, Wk, Wv, Wo, core, n_cores=8):
    bpc = X1.shape[0] // n_cores
    return {
        "xt": np.ascontiguousarray(X1[core * bpc:(core + 1) * bpc]),
        "wq": chunk_T(Wq), "wk": chunk_T(Wk), "wv": chunk_T(Wv), "wo": chunk_T(Wo),
    }


def build_l1attn_standalone(bpc=2, skip_ln=False):
    nc = bass.Bass("TRN2", target_bir_lowering=False, dynamic_dma_scratch_size=16384,
                   detect_race_conditions=False)
    xt = nc.dram_tensor("xt", [bpc, NDC, 128, S], F32R, kind="ExternalInput").ap()
    w = {}
    for n in ("wq", "wk", "wv", "wo"):
        w[n] = nc.dram_tensor(n, [NDC, NDC, 128, 128], F32R, kind="ExternalInput").ap()
    xo = nc.dram_tensor("xo", [bpc, NDC, 128, S], F32, kind="ExternalOutput").ap()

    es = ExitStack()
    kb = KB(nc)
    B = alloc_bufs(kb, es)
    emit_setup(kb, B)
    for b in range(bpc):
        emit_l0attn(kb, B, [xt[b, dc] for dc in range(NDC)], w, b,
                    skip_ln=skip_ln, layer=1)
        emit_store_sx(kb, B, [xo[b, dc] for dc in range(NDC)], b)
    kb.op("gpsimd", [(n, kb.now(n)) for n in
                     ("dG", "dX", "dA0", "dA1", "dS", "dR", "dR2", "dM", "dB") if kb.cnt.get(n)],
          lambda e: e.engine_nop(), None)
    kb.finalize()
    es.close()
    return nc


def build_full_fused(bpc=2, debug=False):
    nc = bass.Bass("TRN2", target_bir_lowering=False, dynamic_dma_scratch_size=16384,
                   detect_race_conditions=False)
    xt = nc.dram_tensor("xt", [bpc, NDC, 128, S], F32R, kind="ExternalInput").ap()
    w0 = {}
    for n in ("wq", "wk", "wv", "wo", "we"):
        shp = [1, NDC, 128, 128] if n == "we" else [NDC, NDC, 128, 128]
        w0[n] = nc.dram_tensor("w0" + n, shp, F32R, kind="ExternalInput").ap()
    w8 = nc.dram_tensor("w8", [NDC, NDC, 128, 128], F32R, kind="ExternalInput").ap()
    bias = nc.dram_tensor("bias", [128, 8], F32, kind="ExternalInput").ap()
    w1 = {n: nc.dram_tensor("w1" + n, [NDC, NDC, 128, 128], F32R,
                            kind="ExternalInput").ap()
          for n in ("wq", "wk", "wv", "wo")}
    wsh = nc.dram_tensor("wsh", [7, 128, 2816], F32R, kind="ExternalInput").ap()
    ccg_in = nc.dram_tensor("ccg_in", [7, 128, 2816], BF16).ap()
    ccg_out = nc.dram_tensor("ccg_out", [56, 128, 2816], BF16).ap()
    xo = nc.dram_tensor("xo", [bpc, NDC, 128, S], F32, kind="ExternalOutput").ap()
    dbgk = "ExternalOutput" if debug else "Internal"
    x1scr = nc.dram_tensor("x1scr", [NDC, 128, S], F32R, kind=dbgk).ap()
    x2scr = nc.dram_tensor("x2scr", [NDC, 128, S], F32R, kind=dbgk).ap()
    if debug:
        xd2b0 = nc.dram_tensor("xd2b0", [NDC, 128, S], F32R, kind="ExternalOutput").ap()
        xd3b0 = nc.dram_tensor("xd3b0", [NDC, 128, S], F32R, kind="ExternalOutput").ap()
        xd3b1 = nc.dram_tensor("xd3b1", [NDC, 128, S], F32R, kind="ExternalOutput").ap()
    amscr = nc.dram_tensor("amscr", [128, 1], F32).ap()
    amscr2 = nc.dram_tensor("amscr2", [128, 1], F32).ap()
    sclscr = nc.dram_tensor("sclscr", [1, 4], F32).ap()
    cc_in_t = nc.dram_tensor("cc_in", [1, 8], F32)
    cc_out_t = nc.dram_tensor("cc_out", [1, 8], F32)
    spills = {n: nc.dram_tensor("sp_" + n, [8, 128, 2816], BF16).ap()
              for n in ("wg", "wu", "wd")}

    es = ExitStack()
    kb = KB(nc)
    B = alloc_bufs(kb, es)
    emit_setup(kb, B)
    emit_l1ffn_prep_sharded(kb, B, wsh, ccg_in, ccg_out, spills)
    emit_fence(kb)
    emit_l0attn(kb, B, [xt[0, dc] for dc in range(NDC)], w0, 0)
    emit_spill_sx_r(kb, B, x1scr)
    emit_l0attn(kb, B, [xt[1, dc] for dc in range(NDC)], w0, 1)
    emit_fence(kb)
    emit_l0ffn_prep(kb, B, w8, bias, None, amscr, sclscr,
                    fused_cc=(cc_in_t, cc_out_t, amscr2))
    emit_fence(kb)
    emit_l0ffn(kb, B, 1)
    emit_fence(kb)
    emit_spill_sx_r(kb, B, x2scr)
    emit_fence(kb)
    emit_load_sx_r(kb, B, x1scr)
    emit_l0ffn(kb, B, 0)
    emit_fence(kb)
    if debug:
        emit_spill_sx_r(kb, B, xd2b0)
        emit_fence(kb)
    emit_l0attn(kb, B, None, w1, 0, layer=1)
    emit_fence(kb)
    if debug:
        emit_spill_sx_r(kb, B, xd3b0)
        emit_fence(kb)
    emit_l1ffn(kb, B, spills, 0)
    emit_fence(kb)
    emit_store_sx(kb, B, [xo[0, dc] for dc in range(NDC)], 0)
    emit_fence(kb)
    emit_load_sx_r(kb, B, x2scr)
    if debug:
        emit_tap(kb, nc, B, "sxldD", B["sx"][:, :].bitcast(F32),
                 kb.need(*[("sx", dc) for dc in range(NDC)]))
        emit_fence(kb)
    emit_l0attn(kb, B, None, w1, 1, layer=1,
                dbg=(emit_tap if debug else None), nc_=nc)
    emit_fence(kb)
    if debug:
        emit_spill_sx_r(kb, B, xd3b1)
        emit_fence(kb)
    emit_l1ffn(kb, B, spills, 1)
    emit_fence(kb)
    emit_store_sx(kb, B, [xo[1, dc] for dc in range(NDC)], 1)
    emit_fence(kb)
    kb.finalize()
    es.close()
    return nc


def prep_l1ffn_inputs(X1, wg, wu, wd, core, n_cores=8):
    bpc = X1.shape[0] // n_cores
    return {
        "xt": np.ascontiguousarray(X1[core * bpc:(core + 1) * bpc]),
        "wg": np.ascontiguousarray(wg.reshape(22, 128, 1024)),
        "wu": np.ascontiguousarray(wu.reshape(22, 128, 1024)),
        "wd": np.ascontiguousarray(wd.reshape(8, 128, 2816)),
    }


def build_l1ffn_standalone(bpc=2, skip_ln=False, debug_spills=False, debug_taps=False, sc_list=None):
    nc = bass.Bass("TRN2", target_bir_lowering=False, dynamic_dma_scratch_size=16384,
                   detect_race_conditions=False)
    xt = nc.dram_tensor("xt", [bpc, NDC, 128, S], F32R, kind="ExternalInput").ap()
    wg = nc.dram_tensor("wg", [22, 128, 1024], F32R, kind="ExternalInput").ap()
    wu = nc.dram_tensor("wu", [22, 128, 1024], F32R, kind="ExternalInput").ap()
    wd = nc.dram_tensor("wd", [8, 128, 2816], F32R, kind="ExternalInput").ap()
    xo = nc.dram_tensor("xo", [bpc, NDC, 128, S], F32, kind="ExternalOutput").ap()
    spills = {n: nc.dram_tensor("sp_" + n, [8, 128, 2816], BF16,
                                kind=("ExternalOutput" if debug_spills else "Internal")).ap()
              for n in ("wg", "wu", "wd")}

    es = ExitStack()
    kb = KB(nc)
    B = alloc_bufs(kb, es)
    emit_setup(kb, B)
    emit_l1ffn_prep(kb, B, wg, wu, wd, spills)
    for b in range(bpc):
        for dc in range(NDC):
            v = kb.op("sync", kb.get_guard(("sx", dc)),
                      lambda e, b=b, dc=dc: e.dma_start(
                          out=B["sx"][:, dc * S:(dc + 1) * S], in_=xt[b, dc]),
                      ("dX", 16))
        for dc in range(NDC):
            kb.mark(("sx", dc), "dX", v)
        dbgfn = None
        if debug_taps and b == 0:
            def dbgfn(kb_, name, ap, waits, guard_keys):
                t = nc.dram_tensor(name, list(ap.shape), ap.dtype, kind="ExternalOutput").ap()
                vv_ = kb_.op("gpsimd", waits,
                             lambda e, t=t, ap=ap: e.dma_start(out=t, in_=ap),
                             ("dG", 16))
                for kk in guard_keys:
                    kb_.set_guard(kk, "dG", vv_)
        emit_l1ffn(kb, B, spills, b, dbg=dbgfn, sc_list=sc_list, skip_ln=skip_ln)
        emit_store_sx(kb, B, [xo[b, dc] for dc in range(NDC)], b)
    kb.op("gpsimd", [(n, kb.now(n)) for n in
                     ("dG", "dX", "dA0", "dA1", "dS0", "dS1", "dR", "dR2", "dM", "dB",
                      "dW", "dV0", "dV1", "dV2", "dV3", "dD") if kb.cnt.get(n)],
          lambda e: e.engine_nop(), None)
    kb.finalize()
    es.close()
    return nc


def kernel(L0_attn_X, L0_attn_Wq, L0_attn_Wk, L0_attn_Wv, L0_attn_Wo, L0_attn_We,
           L0_ffn_weight, L0_ffn_bias, L1_attn_Wq, L1_attn_Wk, L1_attn_Wv,
           L1_attn_Wo, L1_ffn_w_gate, L1_ffn_w_up, L1_ffn_w_down):
    from concourse.bass_utils import run_bass_kernel_spmd
    n_cores = 8
    cores = list(range(n_cores))

    nc1 = build_full_fused(bpc=2)
    shared = {
        "w0wq": chunk_T(L0_attn_Wq.T), "w0wk": chunk_T(L0_attn_Wk.T),
        "w0wv": chunk_T(L0_attn_Wv.T), "w0wo": chunk_T(L0_attn_Wo.T),
        "w0we": chunk_T(L0_attn_We.T),
        "w8": chunk_T(np.ascontiguousarray(L0_ffn_weight.T)),
        "bias": np.ascontiguousarray(L0_ffn_bias.reshape(8, 128).T),
        "w1wq": chunk_T(L1_attn_Wq), "w1wk": chunk_T(L1_attn_Wk),
        "w1wv": chunk_T(L1_attn_Wv), "w1wo": chunk_T(L1_attn_Wo),
    }
    wg3 = L1_ffn_w_gate.reshape(22, 128, 1024)
    wu3 = L1_ffn_w_up.reshape(22, 128, 1024)
    wd3 = L1_ffn_w_down.reshape(8, 128, 2816)
    ins = []
    for c in cores:
        xs = L0_attn_X[c * 2:(c + 1) * 2]
        xtc = np.ascontiguousarray(xs.transpose(0, 2, 1)).reshape(2, NDC, 128, S)
        wsh = np.zeros((7, 128, 2816), np.float32)
        for s in range(6):
            u = c * 6 + s
            if u < 22:
                wsh[s, :, 0:1024] = wg3[u]
            elif u < 44:
                wsh[s, :, 0:1024] = wu3[u - 22]
        wsh[6] = wd3[c]
        d = {"xt": np.ascontiguousarray(xtc), "wsh": wsh}
        d.update(shared)
        ins.append(d)
    r = run_bass_kernel_spmd(nc1, ins, core_ids=cores).results

    out = np.empty((16, 2048, 1024), dtype=np.float32)
    for c in cores:
        xc = np.asarray(r[c]["xo"]).reshape(2, 1024, 2048)
        out[c * 2] = xc[0].T
        out[c * 2 + 1] = xc[1].T
    return out


# version 09
# speedup vs baseline: 1.1535x; 1.0175x over previous
import numpy as np
import concourse.bass as bass
import concourse.mybir as mybir

F32 = mybir.dt.float32
F32R = mybir.dt.float32r
BF16 = mybir.dt.bfloat16
I32 = mybir.dt.int32
AF = mybir.ActivationFunctionType
ALU = mybir.AluOpType
AX = mybir.AxisListType

D = 1024
S = 2048
H = 16
DK = 64
DE = 128
NDC = D // 128
NSC = S // 512
NST = S // 128
FFI = 2816
SCALE = 0.125
EPS = 1e-5


class KB:

    def __init__(self, nc):
        self.nc = nc
        self.items = []
        self.sems = {}
        self.cnt = {}
        self.guard = {}
        self.ready = {}
        self._waited = {}
        self._es = None

    def sem(self, name):
        if name not in self.sems:
            h = self.nc.semaphore(name)
            self.sems[name] = h.__enter__()
            self.cnt[name] = 0
        return self.sems[name]

    def inc(self, name, amount=1):
        self.sem(name)
        self.cnt[name] += amount
        return self.cnt[name]

    def now(self, name):
        self.sem(name)
        return self.cnt[name]

    def add(self, engine, fn):
        self.items.append((engine, fn))

    def set_guard(self, res, semname, val):
        d = self.guard.setdefault(res, {})
        d[semname] = max(d.get(semname, 0), val)

    def clear_guard(self, res):
        self.guard.pop(res, None)

    def get_guard(self, res):
        return list(self.guard.get(res, {}).items())

    def mark(self, res, semname, val):
        self.ready[res] = (semname, val)

    def need(self, *res):
        out = []
        for r in res:
            g = self.ready.get(r)
            if g:
                out.append(g)
        return out

    def vchain(self, steps, waits0=()):
        prev = None
        for i, emit in enumerate(steps):
            w = list(waits0) if i == 0 else [("vC", prev)]
            prev = self.op("vector", w, emit, ("vC", 1))
        return prev

    def op(self, engine, waits, emit, inc=None):
        newval = None
        incsem = None
        incamt = 0
        if inc is not None:
            name, incamt = (inc[0], inc[1]) if len(inc) > 1 else (inc[0], 1)
            newval = self.inc(name, incamt)
            incsem = self.sems[name]
        wl = []
        wmax = self._waited.setdefault(engine, {})
        for n, v in waits:
            if v and v > wmax.get(n, 0):
                wmax[n] = v
                wl.append((self.sem(n), v))

        def fn(eng, wl=wl, emit=emit, incsem=incsem, incamt=incamt):
            for s, v in wl:
                eng.wait_ge(s, v)
            ins = emit(eng)
            if incsem is not None and ins is not None:
                ins.then_inc(incsem, incamt)
        self.add(engine, fn)
        return newval

    def finalize(self):
        self._lp = self.nc.allow_low_precision("f32r intermediates are intentional")
        self._lp.__enter__()
        block = self.nc.Block()
        block.__enter__()
        by_eng = {}
        for e, fn in self.items:
            by_eng.setdefault(e, []).append(fn)
        for e, fns in by_eng.items():
            def body(engine, fns=fns):
                for f in fns:
                    f(engine)
            getattr(block, e)(body)
        block.__exit__(None, None, None)
        self._lp.__exit__(None, None, None)


def fn_engine(e):
    return e


def ceil_div(a, b):
    return (a + b - 1) // b



def chunk_T(w):
    w = np.asarray(w)
    d, f = w.shape
    t = w.reshape(d // 128, 128, f // 128, 128)
    t = t.transpose(2, 0, 1, 3)
    return np.ascontiguousarray(t)


"""Static SBUF / PSUM allocation shared by all sublayer emitters."""


def alloc_bufs(kb, es):
    nc = kb.nc
    B = {}

    def sb(name, shape, dtype):
        h = es.enter_context(nc.sbuf_tensor(name, shape, dtype))
        B[name] = h
        return h

    sb("sx", [128, NDC * S], F32R)
    sb("kt", [128, NDC * S], BF16)
    sb("qt", [128, NDC * S], BF16)
    sb("vv", [128, NST * (H * 65 + 0)], BF16)
    sb("cx", [128, S], BF16)
    sb("ao", [128, NDC * 512], BF16)
    sb("at", [128, 4 * 512], BF16)
    sb("wa", [128, 2 * 1024], F32R)
    sb("wc", [128, 2 * 1024], BF16)
    sb("sq", [128, 2 * 512], F32R)
    sb("rr", [1, 512], F32R)
    sb("br", [64, 512], F32R)
    sb("rows", [1, 3 * 512], F32R)
    sb("onesc", [128, 1], F32R)
    sb("ones1", [1, 128], F32R)
    sb("ident_b", [128, 128], BF16)
    sb("ident_r", [128, 128], F32R)
    sb("ident_f", [128, 128], F32)
    sb("am8", [128, 8], F32)
    sb("onesb", [128, 16], BF16)
    sb("scl", [128, 4], F32)
    sb("biasb", [128, 8], F32)
    sb("am2", [128, 2], F32)
    sb("srow", [1, 160], F32)

    banks = {}
    for n in ("ps0a", "ps1a", "ps0b", "ps1b", "po0", "po1", "pu"):
        banks[n] = es.enter_context(nc.psum_tensor(n, [128, 512], F32))
    banks["pvB"] = es.enter_context(nc.psum_tensor("pvB", [128, 1024], BF16))
    B["banks"] = banks
    return B


def emit_vv_ones(kb, B):
    v1 = None
    for st in range(NST):
        dst = B["vv"][:, st * 1040: (st + 1) * 1040]             .rearrange("p (h c) -> p h c", c=65)[:, :, 64:65]
        src = B["onesb"][:, :].rearrange("p (h o) -> p h o", o=1)
        v1 = kb.op("vector", kb.get_guard("vv") if st == 0 else [],
                   lambda e, dst=dst, src=src: e.tensor_copy(dst, src),
                   ("vC", 1) if st == NST - 1 else None)
    return v1


def emit_setup(kb, B):
    gv = kb.op("gpsimd", [], lambda e: e.memset(B["ident_f"][:, :], 1.0), ("gP", 1))
    v1 = kb.op("vector", [("gP", gv)],
               lambda e: e.tensor_copy(B["onesb"][:, :], B["ident_f"][:, 0:16]), None)
    v1 = kb.op("vector", [],
               lambda e: e.tensor_copy(B["onesc"][:, :], B["ident_f"][:, 0:1]), None)
    v1 = kb.op("vector", [],
               lambda e: e.tensor_copy(B["ones1"][:, :], B["ident_f"][0:1, :]), ("vC", 1))
    v1 = emit_vv_ones(kb, B)
    gv = kb.op("gpsimd", [("vC", v1)], lambda e: e.memset(B["ident_f"][:, :], 0.0), None)
    gv = kb.op("gpsimd", [],
               lambda e: e.affine_select(
                   out=B["ident_f"][:, :], in_=B["ident_f"][:, :],
                   compare_op=ALU.not_equal, fill=1.0, base=0,
                   pattern=[[-1, 128]], channel_multiplier=1),
               ("gP", 1))
    v1 = kb.op("vector", [("gP", gv)],
               lambda e: e.tensor_copy(B["ident_b"][:, :], B["ident_f"][:, :]), None)
    v1 = kb.op("vector", [],
               lambda e: e.tensor_copy(B["ident_r"][:, :], B["ident_f"][:, :]), ("vC", 1))
    kb.mark("consts", "vC", v1)
    kb.mark("consts_g", "gP", gv)


"""LayerNorm over the feature dim, in place on bufs['sx'] ([d,seq] layout).

Stats via ones-matmuls over partitions (accumulated across the 8 d-chunks),
row math on [1,512] vectors, rstd = Exp(-0.5*Ln(var+eps)) (one act table),
broadcast back via rank-1 matmuls, normalize with two TT passes per chunk.
Marks ("sx", dc) and ("sx", dc, "ln") when chunk dc is fully normalized.
"""


def emit_ln(kb, B, rdeps, dbg=None):
    sx, sq = B["sx"], B["sq"]
    rows = B["rows"]
    banks = B["banks"]
    st_name, st = "po1", banks["po1"]
    s2_name, st2 = "pu", banks["pu"]
    b1n, b1 = "ps0a", banks["ps0a"]
    b2n, b2 = "ps1a", banks["ps1a"]
    onesc, ones1 = B["onesc"], B["ones1"]
    lastv = {}

    def sxsl(dc, c0, n):
        return sx[:, dc * S + c0: dc * S + c0 + n]

    for sc in range(NSC):
        tv = None
        for dc in range(NDC):
            waits = [w for w in [kb.ready.get(("sx", dc, "r", sc))] if w]
            waits = [(n, v) for (n, v) in waits]
            if dc == 0:
                waits = waits + kb.get_guard(st_name) + kb.need("consts")
            tv = kb.op("tensor", waits,
                       lambda e, dc=dc, sc=sc: e.matmul(
                           st[0:1, :], onesc[:, 0:1], sxsl(dc, sc * 512, 512),
                           start=(dc == 0), stop=(dc == NDC - 1)),
                       ("tP", 1) if dc == NDC - 1 else None)
        for dc in range(NDC):
            slot = dc % 2
            waits = kb.get_guard(("sq", slot))
            w2 = kb.ready.get(("sx", dc, "r", sc))
            if w2:
                waits = waits + [w2]
            sv = kb.op("scalar", waits,
                       lambda e, dc=dc, sc=sc, slot=slot: e.activation(
                           sq[:, slot * 512:(slot + 1) * 512],
                           sxsl(dc, sc * 512, 512).bitcast(F32), AF.Square),
                       ("sC", 1))
            w3 = [("sC", sv)] + (kb.get_guard(s2_name) if dc == 0 else [])
            tv = kb.op("tensor", w3,
                       lambda e, slot=slot, dc=dc: e.matmul(
                           st2[0:1, :], onesc[:, 0:1], sq[:, slot * 512:(slot + 1) * 512],
                           start=(dc == 0), stop=(dc == NDC - 1)),
                       ("tP", 1) if dc == NDC - 1 else None)
            kb.set_guard(("sq", slot), "tP", tv if tv is not None else kb.now("tP"))
        r_m = rows[:, 0:512]
        r_v = rows[:, 512:1024]
        r_t = rows[:, 1024:1536]
        rr_s = rows[:, 512:1024]
        rr_mr = rows[:, 1024:1536]
        kb.op("vector", [("tP", tv)] + kb.get_guard("rows"),
              lambda e: e.tensor_scalar(r_m, st[0:1, :], 1.0 / D, None, ALU.mult), None)
        kb.op("vector", [],
              lambda e: e.tensor_scalar(r_v, st2[0:1, :], 1.0 / D, None, ALU.mult), None)
        kb.op("vector", [], lambda e: e.tensor_tensor(r_t, r_m, r_m, ALU.mult), None)
        kb.op("vector", [], lambda e: e.tensor_tensor(r_v, r_v, r_t, ALU.subtract), None)
        vv_ = kb.op("vector", [],
                    lambda e: e.tensor_scalar(r_v, r_v, EPS, None, ALU.add), ("vC", 1))
        kb.set_guard(st_name, "vC", vv_)
        kb.set_guard(s2_name, "vC", vv_)
        sv = kb.op("scalar", [("vC", vv_)],
                   lambda e: e.activation(r_t, r_v.bitcast(F32), AF.Ln), None)
        sv = kb.op("scalar", [],
                   lambda e: e.activation(rr_s, r_t.bitcast(F32), AF.Exp, scale=-0.5),
                   ("sC", 1))
        mv = kb.op("vector", [("sC", sv)],
                   lambda e: e.tensor_tensor(rr_mr, r_m, rr_s, ALU.mult), ("vC", 1))
        if dbg is not None and sc == 0:
            dbg(kb, mv)
        bv1 = kb.op("tensor", kb.get_guard(b1n) + [("sC", sv)],
                    lambda e: e.matmul(b1[:, :], ones1[0:1, :], rr_s,
                                       start=True, stop=True), ("tP", 1))
        bv2 = kb.op("tensor", kb.get_guard(b2n) + [("vC", mv)],
                    lambda e: e.matmul(b2[:, :], ones1[0:1, :], rr_mr,
                                       start=True, stop=True), ("tP", 1))
        kb.set_guard("rows", "tP", bv2)
        nv = None
        for dc in range(NDC):
            slot = dc % 2
            waits = [("tP", bv2)] + kb.get_guard(("sq", slot))
            kb.op("vector", waits,
                  lambda e, dc=dc, sc=sc, slot=slot: e.tensor_tensor(
                      sq[:, slot * 512:(slot + 1) * 512],
                      sxsl(dc, sc * 512, 512).bitcast(F32), b1[:, :], ALU.mult),
                  None)
            nv = kb.op("vector", [],
                       lambda e, dc=dc, sc=sc, slot=slot: e.tensor_tensor(
                           sxsl(dc, sc * 512, 512),
                           sq[:, slot * 512:(slot + 1) * 512].bitcast(F32),
                           b2[:, :], ALU.subtract),
                       ("vC", 1))
            kb.set_guard(("sq", slot), "vC", nv)
            lastv[dc] = nv
        kb.set_guard(b1n, "vC", nv)
        kb.set_guard(b2n, "vC", nv)
    for dc in range(NDC):
        kb.mark(("sx", dc), "vC", lastv[dc])
        kb.mark(("sx", dc, "ln"), "vC", lastv[dc])


"""L0 attention sublayer emitter (softmax attn + ctx-influence bias).

x_out = LN(x + Attn(x)) written in place into bufs['sx'];
per-batch |x_out| partition-max column into bufs['am2'][:, b].

See common.py for layouts. psum banks (f32 [128,512] unless noted):
ps0a ps1a ps0b ps1b po0 po1 pu, pvB ([128,1024] bf16).
"""


def wcol_ap(wdram, fc):
    return wdram[fc].transpose([1, 0, 2])


def wdst_ap(buf, off, n=1024):
    return buf[:, off: off + n].rearrange("p (c k) -> p c k", k=128)


def emit_l0attn(kb, B, xt_dram, w, b, dbg=None, nc_=None, skip_ln=False, layer=0):
    sx, kt, qt, vv, cx, ao, at = B["sx"], B["kt"], B["qt"], B["vv"], B["cx"], B["ao"], B["at"]
    wa, wc, rr, br = B["wa"], B["wc"], B["rr"], B["br"]
    banks = B["banks"]
    ps = [banks[n] for n in ("ps0a", "ps1a", "ps0b", "ps1b")]
    psn = ["ps0a", "ps1a", "ps0b", "ps1b"]
    po = [banks["po0"], banks["po1"]]
    pon = ["po0", "po1"]
    pu, po1b, pvB = banks["pu"], banks["po1"], banks["pvB"]

    def sxsl(dc, c0, n):
        return sx[:, dc * S + c0: dc * S + c0 + n]

    emit_vv_ones(kb, B)

    if xt_dram is not None:
        for dc in range(NDC):
            v = kb.op("sync", kb.get_guard(("sx", dc)),
                      lambda e, dc=dc: e.dma_start(
                          out=sx[:, dc * S:(dc + 1) * S], in_=xt_dram[dc]),
                      ("dX", 16))
        for dc in range(NDC):
            kb.mark(("sx", dc), "dX", v)

    def proj_phase(wdram, evac, nfc=NDC):
        util = [("pu", pu), ("po1", po1b)]
        for fc in range(nfc):
            slot = fc % 2
            v = kb.op("sync", kb.get_guard(("wa", slot)),
                      lambda e, wdram=wdram, fc=fc, slot=slot: e.dma_start(
                          out=wdst_ap(wa, slot * 1024), in_=wcol_ap(wdram, fc)),
                      ("dA%d" % slot, 16))
            kb.mark(("wa", slot), "dA%d" % slot, v)
            for sc in range(NSC):
                j = fc * NSC + sc
                bname, bank = util[j % 2]
                tv = None
                for dc in range(NDC):
                    waits = []
                    if dc == 0:
                        waits = kb.get_guard(bname) + kb.need(("wa", slot))
                    waits = waits + kb.need(("sx", dc))
                    lhs = wa[:, slot * 1024 + dc * 128: slot * 1024 + (dc + 1) * 128]
                    rhs = sxsl(dc, sc * 512, 512)
                    last = dc == NDC - 1
                    tv = kb.op("tensor", waits,
                               lambda e, bank=bank, lhs=lhs, rhs=rhs, dc=dc, last=last: e.matmul(
                                   bank[:, :], lhs, rhs, start=(dc == 0), stop=last),
                               ("tP", 1) if last else None)
                if sc == NSC - 1:
                    kb.set_guard(("wa", slot), "tP", tv)
                evac(bname, bank, fc, sc, tv)

    def scalar_evac(dst_ap, bank, bname, tv, scale):
        v = kb.op("scalar", [("tP", tv)],
                  lambda e, dst=dst_ap, bank=bank, scale=scale: e.activation(
                      dst, bank[:, :], AF.Copy, scale=scale),
                  ("sC", 1))
        kb.set_guard(bname, "sC", v)
        return v

    if layer == 0:
        def ctx_evac(bname, bank, fc, sc, tv):
            v = scalar_evac(cx[:, sc * 512:(sc + 1) * 512], bank, bname, tv,
                            float(np.sqrt(0.1)))
            if sc == NSC - 1:
                kb.mark("cx", "sC", v)
        proj_phase(w["we"], ctx_evac, nfc=1)

    def k_evac(bname, bank, fc, sc, tv):
        v = scalar_evac(kt[:, fc * S + sc * 512: fc * S + (sc + 1) * 512], bank, bname, tv, 1.0)
        if fc == NDC - 1 and sc == NSC - 1:
            kb.mark("kt", "sC", v)
    proj_phase(w["wk"], k_evac)

    def v_evac(bname, bank, fc, sc, tv):
        v = scalar_evac(qt[:, fc * S + sc * 512: fc * S + (sc + 1) * 512], bank, bname, tv, 1.0)
        if fc == NDC - 1 and sc == NSC - 1:
            kb.mark("qtv", "sC", v)
    proj_phase(w["wv"], v_evac)

    for fc in range(NDC):
        for half in range(2):
            tv = None
            for stl in range(8):
                st = half * 8 + stl
                waits = kb.need("qtv", "consts") + (kb.get_guard("pvB") if stl == 0 else [])
                tv = kb.op("tensor", waits,
                           lambda e, fc=fc, st=st, stl=stl: e.transpose(
                               pvB[:, stl * 128:(stl + 1) * 128],
                               qt[:, fc * S + st * 128: fc * S + (st + 1) * 128],
                               B["ident_b"][:, :]),
                           ("tP", 1))
            src = pvB[:, 0:1024].rearrange("p (st h c) -> p st h c", h=2, c=64)
            dst = vv[:, :].rearrange("p (st x) -> p st x", x=1040)[
                :, half * 8:(half + 1) * 8, 2 * fc * 65: 2 * fc * 65 + 130] \
                .rearrange("p st (h c) -> p st h c", c=65)[:, :, :, 0:64]
            vcv = kb.op("vector", [("tP", tv)],
                        lambda e, src=src, dst=dst: e.tensor_copy(dst, src),
                        ("vC", 1))
            kb.set_guard("pvB", "vC", vcv)
    kb.set_guard("qtbuf", "tP", kb.now("tP"))
    kb.mark("vv", "vC", kb.now("vC"))

    if dbg:
        dbg(kb, nc_, B, "cxD", cx[:, :], kb.need("cx"))
        dbg(kb, nc_, B, "ktD", kt[:, :], kb.need("kt"))
        dbg(kb, nc_, B, "qtvD", qt[:, :], kb.need("qtv"), guard_res="qtbuf")
        dbg(kb, nc_, B, "vvD", vv[:, :], kb.need("vv"))

    def q_evac(bname, bank, fc, sc, tv):
        waits_extra = kb.get_guard("qtbuf")
        v = kb.op("scalar", [("tP", tv)] + waits_extra,
                  lambda e, fc=fc, sc=sc, bank=bank: e.activation(
                      qt[:, fc * S + sc * 512: fc * S + (sc + 1) * 512],
                      bank[:, :], AF.Copy, scale=SCALE),
                  ("sC", 1))
        kb.set_guard(bname, "sC", v)
        if fc == NDC - 1 and sc == NSC - 1:
            kb.mark("qt", "sC", v)
    proj_phase(w["wq"], q_evac)

    if dbg:
        dbg(kb, nc_, B, "qtD", qt[:, :], kb.need("qt"))

    for qc in range(NSC):
        for hp in range(NDC):
            hpo = hp * S
            qco = qc * 512
            avdone = [None, None]
            nkt = NST if layer == 0 else 4 * qc + 4
            for ktile in range(nkt):
                pr = (ktile % 2) * 2
                off = 0
                if layer == 1:
                    r = ktile - 4 * qc
                    off = max(0, r) * 128
                expv = [None, None]
                for par in (0, 1):
                    bname, bank = psn[pr + par], ps[pr + par]
                    r0, r1 = par * 64, par * 64 + 64
                    waits = kb.get_guard(bname) + kb.need("kt", "qt")
                    kb.op("tensor", waits,
                          lambda e, bank=bank, hpo=hpo, qco=qco, kt0=ktile * 128,
                                 r0=r0, r1=r1, off=off: e.matmul(
                              bank[:, off:512], kt[r0:r1, hpo + kt0: hpo + kt0 + 128],
                              qt[r0:r1, hpo + qco + off: hpo + qco + 512],
                              start=True, stop=(layer == 1),
                              tile_position=(r0, 0)),
                          ("tP", 1) if layer == 1 else None)
                tvs = [None, None]
                if layer == 0:
                    for par in (0, 1):
                        bname, bank = psn[pr + par], ps[pr + par]
                        tvs[par] = kb.op("tensor", kb.need("cx"),
                                         lambda e, bank=bank, qco=qco, kt0=ktile * 128: e.matmul(
                                             bank[:, :], cx[:, kt0: kt0 + 128],
                                             cx[:, qco: qco + 512],
                                             start=False, stop=True),
                                         ("tP", 1))
                else:
                    tvs = [kb.now("tP"), kb.now("tP")]
                for par in (0, 1):
                    bname, bank = psn[pr + par], ps[pr + par]
                    slot = pr + par
                    waits = [("tP", tvs[par])] + kb.get_guard(("at", slot))
                    ev = kb.op("scalar", waits,
                               lambda e, bank=bank, slot=slot, off=off: e.activation(
                                   at[:, slot * 512 + off:(slot + 1) * 512],
                                   bank[:, off:512], AF.Exp),
                               ("sC", 1))
                    kb.set_guard(bname, "sC", ev)
                    expv[par] = ev
                for par in (0, 1):
                    slot = pr + par
                    h = 2 * hp + par
                    waits = [("sC", expv[par])] + kb.need("vv", "consts")
                    if ktile == 0:
                        waits = waits + kb.get_guard(pon[par])
                    tv = kb.op("tensor", waits,
                               lambda e, par=par, slot=slot, h=h, kt0=ktile, off=off,
                                      last=(ktile == nkt - 1): e.matmul(
                                   po[par][0:65, off:512],
                                   vv[:, kt0 * 1040 + h * 65: kt0 * 1040 + (h + 1) * 65],
                                   at[:, slot * 512 + off:(slot + 1) * 512],
                                   start=(kt0 == 0), stop=last),
                               ("tP", 1))
                    kb.set_guard(("at", slot), "tP", tv)
                    avdone[par] = tv
            for par in (0, 1):
                h = 2 * hp + par
                waits = [("tP", avdone[par])] + kb.get_guard("rr")
                rv = kb.op("vector", waits,
                           lambda e, par=par: e.reciprocal(rr[:, :], po[par][64:65, :]),
                           ("vC", 1))
                bw = kb.get_guard("pu") + [("vC", rv)] + kb.need("consts")
                bv = kb.op("tensor", bw,
                           lambda e: e.matmul(pu[0:64, :], B["ones1"][0:1, 0:64], rr[:, :],
                                              start=True, stop=True),
                           ("tP", 1))
                kb.set_guard("rr", "tP", bv)
                cv = kb.op("vector", [("tP", bv)],
                           lambda e: e.tensor_copy(br[:, :], pu[0:64, :]),
                           ("vC", 1))
                kb.set_guard("pu", "vC", cv)
                dst = ao[par * 64: par * 64 + 64, hp * 512:(hp + 1) * 512]
                mv = kb.op("vector", [("vC", cv)] + kb.get_guard(("ao", hp, par)),
                           lambda e, par=par, dst=dst: e.tensor_tensor(
                               dst, po[par][0:64, :], br[:, :], ALU.mult),
                           ("vC", 1))
                kb.set_guard(pon[par], "vC", mv)
                kb.set_guard("br", "vC", mv)
                kb.mark(("ao", hp, par), "vC", mv)

        if dbg and qc == 0:
            v_ = dbg(kb, nc_, B, "aoD", ao[:, :],
                     kb.need(*[("ao", d_, p_) for d_ in range(NDC) for p_ in (0, 1)]))
            for d_ in range(NDC):
                for p_ in (0, 1):
                    kb.set_guard(("ao", d_, p_), "dG", v_)

        util = [("pu", pu), ("po1", po1b)]
        for fc in range(NDC):
            slot = fc % 2
            v = kb.op("sync", kb.get_guard(("wa", slot)),
                      lambda e, fc=fc, slot=slot: e.dma_start(
                          out=wdst_ap(wa, slot * 1024), in_=wcol_ap(w["wo"], fc)),
                      ("dA%d" % slot, 16))
            kb.mark(("wa", slot), "dA%d" % slot, v)
            vcw = kb.op("vector", [("dA%d" % slot, v)] + kb.get_guard(("wc", slot)),
                        lambda e, slot=slot: e.tensor_copy(
                            wc[:, slot * 1024:(slot + 1) * 1024],
                            wa[:, slot * 1024:(slot + 1) * 1024].bitcast(F32)),
                        ("vC", 1))
            kb.set_guard(("wa", slot), "vC", vcw)
            kb.mark(("wc", slot), "vC", vcw)
            bname, bank = util[fc % 2]
            tv = None
            for dc in range(NDC):
                waits = kb.need(("ao", dc, 0), ("ao", dc, 1))
                if dc == 0:
                    waits = waits + kb.get_guard(bname) + kb.need(("wc", slot))
                lhs = wc[:, slot * 1024 + dc * 128: slot * 1024 + (dc + 1) * 128]
                rhs = ao[:, dc * 512:(dc + 1) * 512]
                last = dc == NDC - 1
                tv = kb.op("tensor", waits,
                           lambda e, bank=bank, lhs=lhs, rhs=rhs, dc=dc, last=last: e.matmul(
                               bank[:, :], lhs, rhs, start=(dc == 0), stop=last),
                           ("tP", 1) if last else None)
            kb.set_guard(("wc", slot), "tP", tv)
            dst = sxsl(fc, qc * 512, 512)
            v = kb.op("vector", [("tP", tv)],
                      lambda e, dst=dst, bank=bank: e.tensor_tensor(
                          dst, bank[:, :], dst.bitcast(F32), ALU.add),
                      ("vC", 1))
            kb.set_guard(bname, "vC", v)
            kb.mark(("sx", fc, "r", qc), "vC", v)
        for dc in range(NDC):
            for par in (0, 1):
                kb.set_guard(("ao", dc, par), "tP", kb.now("tP"))

    tnow = kb.now("tP")
    for rname in ("vv", "cx", "kt", "qt"):
        kb.set_guard(rname, "tP", tnow)

    if skip_ln:
        for dc in range(NDC):
            lv = max(kb.ready[("sx", dc, "r", qc)][1] for qc in range(NSC))
            kb.mark(("sx", dc), "vC", lv)
            kb.mark(("sx", dc, "ln"), "vC", lv)
    else:
        emit_ln(kb, B, rdeps=None)

    if layer == 1:
        return
    last = None
    for dc in range(NDC):
        src = sx[:, dc * S:(dc + 1) * S]
        dst = B["am8"][:, dc:dc + 1]
        last = kb.op("vector", kb.need(("sx", dc, "ln")) +
                     ([("vC", last)] if last else []),
                     lambda e, src=src, dst=dst: e.reduce_max(
                         dst, src.bitcast(F32), axis=AX.X, apply_absolute_value=True),
                     ("vC", 1))
        kb.set_guard(("sx", dc), "vC", last)
    v = kb.vchain([
        lambda e, b=b: e.reduce_max(B["am2"][:, b:b + 1], B["am8"][:, :],
                                    axis=AX.X, apply_absolute_value=True),
        lambda e: e.engine_nop(),
    ], waits0=[("vC", last)])
    kb.mark(("am2", b), "vC", v)


"""L0 FFN sublayer: global-amax fp8(e4m3fn) quantized linear + bias, bf16
round, residual, LN.   x_out = LN(x + bf16(acc*osc + bias)) in place on sx.

Buffer reuse during this phase: vv (bitcast f32r) holds the staged weight,
qt holds the quantized weight (bf16), kt holds quantized x (bf16),
wc/at (bitcast) are elementwise scratch, ao holds bf16 ffn outputs.
scl [128, 4] f32: col0 = w-scale, col1 = x-scale, col2 = osc.
"""


def emit_fp8_quant(kb, B, src, dst, scl_col, n, extra=()):
    sqf = B["ao"][:, 0:2 * n].bitcast(F32)[:, 0:n]
    sqi = B["ao"][:, 0:2 * n].bitcast(I32)[:, 0:n]
    wcf = B["wc"][:, 0:2 * n].bitcast(F32)[:, 0:n]
    msk = B["at"][:, 0:2 * n].bitcast(I32)[:, 0:n]
    scl = B["scl"]
    g = kb.get_guard(("aoq", 0)) + kb.get_guard(("aoq", 1)) + \
        kb.get_guard("wcscr") + kb.get_guard("atscr") + list(extra)
    kb.op("vector", g + kb.need("scl"),
          lambda e: e.tensor_scalar(sqf, src.bitcast(F32),
                                    scl[:, scl_col:scl_col + 1], None, ALU.mult), None)
    kb.op("vector", [],
          lambda e: e.tensor_scalar(sqf, sqf, 448.0, None, ALU.min), None)
    kb.op("vector", [],
          lambda e: e.tensor_scalar(sqf, sqf, -448.0, None, ALU.max), None)
    kb.op("vector", [],
          lambda e: e.tensor_scalar(wcf, sqf, 24576.0, None, ALU.add), None)
    kb.op("vector", [],
          lambda e: e.tensor_scalar(wcf, wcf, 24576.0, None, ALU.subtract), None)
    kb.op("vector", [],
          lambda e: e.tensor_scalar(msk, sqi, 0x7FFFFFFF, None, ALU.bitwise_and), None)
    kb.op("vector", [],
          lambda e: e.tensor_scalar(msk, msk, 0x3C800000, None, ALU.is_lt), None)
    kb.op("vector", [],
          lambda e: e.tensor_scalar(sqi, sqi, 0x80000, None, ALU.add), None)
    kb.op("vector", [],
          lambda e: e.tensor_scalar(sqi, sqi, ~0xFFFFF, None, ALU.bitwise_and), None)
    kb.op("vector", [],
          lambda e: e.select(sqf, msk, wcf, sqf), None)
    v = kb.op("vector", [],
              lambda e: e.tensor_copy(dst, sqf), ("vC", 1))
    for r in (("aoq", 0), ("aoq", 1), "wcscr", "atscr"):
        kb.set_guard(r, "vC", v)
    return v


def emit_scale_rows(kb, B, col_amax, col_out, col_tmp=152):
    srow = B["srow"]
    return kb.vchain([
        lambda e: e.tensor_scalar(srow[0:1, col_tmp:col_tmp + 1],
                                  srow[0:1, col_amax:col_amax + 1],
                                  1e-12, None, ALU.max),
        lambda e: e.reciprocal(srow[0:1, col_tmp + 1:col_tmp + 2],
                               srow[0:1, col_tmp:col_tmp + 1]),
        lambda e: e.tensor_scalar(srow[0:1, col_out:col_out + 1],
                                  srow[0:1, col_tmp + 1:col_tmp + 2],
                                  448.0, None, ALU.mult),
        lambda e: e.tensor_scalar(srow[0:1, col_out:col_out + 1],
                                  srow[0:1, col_out:col_out + 1],
                                  448.0, None, ALU.min),
    ])


def emit_l0ffn_prep(kb, B, w8_dram, bias_dram, amaxes_dram, amscr_dram, sclscr_dram, fused_cc=None):
    nc = kb.nc
    banks = B["banks"]
    vvr = B["vv"][:, 0:16384].bitcast(F32R)
    qt, kt, sq, srow, scl = B["qt"], B["kt"], B["sq"], B["srow"], B["scl"]

    for fc in range(NDC):
        v = kb.op("sync", kb.get_guard("vv"),
                  lambda e, fc=fc: e.dma_start(
                      out=wdst_ap(vvr, fc * 1024), in_=wcol_ap(w8_dram, fc)),
                  ("dS", 16))
    kb.mark("w8stage", "dS", v)

    last = None
    for fc in range(NDC):
        last = kb.op("vector", kb.need("w8stage") + kb.get_guard("am8") +
                     ([("vC", last)] if last else []),
                     lambda e, fc=fc: e.reduce_max(
                         B["am8"][:, fc:fc + 1], vvr[:, fc * 1024:(fc + 1) * 1024].bitcast(F32),
                         axis=AX.X, apply_absolute_value=True),
                     ("vC", 1))
    v = kb.vchain([
        lambda e: e.reduce_max(sq[:, 0:1], B["am8"][:, :], axis=AX.X,
                               apply_absolute_value=True),
        lambda e: e.engine_nop(),
    ], waits0=[("vC", last)])
    v = kb.op("gpsimd", [("vC", v)],
              lambda e: e.dma_start(out=amscr_dram, in_=sq[:, 0:1].bitcast(F32)),
              ("dM", 16))
    v = kb.op("sync", [("dM", v)],
              lambda e: e.dma_start(out=srow[0:1, 0:128],
                                    in_=amscr_dram.rearrange("p o -> o p")),
              ("dR", 16))
    v = kb.vchain([
        lambda e: e.reduce_max(srow[0:1, 129:130], srow[0:1, 0:128],
                               axis=AX.X, apply_absolute_value=False),
        lambda e: e.engine_nop(),
    ], waits0=[("dR", v)])
    v_w = v

    if amaxes_dram is not None:
        v2 = kb.op("sync", [],
                   lambda e: e.dma_start(out=srow[0:1, 144:152], in_=amaxes_dram),
                   ("dR2", 16))
        v2 = kb.vchain([
            lambda e: e.reduce_max(srow[0:1, 128:129], srow[0:1, 144:152],
                                   axis=AX.X, apply_absolute_value=False),
            lambda e: e.engine_nop(),
        ], waits0=[("dR2", v2)])
    else:
        cc_in_t, cc_out_t, amscr2_dram = fused_cc
        v2 = kb.vchain([
            lambda e: e.tensor_tensor(sq[:, 1:2], B["am2"][:, 0:1],
                                      B["am2"][:, 1:2], ALU.max),
            lambda e: e.engine_nop(),
        ], waits0=kb.need(("am2", 0), ("am2", 1)))
        v2 = kb.op("gpsimd", [("vC", v2)],
                   lambda e: e.dma_start(out=amscr2_dram, in_=sq[:, 1:2].bitcast(F32)),
                   ("dM", 16))
        v2 = kb.op("sync", [("dM", v2), ("vC", v_w)],
                   lambda e: e.dma_start(out=srow[0:1, 0:128],
                                         in_=amscr2_dram.rearrange("p o -> o p")),
                   ("dR", 16))
        v2 = kb.op("vector", [("dR", v2)],
                   lambda e: e.reduce_max(srow[0:1, 130:131], srow[0:1, 0:128],
                                          axis=AX.X, apply_absolute_value=False),
                   ("vC", 1))
        v2 = kb.op("gpsimd", [("vC", v2)],
                   lambda e: e.dma_start(out=cc_in_t.ap(), in_=srow[0:1, 130:138]),
                   ("dM", 16))
        v2 = kb.op("gpsimd", [("dM", v2)],
                   lambda e: e.collective_compute(
                       "AllReduce", ALU.max, replica_groups=[list(range(8))],
                       ins=[cc_in_t.ap()], outs=[cc_out_t.ap()]),
                   ("cc", 1))
        v2 = kb.op("sync", [("cc", v2)],
                   lambda e: e.dma_start(out=srow[0:1, 128:129],
                                         in_=cc_out_t.ap()[0:1, 0:1]),
                   ("dR", 16))
        v2 = kb.op("vector", [("dR", v2)], lambda e: e.engine_nop(), ("vC", 1))

    emit_scale_rows(kb, B, 129, 131, 152)
    emit_scale_rows(kb, B, 128, 132, 154)
    kb.vchain([
        lambda e: e.reciprocal(srow[0:1, 134:135], srow[0:1, 131:132]),
        lambda e: e.reciprocal(srow[0:1, 135:136], srow[0:1, 132:133]),
        lambda e: e.tensor_tensor(srow[0:1, 133:134], srow[0:1, 134:135],
                                  srow[0:1, 135:136], ALU.mult),
    ])
    v3 = kb.op("vector", [], lambda e: e.engine_nop(), ("vC", 1))
    v3 = kb.op("gpsimd", [("vC", v3)],
               lambda e: e.dma_start(out=sclscr_dram, in_=srow[0:1, 131:135]),
               ("dM", 16))
    v3 = kb.op("sync", [("dM", v3)],
               lambda e: e.dma_start(out=scl[:, 0:4],
                                     in_=sclscr_dram.broadcast_to([128, 4])),
               ("dR", 16))
    kb.mark("scl", "dR", v3)

    for fc in range(NDC):
        v = emit_fp8_quant(kb, B, vvr[:, fc * 1024:(fc + 1) * 1024],
                           qt[:, fc * 1024:(fc + 1) * 1024], 0, 1024,
                           extra=kb.get_guard("qt"))
    kb.mark("wq8", "vC", v)
    kb.set_guard("vv", "vC", v)

    v = kb.op("sync", [],
              lambda e: e.dma_start(out=B["biasb"][:, :], in_=bias_dram),
              ("dB", 16))
    kb.mark("biasb", "dB", v)


def emit_l0ffn(kb, B, b, skip_ln=False, ln_dbg=None):
    sx, kt, qt, ao = B["sx"], B["kt"], B["qt"], B["ao"]
    banks = B["banks"]
    util = [("pu", banks["pu"]), ("po1", banks["po1"])]

    def sxsl(dc, c0, nn):
        return sx[:, dc * S + c0: dc * S + c0 + nn]

    for dc in range(NDC):
        for hf in range(2):
            v = emit_fp8_quant(
                kb, B, sxsl(dc, hf * 1024, 1024),
                kt[:, dc * S + hf * 1024: dc * S + (hf + 1) * 1024], 1, 1024,
                extra=kb.get_guard("kt") + kb.need(("sx", dc)))
    kb.mark("xq8", "vC", v)

    for fc in range(NDC):
        for sc in range(NSC):
            j = fc * NSC + sc
            bname, bank = util[j % 2]
            tv = None
            for dc in range(NDC):
                waits = kb.need("xq8", "wq8") + (kb.get_guard(bname) if dc == 0 else [])
                tv = kb.op("tensor", waits,
                           lambda e, bank=bank, fc=fc, sc=sc, dc=dc: e.matmul(
                               bank[:, :],
                               qt[:, fc * 1024 + dc * 128: fc * 1024 + (dc + 1) * 128],
                               kt[:, dc * S + sc * 512: dc * S + (sc + 1) * 512],
                               start=(dc == 0), stop=(dc == NDC - 1)),
                           ("tP", 1) if dc == NDC - 1 else None)
            dst_b = ao[:, (j % 4) * 512: (j % 4 + 1) * 512]
            v = kb.op("vector", [("tP", tv)] + kb.need("scl", "biasb") + kb.get_guard(("aoq", j % 4)),
                      lambda e, bank=bank, dst_b=dst_b, fc=fc: e.tensor_scalar(
                          dst_b, bank[:, :], B["scl"][:, 2:3], B["biasb"][:, fc:fc + 1],
                          ALU.mult, ALU.add),
                      ("vC", 1))
            kb.set_guard(bname, "vC", v)
            dst = sxsl(fc, sc * 512, 512)
            v = kb.op("vector", [],
                      lambda e, dst=dst, dst_b=dst_b: e.tensor_tensor(
                          dst, dst_b, dst.bitcast(F32), ALU.add),
                      ("vC", 1))
            kb.set_guard(("aoq", j % 4), "vC", v)
            kb.mark(("sx", fc, "r", sc), "vC", v)
    if skip_ln:
        for dc in range(NDC):
            lv = max(kb.ready[("sx", dc, "r", qq)][1] for qq in range(NSC))
            kb.mark(("sx", dc), "vC", lv)
            kb.mark(("sx", dc, "ln"), "vC", lv)
    else:
        emit_ln(kb, B, rdeps=None, dbg=ln_dbg)


"""L1 FFN: blockwise-fp4 (E2M1, block 16) quant-dequant SwiGLU MLP.

out = fp4linear(silu(fp4linear(x,wg)) * fp4linear(x,wu), wd);
x_out = LN(x + out) in place on sx.

Quantized+transposed weights are spilled to HBM scratch once (prep) and
streamed back during the matmuls (they don't fit in SBUF).

Buffer roles in this phase:
  kt  = xq4T (quantized x, transposed back to [d,seq])     [128, 8*2048] bf16
  qt  = hqT tiles (cols 0:11264) + int scratch + wdq stream
  vv  = weight staging (prep, f32r view) / gate+up rhs stream slots (4x4096)
  cx  = natural quantized staging (x / h tiles)
  ao(f32 view) = src + value scratch ; wc(f32 view) = acc scratch
  at  = silu output tiles (bf16)
  sq  = blockmax / scale / rcp rows (f32r)
"""

MIDS = [0.25, 0.75, 1.25, 1.75, 2.5, 2.5, 3.5, 3.5, 5.0, 5.0, 5.0, 5.0]


def emit_fp4_quant(kb, B, srcf, srci, dst, n, c_i32, waits0=()):
    nb = n // 16
    sq = B["sq"]
    bm = sq[:, 0:nb]
    s_c = sq[:, 64:64 + nb]
    rcp = sq[:, 128:128 + nb]
    A = B["ao"][:, 2048:2048 + 2 * n].bitcast(F32)[:, 0:n]
    A_i = B["ao"][:, 2048:2048 + 2 * n].bitcast(I32)[:, 0:n]
    AC = B["wc"][:, 0:2 * n].bitcast(F32)[:, 0:n]
    T_f = B["at"][:, 0:2 * n].bitcast(F32)[:, 0:n]
    T_i = B["at"][:, 0:2 * n].bitcast(I32)[:, 0:n]
    g = kb.get_guard("fp4scr") + list(waits0)
    rcp_b = rcp.rearrange("p (b o) -> p b o", o=1).broadcast_to([128, nb, 16])
    kb.vchain([
        lambda e: e.reduce_max(bm, srcf.rearrange("p (b w) -> p b w", w=16),
                               axis=AX.X, apply_absolute_value=True),
        lambda e: e.tensor_scalar(s_c, bm, 1.0 / 6.0, None, ALU.mult),
        lambda e: e.tensor_scalar(s_c, s_c, 1e-12, None, ALU.max),
        lambda e: e.reciprocal(rcp, s_c),
        lambda e: e.tensor_scalar(c_i32, srci, 0x7FFFFFFF, None, ALU.bitwise_and),
        lambda e: e.tensor_tensor(A.rearrange("p (b w) -> p b w", w=16),
                                  c_i32.bitcast(F32).rearrange("p (b w) -> p b w", w=16),
                                  rcp_b, ALU.mult),
    ], waits0=g)
    kb.op("vector", [],
          lambda e: e.tensor_scalar(c_i32, A_i, 0x3FFFFF, None, ALU.bitwise_and), None)
    kb.op("vector", [],
          lambda e: e.tensor_scalar(c_i32, c_i32, 0x200000, None, ALU.is_gt), None)
    kb.op("vector", [],
          lambda e: e.tensor_scalar(T_i, A_i, ~0x3FFFFF, None, ALU.bitwise_and), None)
    kb.op("vector", [],
          lambda e: e.scalar_tensor_tensor(T_i, c_i32, 0x400000, T_i,
                                           ALU.mult, ALU.add), None)
    kb.op("vector", [],
          lambda e: e.tensor_scalar(T_i, T_i, ~0x3FFFFF, None, ALU.bitwise_and), None)
    kb.op("vector", [],
          lambda e: e.tensor_scalar(AC, A, 0.25, None, ALU.is_gt), None)
    kb.op("vector", [],
          lambda e: e.scalar_tensor_tensor(AC, A, 0.75, AC, ALU.is_gt, ALU.add), None)
    kb.op("vector", [],
          lambda e: e.tensor_scalar(AC, AC, 0.5, None, ALU.mult), None)
    kb.op("vector", [],
          lambda e: e.tensor_scalar(c_i32, A_i, 0x3F800000, None, ALU.is_lt), None)
    kb.op("vector", [],
          lambda e: e.select(A, c_i32, AC, T_f), None)
    sc_b = s_c.rearrange("p (b o) -> p b o", o=1).broadcast_to([128, nb, 16])
    kb.op("vector", [],
          lambda e: e.tensor_tensor(T_f.rearrange("p (b w) -> p b w", w=16),
                                    A.rearrange("p (b w) -> p b w", w=16),
                                    sc_b, ALU.mult), None)
    kb.op("vector", [],
          lambda e: e.tensor_scalar(c_i32, srci, -0x80000000, None, ALU.bitwise_and), None)
    kb.op("vector", [],
          lambda e: e.tensor_tensor(T_i, T_i, c_i32, ALU.bitwise_or), None)
    v = kb.op("vector", [], lambda e: e.tensor_copy(dst, T_f), ("vC", 1))
    kb.set_guard("fp4scr", "vC", v)
    return v





def emit_l1ffn_prep(kb, B, wg_dram, wu_dram, wd_dram, spills):
    nc = kb.nc
    vv, kt, qt, pvB = B["vv"], B["kt"], B["qt"], B["banks"]["pvB"]
    vvr = vv[:, 0:16640].bitcast(F32R)

    def quant_transpose_spill(src_slot_f32r, n, dst_stage, spill_ap, qdst, skey):
        nblk = n // 128
        off = 0
        lastq = None
        while off < n:
            m = min(1024, n - off)
            lastq = emit_fp4_quant(
                kb, B, src_slot_f32r[:, off:off + m].bitcast(F32),
                src_slot_f32r[:, off:off + m].bitcast(I32),
                qdst[:, off:off + m], m,
                B["qt"][:, 11264:11264 + 2 * m].bitcast(I32)[:, 0:m])
            off += m
        done = 0
        while done < nblk:
            cnt = min(8, nblk - done)
            tv = None
            for j in range(cnt):
                waits = ([("vC", lastq)] + kb.get_guard("pvB") + kb.need("consts")) if j == 0 else []
                tv = kb.op("tensor", waits,
                           lambda e, j=j, done=done, qdst=qdst: e.transpose(
                               pvB[:, j * 128:(j + 1) * 128],
                               qdst[:, (done + j) * 128:(done + j + 1) * 128],
                               B["ident_b"][:, :]),
                           ("tP", 1))
            v = kb.op("vector", [("tP", tv)] + kb.get_guard(skey),
                      lambda e, cnt=cnt, done=done: e.tensor_copy(
                          dst_stage[:, done * 128:(done + cnt) * 128],
                          pvB[:, 0:cnt * 128]),
                      ("vC", 1))
            kb.set_guard("pvB", "vC", v)
            done += cnt
        kb.set_guard("fp4scr", "tP", kb.now("tP"))
        sv = None
        for dst_ap, c0, c1 in spill_ap:
            sv = kb.op("gpsimd", [("vC", v)],
                       lambda e, dst_ap=dst_ap, c0=c0, c1=c1: e.dma_start(
                           out=dst_ap, in_=dst_stage[:, c0:c1]),
                       ("dW", 16))
        kb.set_guard(skey, "dW", sv)
        return sv

    sv = None
    for name, wdram in (("wg", wg_dram), ("wu", wu_dram)):
        for nt in range(22):
            slot = nt % 2
            lv = kb.op("sync", kb.get_guard(("vvs", slot)),
                       lambda e, wdram=wdram, nt=nt, slot=slot: e.dma_start(
                           out=vvr[:, slot * 1024:(slot + 1) * 1024], in_=wdram[nt]),
                       ("dS%d" % slot, 16))
            src = vvr[:, slot * 1024:(slot + 1) * 1024]
            stage = kt[:, 2048 + slot * 1024: 2048 + (slot + 1) * 1024]
            qdst = kt[:, slot * 1024:(slot + 1) * 1024]
            kb.set_guard("fp4scr", "dS%d" % slot, lv)
            sv = quant_transpose_spill(
                src, 1024, stage,
                [(spills[name][j, :, nt * 128:(nt + 1) * 128], j * 128, (j + 1) * 128)
                 for j in range(8)],
                qdst, ("stg", slot))
            kb.set_guard(("vvs", slot), "vC", kb.now("vC"))
    kb.set_guard("fp4scr", "dW", kb.now("dW"))
    kb.set_guard(("vvs2", 0), "vC", kb.now("vC"))
    kb.set_guard(("vvs2", 1), "vC", kb.now("vC"))
    for dc in range(8):
        slot = dc % 2
        lv = kb.op("sync", kb.get_guard(("vvs2", slot)),
                   lambda e, dc=dc, slot=slot: e.dma_start(
                       out=vvr[:, slot * 2816:(slot + 1) * 2816], in_=wd_dram[dc]),
                   ("dS%d" % slot, 16))
        src = vvr[:, slot * 2816:(slot + 1) * 2816]
        stage = kt[:, 8192 + slot * 2816: 8192 + (slot + 1) * 2816]
        qdst = kt[:, slot * 2816 + 0: slot * 2816 + 2816]
        kb.set_guard("fp4scr", "dS%d" % slot, lv)
        sv = quant_transpose_spill(src, 2816, stage,
                                   [(spills["wd"][dc], 0, 2816)], qdst,
                                   ("stg2", slot))
        kb.set_guard(("vvs2", slot), "vC", kb.now("vC"))
    kb.mark("spilled", "dW", kb.now("dW"))
    tnow = kb.now("vC")
    for r in ("vv", "kt", "qt"):
        kb.set_guard(r, "vC", tnow)


def emit_l1ffn_prep_sharded(kb, B, wsh_dram, ccg_in, ccg_out, spills):
    nc = kb.nc
    vv, kt, pvB = B["vv"], B["kt"], B["banks"]["pvB"]
    vvr = vv[:, 0:16640].bitcast(F32R)

    sv = None
    for slot in range(7):
        n = 2816 if slot == 6 else 1024
        dslot = slot % 2
        lv = kb.op("sync", kb.get_guard(("vvs", dslot)),
                   lambda e, slot=slot, dslot=dslot, n=n: e.dma_start(
                       out=vvr[:, dslot * 2816: dslot * 2816 + n],
                       in_=wsh_dram[slot][:, 0:n]),
                   ("dS%d" % dslot, 16))
        src = vvr[:, dslot * 2816: dslot * 2816 + n]
        stage = kt[:, 8192 + dslot * 2816: 8192 + dslot * 2816 + n]
        qdst = kt[:, dslot * 2816: dslot * 2816 + n]
        kb.set_guard("fp4scr", "dS%d" % dslot, lv)
        nblk = n // 128
        off = 0
        lastq = None
        while off < n:
            m = min(1024, n - off)
            lastq = emit_fp4_quant(
                kb, B, src[:, off:off + m].bitcast(F32),
                src[:, off:off + m].bitcast(I32),
                qdst[:, off:off + m], m,
                B["qt"][:, 11264:11264 + 2 * m].bitcast(I32)[:, 0:m])
            off += m
        done = 0
        while done < nblk:
            cnt = min(8, nblk - done)
            tv = None
            for j in range(cnt):
                waits = ([("vC", lastq)] + kb.get_guard("pvB") + kb.need("consts")) if j == 0 else []
                tv = kb.op("tensor", waits,
                           lambda e, j=j, done=done, qdst=qdst: e.transpose(
                               pvB[:, j * 128:(j + 1) * 128],
                               qdst[:, (done + j) * 128:(done + j + 1) * 128],
                               B["ident_b"][:, :]),
                           ("tP", 1))
            v = kb.op("vector", [("tP", tv)] + kb.get_guard(("stg", dslot)),
                      lambda e, cnt=cnt, done=done, stage=stage: e.tensor_copy(
                          stage[:, done * 128:(done + cnt) * 128],
                          pvB[:, 0:cnt * 128]),
                      ("vC", 1))
            kb.set_guard("pvB", "vC", v)
            done += cnt
        kb.set_guard("fp4scr", "tP", kb.now("tP"))
        sv = kb.op("gpsimd", [("vC", v)],
                   lambda e, slot=slot, stage=stage, n=n: e.dma_start(
                       out=ccg_in[slot][:, 0:n], in_=stage[:, 0:n]),
                   ("dW", 16))
        kb.set_guard(("stg", dslot), "dW", sv)
        kb.set_guard(("vvs", dslot), "vC", kb.now("vC"))
    cv = kb.op("gpsimd", [("dW", kb.now("dW"))],
               lambda e: e.collective_compute(
                   "AllGather", ALU.bypass, replica_groups=[list(range(8))],
                   ins=[ccg_in], outs=[ccg_out]),
               ("cc", 1))
    def runs(units):
        out = []
        cur = None
        for u in units:
            gi = (u // 6) * 7 + u % 6
            if cur and gi == cur[0] + cur[2]:
                cur[2] += 1
            else:
                cur = [gi, u, 1]
                out.append(cur)
        return out
    rv = None
    for mat, base in (("wg", 0), ("wu", 22)):
        for kc in range(NDC):
            for gi, u0, cnt in runs(range(base, base + 22)):
                nt0 = u0 - base
                rv = kb.op("sync", [("cc", cv)],
                           lambda e, mat=mat, kc=kc, gi=gi, nt0=nt0, cnt=cnt: e.dma_start(
                               out=spills[mat][kc][:, nt0 * 128:(nt0 + cnt) * 128]
                                   .rearrange("p (t c) -> p t c", c=128),
                               in_=ccg_out[gi:gi + cnt, :, kc * 128:(kc + 1) * 128]
                                   .transpose([1, 0, 2])),
                           ("dW", 16))
    for dc in range(NDC):
        rv = kb.op("sync", [("cc", cv)],
                   lambda e, dc=dc: e.dma_start(
                       out=spills["wd"][dc], in_=ccg_out[dc * 7 + 6]),
                   ("dW", 16))
    kb.mark("spilled", "dW", rv)
    tnow = kb.now("vC")
    for r in ("vv", "kt", "qt"):
        kb.set_guard(r, "vC", tnow)


def emit_l1ffn(kb, B, spills, b, dbg=None, sc_list=None, skip_ln=False):
    nc = kb.nc
    sx, kt, qt, vv, cx, at, ao = B["sx"], B["kt"], B["qt"], B["vv"], B["cx"], B["at"], B["ao"]
    for r in ("cx", "kt", "qt", "vv", "at"):
        for (sn, gv) in kb.get_guard(r):
            kb.set_guard("fp4scr", sn, gv)
    for dcq in range(NDC):
        for par in (0, 1):
            for (sn, gv) in kb.get_guard(("ao", dcq, par)):
                kb.set_guard("xnat", sn, gv)
                kb.set_guard("fp4scr", sn, gv)
    for (sn, gv) in kb.get_guard("qt"):
        for ntc_ in range(6):
            for st_ in range(4):
                kb.set_guard(("hqT", ntc_, st_), sn, gv)
        kb.set_guard(("wdqs", 0), sn, gv)
        kb.set_guard(("wdqs", 1), sn, gv)
    banks = B["banks"]
    pvB = banks["pvB"]
    pu, po1 = banks["pu"], banks["po1"]
    gb = [banks["ps0a"], banks["ps0b"]]
    gbn = ["ps0a", "ps0b"]
    ub = [banks["ps1a"], banks["ps1b"]]
    ubn = ["ps1a", "ps1b"]

    def sxsl(dc, c0, nn):
        return sx[:, dc * S + c0: dc * S + c0 + nn]

    for st in range(NST):
        for half in range(2):
            bank = pu if half == 0 else po1
            bname = "pu" if half == 0 else "po1"
            bview = bank[:, :].bitcast(F32R)
            tv = None
            for j in range(4):
                dc = half * 4 + j
                waits = (kb.get_guard(bname) + kb.need(("sx", dc), "consts")) if j == 0 else kb.need(("sx", dc))
                tv = kb.op("tensor", waits,
                           lambda e, dc=dc, st=st, j=j, bview=bview: e.transpose(
                               bview[:, j * 128:(j + 1) * 128],
                               sx[:, dc * S + st * 128: dc * S + (st + 1) * 128],
                               B["ident_r"][:, :]),
                           ("tP", 1))
            v = kb.op("vector", [("tP", tv)] + (kb.get_guard("xnat") if half == 0 else []),
                      lambda e, half=half, bview=bview: e.tensor_copy(
                          ao[:, 0:2048].bitcast(F32)[:, half * 512:(half + 1) * 512], bview),
                      ("vC", 1))
            kb.set_guard(bname, "vC", v)
        xnat_f = ao[:, 0:2048].bitcast(F32)
        xnat_i = ao[:, 0:2048].bitcast(I32)
        qv = emit_fp4_quant(kb, B, xnat_f, xnat_i, cx[:, 0:1024], 1024,
                            qt[:, 11264:13312].bitcast(I32)[:, 0:1024])
        kb.set_guard("xnat", "vC", qv)
        tv = None
        for j in range(8):
            waits = ([("vC", qv)] + kb.get_guard("pvB")) if j == 0 else []
            tv = kb.op("tensor", waits,
                       lambda e, j=j: e.transpose(
                           pvB[:, j * 128:(j + 1) * 128],
                           cx[:, j * 128:(j + 1) * 128], B["ident_b"][:, :]),
                       ("tP", 1))
        dst = kt[:, :].rearrange("p (dc s) -> p dc s", s=S)[:, :, st * 128:(st + 1) * 128]
        v = kb.op("vector", [("tP", tv)] + (kb.get_guard("kt") if st == 0 else []),
                  lambda e, dst=dst: e.tensor_copy(dst, pvB[:, 0:1024].rearrange(
                      "p (dc s) -> p dc s", s=128)),
                  ("vC", 1))
        kb.set_guard("pvB", "vC", v)
        kb.set_guard("cxq", "vC", v)
    kb.mark("xq4", "vC", kb.now("vC"))

    NTC = [512, 512, 512, 512, 512, 256]
    for sc in (sc_list if sc_list is not None else range(NSC)):
        hq_done = None
        for ntc in range(6):
            nw = NTC[ntc]
            nnt = nw // 128
            gslot = ntc % 2
            uslot = 2 + ntc % 2
            for name, slot in (("wg", gslot), ("wu", uslot)):
                lv = None
                for kc in range(NDC):
                    w0 = (kb.get_guard(("vvq", slot)) + kb.get_guard("vv")
                          + kb.need("spilled")) if kc == 0 else []
                    lv = kb.op("sync", w0,
                               lambda e, name=name, slot=slot, ntc=ntc,
                                      kc=kc, nw=nw: e.dma_start(
                                   out=vv[:, slot * 4096 + kc * nw: slot * 4096 + (kc + 1) * nw],
                                   in_=spills[name][kc, :, ntc * 512: ntc * 512 + nw]),
                               ("dV%d" % slot, 16))
                kb.mark(("vvq", slot), "dV%d" % slot, lv)
            for st in range(4):
                stg = sc * 4 + st
                gbank, gname = gb[st % 2], gbn[st % 2]
                ubank, uname = ub[st % 2], ubn[st % 2]
                tg = tu = None
                for kc in range(NDC):
                    waits = kb.need("xq4", ("vvq", gslot)) + (kb.get_guard(gname) if kc == 0 else [])
                    tg = kb.op("tensor", waits,
                               lambda e, kc=kc, stg=stg, gslot=gslot, nw=nw, gbank=gbank: e.matmul(
                                   gbank[:, 0:nw],
                                   kt[:, kc * S + stg * 128: kc * S + (stg + 1) * 128],
                                   vv[:, gslot * 4096 + kc * nw: gslot * 4096 + (kc + 1) * nw],
                                   start=(kc == 0), stop=(kc == NDC - 1)),
                               ("tP", 1) if kc == NDC - 1 else None)
                for kc in range(NDC):
                    waits = kb.need(("vvq", uslot)) + (kb.get_guard(uname) if kc == 0 else [])
                    tu = kb.op("tensor", waits,
                               lambda e, kc=kc, stg=stg, uslot=uslot, nw=nw, ubank=ubank: e.matmul(
                                   ubank[:, 0:nw],
                                   kt[:, kc * S + stg * 128: kc * S + (stg + 1) * 128],
                                   vv[:, uslot * 4096 + kc * nw: uslot * 4096 + (kc + 1) * nw],
                                   start=(kc == 0), stop=(kc == NDC - 1)),
                               ("tP", 1) if kc == NDC - 1 else None)
                atsl = ao[:, 2048:2048 + 2 * nw].bitcast(F32)[:, 0:nw]
                hv = kb.op("scalar", [("tP", tg)] + kb.get_guard("xnat"),
                           lambda e, gbank=gbank, atsl=atsl, nw=nw: e.activation(
                               atsl, gbank[:, 0:nw], AF.Silu),
                           ("sC", 1))
                kb.set_guard(gname, "sC", hv)
                srcsl = ao[:, 0:2048].bitcast(F32)[:, 0:nw]
                hv2 = kb.op("vector", [("sC", hv), ("tP", tu)] + kb.get_guard("xnat"),
                            lambda e, atsl=atsl, srcsl=srcsl, ubank=ubank, nw=nw: e.tensor_tensor(
                                srcsl, atsl, ubank[:, 0:nw], ALU.mult),
                            ("vC", 1))
                kb.set_guard(uname, "vC", hv2)
                kb.set_guard("xnat", "vC", hv2)
                qv = emit_fp4_quant(kb, B, srcsl, ao[:, 0:2048].bitcast(I32)[:, 0:nw],
                                    cx[:, 0:nw], nw,
                                    qt[:, 11264:13312].bitcast(I32)[:, 0:nw],
                                    waits0=[("vC", hv2)])
                kb.set_guard("xnat", "vC", qv)
                tv = None
                for j in range(nnt):
                    waits = ([("vC", qv)] + kb.get_guard("pvB")) if j == 0 else []
                    tv = kb.op("tensor", waits,
                               lambda e, j=j: e.transpose(
                                   pvB[:, j * 128:(j + 1) * 128],
                                   cx[:, j * 128:(j + 1) * 128], B["ident_b"][:, :]),
                               ("tP", 1))
                dst = qt[:, :].rearrange("p (kc s) -> p kc s", s=512)[
                    :, ntc * 4: ntc * 4 + nnt, st * 128:(st + 1) * 128]
                hq_done = kb.op("vector", [("tP", tv)] + kb.get_guard(("hqT", ntc, st)),
                                lambda e, dst=dst, nnt=nnt: e.tensor_copy(
                                    dst, pvB[:, 0:nnt * 128].rearrange(
                                        "p (kc s) -> p kc s", s=128)),
                                ("vC", 1))
                kb.set_guard("pvB", "vC", hq_done)
                kb.mark(("hqT", ntc, st), "vC", hq_done)
                if st == 3:
                    kb.set_guard(("vvq", gslot), "tP", kb.now("tP"))
                    kb.set_guard(("vvq", uslot), "tP", kb.now("tP"))
        if dbg is not None and sc == 0:
            dbg(kb, "hq0D", qt[:, 0:11264],
                kb.need(*[("hqT", n_, s_) for n_ in range(6) for s_ in range(4)]),
                [("hqT", n_, s_) for n_ in range(6) for s_ in range(4)])
        util = [("pu", pu), ("po1", po1)]
        for fc in range(NDC):
            bname, bank = util[fc % 2]
            tv = None
            for half in range(2):
                k0, k1 = (0, 11) if half == 0 else (11, 22)
                lv = kb.op("sync", kb.get_guard(("wdqs", half)) + kb.need("spilled"),
                           lambda e, fc=fc, k0=k0, k1=k1: e.dma_start(
                               out=qt[:, 13312 + k0 * 128: 13312 + k1 * 128],
                               in_=spills["wd"][fc][:, k0 * 128: k1 * 128]),
                           ("dD", 16))
                for kc in range(k0, k1):
                    waits = [("dD", lv)] + kb.need(*[("hqT", kc // 4, stq) for stq in range(4)])
                    if kc == 0:
                        waits = waits + kb.get_guard(bname)
                    tv = kb.op("tensor", waits,
                               lambda e, kc=kc, bank=bank: e.matmul(
                                   bank[:, :],
                                   qt[:, 13312 + kc * 128: 13312 + (kc + 1) * 128],
                                   qt[:, kc * 512:(kc + 1) * 512],
                                   start=(kc == 0), stop=(kc == 21)),
                               ("tP", 1) if kc == k1 - 1 else None)
                kb.set_guard(("wdqs", half), "tP", tv)
            dst = sxsl(fc, sc * 512, 512)
            v = kb.op("vector", [("tP", tv)],
                      lambda e, dst=dst, bank=bank: e.tensor_tensor(
                          dst, bank[:, :], dst.bitcast(F32), ALU.add),
                      ("vC", 1))
            kb.set_guard(bname, "vC", v)
            kb.mark(("sx", fc, "r", sc), "vC", v)
        if dbg is not None and sc == 0:
            dbg(kb, "r0D", sx[:, :],
                kb.need(*[("sx", f_, "r", 0) for f_ in range(NDC)]),
                [("sx", f_) for f_ in range(NDC)])
        for ntc in range(6):
            for st in range(4):
                kb.set_guard(("hqT", ntc, st), "tP", kb.now("tP"))
    if skip_ln:
        for dc in range(NDC):
            vals = [kb.ready[("sx", dc, "r", qq)][1] for qq in (sc_list or range(NSC))
                    if ("sx", dc, "r", qq) in kb.ready]
            kb.mark(("sx", dc), "vC", max(vals))
            kb.mark(("sx", dc, "ln"), "vC", max(vals))
    else:
        vnow = kb.now("vC")
        kb.set_guard(("sq", 0), "vC", vnow)
        kb.set_guard(("sq", 1), "vC", vnow)
        emit_ln(kb, B, rdeps=None)


"""Standalone single-sublayer launch builders (for incremental validation,
and for the UNFUSED multi-launch kernel checkpoint)."""
from contextlib import ExitStack
import numpy as np
import concourse.bass as bass


def prep_l0attn_inputs(X, Wq, Wk, Wv, Wo, We, core, n_cores=8):
    bpc = X.shape[0] // n_cores
    xs = X[core * bpc:(core + 1) * bpc]
    xt = np.ascontiguousarray(xs.transpose(0, 2, 1))
    xt = np.ascontiguousarray(xt.reshape(bpc, NDC, 128, S))
    return {
        "xt": xt,
        "wq": chunk_T(Wq.T), "wk": chunk_T(Wk.T),
        "wv": chunk_T(Wv.T), "wo": chunk_T(Wo.T), "we": chunk_T(We.T),
    }


def emit_store_sx(kb, B, xo_dram, b):
    for dc in range(NDC):
        v = kb.op("gpsimd", kb.need(("sx", dc)),
                  lambda e, dc=dc: e.dma_start(
                      out=xo_dram[dc], in_=B["sx"][:, dc * S:(dc + 1) * S].bitcast(F32)),
                  ("dG", 16))
    for dc in range(NDC):
        kb.set_guard(("sx", dc), "dG", v)
    return v


def emit_amax_local(kb, B, amscr_dram):
    sq = B["sq"]
    v = kb.vchain([
        lambda e: e.tensor_tensor(sq[:, 0:1], B["am2"][:, 0:1], B["am2"][:, 1:2], ALU.max),
        lambda e: e.engine_nop(),
    ], waits0=kb.need(("am2", 0), ("am2", 1)))
    v = kb.op("gpsimd", [("vC", v)],
              lambda e: e.dma_start(out=amscr_dram, in_=sq[:, 0:1].bitcast(F32)),
              ("dM", 16))
    v = kb.op("sync", [("dM", v)],
              lambda e: e.dma_start(out=B["srow"][0:1, 0:128],
                                    in_=amscr_dram.rearrange("p o -> o p")),
              ("dR", 16))
    v = kb.vchain([
        lambda e: e.reduce_max(B["srow"][0:1, 128:129], B["srow"][0:1, 0:128],
                               axis=AX.X, apply_absolute_value=False),
        lambda e: e.engine_nop(),
    ], waits0=[("dR", v)])
    kb.mark("amax_local", "vC", v)
    return v


def emit_fence(kb):
    sems = [(n, kb.cnt[n]) for n in list(kb.cnt) if kb.cnt[n] > 0]
    for eng in ("tensor", "vector", "scalar", "sync", "gpsimd"):
        kb.op(eng, sems, lambda e: None, None)


def emit_spill_sx_r(kb, B, dst_dram):
    v = None
    for dc in range(NDC):
        v = kb.op("gpsimd", kb.need(("sx", dc)) + kb.get_guard(("sxsp", dc)),
                  lambda e, dc=dc: e.dma_start(
                      out=dst_dram[dc], in_=B["sx"][:, dc * S:(dc + 1) * S]),
                  ("dG", 16))
    for dc in range(NDC):
        kb.set_guard(("sx", dc), "dG", v)
    return v


def emit_load_sx_r(kb, B, src_dram):
    v = None
    for dc in range(NDC):
        v = kb.op("sync", kb.get_guard(("sx", dc)),
                  lambda e, dc=dc: e.dma_start(
                      out=B["sx"][:, dc * S:(dc + 1) * S], in_=src_dram[dc]),
                  ("dX", 16))
    for dc in range(NDC):
        kb.mark(("sx", dc), "dX", v)
    return v


def emit_tap(kb, nc, B, name, src_ap, waits, guard_res=None):
    shp = list(src_ap.shape)
    t = nc.dram_tensor(name, shp, src_ap.dtype, kind="ExternalOutput").ap()
    v = kb.op("gpsimd", waits,
              lambda e, t=t, src_ap=src_ap: e.dma_start(out=t, in_=src_ap),
              ("dG", 16))
    if guard_res:
        kb.set_guard(guard_res, "dG", v)
    return v


def build_l0attn_standalone(bpc=2, debug=False, skip_ln=False):
    nc = bass.Bass("TRN2", target_bir_lowering=False, dynamic_dma_scratch_size=16384,
                   detect_race_conditions=False)
    xt = nc.dram_tensor("xt", [bpc, NDC, 128, S], F32R, kind="ExternalInput").ap()
    w = {}
    for n in ("wq", "wk", "wv", "we"):
        shp = [1, NDC, 128, 128] if n == "we" else [NDC, NDC, 128, 128]
        w[n] = nc.dram_tensor(n, shp, F32R, kind="ExternalInput").ap()
    w["wo"] = nc.dram_tensor("wo", [NDC, NDC, 128, 128], F32R, kind="ExternalInput").ap()
    xo = nc.dram_tensor("xo", [bpc, NDC, 128, S], F32, kind="ExternalOutput").ap()
    am = nc.dram_tensor("am", [1, 1], F32, kind="ExternalOutput").ap()
    amscr = nc.dram_tensor("amscr", [128, 1], F32).ap()

    es = ExitStack()
    kb = KB(nc)
    B = alloc_bufs(kb, es)
    emit_setup(kb, B)
    for b in range(bpc):
        emit_l0attn(kb, B, [xt[b, dc] for dc in range(NDC)], w, b,
                    dbg=(emit_tap if (debug and b == 0) else None), nc_=nc,
                    skip_ln=skip_ln)
        emit_store_sx(kb, B, [xo[b, dc] for dc in range(NDC)], b)
    emit_amax_local(kb, B, amscr)
    v = kb.op("gpsimd", kb.need("amax_local"),
              lambda e: e.dma_start(out=am, in_=B["srow"][0:1, 128:129]),
              ("dG", 16))
    kb.op("gpsimd", [(n, kb.now(n)) for n in
                     ("dG", "dX", "dA0", "dA1", "dS", "dR", "dR2", "dM", "dB") if kb.cnt.get(n)],
          lambda e: e.engine_nop(), None)
    kb.finalize()
    es.close()
    return nc


def prep_l0ffn_inputs(X1, weight, bias, amaxes, core, n_cores=8):
    bpc = X1.shape[0] // n_cores
    xt = X1[core * bpc:(core + 1) * bpc]
    return {
        "xt": np.ascontiguousarray(xt),
        "w8": chunk_T(np.ascontiguousarray(weight.T)),
        "bias": np.ascontiguousarray(bias.reshape(8, 128).T),
        "amaxes": np.ascontiguousarray(amaxes.reshape(1, 8).astype(np.float32)),
    }


def build_l0ffn_standalone(bpc=2, skip_ln=False, debug=False):
    nc = bass.Bass("TRN2", target_bir_lowering=False, dynamic_dma_scratch_size=16384,
                   detect_race_conditions=False)
    xt = nc.dram_tensor("xt", [bpc, NDC, 128, S], F32R, kind="ExternalInput").ap()
    w8 = nc.dram_tensor("w8", [NDC, NDC, 128, 128], F32R, kind="ExternalInput").ap()
    bias = nc.dram_tensor("bias", [128, 8], F32, kind="ExternalInput").ap()
    amaxes = nc.dram_tensor("amaxes", [1, 8], F32, kind="ExternalInput").ap()
    xo = nc.dram_tensor("xo", [bpc, NDC, 128, S], F32, kind="ExternalOutput").ap()
    amscr = nc.dram_tensor("amscr", [128, 1], F32).ap()
    sclscr = nc.dram_tensor("sclscr", [1, 4], F32).ap()

    es = ExitStack()
    kb = KB(nc)
    B = alloc_bufs(kb, es)
    emit_setup(kb, B)
    emit_l0ffn_prep(kb, B, w8, bias, amaxes, amscr, sclscr)
    for b in range(bpc):
        for dc in range(NDC):
            v = kb.op("sync", kb.get_guard(("sx", dc)),
                      lambda e, b=b, dc=dc: e.dma_start(
                          out=B["sx"][:, dc * S:(dc + 1) * S], in_=xt[b, dc]),
                      ("dX", 16))
        for dc in range(NDC):
            kb.mark(("sx", dc), "dX", v)
        rowtap = None
        if debug and b == 0:
            rt_t = nc.dram_tensor("rowsD", [1, 1536], F32, kind="ExternalOutput").ap()
            def rowtap(kb_, mv, rt_t=rt_t):
                vv2 = kb_.op("gpsimd", [("vC", mv)],
                       lambda e: e.dma_start(out=rt_t[0:1, 0:1536],
                                             in_=B["rows"][0:1, :].bitcast(F32)),
                       ("dG", 16))
                kb_.set_guard("rows", "dG", vv2)
        emit_l0ffn(kb, B, b, skip_ln=skip_ln, ln_dbg=rowtap)
        if debug and b == 0:
            emit_tap(kb, nc, B, "ktD", B["kt"][:, :], kb.need("xq8"), guard_res="kt")
            emit_tap(kb, nc, B, "qtD", B["qt"][:, 0:8192], kb.need("wq8"), guard_res="qt")
            emit_tap(kb, nc, B, "sclD", B["scl"][:, 0:3], kb.need("scl"))
            emit_tap(kb, nc, B, "srowD", B["srow"][0:1, 128:130], kb.need("scl"))
        emit_store_sx(kb, B, [xo[b, dc] for dc in range(NDC)], b)
    kb.op("gpsimd", [(n, kb.now(n)) for n in
                     ("dG", "dX", "dA0", "dA1", "dS", "dR", "dR2", "dM", "dB") if kb.cnt.get(n)],
          lambda e: e.engine_nop(), None)
    kb.finalize()
    es.close()
    return nc


def prep_l1attn_inputs(X1, Wq, Wk, Wv, Wo, core, n_cores=8):
    bpc = X1.shape[0] // n_cores
    return {
        "xt": np.ascontiguousarray(X1[core * bpc:(core + 1) * bpc]),
        "wq": chunk_T(Wq), "wk": chunk_T(Wk), "wv": chunk_T(Wv), "wo": chunk_T(Wo),
    }


def build_l1attn_standalone(bpc=2, skip_ln=False):
    nc = bass.Bass("TRN2", target_bir_lowering=False, dynamic_dma_scratch_size=16384,
                   detect_race_conditions=False)
    xt = nc.dram_tensor("xt", [bpc, NDC, 128, S], F32R, kind="ExternalInput").ap()
    w = {}
    for n in ("wq", "wk", "wv", "wo"):
        w[n] = nc.dram_tensor(n, [NDC, NDC, 128, 128], F32R, kind="ExternalInput").ap()
    xo = nc.dram_tensor("xo", [bpc, NDC, 128, S], F32, kind="ExternalOutput").ap()

    es = ExitStack()
    kb = KB(nc)
    B = alloc_bufs(kb, es)
    emit_setup(kb, B)
    for b in range(bpc):
        emit_l0attn(kb, B, [xt[b, dc] for dc in range(NDC)], w, b,
                    skip_ln=skip_ln, layer=1)
        emit_store_sx(kb, B, [xo[b, dc] for dc in range(NDC)], b)
    kb.op("gpsimd", [(n, kb.now(n)) for n in
                     ("dG", "dX", "dA0", "dA1", "dS", "dR", "dR2", "dM", "dB") if kb.cnt.get(n)],
          lambda e: e.engine_nop(), None)
    kb.finalize()
    es.close()
    return nc


def build_full_fused(bpc=2, debug=False):
    nc = bass.Bass("TRN2", target_bir_lowering=False, dynamic_dma_scratch_size=16384,
                   detect_race_conditions=False)
    xt = nc.dram_tensor("xt", [bpc, NDC, 128, S], F32R, kind="ExternalInput").ap()
    w0 = {}
    for n in ("wq", "wk", "wv", "wo", "we"):
        shp = [1, NDC, 128, 128] if n == "we" else [NDC, NDC, 128, 128]
        w0[n] = nc.dram_tensor("w0" + n, shp, F32R, kind="ExternalInput").ap()
    w8 = nc.dram_tensor("w8", [NDC, NDC, 128, 128], F32R, kind="ExternalInput").ap()
    bias = nc.dram_tensor("bias", [128, 8], F32, kind="ExternalInput").ap()
    w1 = {n: nc.dram_tensor("w1" + n, [NDC, NDC, 128, 128], F32R,
                            kind="ExternalInput").ap()
          for n in ("wq", "wk", "wv", "wo")}
    wsh = nc.dram_tensor("wsh", [7, 128, 2816], F32R, kind="ExternalInput").ap()
    ccg_in = nc.dram_tensor("ccg_in", [7, 128, 2816], BF16).ap()
    ccg_out = nc.dram_tensor("ccg_out", [56, 128, 2816], BF16).ap()
    xo = nc.dram_tensor("xo", [bpc, NDC, 128, S], F32, kind="ExternalOutput").ap()
    dbgk = "ExternalOutput" if debug else "Internal"
    x1scr = nc.dram_tensor("x1scr", [NDC, 128, S], F32R, kind=dbgk).ap()
    x2scr = nc.dram_tensor("x2scr", [NDC, 128, S], F32R, kind=dbgk).ap()
    if debug:
        xd2b0 = nc.dram_tensor("xd2b0", [NDC, 128, S], F32R, kind="ExternalOutput").ap()
        xd3b0 = nc.dram_tensor("xd3b0", [NDC, 128, S], F32R, kind="ExternalOutput").ap()
        xd3b1 = nc.dram_tensor("xd3b1", [NDC, 128, S], F32R, kind="ExternalOutput").ap()
    amscr = nc.dram_tensor("amscr", [128, 1], F32).ap()
    amscr2 = nc.dram_tensor("amscr2", [128, 1], F32).ap()
    sclscr = nc.dram_tensor("sclscr", [1, 4], F32).ap()
    cc_in_t = nc.dram_tensor("cc_in", [1, 8], F32)
    cc_out_t = nc.dram_tensor("cc_out", [1, 8], F32)
    spills = {n: nc.dram_tensor("sp_" + n, [8, 128, 2816], BF16).ap()
              for n in ("wg", "wu", "wd")}

    es = ExitStack()
    kb = KB(nc)
    B = alloc_bufs(kb, es)
    emit_setup(kb, B)
    emit_l1ffn_prep_sharded(kb, B, wsh, ccg_in, ccg_out, spills)
    emit_fence(kb)
    emit_l0attn(kb, B, [xt[0, dc] for dc in range(NDC)], w0, 0)
    emit_spill_sx_r(kb, B, x1scr)
    emit_l0attn(kb, B, [xt[1, dc] for dc in range(NDC)], w0, 1)
    emit_fence(kb)
    emit_l0ffn_prep(kb, B, w8, bias, None, amscr, sclscr,
                    fused_cc=(cc_in_t, cc_out_t, amscr2))
    emit_fence(kb)
    emit_l0ffn(kb, B, 1)
    emit_fence(kb)
    emit_spill_sx_r(kb, B, x2scr)
    emit_fence(kb)
    emit_load_sx_r(kb, B, x1scr)
    emit_l0ffn(kb, B, 0)
    emit_fence(kb)
    if debug:
        emit_spill_sx_r(kb, B, xd2b0)
        emit_fence(kb)
    emit_l0attn(kb, B, None, w1, 0, layer=1)
    emit_fence(kb)
    if debug:
        emit_spill_sx_r(kb, B, xd3b0)
        emit_fence(kb)
    emit_l1ffn(kb, B, spills, 0)
    emit_fence(kb)
    emit_store_sx(kb, B, [xo[0, dc] for dc in range(NDC)], 0)
    emit_fence(kb)
    emit_load_sx_r(kb, B, x2scr)
    if debug:
        emit_tap(kb, nc, B, "sxldD", B["sx"][:, :].bitcast(F32),
                 kb.need(*[("sx", dc) for dc in range(NDC)]))
        emit_fence(kb)
    emit_l0attn(kb, B, None, w1, 1, layer=1,
                dbg=(emit_tap if debug else None), nc_=nc)
    emit_fence(kb)
    if debug:
        emit_spill_sx_r(kb, B, xd3b1)
        emit_fence(kb)
    emit_l1ffn(kb, B, spills, 1)
    emit_fence(kb)
    emit_store_sx(kb, B, [xo[1, dc] for dc in range(NDC)], 1)
    emit_fence(kb)
    kb.finalize()
    es.close()
    return nc


def prep_l1ffn_inputs(X1, wg, wu, wd, core, n_cores=8):
    bpc = X1.shape[0] // n_cores
    return {
        "xt": np.ascontiguousarray(X1[core * bpc:(core + 1) * bpc]),
        "wg": np.ascontiguousarray(wg.reshape(22, 128, 1024)),
        "wu": np.ascontiguousarray(wu.reshape(22, 128, 1024)),
        "wd": np.ascontiguousarray(wd.reshape(8, 128, 2816)),
    }


def build_l1ffn_standalone(bpc=2, skip_ln=False, debug_spills=False, debug_taps=False, sc_list=None):
    nc = bass.Bass("TRN2", target_bir_lowering=False, dynamic_dma_scratch_size=16384,
                   detect_race_conditions=False)
    xt = nc.dram_tensor("xt", [bpc, NDC, 128, S], F32R, kind="ExternalInput").ap()
    wg = nc.dram_tensor("wg", [22, 128, 1024], F32R, kind="ExternalInput").ap()
    wu = nc.dram_tensor("wu", [22, 128, 1024], F32R, kind="ExternalInput").ap()
    wd = nc.dram_tensor("wd", [8, 128, 2816], F32R, kind="ExternalInput").ap()
    xo = nc.dram_tensor("xo", [bpc, NDC, 128, S], F32, kind="ExternalOutput").ap()
    spills = {n: nc.dram_tensor("sp_" + n, [8, 128, 2816], BF16,
                                kind=("ExternalOutput" if debug_spills else "Internal")).ap()
              for n in ("wg", "wu", "wd")}

    es = ExitStack()
    kb = KB(nc)
    B = alloc_bufs(kb, es)
    emit_setup(kb, B)
    emit_l1ffn_prep(kb, B, wg, wu, wd, spills)
    for b in range(bpc):
        for dc in range(NDC):
            v = kb.op("sync", kb.get_guard(("sx", dc)),
                      lambda e, b=b, dc=dc: e.dma_start(
                          out=B["sx"][:, dc * S:(dc + 1) * S], in_=xt[b, dc]),
                      ("dX", 16))
        for dc in range(NDC):
            kb.mark(("sx", dc), "dX", v)
        dbgfn = None
        if debug_taps and b == 0:
            def dbgfn(kb_, name, ap, waits, guard_keys):
                t = nc.dram_tensor(name, list(ap.shape), ap.dtype, kind="ExternalOutput").ap()
                vv_ = kb_.op("gpsimd", waits,
                             lambda e, t=t, ap=ap: e.dma_start(out=t, in_=ap),
                             ("dG", 16))
                for kk in guard_keys:
                    kb_.set_guard(kk, "dG", vv_)
        emit_l1ffn(kb, B, spills, b, dbg=dbgfn, sc_list=sc_list, skip_ln=skip_ln)
        emit_store_sx(kb, B, [xo[b, dc] for dc in range(NDC)], b)
    kb.op("gpsimd", [(n, kb.now(n)) for n in
                     ("dG", "dX", "dA0", "dA1", "dS0", "dS1", "dR", "dR2", "dM", "dB",
                      "dW", "dV0", "dV1", "dV2", "dV3", "dD") if kb.cnt.get(n)],
          lambda e: e.engine_nop(), None)
    kb.finalize()
    es.close()
    return nc


def kernel(L0_attn_X, L0_attn_Wq, L0_attn_Wk, L0_attn_Wv, L0_attn_Wo, L0_attn_We,
           L0_ffn_weight, L0_ffn_bias, L1_attn_Wq, L1_attn_Wk, L1_attn_Wv,
           L1_attn_Wo, L1_ffn_w_gate, L1_ffn_w_up, L1_ffn_w_down):
    from concourse.bass_utils import run_bass_kernel_spmd
    n_cores = 8
    cores = list(range(n_cores))

    nc1 = build_full_fused(bpc=2)
    shared = {
        "w0wq": chunk_T(L0_attn_Wq.T), "w0wk": chunk_T(L0_attn_Wk.T),
        "w0wv": chunk_T(L0_attn_Wv.T), "w0wo": chunk_T(L0_attn_Wo.T),
        "w0we": chunk_T(L0_attn_We.T),
        "w8": chunk_T(np.ascontiguousarray(L0_ffn_weight.T)),
        "bias": np.ascontiguousarray(L0_ffn_bias.reshape(8, 128).T),
        "w1wq": chunk_T(L1_attn_Wq), "w1wk": chunk_T(L1_attn_Wk),
        "w1wv": chunk_T(L1_attn_Wv), "w1wo": chunk_T(L1_attn_Wo),
    }
    wg3 = L1_ffn_w_gate.reshape(22, 128, 1024)
    wu3 = L1_ffn_w_up.reshape(22, 128, 1024)
    wd3 = L1_ffn_w_down.reshape(8, 128, 2816)
    ins = []
    for c in cores:
        xs = L0_attn_X[c * 2:(c + 1) * 2]
        xtc = np.ascontiguousarray(xs.transpose(0, 2, 1)).reshape(2, NDC, 128, S)
        wsh = np.zeros((7, 128, 2816), np.float32)
        for s in range(6):
            u = c * 6 + s
            if u < 22:
                wsh[s, :, 0:1024] = wg3[u]
            elif u < 44:
                wsh[s, :, 0:1024] = wu3[u - 22]
        wsh[6] = wd3[c]
        d = {"xt": np.ascontiguousarray(xtc), "wsh": wsh}
        d.update(shared)
        ins.append(d)
    r = run_bass_kernel_spmd(nc1, ins, core_ids=cores).results

    out = np.empty((16, 2048, 1024), dtype=np.float32)
    for c in cores:
        xc = np.asarray(r[c]["xo"]).reshape(2, 1024, 2048)
        out[c * 2] = xc[0].T
        out[c * 2 + 1] = xc[1].T
    return out


# version 10
# speedup vs baseline: 1.3946x; 1.2091x over previous
import numpy as np
import concourse.bass as bass
import concourse.mybir as mybir

F32 = mybir.dt.float32
F32R = mybir.dt.float32r
BF16 = mybir.dt.bfloat16
I32 = mybir.dt.int32
AF = mybir.ActivationFunctionType
ALU = mybir.AluOpType
AX = mybir.AxisListType

D = 1024
S = 2048
H = 16
DK = 64
DE = 128
NDC = D // 128
NSC = S // 512
NST = S // 128
FFI = 2816
SCALE = 0.125
EPS = 1e-5


class KB:

    def __init__(self, nc):
        self.nc = nc
        self.items = []
        self.sems = {}
        self.cnt = {}
        self.guard = {}
        self.ready = {}
        self._waited = {}
        self._es = None

    def sem(self, name):
        if name not in self.sems:
            h = self.nc.semaphore(name)
            self.sems[name] = h.__enter__()
            self.cnt[name] = 0
        return self.sems[name]

    def inc(self, name, amount=1):
        self.sem(name)
        self.cnt[name] += amount
        return self.cnt[name]

    def now(self, name):
        self.sem(name)
        return self.cnt[name]

    def add(self, engine, fn):
        self.items.append((engine, fn))

    def set_guard(self, res, semname, val):
        d = self.guard.setdefault(res, {})
        d[semname] = max(d.get(semname, 0), val)

    def clear_guard(self, res):
        self.guard.pop(res, None)

    def get_guard(self, res):
        return list(self.guard.get(res, {}).items())

    def mark(self, res, semname, val):
        self.ready[res] = (semname, val)

    def need(self, *res):
        out = []
        for r in res:
            g = self.ready.get(r)
            if g:
                out.append(g)
        return out

    def vchain(self, steps, waits0=()):
        prev = None
        for i, emit in enumerate(steps):
            w = list(waits0) if i == 0 else [("vC", prev)]
            prev = self.op("vector", w, emit, ("vC", 1))
        return prev

    def op(self, engine, waits, emit, inc=None):
        newval = None
        incsem = None
        incamt = 0
        if inc is not None:
            name, incamt = (inc[0], inc[1]) if len(inc) > 1 else (inc[0], 1)
            newval = self.inc(name, incamt)
            incsem = self.sems[name]
        wl = []
        wmax = self._waited.setdefault(engine, {})
        for n, v in waits:
            if v and v > wmax.get(n, 0):
                wmax[n] = v
                wl.append((self.sem(n), v))

        def fn(eng, wl=wl, emit=emit, incsem=incsem, incamt=incamt):
            for s, v in wl:
                eng.wait_ge(s, v)
            ins = emit(eng)
            if incsem is not None and ins is not None:
                ins.then_inc(incsem, incamt)
        self.add(engine, fn)
        return newval

    def finalize(self):
        self._lp = self.nc.allow_low_precision("f32r intermediates are intentional")
        self._lp.__enter__()
        block = self.nc.Block()
        block.__enter__()
        by_eng = {}
        for e, fn in self.items:
            by_eng.setdefault(e, []).append(fn)
        for e, fns in by_eng.items():
            def body(engine, fns=fns):
                for f in fns:
                    f(engine)
            getattr(block, e)(body)
        block.__exit__(None, None, None)
        self._lp.__exit__(None, None, None)


def fn_engine(e):
    return e


def ceil_div(a, b):
    return (a + b - 1) // b



def chunk_T(w):
    w = np.asarray(w)
    d, f = w.shape
    t = w.reshape(d // 128, 128, f // 128, 128)
    t = t.transpose(2, 0, 1, 3)
    return np.ascontiguousarray(t)


"""Static SBUF / PSUM allocation shared by all sublayer emitters."""


def alloc_bufs(kb, es):
    nc = kb.nc
    B = {}

    def sb(name, shape, dtype):
        h = es.enter_context(nc.sbuf_tensor(name, shape, dtype))
        B[name] = h
        return h

    sb("sx", [128, NDC * S], F32R)
    sb("kt", [128, NDC * S], BF16)
    sb("qt", [128, NDC * S], BF16)
    sb("vv", [128, NST * (H * 65 + 0)], BF16)
    sb("cx", [128, S], BF16)
    sb("ao", [128, NDC * 512], BF16)
    sb("at", [128, 4 * 512], BF16)
    sb("wa", [128, 2 * 1024], F32R)
    sb("wc", [128, 2 * 1024], BF16)
    sb("sq", [128, 2 * 512], F32R)
    sb("rr", [1, 512], F32R)
    sb("br", [64, 512], F32R)
    sb("rows", [1, 3 * 512], F32R)
    sb("onesc", [128, 1], F32R)
    sb("ones1", [1, 128], F32R)
    sb("ident_b", [128, 128], BF16)
    sb("ident_r", [128, 128], F32R)
    sb("ident_f", [128, 128], F32)
    sb("am8", [128, 8], F32)
    sb("onesb", [128, 16], BF16)
    sb("scl", [128, 4], F32)
    sb("biasb", [128, 8], F32)
    sb("am2", [128, 2], F32)
    sb("srow", [1, 160], F32)

    banks = {}
    for n in ("ps0a", "ps1a", "ps0b", "ps1b", "po0", "po1", "pu"):
        banks[n] = es.enter_context(nc.psum_tensor(n, [128, 512], F32))
    banks["pvB"] = es.enter_context(nc.psum_tensor("pvB", [128, 1024], BF16))
    B["banks"] = banks
    return B


def emit_vv_ones(kb, B):
    v1 = None
    for st in range(NST):
        dst = B["vv"][:, st * 1040: (st + 1) * 1040]             .rearrange("p (h c) -> p h c", c=65)[:, :, 64:65]
        src = B["onesb"][:, :].rearrange("p (h o) -> p h o", o=1)
        v1 = kb.op("vector", kb.get_guard("vv") if st == 0 else [],
                   lambda e, dst=dst, src=src: e.tensor_copy(dst, src),
                   ("vC", 1) if st == NST - 1 else None)
    return v1


def emit_setup(kb, B):
    gv = kb.op("gpsimd", [], lambda e: e.memset(B["ident_f"][:, :], 1.0), ("gP", 1))
    v1 = kb.op("vector", [("gP", gv)],
               lambda e: e.tensor_copy(B["onesb"][:, :], B["ident_f"][:, 0:16]), None)
    v1 = kb.op("vector", [],
               lambda e: e.tensor_copy(B["onesc"][:, :], B["ident_f"][:, 0:1]), None)
    v1 = kb.op("vector", [],
               lambda e: e.tensor_copy(B["ones1"][:, :], B["ident_f"][0:1, :]), ("vC", 1))
    v1 = emit_vv_ones(kb, B)
    gv = kb.op("gpsimd", [("vC", v1)], lambda e: e.memset(B["ident_f"][:, :], 0.0), None)
    gv = kb.op("gpsimd", [],
               lambda e: e.affine_select(
                   out=B["ident_f"][:, :], in_=B["ident_f"][:, :],
                   compare_op=ALU.not_equal, fill=1.0, base=0,
                   pattern=[[-1, 128]], channel_multiplier=1),
               ("gP", 1))
    v1 = kb.op("vector", [("gP", gv)],
               lambda e: e.tensor_copy(B["ident_b"][:, :], B["ident_f"][:, :]), None)
    v1 = kb.op("vector", [],
               lambda e: e.tensor_copy(B["ident_r"][:, :], B["ident_f"][:, :]), ("vC", 1))
    kb.mark("consts", "vC", v1)
    kb.mark("consts_g", "gP", gv)


"""LayerNorm over the feature dim, in place on bufs['sx'] ([d,seq] layout).

Stats via ones-matmuls over partitions (accumulated across the 8 d-chunks),
row math on [1,512] vectors, rstd = Exp(-0.5*Ln(var+eps)) (one act table),
broadcast back via rank-1 matmuls, normalize with two TT passes per chunk.
Marks ("sx", dc) and ("sx", dc, "ln") when chunk dc is fully normalized.
"""


def emit_ln(kb, B, rdeps, dbg=None):
    sx, sq = B["sx"], B["sq"]
    rows = B["rows"]
    banks = B["banks"]
    st_name, st = "po1", banks["po1"]
    s2_name, st2 = "pu", banks["pu"]
    b1n, b1 = "ps0a", banks["ps0a"]
    b2n, b2 = "ps1a", banks["ps1a"]
    onesc, ones1 = B["onesc"], B["ones1"]
    lastv = {}

    def sxsl(dc, c0, n):
        return sx[:, dc * S + c0: dc * S + c0 + n]

    for sc in range(NSC):
        tv = None
        for dc in range(NDC):
            waits = [w for w in [kb.ready.get(("sx", dc, "r", sc))] if w]
            waits = [(n, v) for (n, v) in waits]
            if dc == 0:
                waits = waits + kb.get_guard(st_name) + kb.need("consts")
            tv = kb.op("tensor", waits,
                       lambda e, dc=dc, sc=sc: e.matmul(
                           st[0:1, :], onesc[:, 0:1], sxsl(dc, sc * 512, 512),
                           start=(dc == 0), stop=(dc == NDC - 1)),
                       ("tP", 1) if dc == NDC - 1 else None)
        for dc in range(NDC):
            slot = dc % 2
            waits = kb.get_guard(("sq", slot))
            w2 = kb.ready.get(("sx", dc, "r", sc))
            if w2:
                waits = waits + [w2]
            sv = kb.op("scalar", waits,
                       lambda e, dc=dc, sc=sc, slot=slot: e.activation(
                           sq[:, slot * 512:(slot + 1) * 512],
                           sxsl(dc, sc * 512, 512).bitcast(F32), AF.Square),
                       ("sC", 1))
            w3 = [("sC", sv)] + (kb.get_guard(s2_name) if dc == 0 else [])
            tv = kb.op("tensor", w3,
                       lambda e, slot=slot, dc=dc: e.matmul(
                           st2[0:1, :], onesc[:, 0:1], sq[:, slot * 512:(slot + 1) * 512],
                           start=(dc == 0), stop=(dc == NDC - 1)),
                       ("tP", 1) if dc == NDC - 1 else None)
            kb.set_guard(("sq", slot), "tP", tv if tv is not None else kb.now("tP"))
        r_m = rows[:, 0:512]
        r_v = rows[:, 512:1024]
        r_t = rows[:, 1024:1536]
        rr_s = rows[:, 512:1024]
        rr_mr = rows[:, 1024:1536]
        kb.op("vector", [("tP", tv)] + kb.get_guard("rows"),
              lambda e: e.tensor_scalar(r_m, st[0:1, :], 1.0 / D, None, ALU.mult), None)
        kb.op("vector", [],
              lambda e: e.tensor_scalar(r_v, st2[0:1, :], 1.0 / D, None, ALU.mult), None)
        kb.op("vector", [], lambda e: e.tensor_tensor(r_t, r_m, r_m, ALU.mult), None)
        kb.op("vector", [], lambda e: e.tensor_tensor(r_v, r_v, r_t, ALU.subtract), None)
        vv_ = kb.op("vector", [],
                    lambda e: e.tensor_scalar(r_v, r_v, EPS, None, ALU.add), ("vC", 1))
        kb.set_guard(st_name, "vC", vv_)
        kb.set_guard(s2_name, "vC", vv_)
        sv = kb.op("scalar", [("vC", vv_)],
                   lambda e: e.activation(r_t, r_v.bitcast(F32), AF.Ln), None)
        sv = kb.op("scalar", [],
                   lambda e: e.activation(rr_s, r_t.bitcast(F32), AF.Exp, scale=-0.5),
                   ("sC", 1))
        mv = kb.op("vector", [("sC", sv)],
                   lambda e: e.tensor_tensor(rr_mr, r_m, rr_s, ALU.mult), ("vC", 1))
        if dbg is not None and sc == 0:
            dbg(kb, mv)
        bv1 = kb.op("tensor", kb.get_guard(b1n) + [("sC", sv)],
                    lambda e: e.matmul(b1[:, :], ones1[0:1, :], rr_s,
                                       start=True, stop=True), ("tP", 1))
        bv2 = kb.op("tensor", kb.get_guard(b2n) + [("vC", mv)],
                    lambda e: e.matmul(b2[:, :], ones1[0:1, :], rr_mr,
                                       start=True, stop=True), ("tP", 1))
        kb.set_guard("rows", "tP", bv2)
        nv = None
        for dc in range(NDC):
            slot = dc % 2
            waits = [("tP", bv2)] + kb.get_guard(("sq", slot))
            kb.op("vector", waits,
                  lambda e, dc=dc, sc=sc, slot=slot: e.tensor_tensor(
                      sq[:, slot * 512:(slot + 1) * 512],
                      sxsl(dc, sc * 512, 512).bitcast(F32), b1[:, :], ALU.mult),
                  None)
            nv = kb.op("vector", [],
                       lambda e, dc=dc, sc=sc, slot=slot: e.tensor_tensor(
                           sxsl(dc, sc * 512, 512),
                           sq[:, slot * 512:(slot + 1) * 512].bitcast(F32),
                           b2[:, :], ALU.subtract),
                       ("vC", 1))
            kb.set_guard(("sq", slot), "vC", nv)
            lastv[dc] = nv
        kb.set_guard(b1n, "vC", nv)
        kb.set_guard(b2n, "vC", nv)
    for dc in range(NDC):
        kb.mark(("sx", dc), "vC", lastv[dc])
        kb.mark(("sx", dc, "ln"), "vC", lastv[dc])


"""L0 attention sublayer emitter (softmax attn + ctx-influence bias).

x_out = LN(x + Attn(x)) written in place into bufs['sx'];
per-batch |x_out| partition-max column into bufs['am2'][:, b].

See common.py for layouts. psum banks (f32 [128,512] unless noted):
ps0a ps1a ps0b ps1b po0 po1 pu, pvB ([128,1024] bf16).
"""


def wcol_ap(wdram, fc):
    return wdram[fc].transpose([1, 0, 2])


def wdst_ap(buf, off, n=1024):
    return buf[:, off: off + n].rearrange("p (c k) -> p c k", k=128)


def emit_l0attn(kb, B, xt_dram, w, b, dbg=None, nc_=None, skip_ln=False, layer=0):
    sx, kt, qt, vv, cx, ao, at = B["sx"], B["kt"], B["qt"], B["vv"], B["cx"], B["ao"], B["at"]
    wa, wc, rr, br = B["wa"], B["wc"], B["rr"], B["br"]
    banks = B["banks"]
    ps = [banks[n] for n in ("ps0a", "ps1a", "ps0b", "ps1b")]
    psn = ["ps0a", "ps1a", "ps0b", "ps1b"]
    po = [banks["po0"], banks["po1"]]
    pon = ["po0", "po1"]
    pu, po1b, pvB = banks["pu"], banks["po1"], banks["pvB"]

    def sxsl(dc, c0, n):
        return sx[:, dc * S + c0: dc * S + c0 + n]

    emit_vv_ones(kb, B)

    if xt_dram is not None:
        for dc in range(NDC):
            v = kb.op("sync", kb.get_guard(("sx", dc)),
                      lambda e, dc=dc: e.dma_start(
                          out=sx[:, dc * S:(dc + 1) * S], in_=xt_dram[dc]),
                      ("dX", 16))
        for dc in range(NDC):
            kb.mark(("sx", dc), "dX", v)

    def proj_phase(wdram, evac, nfc=NDC):
        util = [("pu", pu), ("po1", po1b)]
        for fc in range(nfc):
            slot = fc % 2
            v = kb.op("sync", kb.get_guard(("wa", slot)),
                      lambda e, wdram=wdram, fc=fc, slot=slot: e.dma_start(
                          out=wdst_ap(wa, slot * 1024), in_=wcol_ap(wdram, fc)),
                      ("dA%d" % slot, 16))
            kb.mark(("wa", slot), "dA%d" % slot, v)
            for sc in range(NSC):
                j = fc * NSC + sc
                bname, bank = util[j % 2]
                tv = None
                for dc in range(NDC):
                    waits = []
                    if dc == 0:
                        waits = kb.get_guard(bname) + kb.need(("wa", slot))
                    waits = waits + kb.need(("sx", dc))
                    lhs = wa[:, slot * 1024 + dc * 128: slot * 1024 + (dc + 1) * 128]
                    rhs = sxsl(dc, sc * 512, 512)
                    last = dc == NDC - 1
                    tv = kb.op("tensor", waits,
                               lambda e, bank=bank, lhs=lhs, rhs=rhs, dc=dc, last=last: e.matmul(
                                   bank[:, :], lhs, rhs, start=(dc == 0), stop=last),
                               ("tP", 1) if last else None)
                if sc == NSC - 1:
                    kb.set_guard(("wa", slot), "tP", tv)
                evac(bname, bank, fc, sc, tv)

    def scalar_evac(dst_ap, bank, bname, tv, scale):
        v = kb.op("scalar", [("tP", tv)],
                  lambda e, dst=dst_ap, bank=bank, scale=scale: e.activation(
                      dst, bank[:, :], AF.Copy, scale=scale),
                  ("sC", 1))
        kb.set_guard(bname, "sC", v)
        return v

    if layer == 0:
        def ctx_evac(bname, bank, fc, sc, tv):
            v = scalar_evac(cx[:, sc * 512:(sc + 1) * 512], bank, bname, tv,
                            float(np.sqrt(0.1)))
            if sc == NSC - 1:
                kb.mark("cx", "sC", v)
        proj_phase(w["we"], ctx_evac, nfc=1)

    def k_evac(bname, bank, fc, sc, tv):
        v = scalar_evac(kt[:, fc * S + sc * 512: fc * S + (sc + 1) * 512], bank, bname, tv, 1.0)
        if fc == NDC - 1 and sc == NSC - 1:
            kb.mark("kt", "sC", v)
    proj_phase(w["wk"], k_evac)

    def v_evac(bname, bank, fc, sc, tv):
        v = scalar_evac(qt[:, fc * S + sc * 512: fc * S + (sc + 1) * 512], bank, bname, tv, 1.0)
        if fc == NDC - 1 and sc == NSC - 1:
            kb.mark("qtv", "sC", v)
    proj_phase(w["wv"], v_evac)

    for fc in range(NDC):
        for half in range(2):
            tv = None
            for stl in range(8):
                st = half * 8 + stl
                waits = kb.need("qtv", "consts") + (kb.get_guard("pvB") if stl == 0 else [])
                tv = kb.op("tensor", waits,
                           lambda e, fc=fc, st=st, stl=stl: e.transpose(
                               pvB[:, stl * 128:(stl + 1) * 128],
                               qt[:, fc * S + st * 128: fc * S + (st + 1) * 128],
                               B["ident_b"][:, :]),
                           ("tP", 1))
            src = pvB[:, 0:1024].rearrange("p (st h c) -> p st h c", h=2, c=64)
            dst = vv[:, :].rearrange("p (st x) -> p st x", x=1040)[
                :, half * 8:(half + 1) * 8, 2 * fc * 65: 2 * fc * 65 + 130] \
                .rearrange("p st (h c) -> p st h c", c=65)[:, :, :, 0:64]
            vcv = kb.op("vector", [("tP", tv)],
                        lambda e, src=src, dst=dst: e.tensor_copy(dst, src),
                        ("vC", 1))
            kb.set_guard("pvB", "vC", vcv)
    kb.set_guard("qtbuf", "tP", kb.now("tP"))
    kb.mark("vv", "vC", kb.now("vC"))

    if dbg:
        dbg(kb, nc_, B, "cxD", cx[:, :], kb.need("cx"))
        dbg(kb, nc_, B, "ktD", kt[:, :], kb.need("kt"))
        dbg(kb, nc_, B, "qtvD", qt[:, :], kb.need("qtv"), guard_res="qtbuf")
        dbg(kb, nc_, B, "vvD", vv[:, :], kb.need("vv"))

    def q_evac(bname, bank, fc, sc, tv):
        waits_extra = kb.get_guard("qtbuf")
        v = kb.op("scalar", [("tP", tv)] + waits_extra,
                  lambda e, fc=fc, sc=sc, bank=bank: e.activation(
                      qt[:, fc * S + sc * 512: fc * S + (sc + 1) * 512],
                      bank[:, :], AF.Copy, scale=SCALE),
                  ("sC", 1))
        kb.set_guard(bname, "sC", v)
        if fc == NDC - 1 and sc == NSC - 1:
            kb.mark("qt", "sC", v)
    proj_phase(w["wq"], q_evac)

    if dbg:
        dbg(kb, nc_, B, "qtD", qt[:, :], kb.need("qt"))

    for qc in range(NSC):
        for hp in range(NDC):
            hpo = hp * S
            qco = qc * 512
            avdone = [None, None]
            nkt = NST if layer == 0 else 4 * qc + 4

            def emit_av(ktile, expv, off):
                pr = (ktile % 2) * 2
                for par in (0, 1):
                    slot = pr + par
                    h = 2 * hp + par
                    waits = [("sC", expv[par])] + kb.need("vv", "consts")
                    if ktile == 0:
                        waits = waits + kb.get_guard(pon[par])
                    tv = kb.op("tensor", waits,
                               lambda e, par=par, slot=slot, h=h, kt0=ktile, off=off,
                                      last=(ktile == nkt - 1): e.matmul(
                                   po[par][0:65, off:512],
                                   vv[:, kt0 * 1040 + h * 65: kt0 * 1040 + (h + 1) * 65],
                                   at[:, slot * 512 + off:(slot + 1) * 512],
                                   start=(kt0 == 0), stop=last),
                               ("tP", 1))
                    kb.set_guard(("at", slot), "tP", tv)
                    avdone[par] = tv

            pend = None
            for ktile in range(nkt):
                pr = (ktile % 2) * 2
                off = 0
                if layer == 1:
                    r = ktile - 4 * qc
                    off = max(0, r) * 128
                expv = [None, None]
                for par in (0, 1):
                    bname, bank = psn[pr + par], ps[pr + par]
                    r0, r1 = par * 64, par * 64 + 64
                    waits = kb.get_guard(bname) + kb.need("kt", "qt")
                    kb.op("tensor", waits,
                          lambda e, bank=bank, hpo=hpo, qco=qco, kt0=ktile * 128,
                                 r0=r0, r1=r1, off=off: e.matmul(
                              bank[:, off:512], kt[r0:r1, hpo + kt0: hpo + kt0 + 128],
                              qt[r0:r1, hpo + qco + off: hpo + qco + 512],
                              start=True, stop=(layer == 1),
                              tile_position=(r0, 0)),
                          ("tP", 1) if layer == 1 else None)
                tvs = [None, None]
                if layer == 0:
                    for par in (0, 1):
                        bname, bank = psn[pr + par], ps[pr + par]
                        tvs[par] = kb.op("tensor", kb.need("cx"),
                                         lambda e, bank=bank, qco=qco, kt0=ktile * 128: e.matmul(
                                             bank[:, :], cx[:, kt0: kt0 + 128],
                                             cx[:, qco: qco + 512],
                                             start=False, stop=True),
                                         ("tP", 1))
                else:
                    tvs = [kb.now("tP"), kb.now("tP")]
                for par in (0, 1):
                    bname, bank = psn[pr + par], ps[pr + par]
                    slot = pr + par
                    waits = [("tP", tvs[par])] + kb.get_guard(("at", slot))
                    ev = kb.op("scalar", waits,
                               lambda e, bank=bank, slot=slot, off=off: e.activation(
                                   at[:, slot * 512 + off:(slot + 1) * 512],
                                   bank[:, off:512], AF.Exp),
                               ("sC", 1))
                    kb.set_guard(bname, "sC", ev)
                    expv[par] = ev
                if pend is not None:
                    emit_av(*pend)
                pend = (ktile, expv, off)
            emit_av(*pend)
            for par in (0, 1):
                h = 2 * hp + par
                waits = [("tP", avdone[par])] + kb.get_guard("rr")
                rv = kb.op("vector", waits,
                           lambda e, par=par: e.reciprocal(rr[:, :], po[par][64:65, :]),
                           ("vC", 1))
                bw = kb.get_guard("pu") + [("vC", rv)] + kb.need("consts")
                bv = kb.op("tensor", bw,
                           lambda e: e.matmul(pu[0:64, :], B["ones1"][0:1, 0:64], rr[:, :],
                                              start=True, stop=True),
                           ("tP", 1))
                kb.set_guard("rr", "tP", bv)
                cv = kb.op("vector", [("tP", bv)],
                           lambda e: e.tensor_copy(br[:, :], pu[0:64, :]),
                           ("vC", 1))
                kb.set_guard("pu", "vC", cv)
                dst = ao[par * 64: par * 64 + 64, hp * 512:(hp + 1) * 512]
                mv = kb.op("vector", [("vC", cv)] + kb.get_guard(("ao", hp, par)),
                           lambda e, par=par, dst=dst: e.tensor_tensor(
                               dst, po[par][0:64, :], br[:, :], ALU.mult),
                           ("vC", 1))
                kb.set_guard(pon[par], "vC", mv)
                kb.set_guard("br", "vC", mv)
                kb.mark(("ao", hp, par), "vC", mv)

        if dbg and qc == 0:
            v_ = dbg(kb, nc_, B, "aoD", ao[:, :],
                     kb.need(*[("ao", d_, p_) for d_ in range(NDC) for p_ in (0, 1)]))
            for d_ in range(NDC):
                for p_ in (0, 1):
                    kb.set_guard(("ao", d_, p_), "dG", v_)

        util = [("pu", pu), ("po1", po1b)]
        for fc in range(NDC):
            slot = fc % 2
            v = kb.op("sync", kb.get_guard(("wa", slot)),
                      lambda e, fc=fc, slot=slot: e.dma_start(
                          out=wdst_ap(wa, slot * 1024), in_=wcol_ap(w["wo"], fc)),
                      ("dA%d" % slot, 16))
            kb.mark(("wa", slot), "dA%d" % slot, v)
            vcw = kb.op("vector", [("dA%d" % slot, v)] + kb.get_guard(("wc", slot)),
                        lambda e, slot=slot: e.tensor_copy(
                            wc[:, slot * 1024:(slot + 1) * 1024],
                            wa[:, slot * 1024:(slot + 1) * 1024].bitcast(F32)),
                        ("vC", 1))
            kb.set_guard(("wa", slot), "vC", vcw)
            kb.mark(("wc", slot), "vC", vcw)
            bname, bank = util[fc % 2]
            tv = None
            for dc in range(NDC):
                waits = kb.need(("ao", dc, 0), ("ao", dc, 1))
                if dc == 0:
                    waits = waits + kb.get_guard(bname) + kb.need(("wc", slot))
                lhs = wc[:, slot * 1024 + dc * 128: slot * 1024 + (dc + 1) * 128]
                rhs = ao[:, dc * 512:(dc + 1) * 512]
                last = dc == NDC - 1
                tv = kb.op("tensor", waits,
                           lambda e, bank=bank, lhs=lhs, rhs=rhs, dc=dc, last=last: e.matmul(
                               bank[:, :], lhs, rhs, start=(dc == 0), stop=last),
                           ("tP", 1) if last else None)
            kb.set_guard(("wc", slot), "tP", tv)
            dst = sxsl(fc, qc * 512, 512)
            v = kb.op("vector", [("tP", tv)],
                      lambda e, dst=dst, bank=bank: e.tensor_tensor(
                          dst, bank[:, :], dst.bitcast(F32), ALU.add),
                      ("vC", 1))
            kb.set_guard(bname, "vC", v)
            kb.mark(("sx", fc, "r", qc), "vC", v)
        for dc in range(NDC):
            for par in (0, 1):
                kb.set_guard(("ao", dc, par), "tP", kb.now("tP"))

    tnow = kb.now("tP")
    for rname in ("vv", "cx", "kt", "qt"):
        kb.set_guard(rname, "tP", tnow)

    if skip_ln:
        for dc in range(NDC):
            lv = max(kb.ready[("sx", dc, "r", qc)][1] for qc in range(NSC))
            kb.mark(("sx", dc), "vC", lv)
            kb.mark(("sx", dc, "ln"), "vC", lv)
    else:
        emit_ln(kb, B, rdeps=None)

    if layer == 1:
        return
    last = None
    for dc in range(NDC):
        src = sx[:, dc * S:(dc + 1) * S]
        dst = B["am8"][:, dc:dc + 1]
        last = kb.op("vector", kb.need(("sx", dc, "ln")) +
                     ([("vC", last)] if last else []),
                     lambda e, src=src, dst=dst: e.reduce_max(
                         dst, src.bitcast(F32), axis=AX.X, apply_absolute_value=True),
                     ("vC", 1))
        kb.set_guard(("sx", dc), "vC", last)
    v = kb.vchain([
        lambda e, b=b: e.reduce_max(B["am2"][:, b:b + 1], B["am8"][:, :],
                                    axis=AX.X, apply_absolute_value=True),
        lambda e: e.engine_nop(),
    ], waits0=[("vC", last)])
    kb.mark(("am2", b), "vC", v)


"""L0 FFN sublayer: global-amax fp8(e4m3fn) quantized linear + bias, bf16
round, residual, LN.   x_out = LN(x + bf16(acc*osc + bias)) in place on sx.

Buffer reuse during this phase: vv (bitcast f32r) holds the staged weight,
qt holds the quantized weight (bf16), kt holds quantized x (bf16),
wc/at (bitcast) are elementwise scratch, ao holds bf16 ffn outputs.
scl [128, 4] f32: col0 = w-scale, col1 = x-scale, col2 = osc.
"""


def emit_fp8_quant(kb, B, src, dst, scl_col, n, extra=()):
    sqf = B["ao"][:, 0:2 * n].bitcast(F32)[:, 0:n]
    sqi = B["ao"][:, 0:2 * n].bitcast(I32)[:, 0:n]
    wcf = B["wc"][:, 0:2 * n].bitcast(F32)[:, 0:n]
    msk = B["at"][:, 0:2 * n].bitcast(I32)[:, 0:n]
    scl = B["scl"]
    g = kb.get_guard(("aoq", 0)) + kb.get_guard(("aoq", 1)) + \
        kb.get_guard("wcscr") + kb.get_guard("atscr") + list(extra)
    kb.op("vector", g + kb.need("scl"),
          lambda e: e.tensor_scalar(sqf, src.bitcast(F32),
                                    scl[:, scl_col:scl_col + 1], None, ALU.mult), None)
    kb.op("vector", [],
          lambda e: e.tensor_scalar(sqf, sqf, 448.0, None, ALU.min), None)
    kb.op("vector", [],
          lambda e: e.tensor_scalar(sqf, sqf, -448.0, None, ALU.max), None)
    kb.op("vector", [],
          lambda e: e.tensor_scalar(wcf, sqf, 24576.0, None, ALU.add), None)
    kb.op("vector", [],
          lambda e: e.tensor_scalar(wcf, wcf, 24576.0, None, ALU.subtract), None)
    kb.op("vector", [],
          lambda e: e.tensor_scalar(msk, sqi, 0x7FFFFFFF, None, ALU.bitwise_and), None)
    kb.op("vector", [],
          lambda e: e.tensor_scalar(msk, msk, 0x3C800000, None, ALU.is_lt), None)
    kb.op("vector", [],
          lambda e: e.tensor_scalar(sqi, sqi, 0x80000, None, ALU.add), None)
    kb.op("vector", [],
          lambda e: e.tensor_scalar(sqi, sqi, ~0xFFFFF, None, ALU.bitwise_and), None)
    kb.op("vector", [],
          lambda e: e.select(sqf, msk, wcf, sqf), None)
    v = kb.op("vector", [],
              lambda e: e.tensor_copy(dst, sqf), ("vC", 1))
    for r in (("aoq", 0), ("aoq", 1), "wcscr", "atscr"):
        kb.set_guard(r, "vC", v)
    return v


def emit_scale_rows(kb, B, col_amax, col_out, col_tmp=152):
    srow = B["srow"]
    return kb.vchain([
        lambda e: e.tensor_scalar(srow[0:1, col_tmp:col_tmp + 1],
                                  srow[0:1, col_amax:col_amax + 1],
                                  1e-12, None, ALU.max),
        lambda e: e.reciprocal(srow[0:1, col_tmp + 1:col_tmp + 2],
                               srow[0:1, col_tmp:col_tmp + 1]),
        lambda e: e.tensor_scalar(srow[0:1, col_out:col_out + 1],
                                  srow[0:1, col_tmp + 1:col_tmp + 2],
                                  448.0, None, ALU.mult),
        lambda e: e.tensor_scalar(srow[0:1, col_out:col_out + 1],
                                  srow[0:1, col_out:col_out + 1],
                                  448.0, None, ALU.min),
    ])


def emit_l0ffn_prep(kb, B, w8_dram, bias_dram, amaxes_dram, amscr_dram, sclscr_dram, fused_cc=None):
    nc = kb.nc
    banks = B["banks"]
    vvr = B["vv"][:, 0:16384].bitcast(F32R)
    qt, kt, sq, srow, scl = B["qt"], B["kt"], B["sq"], B["srow"], B["scl"]

    for fc in range(NDC):
        v = kb.op("sync", kb.get_guard("vv"),
                  lambda e, fc=fc: e.dma_start(
                      out=wdst_ap(vvr, fc * 1024), in_=wcol_ap(w8_dram, fc)),
                  ("dS", 16))
    kb.mark("w8stage", "dS", v)

    last = None
    for fc in range(NDC):
        last = kb.op("vector", kb.need("w8stage") + kb.get_guard("am8") +
                     ([("vC", last)] if last else []),
                     lambda e, fc=fc: e.reduce_max(
                         B["am8"][:, fc:fc + 1], vvr[:, fc * 1024:(fc + 1) * 1024].bitcast(F32),
                         axis=AX.X, apply_absolute_value=True),
                     ("vC", 1))
    v = kb.vchain([
        lambda e: e.reduce_max(sq[:, 0:1], B["am8"][:, :], axis=AX.X,
                               apply_absolute_value=True),
        lambda e: e.engine_nop(),
    ], waits0=[("vC", last)])
    v = kb.op("gpsimd", [("vC", v)],
              lambda e: e.dma_start(out=amscr_dram, in_=sq[:, 0:1].bitcast(F32)),
              ("dM", 16))
    v = kb.op("sync", [("dM", v)],
              lambda e: e.dma_start(out=srow[0:1, 0:128],
                                    in_=amscr_dram.rearrange("p o -> o p")),
              ("dR", 16))
    v = kb.vchain([
        lambda e: e.reduce_max(srow[0:1, 129:130], srow[0:1, 0:128],
                               axis=AX.X, apply_absolute_value=False),
        lambda e: e.engine_nop(),
    ], waits0=[("dR", v)])
    v_w = v

    if amaxes_dram is not None:
        v2 = kb.op("sync", [],
                   lambda e: e.dma_start(out=srow[0:1, 144:152], in_=amaxes_dram),
                   ("dR2", 16))
        v2 = kb.vchain([
            lambda e: e.reduce_max(srow[0:1, 128:129], srow[0:1, 144:152],
                                   axis=AX.X, apply_absolute_value=False),
            lambda e: e.engine_nop(),
        ], waits0=[("dR2", v2)])
    else:
        cc_in_t, cc_out_t, amscr2_dram = fused_cc
        v2 = kb.vchain([
            lambda e: e.tensor_tensor(sq[:, 1:2], B["am2"][:, 0:1],
                                      B["am2"][:, 1:2], ALU.max),
            lambda e: e.engine_nop(),
        ], waits0=kb.need(("am2", 0), ("am2", 1)))
        v2 = kb.op("gpsimd", [("vC", v2)],
                   lambda e: e.dma_start(out=amscr2_dram, in_=sq[:, 1:2].bitcast(F32)),
                   ("dM", 16))
        v2 = kb.op("sync", [("dM", v2), ("vC", v_w)],
                   lambda e: e.dma_start(out=srow[0:1, 0:128],
                                         in_=amscr2_dram.rearrange("p o -> o p")),
                   ("dR", 16))
        v2 = kb.op("vector", [("dR", v2)],
                   lambda e: e.reduce_max(srow[0:1, 130:131], srow[0:1, 0:128],
                                          axis=AX.X, apply_absolute_value=False),
                   ("vC", 1))
        v2 = kb.op("gpsimd", [("vC", v2)],
                   lambda e: e.dma_start(out=cc_in_t.ap(), in_=srow[0:1, 130:138]),
                   ("dM", 16))
        v2 = kb.op("gpsimd", [("dM", v2)],
                   lambda e: e.collective_compute(
                       "AllReduce", ALU.max, replica_groups=[list(range(8))],
                       ins=[cc_in_t.ap()], outs=[cc_out_t.ap()]),
                   ("cc", 1))
        v2 = kb.op("sync", [("cc", v2)],
                   lambda e: e.dma_start(out=srow[0:1, 128:129],
                                         in_=cc_out_t.ap()[0:1, 0:1]),
                   ("dR", 16))
        v2 = kb.op("vector", [("dR", v2)], lambda e: e.engine_nop(), ("vC", 1))

    emit_scale_rows(kb, B, 129, 131, 152)
    emit_scale_rows(kb, B, 128, 132, 154)
    kb.vchain([
        lambda e: e.reciprocal(srow[0:1, 134:135], srow[0:1, 131:132]),
        lambda e: e.reciprocal(srow[0:1, 135:136], srow[0:1, 132:133]),
        lambda e: e.tensor_tensor(srow[0:1, 133:134], srow[0:1, 134:135],
                                  srow[0:1, 135:136], ALU.mult),
    ])
    v3 = kb.op("vector", [], lambda e: e.engine_nop(), ("vC", 1))
    v3 = kb.op("gpsimd", [("vC", v3)],
               lambda e: e.dma_start(out=sclscr_dram, in_=srow[0:1, 131:135]),
               ("dM", 16))
    v3 = kb.op("sync", [("dM", v3)],
               lambda e: e.dma_start(out=scl[:, 0:4],
                                     in_=sclscr_dram.broadcast_to([128, 4])),
               ("dR", 16))
    kb.mark("scl", "dR", v3)

    for fc in range(NDC):
        v = emit_fp8_quant(kb, B, vvr[:, fc * 1024:(fc + 1) * 1024],
                           qt[:, fc * 1024:(fc + 1) * 1024], 0, 1024,
                           extra=kb.get_guard("qt"))
    kb.mark("wq8", "vC", v)
    kb.set_guard("vv", "vC", v)

    v = kb.op("sync", [],
              lambda e: e.dma_start(out=B["biasb"][:, :], in_=bias_dram),
              ("dB", 16))
    kb.mark("biasb", "dB", v)


def emit_l0ffn(kb, B, b, skip_ln=False, ln_dbg=None):
    sx, kt, qt, ao = B["sx"], B["kt"], B["qt"], B["ao"]
    banks = B["banks"]
    util = [("pu", banks["pu"]), ("po1", banks["po1"])]

    def sxsl(dc, c0, nn):
        return sx[:, dc * S + c0: dc * S + c0 + nn]

    for dc in range(NDC):
        for hf in range(2):
            v = emit_fp8_quant(
                kb, B, sxsl(dc, hf * 1024, 1024),
                kt[:, dc * S + hf * 1024: dc * S + (hf + 1) * 1024], 1, 1024,
                extra=kb.get_guard("kt") + kb.need(("sx", dc)))
    kb.mark("xq8", "vC", v)

    for fc in range(NDC):
        for sc in range(NSC):
            j = fc * NSC + sc
            bname, bank = util[j % 2]
            tv = None
            for dc in range(NDC):
                waits = kb.need("xq8", "wq8") + (kb.get_guard(bname) if dc == 0 else [])
                tv = kb.op("tensor", waits,
                           lambda e, bank=bank, fc=fc, sc=sc, dc=dc: e.matmul(
                               bank[:, :],
                               qt[:, fc * 1024 + dc * 128: fc * 1024 + (dc + 1) * 128],
                               kt[:, dc * S + sc * 512: dc * S + (sc + 1) * 512],
                               start=(dc == 0), stop=(dc == NDC - 1)),
                           ("tP", 1) if dc == NDC - 1 else None)
            dst_b = ao[:, (j % 4) * 512: (j % 4 + 1) * 512]
            v = kb.op("vector", [("tP", tv)] + kb.need("scl", "biasb") + kb.get_guard(("aoq", j % 4)),
                      lambda e, bank=bank, dst_b=dst_b, fc=fc: e.tensor_scalar(
                          dst_b, bank[:, :], B["scl"][:, 2:3], B["biasb"][:, fc:fc + 1],
                          ALU.mult, ALU.add),
                      ("vC", 1))
            kb.set_guard(bname, "vC", v)
            dst = sxsl(fc, sc * 512, 512)
            v = kb.op("vector", [],
                      lambda e, dst=dst, dst_b=dst_b: e.tensor_tensor(
                          dst, dst_b, dst.bitcast(F32), ALU.add),
                      ("vC", 1))
            kb.set_guard(("aoq", j % 4), "vC", v)
            kb.mark(("sx", fc, "r", sc), "vC", v)
    if skip_ln:
        for dc in range(NDC):
            lv = max(kb.ready[("sx", dc, "r", qq)][1] for qq in range(NSC))
            kb.mark(("sx", dc), "vC", lv)
            kb.mark(("sx", dc, "ln"), "vC", lv)
    else:
        emit_ln(kb, B, rdeps=None, dbg=ln_dbg)


"""L1 FFN: blockwise-fp4 (E2M1, block 16) quant-dequant SwiGLU MLP.

out = fp4linear(silu(fp4linear(x,wg)) * fp4linear(x,wu), wd);
x_out = LN(x + out) in place on sx.

Quantized+transposed weights are spilled to HBM scratch once (prep) and
streamed back during the matmuls (they don't fit in SBUF).

Buffer roles in this phase:
  kt  = xq4T (quantized x, transposed back to [d,seq])     [128, 8*2048] bf16
  qt  = hqT tiles (cols 0:11264) + int scratch + wdq stream
  vv  = weight staging (prep, f32r view) / gate+up rhs stream slots (4x4096)
  cx  = natural quantized staging (x / h tiles)
  ao(f32 view) = src + value scratch ; wc(f32 view) = acc scratch
  at  = silu output tiles (bf16)
  sq  = blockmax / scale / rcp rows (f32r)
"""

MIDS = [0.25, 0.75, 1.25, 1.75, 2.5, 2.5, 3.5, 3.5, 5.0, 5.0, 5.0, 5.0]


def emit_fp4_quant(kb, B, srcf, srci, dst, n, c_i32, waits0=()):
    nb = n // 16
    sq = B["sq"]
    bm = sq[:, 0:nb]
    s_c = sq[:, 64:64 + nb]
    rcp = sq[:, 128:128 + nb]
    A = B["ao"][:, 2048:2048 + 2 * n].bitcast(F32)[:, 0:n]
    A_i = B["ao"][:, 2048:2048 + 2 * n].bitcast(I32)[:, 0:n]
    AC = B["wc"][:, 0:2 * n].bitcast(F32)[:, 0:n]
    T_f = B["at"][:, 0:2 * n].bitcast(F32)[:, 0:n]
    T_i = B["at"][:, 0:2 * n].bitcast(I32)[:, 0:n]
    g = kb.get_guard("fp4scr") + list(waits0)
    rcp_b = rcp.rearrange("p (b o) -> p b o", o=1).broadcast_to([128, nb, 16])
    kb.vchain([
        lambda e: e.reduce_max(bm, srcf.rearrange("p (b w) -> p b w", w=16),
                               axis=AX.X, apply_absolute_value=True),
        lambda e: e.tensor_scalar(s_c, bm, 1.0 / 6.0, None, ALU.mult),
        lambda e: e.tensor_scalar(s_c, s_c, 1e-12, None, ALU.max),
        lambda e: e.reciprocal(rcp, s_c),
        lambda e: e.tensor_scalar(c_i32, srci, 0x7FFFFFFF, None, ALU.bitwise_and),
        lambda e: e.tensor_tensor(A.rearrange("p (b w) -> p b w", w=16),
                                  c_i32.bitcast(F32).rearrange("p (b w) -> p b w", w=16),
                                  rcp_b, ALU.mult),
    ], waits0=g)
    kb.op("vector", [],
          lambda e: e.tensor_scalar(c_i32, A_i, 0x3FFFFF, None, ALU.bitwise_and), None)
    kb.op("vector", [],
          lambda e: e.tensor_scalar(c_i32, c_i32, 0x200000, None, ALU.is_gt), None)
    kb.op("vector", [],
          lambda e: e.tensor_scalar(T_i, A_i, ~0x3FFFFF, None, ALU.bitwise_and), None)
    kb.op("vector", [],
          lambda e: e.scalar_tensor_tensor(T_i, c_i32, 0x400000, T_i,
                                           ALU.mult, ALU.add), None)
    kb.op("vector", [],
          lambda e: e.tensor_scalar(T_i, T_i, ~0x3FFFFF, None, ALU.bitwise_and), None)
    kb.op("vector", [],
          lambda e: e.tensor_scalar(AC, A, 0.25, None, ALU.is_gt), None)
    kb.op("vector", [],
          lambda e: e.scalar_tensor_tensor(AC, A, 0.75, AC, ALU.is_gt, ALU.add), None)
    kb.op("vector", [],
          lambda e: e.tensor_scalar(AC, AC, 0.5, None, ALU.mult), None)
    kb.op("vector", [],
          lambda e: e.tensor_scalar(c_i32, A_i, 0x3F800000, None, ALU.is_lt), None)
    kb.op("vector", [],
          lambda e: e.select(A, c_i32, AC, T_f), None)
    sc_b = s_c.rearrange("p (b o) -> p b o", o=1).broadcast_to([128, nb, 16])
    kb.op("vector", [],
          lambda e: e.tensor_tensor(T_f.rearrange("p (b w) -> p b w", w=16),
                                    A.rearrange("p (b w) -> p b w", w=16),
                                    sc_b, ALU.mult), None)
    kb.op("vector", [],
          lambda e: e.tensor_scalar(c_i32, srci, -0x80000000, None, ALU.bitwise_and), None)
    kb.op("vector", [],
          lambda e: e.tensor_tensor(T_i, T_i, c_i32, ALU.bitwise_or), None)
    v = kb.op("vector", [], lambda e: e.tensor_copy(dst, T_f), ("vC", 1))
    kb.set_guard("fp4scr", "vC", v)
    return v





def emit_l1ffn_prep(kb, B, wg_dram, wu_dram, wd_dram, spills):
    nc = kb.nc
    vv, kt, qt, pvB = B["vv"], B["kt"], B["qt"], B["banks"]["pvB"]
    vvr = vv[:, 0:16640].bitcast(F32R)

    def quant_transpose_spill(src_slot_f32r, n, dst_stage, spill_ap, qdst, skey):
        nblk = n // 128
        off = 0
        lastq = None
        while off < n:
            m = min(1024, n - off)
            lastq = emit_fp4_quant(
                kb, B, src_slot_f32r[:, off:off + m].bitcast(F32),
                src_slot_f32r[:, off:off + m].bitcast(I32),
                qdst[:, off:off + m], m,
                B["qt"][:, 11264:11264 + 2 * m].bitcast(I32)[:, 0:m])
            off += m
        done = 0
        while done < nblk:
            cnt = min(8, nblk - done)
            tv = None
            for j in range(cnt):
                waits = ([("vC", lastq)] + kb.get_guard("pvB") + kb.need("consts")) if j == 0 else []
                tv = kb.op("tensor", waits,
                           lambda e, j=j, done=done, qdst=qdst: e.transpose(
                               pvB[:, j * 128:(j + 1) * 128],
                               qdst[:, (done + j) * 128:(done + j + 1) * 128],
                               B["ident_b"][:, :]),
                           ("tP", 1))
            v = kb.op("vector", [("tP", tv)] + kb.get_guard(skey),
                      lambda e, cnt=cnt, done=done: e.tensor_copy(
                          dst_stage[:, done * 128:(done + cnt) * 128],
                          pvB[:, 0:cnt * 128]),
                      ("vC", 1))
            kb.set_guard("pvB", "vC", v)
            done += cnt
        kb.set_guard("fp4scr", "tP", kb.now("tP"))
        sv = None
        for dst_ap, c0, c1 in spill_ap:
            sv = kb.op("gpsimd", [("vC", v)],
                       lambda e, dst_ap=dst_ap, c0=c0, c1=c1: e.dma_start(
                           out=dst_ap, in_=dst_stage[:, c0:c1]),
                       ("dW", 16))
        kb.set_guard(skey, "dW", sv)
        return sv

    sv = None
    for name, wdram in (("wg", wg_dram), ("wu", wu_dram)):
        for nt in range(22):
            slot = nt % 2
            lv = kb.op("sync", kb.get_guard(("vvs", slot)),
                       lambda e, wdram=wdram, nt=nt, slot=slot: e.dma_start(
                           out=vvr[:, slot * 1024:(slot + 1) * 1024], in_=wdram[nt]),
                       ("dS%d" % slot, 16))
            src = vvr[:, slot * 1024:(slot + 1) * 1024]
            stage = kt[:, 2048 + slot * 1024: 2048 + (slot + 1) * 1024]
            qdst = kt[:, slot * 1024:(slot + 1) * 1024]
            kb.set_guard("fp4scr", "dS%d" % slot, lv)
            sv = quant_transpose_spill(
                src, 1024, stage,
                [(spills[name][j, :, nt * 128:(nt + 1) * 128], j * 128, (j + 1) * 128)
                 for j in range(8)],
                qdst, ("stg", slot))
            kb.set_guard(("vvs", slot), "vC", kb.now("vC"))
    kb.set_guard("fp4scr", "dW", kb.now("dW"))
    kb.set_guard(("vvs2", 0), "vC", kb.now("vC"))
    kb.set_guard(("vvs2", 1), "vC", kb.now("vC"))
    for dc in range(8):
        slot = dc % 2
        lv = kb.op("sync", kb.get_guard(("vvs2", slot)),
                   lambda e, dc=dc, slot=slot: e.dma_start(
                       out=vvr[:, slot * 2816:(slot + 1) * 2816], in_=wd_dram[dc]),
                   ("dS%d" % slot, 16))
        src = vvr[:, slot * 2816:(slot + 1) * 2816]
        stage = kt[:, 8192 + slot * 2816: 8192 + (slot + 1) * 2816]
        qdst = kt[:, slot * 2816 + 0: slot * 2816 + 2816]
        kb.set_guard("fp4scr", "dS%d" % slot, lv)
        sv = quant_transpose_spill(src, 2816, stage,
                                   [(spills["wd"][dc], 0, 2816)], qdst,
                                   ("stg2", slot))
        kb.set_guard(("vvs2", slot), "vC", kb.now("vC"))
    kb.mark("spilled", "dW", kb.now("dW"))
    tnow = kb.now("vC")
    for r in ("vv", "kt", "qt"):
        kb.set_guard(r, "vC", tnow)


def emit_l1ffn_prep_sharded(kb, B, wsh_dram, ccg_in, ccg_out, spills):
    nc = kb.nc
    vv, kt, pvB = B["vv"], B["kt"], B["banks"]["pvB"]
    vvr = vv[:, 0:16640].bitcast(F32R)

    sv = None
    for slot in range(7):
        n = 2816 if slot == 6 else 1024
        dslot = slot % 2
        lv = kb.op("sync", kb.get_guard(("vvs", dslot)),
                   lambda e, slot=slot, dslot=dslot, n=n: e.dma_start(
                       out=vvr[:, dslot * 2816: dslot * 2816 + n],
                       in_=wsh_dram[slot][:, 0:n]),
                   ("dS%d" % dslot, 16))
        src = vvr[:, dslot * 2816: dslot * 2816 + n]
        stage = kt[:, 8192 + dslot * 2816: 8192 + dslot * 2816 + n]
        qdst = kt[:, dslot * 2816: dslot * 2816 + n]
        kb.set_guard("fp4scr", "dS%d" % dslot, lv)
        nblk = n // 128
        off = 0
        lastq = None
        while off < n:
            m = min(1024, n - off)
            lastq = emit_fp4_quant(
                kb, B, src[:, off:off + m].bitcast(F32),
                src[:, off:off + m].bitcast(I32),
                qdst[:, off:off + m], m,
                B["qt"][:, 11264:11264 + 2 * m].bitcast(I32)[:, 0:m])
            off += m
        done = 0
        while done < nblk:
            cnt = min(8, nblk - done)
            tv = None
            for j in range(cnt):
                waits = ([("vC", lastq)] + kb.get_guard("pvB") + kb.need("consts")) if j == 0 else []
                tv = kb.op("tensor", waits,
                           lambda e, j=j, done=done, qdst=qdst: e.transpose(
                               pvB[:, j * 128:(j + 1) * 128],
                               qdst[:, (done + j) * 128:(done + j + 1) * 128],
                               B["ident_b"][:, :]),
                           ("tP", 1))
            v = kb.op("vector", [("tP", tv)] + kb.get_guard(("stg", dslot)),
                      lambda e, cnt=cnt, done=done, stage=stage: e.tensor_copy(
                          stage[:, done * 128:(done + cnt) * 128],
                          pvB[:, 0:cnt * 128]),
                      ("vC", 1))
            kb.set_guard("pvB", "vC", v)
            done += cnt
        kb.set_guard("fp4scr", "tP", kb.now("tP"))
        sv = kb.op("gpsimd", [("vC", v)],
                   lambda e, slot=slot, stage=stage, n=n: e.dma_start(
                       out=ccg_in[slot][:, 0:n], in_=stage[:, 0:n]),
                   ("dW", 16))
        kb.set_guard(("stg", dslot), "dW", sv)
        kb.set_guard(("vvs", dslot), "vC", kb.now("vC"))
    cv = kb.op("gpsimd", [("dW", kb.now("dW"))],
               lambda e: e.collective_compute(
                   "AllGather", ALU.bypass, replica_groups=[list(range(8))],
                   ins=[ccg_in], outs=[ccg_out]),
               ("cc", 1))
    def runs(units):
        out = []
        cur = None
        for u in units:
            gi = (u // 6) * 7 + u % 6
            if cur and gi == cur[0] + cur[2]:
                cur[2] += 1
            else:
                cur = [gi, u, 1]
                out.append(cur)
        return out
    rv = None
    for mat, base in (("wg", 0), ("wu", 22)):
        for kc in range(NDC):
            for gi, u0, cnt in runs(range(base, base + 22)):
                nt0 = u0 - base
                rv = kb.op("sync", [("cc", cv)],
                           lambda e, mat=mat, kc=kc, gi=gi, nt0=nt0, cnt=cnt: e.dma_start(
                               out=spills[mat][kc][:, nt0 * 128:(nt0 + cnt) * 128]
                                   .rearrange("p (t c) -> p t c", c=128),
                               in_=ccg_out[gi:gi + cnt, :, kc * 128:(kc + 1) * 128]
                                   .transpose([1, 0, 2])),
                           ("dW", 16))
    for dc in range(NDC):
        rv = kb.op("sync", [("cc", cv)],
                   lambda e, dc=dc: e.dma_start(
                       out=spills["wd"][dc], in_=ccg_out[dc * 7 + 6]),
                   ("dW", 16))
    kb.mark("spilled", "dW", rv)
    tnow = kb.now("vC")
    for r in ("vv", "kt", "qt"):
        kb.set_guard(r, "vC", tnow)


def emit_l1ffn(kb, B, spills, b, dbg=None, sc_list=None, skip_ln=False):
    nc = kb.nc
    sx, kt, qt, vv, cx, at, ao = B["sx"], B["kt"], B["qt"], B["vv"], B["cx"], B["at"], B["ao"]
    for r in ("cx", "kt", "qt", "vv", "at"):
        for (sn, gv) in kb.get_guard(r):
            kb.set_guard("fp4scr", sn, gv)
    for dcq in range(NDC):
        for par in (0, 1):
            for (sn, gv) in kb.get_guard(("ao", dcq, par)):
                kb.set_guard("xnat", sn, gv)
                kb.set_guard("fp4scr", sn, gv)
    for (sn, gv) in kb.get_guard("qt"):
        for ntc_ in range(6):
            for st_ in range(4):
                kb.set_guard(("hqT", ntc_, st_), sn, gv)
        kb.set_guard(("wdqs", 0), sn, gv)
        kb.set_guard(("wdqs", 1), sn, gv)
    banks = B["banks"]
    pvB = banks["pvB"]
    pu, po1 = banks["pu"], banks["po1"]
    gb = [banks["ps0a"], banks["ps0b"]]
    gbn = ["ps0a", "ps0b"]
    ub = [banks["ps1a"], banks["ps1b"]]
    ubn = ["ps1a", "ps1b"]

    def sxsl(dc, c0, nn):
        return sx[:, dc * S + c0: dc * S + c0 + nn]

    for st in range(NST):
        for half in range(2):
            bank = pu if half == 0 else po1
            bname = "pu" if half == 0 else "po1"
            bview = bank[:, :].bitcast(F32R)
            tv = None
            for j in range(4):
                dc = half * 4 + j
                waits = (kb.get_guard(bname) + kb.need(("sx", dc), "consts")) if j == 0 else kb.need(("sx", dc))
                tv = kb.op("tensor", waits,
                           lambda e, dc=dc, st=st, j=j, bview=bview: e.transpose(
                               bview[:, j * 128:(j + 1) * 128],
                               sx[:, dc * S + st * 128: dc * S + (st + 1) * 128],
                               B["ident_r"][:, :]),
                           ("tP", 1))
            v = kb.op("vector", [("tP", tv)] + (kb.get_guard("xnat") if half == 0 else []),
                      lambda e, half=half, bview=bview: e.tensor_copy(
                          ao[:, 0:2048].bitcast(F32)[:, half * 512:(half + 1) * 512], bview),
                      ("vC", 1))
            kb.set_guard(bname, "vC", v)
        xnat_f = ao[:, 0:2048].bitcast(F32)
        xnat_i = ao[:, 0:2048].bitcast(I32)
        qv = emit_fp4_quant(kb, B, xnat_f, xnat_i, cx[:, 0:1024], 1024,
                            qt[:, 11264:13312].bitcast(I32)[:, 0:1024])
        kb.set_guard("xnat", "vC", qv)
        tv = None
        for j in range(8):
            waits = ([("vC", qv)] + kb.get_guard("pvB")) if j == 0 else []
            tv = kb.op("tensor", waits,
                       lambda e, j=j: e.transpose(
                           pvB[:, j * 128:(j + 1) * 128],
                           cx[:, j * 128:(j + 1) * 128], B["ident_b"][:, :]),
                       ("tP", 1))
        dst = kt[:, :].rearrange("p (dc s) -> p dc s", s=S)[:, :, st * 128:(st + 1) * 128]
        v = kb.op("vector", [("tP", tv)] + (kb.get_guard("kt") if st == 0 else []),
                  lambda e, dst=dst: e.tensor_copy(dst, pvB[:, 0:1024].rearrange(
                      "p (dc s) -> p dc s", s=128)),
                  ("vC", 1))
        kb.set_guard("pvB", "vC", v)
        kb.set_guard("cxq", "vC", v)
    kb.mark("xq4", "vC", kb.now("vC"))

    NTC = [512, 512, 512, 512, 512, 256]
    for sc in (sc_list if sc_list is not None else range(NSC)):
        hq_done = None
        for ntc in range(6):
            nw = NTC[ntc]
            nnt = nw // 128
            gslot = ntc % 2
            uslot = 2 + ntc % 2
            for name, slot in (("wg", gslot), ("wu", uslot)):
                lv = None
                for kc in range(NDC):
                    w0 = (kb.get_guard(("vvq", slot)) + kb.get_guard("vv")
                          + kb.need("spilled")) if kc == 0 else []
                    lv = kb.op("sync", w0,
                               lambda e, name=name, slot=slot, ntc=ntc,
                                      kc=kc, nw=nw: e.dma_start(
                                   out=vv[:, slot * 4096 + kc * nw: slot * 4096 + (kc + 1) * nw],
                                   in_=spills[name][kc, :, ntc * 512: ntc * 512 + nw]),
                               ("dV%d" % slot, 16))
                kb.mark(("vvq", slot), "dV%d" % slot, lv)
            for st in range(4):
                stg = sc * 4 + st
                gbank, gname = gb[st % 2], gbn[st % 2]
                ubank, uname = ub[st % 2], ubn[st % 2]
                tg = tu = None
                for kc in range(NDC):
                    waits = kb.need("xq4", ("vvq", gslot)) + (kb.get_guard(gname) if kc == 0 else [])
                    tg = kb.op("tensor", waits,
                               lambda e, kc=kc, stg=stg, gslot=gslot, nw=nw, gbank=gbank: e.matmul(
                                   gbank[:, 0:nw],
                                   kt[:, kc * S + stg * 128: kc * S + (stg + 1) * 128],
                                   vv[:, gslot * 4096 + kc * nw: gslot * 4096 + (kc + 1) * nw],
                                   start=(kc == 0), stop=(kc == NDC - 1)),
                               ("tP", 1) if kc == NDC - 1 else None)
                for kc in range(NDC):
                    waits = kb.need(("vvq", uslot)) + (kb.get_guard(uname) if kc == 0 else [])
                    tu = kb.op("tensor", waits,
                               lambda e, kc=kc, stg=stg, uslot=uslot, nw=nw, ubank=ubank: e.matmul(
                                   ubank[:, 0:nw],
                                   kt[:, kc * S + stg * 128: kc * S + (stg + 1) * 128],
                                   vv[:, uslot * 4096 + kc * nw: uslot * 4096 + (kc + 1) * nw],
                                   start=(kc == 0), stop=(kc == NDC - 1)),
                               ("tP", 1) if kc == NDC - 1 else None)
                atsl = ao[:, 2048:2048 + 2 * nw].bitcast(F32)[:, 0:nw]
                hv = kb.op("scalar", [("tP", tg)] + kb.get_guard("xnat"),
                           lambda e, gbank=gbank, atsl=atsl, nw=nw: e.activation(
                               atsl, gbank[:, 0:nw], AF.Silu),
                           ("sC", 1))
                kb.set_guard(gname, "sC", hv)
                srcsl = ao[:, 0:2048].bitcast(F32)[:, 0:nw]
                hv2 = kb.op("vector", [("sC", hv), ("tP", tu)] + kb.get_guard("xnat"),
                            lambda e, atsl=atsl, srcsl=srcsl, ubank=ubank, nw=nw: e.tensor_tensor(
                                srcsl, atsl, ubank[:, 0:nw], ALU.mult),
                            ("vC", 1))
                kb.set_guard(uname, "vC", hv2)
                kb.set_guard("xnat", "vC", hv2)
                qv = emit_fp4_quant(kb, B, srcsl, ao[:, 0:2048].bitcast(I32)[:, 0:nw],
                                    cx[:, 0:nw], nw,
                                    qt[:, 11264:13312].bitcast(I32)[:, 0:nw],
                                    waits0=[("vC", hv2)])
                kb.set_guard("xnat", "vC", qv)
                tv = None
                for j in range(nnt):
                    waits = ([("vC", qv)] + kb.get_guard("pvB")) if j == 0 else []
                    tv = kb.op("tensor", waits,
                               lambda e, j=j: e.transpose(
                                   pvB[:, j * 128:(j + 1) * 128],
                                   cx[:, j * 128:(j + 1) * 128], B["ident_b"][:, :]),
                               ("tP", 1))
                dst = qt[:, :].rearrange("p (kc s) -> p kc s", s=512)[
                    :, ntc * 4: ntc * 4 + nnt, st * 128:(st + 1) * 128]
                hq_done = kb.op("vector", [("tP", tv)] + kb.get_guard(("hqT", ntc, st)),
                                lambda e, dst=dst, nnt=nnt: e.tensor_copy(
                                    dst, pvB[:, 0:nnt * 128].rearrange(
                                        "p (kc s) -> p kc s", s=128)),
                                ("vC", 1))
                kb.set_guard("pvB", "vC", hq_done)
                kb.mark(("hqT", ntc, st), "vC", hq_done)
                if st == 3:
                    kb.set_guard(("vvq", gslot), "tP", kb.now("tP"))
                    kb.set_guard(("vvq", uslot), "tP", kb.now("tP"))
        if dbg is not None and sc == 0:
            dbg(kb, "hq0D", qt[:, 0:11264],
                kb.need(*[("hqT", n_, s_) for n_ in range(6) for s_ in range(4)]),
                [("hqT", n_, s_) for n_ in range(6) for s_ in range(4)])
        util = [("pu", pu), ("po1", po1)]
        for fc in range(NDC):
            bname, bank = util[fc % 2]
            tv = None
            for half in range(2):
                k0, k1 = (0, 11) if half == 0 else (11, 22)
                lv = kb.op("sync", kb.get_guard(("wdqs", half)) + kb.need("spilled"),
                           lambda e, fc=fc, k0=k0, k1=k1: e.dma_start(
                               out=qt[:, 13312 + k0 * 128: 13312 + k1 * 128],
                               in_=spills["wd"][fc][:, k0 * 128: k1 * 128]),
                           ("dD", 16))
                for kc in range(k0, k1):
                    waits = [("dD", lv)] + kb.need(*[("hqT", kc // 4, stq) for stq in range(4)])
                    if kc == 0:
                        waits = waits + kb.get_guard(bname)
                    tv = kb.op("tensor", waits,
                               lambda e, kc=kc, bank=bank: e.matmul(
                                   bank[:, :],
                                   qt[:, 13312 + kc * 128: 13312 + (kc + 1) * 128],
                                   qt[:, kc * 512:(kc + 1) * 512],
                                   start=(kc == 0), stop=(kc == 21)),
                               ("tP", 1) if kc == k1 - 1 else None)
                kb.set_guard(("wdqs", half), "tP", tv)
            dst = sxsl(fc, sc * 512, 512)
            v = kb.op("vector", [("tP", tv)],
                      lambda e, dst=dst, bank=bank: e.tensor_tensor(
                          dst, bank[:, :], dst.bitcast(F32), ALU.add),
                      ("vC", 1))
            kb.set_guard(bname, "vC", v)
            kb.mark(("sx", fc, "r", sc), "vC", v)
        if dbg is not None and sc == 0:
            dbg(kb, "r0D", sx[:, :],
                kb.need(*[("sx", f_, "r", 0) for f_ in range(NDC)]),
                [("sx", f_) for f_ in range(NDC)])
        for ntc in range(6):
            for st in range(4):
                kb.set_guard(("hqT", ntc, st), "tP", kb.now("tP"))
    if skip_ln:
        for dc in range(NDC):
            vals = [kb.ready[("sx", dc, "r", qq)][1] for qq in (sc_list or range(NSC))
                    if ("sx", dc, "r", qq) in kb.ready]
            kb.mark(("sx", dc), "vC", max(vals))
            kb.mark(("sx", dc, "ln"), "vC", max(vals))
    else:
        vnow = kb.now("vC")
        kb.set_guard(("sq", 0), "vC", vnow)
        kb.set_guard(("sq", 1), "vC", vnow)
        emit_ln(kb, B, rdeps=None)


"""Standalone single-sublayer launch builders (for incremental validation,
and for the UNFUSED multi-launch kernel checkpoint)."""
from contextlib import ExitStack
import numpy as np
import concourse.bass as bass


def prep_l0attn_inputs(X, Wq, Wk, Wv, Wo, We, core, n_cores=8):
    bpc = X.shape[0] // n_cores
    xs = X[core * bpc:(core + 1) * bpc]
    xt = np.ascontiguousarray(xs.transpose(0, 2, 1))
    xt = np.ascontiguousarray(xt.reshape(bpc, NDC, 128, S))
    return {
        "xt": xt,
        "wq": chunk_T(Wq.T), "wk": chunk_T(Wk.T),
        "wv": chunk_T(Wv.T), "wo": chunk_T(Wo.T), "we": chunk_T(We.T),
    }


def emit_store_sx(kb, B, xo_dram, b):
    for dc in range(NDC):
        v = kb.op("gpsimd", kb.need(("sx", dc)),
                  lambda e, dc=dc: e.dma_start(
                      out=xo_dram[dc], in_=B["sx"][:, dc * S:(dc + 1) * S].bitcast(F32)),
                  ("dG", 16))
    for dc in range(NDC):
        kb.set_guard(("sx", dc), "dG", v)
    return v


def emit_amax_local(kb, B, amscr_dram):
    sq = B["sq"]
    v = kb.vchain([
        lambda e: e.tensor_tensor(sq[:, 0:1], B["am2"][:, 0:1], B["am2"][:, 1:2], ALU.max),
        lambda e: e.engine_nop(),
    ], waits0=kb.need(("am2", 0), ("am2", 1)))
    v = kb.op("gpsimd", [("vC", v)],
              lambda e: e.dma_start(out=amscr_dram, in_=sq[:, 0:1].bitcast(F32)),
              ("dM", 16))
    v = kb.op("sync", [("dM", v)],
              lambda e: e.dma_start(out=B["srow"][0:1, 0:128],
                                    in_=amscr_dram.rearrange("p o -> o p")),
              ("dR", 16))
    v = kb.vchain([
        lambda e: e.reduce_max(B["srow"][0:1, 128:129], B["srow"][0:1, 0:128],
                               axis=AX.X, apply_absolute_value=False),
        lambda e: e.engine_nop(),
    ], waits0=[("dR", v)])
    kb.mark("amax_local", "vC", v)
    return v


def emit_fence(kb):
    sems = [(n, kb.cnt[n]) for n in list(kb.cnt) if kb.cnt[n] > 0]
    for eng in ("tensor", "vector", "scalar", "sync", "gpsimd"):
        kb.op(eng, sems, lambda e: None, None)


def emit_spill_sx_r(kb, B, dst_dram):
    v = None
    for dc in range(NDC):
        v = kb.op("gpsimd", kb.need(("sx", dc)) + kb.get_guard(("sxsp", dc)),
                  lambda e, dc=dc: e.dma_start(
                      out=dst_dram[dc], in_=B["sx"][:, dc * S:(dc + 1) * S]),
                  ("dG", 16))
    for dc in range(NDC):
        kb.set_guard(("sx", dc), "dG", v)
    return v


def emit_load_sx_r(kb, B, src_dram):
    v = None
    for dc in range(NDC):
        v = kb.op("sync", kb.get_guard(("sx", dc)),
                  lambda e, dc=dc: e.dma_start(
                      out=B["sx"][:, dc * S:(dc + 1) * S], in_=src_dram[dc]),
                  ("dX", 16))
    for dc in range(NDC):
        kb.mark(("sx", dc), "dX", v)
    return v


def emit_tap(kb, nc, B, name, src_ap, waits, guard_res=None):
    shp = list(src_ap.shape)
    t = nc.dram_tensor(name, shp, src_ap.dtype, kind="ExternalOutput").ap()
    v = kb.op("gpsimd", waits,
              lambda e, t=t, src_ap=src_ap: e.dma_start(out=t, in_=src_ap),
              ("dG", 16))
    if guard_res:
        kb.set_guard(guard_res, "dG", v)
    return v


def build_l0attn_standalone(bpc=2, debug=False, skip_ln=False):
    nc = bass.Bass("TRN2", target_bir_lowering=False, dynamic_dma_scratch_size=16384,
                   detect_race_conditions=False)
    xt = nc.dram_tensor("xt", [bpc, NDC, 128, S], F32R, kind="ExternalInput").ap()
    w = {}
    for n in ("wq", "wk", "wv", "we"):
        shp = [1, NDC, 128, 128] if n == "we" else [NDC, NDC, 128, 128]
        w[n] = nc.dram_tensor(n, shp, F32R, kind="ExternalInput").ap()
    w["wo"] = nc.dram_tensor("wo", [NDC, NDC, 128, 128], F32R, kind="ExternalInput").ap()
    xo = nc.dram_tensor("xo", [bpc, NDC, 128, S], F32, kind="ExternalOutput").ap()
    am = nc.dram_tensor("am", [1, 1], F32, kind="ExternalOutput").ap()
    amscr = nc.dram_tensor("amscr", [128, 1], F32).ap()

    es = ExitStack()
    kb = KB(nc)
    B = alloc_bufs(kb, es)
    emit_setup(kb, B)
    for b in range(bpc):
        emit_l0attn(kb, B, [xt[b, dc] for dc in range(NDC)], w, b,
                    dbg=(emit_tap if (debug and b == 0) else None), nc_=nc,
                    skip_ln=skip_ln)
        emit_store_sx(kb, B, [xo[b, dc] for dc in range(NDC)], b)
    emit_amax_local(kb, B, amscr)
    v = kb.op("gpsimd", kb.need("amax_local"),
              lambda e: e.dma_start(out=am, in_=B["srow"][0:1, 128:129]),
              ("dG", 16))
    kb.op("gpsimd", [(n, kb.now(n)) for n in
                     ("dG", "dX", "dA0", "dA1", "dS", "dR", "dR2", "dM", "dB") if kb.cnt.get(n)],
          lambda e: e.engine_nop(), None)
    kb.finalize()
    es.close()
    return nc


def prep_l0ffn_inputs(X1, weight, bias, amaxes, core, n_cores=8):
    bpc = X1.shape[0] // n_cores
    xt = X1[core * bpc:(core + 1) * bpc]
    return {
        "xt": np.ascontiguousarray(xt),
        "w8": chunk_T(np.ascontiguousarray(weight.T)),
        "bias": np.ascontiguousarray(bias.reshape(8, 128).T),
        "amaxes": np.ascontiguousarray(amaxes.reshape(1, 8).astype(np.float32)),
    }


def build_l0ffn_standalone(bpc=2, skip_ln=False, debug=False):
    nc = bass.Bass("TRN2", target_bir_lowering=False, dynamic_dma_scratch_size=16384,
                   detect_race_conditions=False)
    xt = nc.dram_tensor("xt", [bpc, NDC, 128, S], F32R, kind="ExternalInput").ap()
    w8 = nc.dram_tensor("w8", [NDC, NDC, 128, 128], F32R, kind="ExternalInput").ap()
    bias = nc.dram_tensor("bias", [128, 8], F32, kind="ExternalInput").ap()
    amaxes = nc.dram_tensor("amaxes", [1, 8], F32, kind="ExternalInput").ap()
    xo = nc.dram_tensor("xo", [bpc, NDC, 128, S], F32, kind="ExternalOutput").ap()
    amscr = nc.dram_tensor("amscr", [128, 1], F32).ap()
    sclscr = nc.dram_tensor("sclscr", [1, 4], F32).ap()

    es = ExitStack()
    kb = KB(nc)
    B = alloc_bufs(kb, es)
    emit_setup(kb, B)
    emit_l0ffn_prep(kb, B, w8, bias, amaxes, amscr, sclscr)
    for b in range(bpc):
        for dc in range(NDC):
            v = kb.op("sync", kb.get_guard(("sx", dc)),
                      lambda e, b=b, dc=dc: e.dma_start(
                          out=B["sx"][:, dc * S:(dc + 1) * S], in_=xt[b, dc]),
                      ("dX", 16))
        for dc in range(NDC):
            kb.mark(("sx", dc), "dX", v)
        rowtap = None
        if debug and b == 0:
            rt_t = nc.dram_tensor("rowsD", [1, 1536], F32, kind="ExternalOutput").ap()
            def rowtap(kb_, mv, rt_t=rt_t):
                vv2 = kb_.op("gpsimd", [("vC", mv)],
                       lambda e: e.dma_start(out=rt_t[0:1, 0:1536],
                                             in_=B["rows"][0:1, :].bitcast(F32)),
                       ("dG", 16))
                kb_.set_guard("rows", "dG", vv2)
        emit_l0ffn(kb, B, b, skip_ln=skip_ln, ln_dbg=rowtap)
        if debug and b == 0:
            emit_tap(kb, nc, B, "ktD", B["kt"][:, :], kb.need("xq8"), guard_res="kt")
            emit_tap(kb, nc, B, "qtD", B["qt"][:, 0:8192], kb.need("wq8"), guard_res="qt")
            emit_tap(kb, nc, B, "sclD", B["scl"][:, 0:3], kb.need("scl"))
            emit_tap(kb, nc, B, "srowD", B["srow"][0:1, 128:130], kb.need("scl"))
        emit_store_sx(kb, B, [xo[b, dc] for dc in range(NDC)], b)
    kb.op("gpsimd", [(n, kb.now(n)) for n in
                     ("dG", "dX", "dA0", "dA1", "dS", "dR", "dR2", "dM", "dB") if kb.cnt.get(n)],
          lambda e: e.engine_nop(), None)
    kb.finalize()
    es.close()
    return nc


def prep_l1attn_inputs(X1, Wq, Wk, Wv, Wo, core, n_cores=8):
    bpc = X1.shape[0] // n_cores
    return {
        "xt": np.ascontiguousarray(X1[core * bpc:(core + 1) * bpc]),
        "wq": chunk_T(Wq), "wk": chunk_T(Wk), "wv": chunk_T(Wv), "wo": chunk_T(Wo),
    }


def build_l1attn_standalone(bpc=2, skip_ln=False):
    nc = bass.Bass("TRN2", target_bir_lowering=False, dynamic_dma_scratch_size=16384,
                   detect_race_conditions=False)
    xt = nc.dram_tensor("xt", [bpc, NDC, 128, S], F32R, kind="ExternalInput").ap()
    w = {}
    for n in ("wq", "wk", "wv", "wo"):
        w[n] = nc.dram_tensor(n, [NDC, NDC, 128, 128], F32R, kind="ExternalInput").ap()
    xo = nc.dram_tensor("xo", [bpc, NDC, 128, S], F32, kind="ExternalOutput").ap()

    es = ExitStack()
    kb = KB(nc)
    B = alloc_bufs(kb, es)
    emit_setup(kb, B)
    for b in range(bpc):
        emit_l0attn(kb, B, [xt[b, dc] for dc in range(NDC)], w, b,
                    skip_ln=skip_ln, layer=1)
        emit_store_sx(kb, B, [xo[b, dc] for dc in range(NDC)], b)
    kb.op("gpsimd", [(n, kb.now(n)) for n in
                     ("dG", "dX", "dA0", "dA1", "dS", "dR", "dR2", "dM", "dB") if kb.cnt.get(n)],
          lambda e: e.engine_nop(), None)
    kb.finalize()
    es.close()
    return nc


def build_full_fused(bpc=2, debug=False):
    nc = bass.Bass("TRN2", target_bir_lowering=False, dynamic_dma_scratch_size=16384,
                   detect_race_conditions=False)
    xt = nc.dram_tensor("xt", [bpc, NDC, 128, S], F32R, kind="ExternalInput").ap()
    w0 = {}
    for n in ("wq", "wk", "wv", "wo", "we"):
        shp = [1, NDC, 128, 128] if n == "we" else [NDC, NDC, 128, 128]
        w0[n] = nc.dram_tensor("w0" + n, shp, F32R, kind="ExternalInput").ap()
    w8 = nc.dram_tensor("w8", [NDC, NDC, 128, 128], F32R, kind="ExternalInput").ap()
    bias = nc.dram_tensor("bias", [128, 8], F32, kind="ExternalInput").ap()
    w1 = {n: nc.dram_tensor("w1" + n, [NDC, NDC, 128, 128], F32R,
                            kind="ExternalInput").ap()
          for n in ("wq", "wk", "wv", "wo")}
    wsh = nc.dram_tensor("wsh", [7, 128, 2816], F32R, kind="ExternalInput").ap()
    ccg_in = nc.dram_tensor("ccg_in", [7, 128, 2816], BF16).ap()
    ccg_out = nc.dram_tensor("ccg_out", [56, 128, 2816], BF16).ap()
    xo = nc.dram_tensor("xo", [bpc, NDC, 128, S], F32, kind="ExternalOutput").ap()
    dbgk = "ExternalOutput" if debug else "Internal"
    x1scr = nc.dram_tensor("x1scr", [NDC, 128, S], F32R, kind=dbgk).ap()
    x2scr = nc.dram_tensor("x2scr", [NDC, 128, S], F32R, kind=dbgk).ap()
    if debug:
        xd2b0 = nc.dram_tensor("xd2b0", [NDC, 128, S], F32R, kind="ExternalOutput").ap()
        xd3b0 = nc.dram_tensor("xd3b0", [NDC, 128, S], F32R, kind="ExternalOutput").ap()
        xd3b1 = nc.dram_tensor("xd3b1", [NDC, 128, S], F32R, kind="ExternalOutput").ap()
    amscr = nc.dram_tensor("amscr", [128, 1], F32).ap()
    amscr2 = nc.dram_tensor("amscr2", [128, 1], F32).ap()
    sclscr = nc.dram_tensor("sclscr", [1, 4], F32).ap()
    cc_in_t = nc.dram_tensor("cc_in", [1, 8], F32)
    cc_out_t = nc.dram_tensor("cc_out", [1, 8], F32)
    spills = {n: nc.dram_tensor("sp_" + n, [8, 128, 2816], BF16).ap()
              for n in ("wg", "wu", "wd")}

    es = ExitStack()
    kb = KB(nc)
    B = alloc_bufs(kb, es)
    emit_setup(kb, B)
    emit_l1ffn_prep_sharded(kb, B, wsh, ccg_in, ccg_out, spills)
    emit_fence(kb)
    emit_l0attn(kb, B, [xt[0, dc] for dc in range(NDC)], w0, 0)
    emit_spill_sx_r(kb, B, x1scr)
    emit_l0attn(kb, B, [xt[1, dc] for dc in range(NDC)], w0, 1)
    emit_fence(kb)
    emit_l0ffn_prep(kb, B, w8, bias, None, amscr, sclscr,
                    fused_cc=(cc_in_t, cc_out_t, amscr2))
    emit_fence(kb)
    emit_l0ffn(kb, B, 1)
    emit_fence(kb)
    emit_spill_sx_r(kb, B, x2scr)
    emit_fence(kb)
    emit_load_sx_r(kb, B, x1scr)
    emit_l0ffn(kb, B, 0)
    emit_fence(kb)
    if debug:
        emit_spill_sx_r(kb, B, xd2b0)
        emit_fence(kb)
    emit_l0attn(kb, B, None, w1, 0, layer=1)
    emit_fence(kb)
    if debug:
        emit_spill_sx_r(kb, B, xd3b0)
        emit_fence(kb)
    emit_l1ffn(kb, B, spills, 0)
    emit_fence(kb)
    emit_store_sx(kb, B, [xo[0, dc] for dc in range(NDC)], 0)
    emit_fence(kb)
    emit_load_sx_r(kb, B, x2scr)
    if debug:
        emit_tap(kb, nc, B, "sxldD", B["sx"][:, :].bitcast(F32),
                 kb.need(*[("sx", dc) for dc in range(NDC)]))
        emit_fence(kb)
    emit_l0attn(kb, B, None, w1, 1, layer=1,
                dbg=(emit_tap if debug else None), nc_=nc)
    emit_fence(kb)
    if debug:
        emit_spill_sx_r(kb, B, xd3b1)
        emit_fence(kb)
    emit_l1ffn(kb, B, spills, 1)
    emit_fence(kb)
    emit_store_sx(kb, B, [xo[1, dc] for dc in range(NDC)], 1)
    emit_fence(kb)
    kb.finalize()
    es.close()
    return nc


def prep_l1ffn_inputs(X1, wg, wu, wd, core, n_cores=8):
    bpc = X1.shape[0] // n_cores
    return {
        "xt": np.ascontiguousarray(X1[core * bpc:(core + 1) * bpc]),
        "wg": np.ascontiguousarray(wg.reshape(22, 128, 1024)),
        "wu": np.ascontiguousarray(wu.reshape(22, 128, 1024)),
        "wd": np.ascontiguousarray(wd.reshape(8, 128, 2816)),
    }


def build_l1ffn_standalone(bpc=2, skip_ln=False, debug_spills=False, debug_taps=False, sc_list=None):
    nc = bass.Bass("TRN2", target_bir_lowering=False, dynamic_dma_scratch_size=16384,
                   detect_race_conditions=False)
    xt = nc.dram_tensor("xt", [bpc, NDC, 128, S], F32R, kind="ExternalInput").ap()
    wg = nc.dram_tensor("wg", [22, 128, 1024], F32R, kind="ExternalInput").ap()
    wu = nc.dram_tensor("wu", [22, 128, 1024], F32R, kind="ExternalInput").ap()
    wd = nc.dram_tensor("wd", [8, 128, 2816], F32R, kind="ExternalInput").ap()
    xo = nc.dram_tensor("xo", [bpc, NDC, 128, S], F32, kind="ExternalOutput").ap()
    spills = {n: nc.dram_tensor("sp_" + n, [8, 128, 2816], BF16,
                                kind=("ExternalOutput" if debug_spills else "Internal")).ap()
              for n in ("wg", "wu", "wd")}

    es = ExitStack()
    kb = KB(nc)
    B = alloc_bufs(kb, es)
    emit_setup(kb, B)
    emit_l1ffn_prep(kb, B, wg, wu, wd, spills)
    for b in range(bpc):
        for dc in range(NDC):
            v = kb.op("sync", kb.get_guard(("sx", dc)),
                      lambda e, b=b, dc=dc: e.dma_start(
                          out=B["sx"][:, dc * S:(dc + 1) * S], in_=xt[b, dc]),
                      ("dX", 16))
        for dc in range(NDC):
            kb.mark(("sx", dc), "dX", v)
        dbgfn = None
        if debug_taps and b == 0:
            def dbgfn(kb_, name, ap, waits, guard_keys):
                t = nc.dram_tensor(name, list(ap.shape), ap.dtype, kind="ExternalOutput").ap()
                vv_ = kb_.op("gpsimd", waits,
                             lambda e, t=t, ap=ap: e.dma_start(out=t, in_=ap),
                             ("dG", 16))
                for kk in guard_keys:
                    kb_.set_guard(kk, "dG", vv_)
        emit_l1ffn(kb, B, spills, b, dbg=dbgfn, sc_list=sc_list, skip_ln=skip_ln)
        emit_store_sx(kb, B, [xo[b, dc] for dc in range(NDC)], b)
    kb.op("gpsimd", [(n, kb.now(n)) for n in
                     ("dG", "dX", "dA0", "dA1", "dS0", "dS1", "dR", "dR2", "dM", "dB",
                      "dW", "dV0", "dV1", "dV2", "dV3", "dD") if kb.cnt.get(n)],
          lambda e: e.engine_nop(), None)
    kb.finalize()
    es.close()
    return nc


def kernel(L0_attn_X, L0_attn_Wq, L0_attn_Wk, L0_attn_Wv, L0_attn_Wo, L0_attn_We,
           L0_ffn_weight, L0_ffn_bias, L1_attn_Wq, L1_attn_Wk, L1_attn_Wv,
           L1_attn_Wo, L1_ffn_w_gate, L1_ffn_w_up, L1_ffn_w_down):
    from concourse.bass_utils import run_bass_kernel_spmd
    n_cores = 8
    cores = list(range(n_cores))

    nc1 = build_full_fused(bpc=2)
    shared = {
        "w0wq": chunk_T(L0_attn_Wq.T), "w0wk": chunk_T(L0_attn_Wk.T),
        "w0wv": chunk_T(L0_attn_Wv.T), "w0wo": chunk_T(L0_attn_Wo.T),
        "w0we": chunk_T(L0_attn_We.T),
        "w8": chunk_T(np.ascontiguousarray(L0_ffn_weight.T)),
        "bias": np.ascontiguousarray(L0_ffn_bias.reshape(8, 128).T),
        "w1wq": chunk_T(L1_attn_Wq), "w1wk": chunk_T(L1_attn_Wk),
        "w1wv": chunk_T(L1_attn_Wv), "w1wo": chunk_T(L1_attn_Wo),
    }
    wg3 = L1_ffn_w_gate.reshape(22, 128, 1024)
    wu3 = L1_ffn_w_up.reshape(22, 128, 1024)
    wd3 = L1_ffn_w_down.reshape(8, 128, 2816)
    ins = []
    for c in cores:
        xs = L0_attn_X[c * 2:(c + 1) * 2]
        xtc = np.ascontiguousarray(xs.transpose(0, 2, 1)).reshape(2, NDC, 128, S)
        wsh = np.zeros((7, 128, 2816), np.float32)
        for s in range(6):
            u = c * 6 + s
            if u < 22:
                wsh[s, :, 0:1024] = wg3[u]
            elif u < 44:
                wsh[s, :, 0:1024] = wu3[u - 22]
        wsh[6] = wd3[c]
        d = {"xt": np.ascontiguousarray(xtc), "wsh": wsh}
        d.update(shared)
        ins.append(d)
    r = run_bass_kernel_spmd(nc1, ins, core_ids=cores).results

    out = np.empty((16, 2048, 1024), dtype=np.float32)
    for c in cores:
        xc = np.asarray(r[c]["xo"]).reshape(2, 1024, 2048)
        out[c * 2] = xc[0].T
        out[c * 2 + 1] = xc[1].T
    return out
